# version 27
# baseline (speedup 1.0000x reference)
import os

import numpy as np
import ml_dtypes

import concourse.bass as bass
import concourse.mybir as mybir
from concourse.bass_utils import run_bass_kernel_spmd

F32 = mybir.dt.float32
BF16 = mybir.dt.bfloat16
FP8 = mybir.dt.float8e4
FP8E5 = mybir.dt.float8e5
ALU = mybir.AluOpType
ACT = mybir.ActivationFunctionType

T = 2048
D = 1024
KC = 8
E_INNER = 2048
EPS = 1e-5
NSUB = int(os.environ.get("NSUB", "8"))
STAGE = int(os.environ.get("STAGE", "99"))
L1BST = int(os.environ.get("L1BST", "9"))

ENGINES = ("T", "V", "S", "G", "Y")
DMA_SEMS = ("dY0", "dY1", "dY2", "dY3", "dYC", "dG", "dS")


class Prog:
    def __init__(self):
        self.ops = {e: [] for e in ENGINES}
        self.cnt = {s: 0 for s in ("sT", "sV", "sS", "sG") + DMA_SEMS}
        self.waited = {e: {} for e in ENGINES}

    OWN = {"T": "sT", "V": "sV", "S": "sS", "G": "sG"}

    def emit(self, eng, fn, waits=(), inc=None, drain=None, nodrain=False):
        own = self.OWN.get(eng)
        waits = [w for w in waits if w is not None and w[0] != own]
        eff = []
        for s, v in waits:
            if eng != "G" and self.waited[eng].get(s, 0) >= v:
                continue
            self.waited[eng][s] = max(self.waited[eng].get(s, 0), v)
            eff.append((s, v))
        tok = None
        if inc is not None:
            amt = 16 if inc in DMA_SEMS else 1
            self.cnt[inc] += amt
            tok = (inc, self.cnt[inc])
        is_dma = inc in DMA_SEMS
        do_drain = ((eng in ("V", "S", "G")) or drain is True) and not is_dma \
            and not (nodrain and inc is None)

        def op(e, sems, eff=eff, fn=fn, inc=inc, do_drain=do_drain,
               is_dma=is_dma):
            for s, v in eff:
                e.wait_ge(sems[s], v)
            inst = fn(e)
            if do_drain:
                inst = e.drain()
            if inc is not None:
                inst.then_inc(sems[inc], 16 if is_dma else 1)

        import traceback
        if os.environ.get("PROV"):
            fr = traceback.extract_stack()[-3:-1]
            op._prov = " | ".join(f"{f.name}:{f.lineno}" for f in fr)
        self.ops[eng].append(op)
        return tok

    bar_toks = {}

    def barrier(self):
        pre = {s: self.cnt[s] for s in ("sT", "sV", "sS", "sG")}
        toks = {}
        for e, sem in (("T", "sT"), ("V", "sV"), ("S", "sS"), ("G", "sG")):
            w = [(s, v) for s, v in pre.items() if s != sem and v > 0]
            toks[e] = self.emit(e, lambda en: en.nop(), waits=w, inc=sem,
                                drain=True)
        for e in ENGINES:
            self.emit(e, lambda en: en.nop(),
                      waits=[toks[o] for o in toks if o != e])
        self.bar_toks = dict(toks)


class Arena:

    ESIZE = {}

    def __init__(self, handle, nbytes, name):
        self.h, self.n, self.name = handle, nbytes, name
        self.off = 0
        self.marks = []
        self.hwm = 0

    def alloc(self, shape, dt=BF16):
        assert shape[0] == 128
        es = mybir.dt.size(dt)
        n = int(np.prod(shape[1:])) * es
        self.off = (self.off + 63) // 64 * 64
        assert self.off + n <= self.n, \
            f"arena {self.name} overflow: {self.off}+{n} > {self.n}"
        ap = self.h[:, self.off:self.off + n].bitcast(dt)
        self.off += n
        self.hwm = max(self.hwm, self.off)
        if len(shape) > 2:
            dims = " ".join(f"d{i}" for i in range(1, len(shape)))
            kw = {f"d{i}": shape[i] for i in range(1, len(shape))}
            ap = ap.rearrange(f"p ({dims}) -> p {dims}", **kw)
        return ap

    def mark(self):
        self.marks.append(self.off)

    def release(self):
        self.off = self.marks.pop()


def ceil_div(a, b):
    return (a + b - 1) // b


VEC = {}
_vc = 0
for _n, _c in [
    ("convw", 64), ("convb", 16), ("mu_r", 8), ("mu_k", 8),
    ("b_in", 16), ("A0", 8), ("C0", 8), ("b_out", 8),
    ("bq", 8), ("bk", 8), ("bv", 8), ("bo", 8),
    ("gq", 8), ("gk", 8), ("gv", 8), ("go", 8),
    ("betaq", 8), ("betak", 8), ("betav", 8), ("betao", 8),
    ("scale_b", 1), ("f1b", 8),
]:
    VEC[_n] = _vc
    _vc += _c
VEC_COLS = _vc


def build_program():
    nc = bass.Bass("TRN2", target_bir_lowering=False)
    sbp = nc.SBUF_PARTITION_SIZE_BYTES

    def hin(name, shape, dt=F32):
        return nc.dram_tensor(name, list(shape), dt, kind="ExternalInput").ap()

    H = {}
    H["xT"] = hin("xT", [D, T])
    for n, sh in [("w_inproj", [D, E_INNER]), ("w_outproj", [E_INNER, D]),
                  ("w0r", [D, D]), ("w0k", [D, D]), ("w0v", [D, D]),
                  ("w_in0", [D, E_INNER]), ("w_out0", [D, D]),
                  ("wqa", [D, D]), ("wka", [D, D]), ("wva", [D, D]),
                  ("woa", [D, D]),
                  ("wqb", [D, D]), ("wkb", [D, D]), ("wvb", [D, D]),
                  ("wob", [D, D]),
                  ("wf1", [D, D]), ("wf2", [D, D]), ("wf3", [D, D])]:
        H[n] = hin(n, sh)
    H["vecs"] = hin("vecs", [128, VEC_COLS])
    H["ident"] = hin("ident", [128, 128])
    H["biasmx"] = hin("biasmx", [T, T], FP8)
    outT = nc.dram_tensor("outT", [D, T], F32, kind="ExternalOutput").ap()

    from contextlib import ExitStack
    with ExitStack() as ctx:
        E = {}
        def sbuf(name, shape, dt=F32):
            E[name] = ctx.enter_context(nc.sbuf_tensor(name, list(shape), dt))
            return E[name]

        R = sbuf("R", [128, KC, T])
        mneg = sbuf("mneg", [128, T])
        a_b = sbuf("a_b", [128, T], BF16)
        vecs = sbuf("vecs_s", [128, VEC_COLS])
        identb = sbuf("identb_s", [128, 128], BF16)
        consts = sbuf("consts", [128, 8])
        constsb = sbuf("constsb", [128, 8], BF16)
        crowb = sbuf("crowb", [1, 2, 128], BF16)
        rows = sbuf("rows", [1, 8, 512])
        smalls = sbuf("smalls", [128, 160])
        stage = sbuf("stage", [128, 4, 512])

        mneg2 = sbuf("mneg2", [128, 512])
        ksb = sbuf("ksb", [128, 512])
        ab2 = sbuf("ab2", [128, 512])
        consts8 = sbuf("consts8", [128, 8], FP8)
        consts85 = sbuf("consts85", [128, 8], FP8E5)
        ar_n = (nc.sbuf_top - nc.sbuf_base - 128) // 64 * 64
        arena_b = ctx.enter_context(
            nc.sbuf_tensor("arena_b", [128, ar_n], mybir.dt.uint8))
        AB = Arena(arena_b, ar_n, "main")
        A8 = AB

        psA = ctx.enter_context(nc.psum_tensor("psA", [128, 2, 512], F32))
        psB = ctx.enter_context(nc.psum_tensor("psB", [128, 2, 512], F32))
        psC = ctx.enter_context(nc.psum_tensor("psC", [128, 512], F32))
        psD = ctx.enter_context(nc.psum_tensor("psD", [128, 512], F32))
        psP = ctx.enter_context(nc.psum_tensor("psP", [128, 8, 128], F32))

        sems = {s: ctx.enter_context(nc.semaphore(s))
                for s in ("sT", "sV", "sS", "sG") + DMA_SEMS}

        P = Prog()
        _build_body(P, nc, H, outT, locals())

        with nc.Block() as block:
            @block.tensor
            def _(e):
                for op in P.ops["T"]:
                    op(e, sems)

            @block.vector
            def _(e):
                for op in P.ops["V"]:
                    op(e, sems)

            @block.scalar
            def _(e):
                for op in P.ops["S"]:
                    op(e, sems)

            @block.gpsimd
            def _(e):
                for op in P.ops["G"]:
                    op(e, sems)

            @block.sync
            def _(e):
                for op in P.ops["Y"]:
                    op(e, sems)
    return nc


def _build_body(P, nc, H, outT, E):
    R = E["R"]; mneg = E["mneg"]; a_b = E["a_b"]; vecs = E["vecs"]
    crowb = E["crowb"]

    def rowv(i):
        return rows[0:1, i, 0:256].bitcast(BF16)
    mneg2 = E["mneg2"]; ab2 = E["ab2"]; consts8 = E["consts8"]
    ksb = E["ksb"]; consts85 = E["consts85"]
    identb = E["identb"]
    consts = E["consts"]; constsb = E["constsb"]
    rows = E["rows"]; smalls = E["smalls"]; stage = E["stage"]
    AB = E["AB"]; A8 = E["A8"]
    psA = E["psA"]; psB = E["psB"]; psC = E["psC"]; psD = E["psD"]
    psP = E["psP"]

    st = {}

    P.emit("Y", lambda e: e.dma_start(
        out=R[:], in_=H["xT"].rearrange("(c p) t -> p c t", p=128)), inc="dYC")
    P.emit("Y", lambda e: e.dma_start(out=vecs[:], in_=H["vecs"][:]), inc="dYC")
    P.emit("Y", lambda e: e.dma_start(out=stage[:, 0, 0:128],
                                      in_=H["ident"][:]),
           inc="dYC")
    t_init = ("dYC", 48)
    t_vec = t_init
    t_id = t_init
    st_init = t_init
    P.emit("V", lambda e: e.memset(crowb[0:1, 0, :], -1.0))
    P.emit("V", lambda e: e.memset(crowb[0:1, 1, :], 1.0))
    P.emit("V", lambda e: e.memset(consts[:, 1:2], -1.0))
    P.emit("V", lambda e: e.memset(consts[:, 2:3], 1.0))
    P.emit("V", lambda e: e.memset(consts[:, 4:5], EPS))
    P.emit("V", lambda e: e.memset(consts[:, 5:6], 1.0 / 1024.0))
    P.emit("V", lambda e: e.memset(consts[:, 6:7], 64.0))
    P.emit("V", lambda e: e.memset(consts[:, 7:8], 0.125))
    P.emit("V", lambda e: e.memset(consts8[:, 0:1], 1.0))
    P.emit("V", lambda e: e.memset(consts85[:, 0:1], 1.0))
    t_cb = P.emit("V", lambda e: e.memset(constsb[:, 0:1], 1.0 / 1024.0),
                  inc="sV", drain=True)
    t_idb = P.emit("S", lambda e: e.copy(out=identb[:],
                                         in_=stage[:, 0, 0:128]),
                   waits=[t_id], inc="sS", drain=True)
    st[("stg", 0)] = t_idb
    st["tvec"] = t_init

    def load_weight(hbm, kk, mm, into=None, wait0=None):
        if into is None:
            wt = AB.alloc([128, kk, mm])
        else:
            wt = into.rearrange("p (k m) -> p k m", k=kk, m=mm) \
                if kk * mm == into.free_size() else into[:, 0:kk * mm].rearrange(
                    "p (k m) -> p k m", k=kk, m=mm)
        toks = []
        src = hbm.rearrange("(k p) m -> p k m", p=128)
        pieces = [(k, o) for k in range(kk) for o in range(0, mm, 512)]
        st.setdefault("stgi", 0)
        for k, o in pieces:
            w = min(512, mm - o)
            i = st["stgi"] % 4
            st["stgi"] += 1
            sl = stage[:, i, 0:w]
            prev = st.get(("stg", i))
            td = P.emit("Y", lambda e, k=k, o=o, w=w, sl=sl, src=src: e.dma_start(
                out=sl, in_=src[:, k, o:o + w]), waits=[prev] if prev else [],
                inc=f"dY{i}")
            cw = [td] + ([wait0] if wait0 else [])
            cw += list(P.bar_toks.values())
            cw.append(st.get("sl_last_mm"))
            tc = P.emit("S", lambda e, k=k, o=o, w=w, sl=sl, wt=wt: e.copy(
                out=wt[:, k, o:o + w], in_=sl), waits=cw, inc="sS", drain=True)
            st[("stg", i)] = tc
            toks.append(tc)
        ktoks = [None] * kk
        for (k, o), tc in zip(pieces, toks):
            ktoks[k] = tc
        return wt, ktoks

    pp = [psA, psB]
    st["ppg"] = 0
    st["ppfree"] = {0: None, 1: None}

    def gemm(wt, kk, m_list, rhs_of, c0, cols, evac, waits0=(), wk=None,
             rk=None):
        ntiles = ceil_div(cols, 512)
        assert ntiles <= 2
        first_m = True
        for m in m_list:
            g = st["ppg"]; pg = pp[g % 2]
            w0 = list(waits0)
            if st["ppfree"][g % 2] is not None:
                w0.append(st["ppfree"][g % 2])
            tok_mm = None
            for k in range(kk):
                for nt in range(ntiles):
                    nn = min(512, cols - nt * 512)
                    last = (k == kk - 1 and nt == ntiles - 1)
                    w = w0 if (k == 0 and nt == 0) else []
                    if wk is not None and nt == 0:
                        w = list(w) + [wk[k]]
                    if rk is not None and first_m and nt == 0:
                        w = list(w) + [rk[k]]
                    tok_mm = P.emit("T", lambda e, k=k, nt=nt, nn=nn, m=m,
                                    pg=pg, wt=wt, rhs_of=rhs_of, c0=c0: e.matmul(
                        pg[:, nt, 0:nn], wt[:, k, m * 128:(m + 1) * 128],
                        rhs_of(k)[:, c0 + nt * 512:c0 + nt * 512 + nn],
                        start=(k == 0), stop=(k == kk - 1)),
                        waits=w, inc="sT" if last else None)
            st["last_mm"] = tok_mm
            tfree = evac(m, pg, [tok_mm])
            st["ppfree"][g % 2] = tfree
            st["ppg"] += 1
            first_m = False

    def col_sums(src_of, kk, base, finish, toks_in, c0=0, cols=T, ones=None):
        ones = constsb[:, 0:1] if ones is None else ones
        out_toks = []
        for nt in range(cols // 512):
            ps, key = (psC, "psC") if nt % 2 == 0 else (psD, "psD")
            tok = None
            for k in range(kk):
                w = [t_cb, toks_in[k] if toks_in else None]
                if k == 0:
                    w.append(st.get(("aux", key)))
                    w.append(st.get(("row", base + nt)))
                tok = P.emit("T", lambda e, k=k, nt=nt, ps=ps, ones=ones: e.matmul(
                    ps[0:1, 0:512], ones,
                    src_of(k)[:, c0 + nt * 512:c0 + (nt + 1) * 512],
                    start=(k == 0), stop=(k == kk - 1)),
                    waits=w, inc="sT" if k == kk - 1 else None)
            te = P.emit("S", lambda e, nt=nt, ps=ps: finish(
                e, base + nt, ps[0:1, 0:512]),
                waits=[tok], inc="sS", drain=True)
            st[("aux", key)] = te
            out_toks.append(te)
        return out_toks

    def bcast_row(base, dst_of, sign, waits, cols=T):
        lhsT = crowb[0:1, 0, :] if sign < 0 else crowb[0:1, 1, :]
        toks = []
        for nt in range(cols // 512):
            ps, key = (psC, "psC") if nt % 2 == 0 else (psD, "psD")
            w = list(waits) + [st.get(("aux", key))]
            tok = P.emit("T", lambda e, nt=nt, ps=ps, lhsT=lhsT: e.matmul(
                ps[:, 0:512], lhsT, rowv(base + nt),
                start=True, stop=True), waits=w, inc="sT")
            st[("row", base + nt)] = tok
            te = P.emit("S", lambda e, nt=nt, ps=ps: e.copy(
                out=dst_of(nt), in_=ps[:, 0:512]),
                waits=[tok, st.get("mneg_dma")], inc="sS", drain=True)
            st[("aux", key)] = te
            toks.append(te)
        return toks

    def _ln_core(src_of, kk):
        accm = [psA[0:1, 0, :], psA[0:1, 1, :], psB[0:1, 0, :], psB[0:1, 1, :]]
        accv = [psC[0:1, 0:512], psD[0:1, 0:512],
                psP[0:1, 0:4, :].rearrange("p a b -> p (a b)"),
                psP[0:1, 4:8, :].rearrange("p a b -> p (a b)")]
        akm = ["psA0", "psA1", "psB0", "psB1"]
        akv = ["psC", "psD", "psP0", "psP1"]
        AB.mark()
        xb = AB.alloc([128, 2, T])
        xq = AB.alloc([128, 2, T])
        mtoks = []
        vtoks = []
        for k in range(kk):
            tc = P.emit("V", lambda e, k=k: e.tensor_copy(
                out=xb[:, k % 2, :], in_=src_of(k)),
                waits=[("dYC", 48), st.get(("lnm", k % 2))],
                inc="sV", drain=True)
            tq = P.emit("V", lambda e, k=k: e.tensor_mul(
                xq[:, k % 2, :], xb[:, k % 2, :], xb[:, k % 2, :]),
                waits=[st.get(("lnv", k % 2))], inc="sV", drain=True)
            tok = tok2 = None
            for nt in range(T // 512):
                w = [tc]
                if k == 0:
                    w += [t_cb, st.get(("aux", akm[nt]))]
                tok = P.emit("T", lambda e, k=k, nt=nt: e.matmul(
                    accm[nt], constsb[:, 0:1],
                    xb[:, k % 2, nt * 512:(nt + 1) * 512],
                    start=(k == 0), stop=(k == kk - 1)),
                    waits=w, inc="sT" if nt == T // 512 - 1 else None)
            st[("lnm", k % 2)] = tok
            mtoks.append(tok)
            for nt in range(T // 512):
                w = [tq]
                if k == 0:
                    w += [st.get(("aux", akv[nt])), st.get(("row", nt)),
                          st.get(("row", 4 + nt))]
                tok2 = P.emit("T", lambda e, k=k, nt=nt: e.matmul(
                    accv[nt], constsb[:, 0:1],
                    xq[:, k % 2, nt * 512:(nt + 1) * 512],
                    start=(k == 0), stop=(k == kk - 1)),
                    waits=w, inc="sT" if nt == T // 512 - 1 else None)
            st[("lnv", k % 2)] = tok2
            vtoks.append(tok2)
        tm = []
        tv = []
        for nt in range(T // 512):
            te = P.emit("S", lambda e, nt=nt: e.activation(
                out=rowv(nt), in_=accm[nt], func=ACT.Copy),
                waits=[mtoks[-1], st.get(("row", nt))], inc="sS", drain=True)
            st[("aux", akm[nt])] = te
            tm.append(te)
            t2 = P.emit("S", lambda e, nt=nt: e.activation(
                out=rows[0:1, 4 + nt, :], in_=accv[nt], func=ACT.Copy),
                waits=[vtoks[-1]], inc="sS", drain=True)
            st[("aux", akv[nt])] = t2
            tv.append(t2)
        t_mneg = bcast_row(0, lambda nt: mneg[:, nt * 512:(nt + 1) * 512],
                           -1.0, tm)
        NT = T // 512
        mrow = mneg[0:1, :].rearrange("p (a b) -> p a b", a=NT)
        tq1 = P.emit("V", lambda e: e.tensor_mul(
            rows[0:1, 0:NT, :], mrow, mrow),
            waits=[t_mneg[-1], tv[-1]] +
                  [st[("row", nt)] for nt in range(NT)],
            inc="sV", drain=True)
        tq2 = P.emit("V", lambda e: e.tensor_sub(
            rows[0:1, 4:4 + NT, :], rows[0:1, 4:4 + NT, :],
            rows[0:1, 0:NT, :]), waits=[tq1], inc="sV", drain=True)
        tq3 = P.emit("S", lambda e: e.activation(
            out=rows[0:1, 0:NT, :], in_=rows[0:1, 4:4 + NT, :],
            func=ACT.Sqrt, bias=consts[0:1, 4:5]),
            waits=[tq2], inc="sS", drain=True)
        tq4 = P.emit("V", lambda e: e.reciprocal(
            out=rows[0:1, 4:4 + NT, :], in_=rows[0:1, 0:NT, :]),
            waits=[tq3], inc="sV", drain=True)
        rtok = P.emit("S", lambda e: e.copy(
            out=rows[0:1, 0:NT, 0:256].bitcast(BF16),
            in_=rows[0:1, 4:4 + NT, :]),
            waits=[tq4], inc="sS", drain=True)
        rt = [rtok] * NT
        t_ab = bcast_row(0, lambda nt: a_b[:, nt * 512:(nt + 1) * 512],
                         1.0, rt)
        AB.release()
        return t_mneg, t_ab, tv[-1]

    def ln_stats(src_of, kk=KC):
        t_mneg, t_ab, t_scr = _ln_core(src_of, kk)
        return t_ab[-1], t_scr, t_mneg[-1]

    def ln_mat(dst_of, src_of, k_list, c0, cols, tok, tok_mul=None):
        mt = []
        w0 = tok if isinstance(tok, list) else [tok]
        wm = [tok_mul] if tok_mul is not None else []
        for k in k_list:
            ta = P.emit("V", lambda e, k=k: e.tensor_add(
                dst_of(k), src_of(k)[:, c0:c0 + cols],
                mneg[:, c0:c0 + cols]), waits=w0, inc="sV", drain=True)
            mt.append(P.emit("V", lambda e, k=k: e.tensor_mul(
                dst_of(k), dst_of(k), a_b[:, c0:c0 + cols]),
                waits=[ta] + wm, inc="sV", drain=True))
        return mt

    def sub_mamba():
        if STAGE < 1:
            return
        AB.mark()
        QT = 512
        halo = AB.alloc([128, 16, 4])
        slotA = AB.alloc([128, 8 * E_INNER])
        slotB = AB.alloc([128, 16 * D])
        t_ab, t_scr, t_mneg = ln_stats(lambda k: R[:, k, :])
        wt_in, tk_in = load_weight(H["w_inproj"], KC, E_INNER, into=slotA)
        wt_out, tk_out = load_weight(H["w_outproj"], 16, D, into=slotB)
        if STAGE < 2:
            AB.release()
            return
        for q in range(T // QT):
            c0 = q * QT
            AB.mark()
            ln = AB.alloc([128, KC, QT])
            s1 = AB.alloc([128, 16, QT])
            x1 = AB.alloc([128, 2, QT])
            acc = AB.alloc([128, 2, QT])
            lt = []
            for k in range(KC):
                lt.append(P.emit("V", lambda e, k=k, c0=c0, ln=ln: e.tensor_add(
                    ln[:, k, :], R[:, k, c0:c0 + QT], mneg[:, c0:c0 + QT]),
                    waits=[t_mneg], inc="sV"))

            def evac_conv(m, pg, waits, q=q, s1=s1, x1=x1, acc=acc, c0=c0):
                sl = m % 2
                w = list(waits) + [st.get(("x1f", sl)), st.get(("accf", sl)),
                                   t_ab]
                tcp = P.emit("V", lambda e, sl=sl, pg=pg, c0=c0: e.tensor_mul(
                    x1[:, sl, :], pg[:, 0, 0:QT], a_b[:, c0:c0 + QT]),
                    waits=w, inc="sV", drain=True)
                tap = VEC["convw"] + m * 4
                P.emit("V", lambda e, sl=sl, tap=tap, x1=x1, acc=acc:
                       e.tensor_scalar(
                           out=acc[:, sl, :], in0=x1[:, sl, :],
                           scalar1=vecs[:, tap + 3:tap + 4], scalar2=None,
                           op0=ALU.mult), waits=[st["tvec"], tcp])
                for j in (2, 1, 0):
                    sh = 3 - j
                    P.emit("V", lambda e, sl=sl, tap=tap, j=j, sh=sh, x1=x1,
                           acc=acc: e.scalar_tensor_tensor(
                               out=acc[:, sl, sh:QT], in0=x1[:, sl, 0:QT - sh],
                               scalar=vecs[:, tap + j:tap + j + 1],
                               in1=acc[:, sl, sh:QT],
                               op0=ALU.mult, op1=ALU.add))
                    if q > 0:
                        P.emit("V", lambda e, m=m, sl=sl, tap=tap, j=j, sh=sh,
                               acc=acc: e.scalar_tensor_tensor(
                                   out=acc[:, sl, 0:sh],
                                   in0=halo[:, m, 4 - sh:4],
                                   scalar=vecs[:, tap + j:tap + j + 1],
                                   in1=acc[:, sl, 0:sh],
                                   op0=ALU.mult, op1=ALU.add))
                tv = P.emit("V", lambda e, m=m, sl=sl, x1=x1: e.tensor_copy(
                    out=halo[:, m, 1:4], in_=x1[:, sl, QT - 3:QT]),
                    inc="sV", drain=True)
                st[("x1f", sl)] = tv
                ts = P.emit("S", lambda e, m=m, sl=sl, s1=s1, acc=acc:
                            e.activation(
                                out=s1[:, m, :], in_=acc[:, sl, :],
                                func=ACT.Silu,
                                bias=vecs[:, VEC["convb"] + m:
                                          VEC["convb"] + m + 1]),
                            waits=[tv], inc="sS", drain=True)
                st[("accf", sl)] = ts
                st[("s1", m)] = ts
                return tcp

            gemm(wt_in, KC, list(range(16)), lambda k, ln=ln: ln[:, k, :],
                 0, QT, evac_conv, wk=tk_in if q == 0 else None, rk=lt)

            def evac_res(m, pg, waits, c0=c0):
                return P.emit("V", lambda e, m=m, pg=pg, c0=c0: e.tensor_add(
                    R[:, m, c0:c0 + QT], R[:, m, c0:c0 + QT], pg[:, 0, 0:QT]),
                    waits=waits, inc="sV", drain=True)

            if STAGE < 3:
                AB.release()
                break
            w_s1 = [st[("s1", m)] for m in range(16)]
            gemm(wt_out, 16, list(range(KC)), lambda k, s1=s1: s1[:, k, :],
                 0, QT, evac_res, waits0=w_s1,
                 wk=tk_out if q == 0 else None)
            AB.release()
        AB.release()


    def sub_ffn0():
        QT = 512
        NQ = T // QT
        AB.mark()
        slot_r = AB.alloc([128, 8 * D])
        slot_k = AB.alloc([128, 8 * D])
        slot_v = AB.alloc([128, 8 * D])
        t_ab, t_scr, t_mneg = ln_stats(lambda k: R[:, k, :])
        wr, wrt = load_weight(H["w0r"], KC, D, into=slot_r)
        wk_, wkt = load_weight(H["w0k"], KC, D, into=slot_k)
        wv, wvt = load_weight(H["w0v"], KC, D, into=slot_v)
        lnp = AB.alloc([128, 1, QT + 1])
        d = AB.alloc([128, KC, QT])
        xrp = [AB.alloc([128, KC, QT]), AB.alloc([128, KC, QT])]
        xk = AB.alloc([128, KC, QT])
        r_sb = AB.alloc([128, KC, QT])
        kact = AB.alloc([128, KC, QT])
        dtmp = AB.alloc([128, 2, QT])
        P.emit("V", lambda e: e.memset(smalls[:, 0:8], 0.0))
        st["ffn_xf"] = None
        st["ffn_xkf"] = None

        def prep_xr(q):
            c0 = q * QT
            xr = xrp[q % 2]
            wxf = st.get(("ffn_xf", q % 2))
            tk = None
            for k in range(KC):
                sl = 0
                mur = vecs[:, VEC["mu_r"] + k:VEC["mu_r"] + k + 1]
                P.emit("V", lambda e, k=k, sl=sl: e.tensor_copy(
                    out=lnp[:, sl, 0:1], in_=smalls[:, k:k + 1]))
                P.emit("V", lambda e, k=k, sl=sl, c0=c0: e.tensor_add(
                    lnp[:, sl, 1:QT + 1], R[:, k, c0:c0 + QT],
                    mneg[:, c0:c0 + QT]),
                    waits=[t_mneg, st["tvec"], wxf,
                           st.get("ffn_dxf")])
                P.emit("V", lambda e, k=k, sl=sl, c0=c0: e.tensor_mul(
                    lnp[:, sl, 1:QT + 1], lnp[:, sl, 1:QT + 1],
                    a_b[:, c0:c0 + QT]),
                    waits=[t_ab])
                P.emit("V", lambda e, k=k, sl=sl: e.tensor_sub(
                    d[:, k, :], lnp[:, sl, 1:QT + 1], lnp[:, sl, 0:QT]))
                tk = P.emit("V", lambda e, k=k, sl=sl, mur=mur:
                            e.scalar_tensor_tensor(
                                out=xr[:, k, :], in0=d[:, k, :],
                                scalar=mur, in1=lnp[:, sl, 0:QT],
                                op0=ALU.mult, op1=ALU.add),
                            inc="sV" if k == KC - 1 else None, drain=False)
                P.emit("V", lambda e, k=k, sl=sl: e.tensor_copy(
                    out=smalls[:, k:k + 1], in_=lnp[:, sl, QT:QT + 1]))
            return tk

        def prep_xk(q):
            c0 = q * QT
            xr = xrp[q % 2]
            tk = None
            for k in range(KC):
                muk = vecs[:, VEC["mu_k"] + k:VEC["mu_k"] + k + 1]
                mur = vecs[:, VEC["mu_r"] + k:VEC["mu_r"] + k + 1]
                P.emit("V", lambda e, k=k, mur=mur: e.scalar_tensor_tensor(
                    out=xk[:, k, :], in0=d[:, k, :], scalar=mur,
                    in1=xr[:, k, :], op0=ALU.mult, op1=ALU.subtract),
                    waits=[st.get("ffn_xkf")])
                tk = P.emit("V", lambda e, k=k, muk=muk:
                            e.scalar_tensor_tensor(
                                out=xk[:, k, :], in0=d[:, k, :], scalar=muk,
                                in1=xk[:, k, :], op0=ALU.mult,
                                op1=ALU.subtract),
                            inc="sV" if k == KC - 1 else None, drain=False)
            return tk

        t_prep = prep_xr(0)
        t_prepk = prep_xk(0)
        for q in range(NQ):
            c0 = q * QT

            def evac_r(m, pg, waits):
                return P.emit("S", lambda e, m=m, pg=pg: e.activation(
                    out=r_sb[:, m, :], in_=pg[:, 0, :], func=ACT.Sigmoid),
                    waits=waits, inc="sS", drain=True)

            xr_q = xrp[q % 2]
            gemm(wr, KC, list(range(KC)),
                 lambda k, xr_q=xr_q: xr_q[:, k, :], 0, QT,
                 evac_r, waits0=[t_prep], wk=wrt if q == 0 else None)
            tmm_r = st["last_mm"]
            st[("ffn_xf", q % 2)] = tmm_r
            if q + 1 < NQ:
                t_prep = prep_xr(q + 1)

            def evac_k(m, pg, waits):
                sl = m % 2
                tr_ = P.emit("V", lambda e, m=m, pg=pg, sl=sl: e.tensor_scalar(
                    out=dtmp[:, sl, :], in0=pg[:, 0, :],
                    scalar1=0.0, scalar2=None, op0=ALU.max),
                    waits=list(waits) + [st.get(("dt", sl))],
                    inc="sV", drain=True)
                tk_ = P.emit("S", lambda e, m=m, sl=sl: e.activation(
                    out=kact[:, m, :], in_=dtmp[:, sl, :], func=ACT.Square),
                    waits=[tr_], inc="sS", drain=True)
                st[("dt", sl)] = tk_
                st[("kact", m)] = tk_
                return tr_

            gemm(wk_, KC, list(range(KC)), lambda k: xk[:, k, :], 0, QT,
                 evac_k, waits0=[t_prepk], wk=wkt if q == 0 else None)
            st["ffn_xkf"] = st["last_mm"]
            st["ffn_dxf"] = st["last_mm"]

            def evac_v(m, pg, waits, c0=c0):
                sl = m % 2
                tm_ = P.emit("V", lambda e, m=m, pg=pg, sl=sl: e.tensor_mul(
                    dtmp[:, sl, :], pg[:, 0, :], r_sb[:, m, :]),
                    waits=list(waits) + [st.get(("dt", sl))],
                    inc="sV", drain=True)
                st[("dt", sl)] = P.emit("V", lambda e, m=m, sl=sl, c0=c0:
                                        e.tensor_add(
                    R[:, m, c0:c0 + QT], R[:, m, c0:c0 + QT], dtmp[:, sl, :]),
                    inc="sV", drain=True)
                return tm_

            gemm(wv, KC, list(range(KC)), lambda k: kact[:, k, :], 0, QT,
                 evac_v, waits0=[st[("kact", m)] for m in range(KC)],
                 wk=wvt if q == 0 else None)
            if q + 1 < NQ:
                t_prepk = prep_xk(q + 1)
        AB.release()

    def sub_attnb0():
        HT = T // 2
        AB.mark()
        st["wfree2"] = None
        wslot = AB.alloc([128, 8 * E_INNER])
        t_ab, t_scr, t_mneg = ln_stats(lambda k: R[:, k, :])
        ln = AB.alloc([128, KC, HT])
        gate = AB.alloc([128, KC, HT])
        y = AB.alloc([128, KC, HT])
        vtmp = AB.alloc([128, 2, HT])
        hs = AB.alloc([128, 2, HT])
        P.emit("V", lambda e: e.memset(smalls[:, 8:16], 0.0))
        t_dec = P.emit("S", lambda e: e.activation(
            out=smalls[:, 16:24], in_=vecs[:, VEC["A0"]:VEC["A0"] + 8],
            func=ACT.Sigmoid), waits=[st["tvec"]], inc="sS", drain=True)
        st["wfree2"] = None
        for h in range(2):
            c0 = h * HT
            lt = ln_mat(lambda k, ln=ln: ln[:, k, :], lambda k: R[:, k, :],
                        range(KC), c0, HT, t_mneg, t_ab)
            win, wint = load_weight(H["w_in0"], KC, E_INNER, into=wslot,
                                    wait0=st["wfree2"])

            def evac_g(m, pg, waits):
                return P.emit("S", lambda e, m=m, pg=pg: e.activation(
                    out=gate[:, m - 8, :], in_=pg[:, :, :].rearrange(
                        "p a b -> p (a b)"), func=ACT.Sigmoid,
                    bias=vecs[:, VEC["b_in"] + m:VEC["b_in"] + m + 1]),
                    waits=waits, inc="sS", drain=True)

            gemm(win, KC, list(range(8, 16)), lambda k: ln[:, k, :], 0, HT,
                 evac_g, wk=wint, rk=lt)

            def evac_vch(m, pg, waits, h=h):
                sl = m % 2
                tv_ = P.emit("S", lambda e, m=m, pg=pg, sl=sl: e.activation(
                    out=vtmp[:, sl, :], in_=pg[:, :, :].rearrange(
                        "p a b -> p (a b)"), func=ACT.Identity,
                    bias=vecs[:, VEC["b_in"] + m:VEC["b_in"] + m + 1]),
                    waits=list(waits) + [st.get(("vt", sl))],
                    inc="sS", drain=True)
                tsc = P.emit("V", lambda e, m=m, sl=sl: e.tensor_tensor_scan(
                    out=hs[:, sl, :],
                    data0=smalls[:, 16 + m:17 + m].broadcast_to([128, HT]),
                    data1=vtmp[:, sl, :],
                    initial=smalls[:, 8 + m:9 + m],
                    op0=ALU.mult, op1=ALU.add),
                    waits=[tv_, t_dec, st.get(("hs", sl))],
                    inc="sV", drain=True)
                st[("vt", sl)] = tsc
                P.emit("V", lambda e, m=m, sl=sl: e.tensor_copy(
                    out=smalls[:, 8 + m:9 + m], in_=hs[:, sl, HT - 1:HT]))
                st[("hs", sl)] = P.emit("V", lambda e, m=m, sl=sl:
                                        e.scalar_tensor_tensor(
                    out=y[:, m, :], in0=hs[:, sl, :],
                    scalar=vecs[:, VEC["C0"] + m:VEC["C0"] + m + 1],
                    in1=gate[:, m, :], op0=ALU.mult, op1=ALU.mult),
                    inc="sV", drain=True)
                return tv_

            gemm(win, KC, list(range(8)), lambda k: ln[:, k, :], 0, HT,
                 evac_vch, waits0=[], wk=None)
            tmm_in = st["last_mm"]

            wout, woutt = load_weight(H["w_out0"], KC, D, into=wslot,
                                      wait0=tmm_in)

            def evac_o(m, pg, waits, c0=c0):
                return P.emit("V", lambda e, m=m, pg=pg, c0=c0:
                              e.scalar_tensor_tensor(
                    out=R[:, m, c0:c0 + HT],
                    in0=pg[:, :, :].rearrange("p a b -> p (a b)"),
                    scalar=vecs[:, VEC["b_out"] + m:VEC["b_out"] + m + 1],
                    in1=R[:, m, c0:c0 + HT], op0=ALU.add, op1=ALU.add),
                    waits=waits, inc="sV", drain=True)

            w_y = [st[("hs", 0)], st[("hs", 1)]]
            gemm(wout, KC, list(range(KC)), lambda k: y[:, k, :], 0, HT,
                 evac_o, waits0=w_y, wk=woutt)
            st["wfree2"] = st["last_mm"]
        AB.release()


    def ln_spike(pre, prets, gbase, ombase, out_of, scratch):
        mt = vt = None
        sqt = []
        for k in range(KC):
            tq = P.emit("S", lambda e, k=k: e.activation(
                out=scratch[:, k, :], in_=pre[:, k, :], func=ACT.Square),
                waits=[prets[k]], inc="sS", drain=True)
            sqt.append(tq)
            w = [prets[k]]
            if k == 0:
                w += [t_cb, st.get(("aux", "psC")), st.get(("row", 0))]
            mt = P.emit("T", lambda e, k=k: e.matmul(
                psC[0:1, 0:512], constsb[:, 0:1], pre[:, k, :],
                start=(k == 0), stop=(k == KC - 1)),
                waits=w, inc="sT" if k == KC - 1 else None)
            w2 = [tq]
            if k == 0:
                w2 += [st.get(("aux", "psD")), st.get(("row", 4)),
                       st.get(("row", 1))]
            vt = P.emit("T", lambda e, k=k: e.matmul(
                psD[0:1, 0:512], constsb[:, 0:1], scratch[:, k, :],
                start=(k == 0), stop=(k == KC - 1)),
                waits=w2, inc="sT" if k == KC - 1 else None)
        tme = P.emit("S", lambda e: e.activation(
            out=rowv(0), in_=psC[0:1, 0:512], func=ACT.Copy),
            waits=[mt], inc="sS", drain=True)
        st[("aux", "psC")] = tme
        tve = P.emit("S", lambda e: e.activation(
            out=rows[0:1, 4, :], in_=psD[0:1, 0:512], func=ACT.Copy),
            waits=[vt], inc="sS", drain=True)
        st[("aux", "psD")] = tve
        tmn = bcast_row(0, lambda nt: mneg2[:, :], -1.0, [tme], cols=512)
        tq1 = P.emit("V", lambda e: e.tensor_mul(
            rows[0:1, 5, :], mneg2[0:1, :], mneg2[0:1, :]),
            waits=[tmn[-1], st.get(("row", 5))], inc="sV", drain=True)
        tq2 = P.emit("V", lambda e: e.tensor_sub(
            rows[0:1, 4, :], rows[0:1, 4, :], rows[0:1, 5, :]),
            waits=[tq1, tve], inc="sV", drain=True)
        tq3 = P.emit("S", lambda e: e.activation(
            out=rows[0:1, 5, :], in_=rows[0:1, 4, :],
            func=ACT.Sqrt, bias=consts[0:1, 4:5]),
            waits=[tq2], inc="sS", drain=True)
        st[("row", 5)] = tq3
        tq4 = P.emit("V", lambda e: e.reciprocal(
            out=rows[0:1, 4, :], in_=rows[0:1, 5, :]),
            waits=[tq3], inc="sV", drain=True)
        st[("row", 4)] = tq4
        tq5 = P.emit("S", lambda e: e.copy(
            out=rowv(1), in_=rows[0:1, 4, :]),
            waits=[tq4, st.get(("row", 1))], inc="sS", drain=True)
        tab = bcast_row(1, lambda nt: ab2[:, :], 1.0, [tq5], cols=512)
        out_toks = []
        for k in range(KC):
            P.emit("V", lambda e, k=k: e.tensor_add(
                pre[:, k, :], pre[:, k, :], mneg2[:]),
                waits=[tmn[-1], sqt[k]])
            P.emit("V", lambda e, k=k: e.tensor_mul(
                pre[:, k, :], pre[:, k, :], ab2[:]), waits=[tab[-1]])
            out_toks.append(P.emit("V", lambda e, k=k: e.tensor_scalar(
                out=out_of(k), in0=pre[:, k, :],
                scalar1=vecs[:, gbase + k:gbase + k + 1],
                scalar2=smalls[:, ombase + k:ombase + k + 1],
                op0=ALU.mult, op1=ALU.is_ge),
                inc="sV", drain=True))
        return out_toks

    def sub_l1a():
        HT = T // 2
        AB.mark()
        bd = AB.alloc([128, 8, 128])
        P.emit("V", lambda e: e.memset(bd[:, :, :], 0.0))
        ln = AB.alloc([128, KC, T])
        AB.mark()
        slot_ka = AB.alloc([128, 8 * D])
        slot_va = AB.alloc([128, 8 * D])
        t_ab, t_scr, t_mneg = ln_stats(lambda k: R[:, k, :])
        wk, wkt = load_weight(H["wka"], KC, D, into=slot_ka)
        wv, wvt = load_weight(H["wva"], KC, D, into=slot_va)
        lt = ln_mat(lambda k, ln=ln: ln[:, k, :], lambda k: R[:, k, :],
                    range(KC), 0, T, t_mneg, t_ab)
        kblk = AB.alloc([128, 2, D])
        vblk = AB.alloc([128, 2, D])
        tpair = None
        for tb in range(16):
            sl = tb % 2
            for which, w_, wt_, blk in (("k", wk, wkt, kblk), ("v", wv, wvt, vblk)):
                ps = psA if which == "k" else psB
                tok = None
                for kc in range(KC):
                    for nt in range(2):
                        w0 = []
                        if kc == 0 and nt == 0:
                            w0 = [lt[-1], st.get(("blkf", which, sl))]
                        if nt == 0:
                            w0 = list(w0) + [wt_[kc]]
                        tok = P.emit("T", lambda e, kc=kc, nt=nt, tb=tb, ps=ps,
                                     w_=w_: e.matmul(
                            ps[:, nt, :], ln[:, kc, tb * 128:(tb + 1) * 128],
                            w_[:, kc, nt * 512:(nt + 1) * 512],
                            start=(kc == 0), stop=(kc == KC - 1)),
                            waits=w0,
                            inc="sT" if (kc == KC - 1 and nt == 1) else None)
                te = P.emit("S", lambda e, ps=ps, blk=blk, sl=sl: e.copy(
                    out=blk[:, sl, :], in_=ps[:, :, :].rearrange(
                        "p a b -> p (a b)")), waits=[tok], inc="sS", drain=True)
                st[("blkev", which, sl)] = te
            for pc in range(8):
                w0 = []
                if pc == 0:
                    w0 = [st[("blkev", "k", sl)], st[("blkev", "v", sl)]]
                tpair = P.emit("T", lambda e, pc=pc, sl=sl, tb=tb: e.matmul(
                    psP[:, pc, :], kblk[:, sl, pc * 128:(pc + 1) * 128],
                    vblk[:, sl, pc * 128:(pc + 1) * 128],
                    start=(tb == 0 and pc % 4 == 0),
                    stop=(tb == 15 and pc % 4 == 3)),
                    waits=w0, inc="sT" if pc == 7 else None)
            st[("blkf", "k", sl)] = tpair
            st[("blkf", "v", sl)] = tpair
        st["l1a_kv_done"] = tpair
        tbd = []
        for pc in range(8):
            P.emit("S", lambda e, pc=pc: e.mul(
                out=bd[0:64, pc, 0:64], in_=psP[0:64, pc, 0:64],
                mul=consts[0:64, 7:8]), waits=[tpair])
            tbd.append(P.emit("S", lambda e, pc=pc: e.mul(
                out=bd[64:128, pc, 64:128], in_=psP[64:128, pc, 64:128],
                mul=consts[64:128, 7:8]), inc="sS", drain=True))
        AB.release()
        for h in range(2):
            c0 = h * HT
            AB.mark()
            q = AB.alloc([128, KC, HT])
            wslot_a = AB.alloc([128, 8 * D])
            wq, wqt = load_weight(H["wqa"], KC, D, into=wslot_a,
                                  wait0=st.get("l1a_wq_free")
                                  or st["l1a_kv_done"])

            def evac_q(m, pg, waits, q=q):
                tq = P.emit("S", lambda e, m=m, pg=pg, q=q: e.copy(
                    out=q[:, m, :], in_=pg[:, :, :].rearrange(
                        "p a b -> p (a b)")), waits=waits, inc="sS", drain=True)
                st[("qv", m)] = tq
                return tq

            gemm(wq, KC, list(range(KC)), lambda k: ln[:, k, :], c0, HT,
                 evac_q, waits0=[lt[-1]], wk=wqt)
            st["l1a_wq_free"] = st["last_mm"]
            attn = AB.alloc([128, KC, HT])
            for ec in range(8):
                g = st["ppg"]; pg = pp[g % 2]
                w0 = [tbd[-1], st[("qv", ec)]]
                if st["ppfree"][g % 2] is not None:
                    w0.append(st["ppfree"][g % 2])
                tok = None
                for nt in range(2):
                    tok = P.emit("T", lambda e, ec=ec, nt=nt, pg=pg, q=q:
                                 e.matmul(
                        pg[:, nt, :], bd[:, ec, :],
                        q[:, ec, nt * 512:(nt + 1) * 512],
                        start=True, stop=True),
                        waits=w0 if nt == 0 else [],
                        inc="sT" if nt == 1 else None)
                ta = P.emit("S", lambda e, ec=ec, pg=pg, attn=attn: e.copy(
                    out=attn[:, ec, :], in_=pg[:, :, :].rearrange(
                        "p a b -> p (a b)")), waits=[tok], inc="sS", drain=True)
                st[("at", ec)] = ta
                st["ppfree"][g % 2] = ta
                st["ppg"] += 1
            wo, wot = load_weight(H["woa"], KC, D, into=wslot_a,
                                   wait0=st["last_mm"])

            def evac_oa(m, pg, waits, c0=c0):
                return P.emit("V", lambda e, m=m, pg=pg, c0=c0: e.tensor_add(
                    R[:, m, c0:c0 + HT], R[:, m, c0:c0 + HT],
                    pg[:, :, :].rearrange("p a b -> p (a b)")),
                    waits=waits, inc="sV", drain=True)

            gemm(wo, KC, list(range(KC)), lambda k, attn=attn: attn[:, k, :],
                 0, HT, evac_oa,
                 waits0=[st[("at", m)] for m in range(8)], wk=wot)
            st["l1a_wq_free"] = st["last_mm"]
            AB.release()
        AB.release()

    def sub_l1ffn():
        HT = T // 2
        AB.mark()
        st["l1f_wfree"] = None
        wslot = AB.alloc([128, 8 * D])
        t_ab, t_scr, t_mneg = ln_stats(lambda k: R[:, k, :])
        ln = AB.alloc([128, KC, HT])
        h1 = AB.alloc([128, KC, HT])
        h2 = AB.alloc([128, KC, HT])
        dtmp = AB.alloc([128, 2, HT])
        def emit_lt(c0, wtok):
            lts = []
            for k in range(KC):
                lts.append(P.emit("V", lambda e, k=k, c0=c0: e.tensor_add(
                    ln[:, k, :], R[:, k, c0:c0 + HT], mneg[:, c0:c0 + HT]),
                    waits=[t_mneg, wtok], inc="sV"))
            return lts

        lt = emit_lt(0, st.get("l1f_lnf"))
        for h in range(2):
            c0 = h * HT

            def evac_h(dst, scale_ab=False):
                def f(m, pg, waits, dst=dst, scale_ab=scale_ab):
                    if scale_ab:
                        t_ = P.emit("V", lambda e, m=m, pg=pg, dst=dst, c0=c0:
                                    e.tensor_mul(
                            dst[:, m, :], pg[:, :, :].rearrange(
                                "p a b -> p (a b)"), a_b[:, c0:c0 + HT]),
                            waits=list(waits) + [t_ab], inc="sV", drain=True)
                    else:
                        t_ = P.emit("S", lambda e, m=m, pg=pg, dst=dst: e.copy(
                            out=dst[:, m, :], in_=pg[:, :, :].rearrange(
                                "p a b -> p (a b)")), waits=waits,
                            inc="sS", drain=True)
                    st[("hv", m)] = t_
                    return t_
                return f

            w1, w1t = load_weight(H["wf1"], KC, D, into=wslot,
                                  wait0=st["l1f_wfree"])
            gemm(w1, KC, list(range(KC)), lambda k: ln[:, k, :], 0, HT,
                 evac_h(h1, True), wk=w1t, rk=lt)
            if h == 0:
                lt = emit_lt(HT, st["last_mm"])
            w2, w2t = load_weight(H["wf2"], KC, D, into=wslot,
                                  wait0=st["last_mm"])
            gemm(w2, KC, list(range(KC)), lambda k: h1[:, k, :], 0, HT,
                 evac_h(h2), waits0=[st[("hv", m)] for m in range(KC)], wk=w2t)
            w3, w3t = load_weight(H["wf3"], KC, D, into=wslot,
                                  wait0=st["last_mm"])

            def evac_t(m, pg, waits, c0=c0):
                sl = m % 2
                tt_ = P.emit("S", lambda e, m=m, pg=pg, sl=sl: e.activation(
                    out=dtmp[:, sl, :], in_=pg[:, :, :].rearrange(
                        "p a b -> p (a b)"), func=ACT.Tanh,
                    bias=vecs[:, VEC["f1b"] + m:VEC["f1b"] + m + 1]),
                    waits=list(waits) + [st.get(("dt2", sl))],
                    inc="sS", drain=True)
                st[("dt2", sl)] = P.emit("V", lambda e, m=m, sl=sl, c0=c0:
                                         e.tensor_add(
                    R[:, m, c0:c0 + HT], R[:, m, c0:c0 + HT], dtmp[:, sl, :]),
                    waits=[tt_], inc="sV", drain=True)
                return tt_

            gemm(w3, KC, list(range(KC)), lambda k: h2[:, k, :], 0, HT,
                 evac_t, waits0=[st[("hv", m)] for m in range(KC)], wk=w3t)
            st["l1f_wfree"] = st["last_mm"]
            st["l1f_lnf"] = st["last_mm"]
        AB.release()

    def sub_l1b():
        QT = 512
        AB.mark()
        t_ab, t_scr, t_mneg = ln_stats(lambda k: R[:, k, :])
        st["l1b_wfree"] = None
        for xi, nm in enumerate(("betaq", "betak", "betav", "betao")):
            P.emit("V", lambda e, xi=xi, nm=nm: e.tensor_scalar(
                out=smalls[:, 24 + 8 * xi:32 + 8 * xi],
                in0=vecs[:, VEC[nm]:VEC[nm] + 8],
                scalar1=consts[:, 1:2], scalar2=consts[:, 2:3],
                op0=ALU.mult, op1=ALU.add), waits=[st["tvec"], t_cb])
        P.emit("V", lambda e: e.tensor_scalar(
            out=smalls[:, 113:114], in0=vecs[:, VEC["scale_b"]:VEC["scale_b"] + 1],
            scalar1=2.0, scalar2=None, op0=ALU.mult))
        t_thr = P.emit("V", lambda e: e.reciprocal(
            out=smalls[:, 112:113], in_=smalls[:, 113:114]),
            inc="sV", drain=True)
        Q8 = AB.alloc([128, KC, T], FP8E5)
        Vtm = AB.alloc([128, 16, D], FP8)
        Ktm = AB.alloc([128, 16, D], FP8)
        AB.mark()
        wslot = AB.alloc([128, 8 * D])
        lnq = AB.alloc([128, KC, QT])
        pre = AB.alloc([128, KC, QT])
        spk = AB.alloc([128, KC, QT])
        for qt in range(4):
            c0 = qt * QT
            lt = ln_mat(lambda k, lnq=lnq: lnq[:, k, :], lambda k: R[:, k, :],
                        range(KC), c0, QT,
                        [t_mneg, st.get("l1b_wfree")], t_ab)
            st["l1b_lnmat_last"] = lt[-1]
            fin_prev = st.pop("l1b_fin", None)
            if fin_prev:
                fin_prev()
            for X, wh, bbase, gbase, ombase in (
                    ("k", "wkb", VEC["bk"], VEC["gk"], 32),
                    ("v", "wvb", VEC["bv"], VEC["gv"], 40),
                    ("q", "wqb", VEC["bq"], VEC["gq"], 24)):
                wx, wxt = load_weight(H[wh], KC, D, into=wslot,
                                      wait0=st["l1b_wfree"])

                def evac_p(m, pg, waits, bbase=bbase):
                    t_ = P.emit("S", lambda e, m=m, pg=pg, bbase=bbase:
                                e.activation(
                        out=pre[:, m, :], in_=pg[:, 0, :], func=ACT.Identity,
                        bias=vecs[:, bbase + m:bbase + m + 1]),
                        waits=list(waits) + [st.get("prejunk")],
                        inc="sS", drain=True)
                    st[("pv", m)] = t_
                    return t_

                gemm(wx, KC, list(range(KC)), lambda k, lnq=lnq: lnq[:, k, :],
                     0, QT, evac_p, wk=wxt, rk=lt)
                st["l1b_wfree"] = st["last_mm"]
                if X == "q":
                    pv = [st[("pv", m)] for m in range(KC)]

                    def fin(c0=c0, pv=pv, gbase=gbase, ombase=ombase):
                        spk_toks = ln_spike(pre, pv, gbase, ombase,
                                            lambda k: spk[:, k, :], spk)
                        st["prejunk"] = spk_toks[-1]
                        for k in range(KC):
                            P.emit("S", lambda e, k=k, c0=c0: e.copy(
                                out=Q8[:, k, c0:c0 + QT], in_=spk[:, k, :]),
                                waits=[spk_toks[k]], inc="sS", drain=True)
                    st["l1b_fin"] = fin
                else:
                    spk_toks = ln_spike(pre, [st[("pv", m)] for m in range(KC)],
                                        gbase, ombase, lambda k: spk[:, k, :],
                                        spk)
                    dst = Ktm if X == "k" else Vtm
                    for ec in range(KC):
                        for lb in range(4):
                            slot = ((ec * 4 + lb) % 2) * 4
                            tb = qt * 4 + lb
                            ttr = P.emit("T", lambda e, ec=ec, lb=lb, slot=slot:
                                         e.transpose(
                                psP[:, slot, 0:64].bitcast(BF16),
                                spk[:, ec, lb * 128:(lb + 1) * 128],
                                identb[:]),
                                waits=[spk_toks[ec], t_idb,
                                       st.get(("tps", slot))],
                                inc="sT")
                            st[("tps", slot)] = P.emit(
                                "S", lambda e, slot=slot, tb=tb, ec=ec, dst=dst:
                                e.copy(out=dst[:, tb, ec * 128:(ec + 1) * 128],
                                       in_=psP[:, slot, 0:64].bitcast(BF16)),
                                waits=[ttr], inc="sS", drain=True)
                if X == "k":
                    tks = col_sums(lambda k: spk[:, k, :], KC, 0,
                                   lambda e, i, p: e.activation(
                                       out=rowv(i), in_=p, func=ACT.Copy),
                                   spk_toks, cols=QT)
                    st["tksb"] = bcast_row(0, lambda nt: ksb[:, :], 1.0, tks,
                                           cols=QT)[-1]
                if X == "v":
                    for k in range(KC):
                        P.emit("V", lambda e, k=k, qt=qt: e.scalar_tensor_tensor(
                            out=pre[:, k, :], in0=spk[:, k, :], scalar=0.0,
                            in1=ksb[:], op0=ALU.bypass, op1=ALU.mult,
                            accum_out=smalls[:, 88 + qt * 8 + k:89 + qt * 8 + k]),
                            waits=[st["tksb"], spk_toks[k]])
                        st["prejunk"] = P.emit(
                            "V", lambda e, k=k, qt=qt: e.scalar_tensor_tensor(
                                out=pre[:, k, :], in0=spk[:, k, :], scalar=0.0,
                                in1=spk[:, k, :], op0=ALU.bypass, op1=ALU.max,
                                accum_out=smalls[:, 56 + qt * 8 + k:
                                                 57 + qt * 8 + k]),
                            inc="sV", drain=True)
        st.pop("l1b_fin")()
        AB.release()
        if L1BST < 2:
            AB.release()
            return
        for k in range(KC):
            P.emit("V", lambda e, k=k: e.reduce_sum(
                out=smalls[:, 120 + k:121 + k],
                in_=smalls[:, 56 + k:88 + k:8], axis=mybir.AxisListType.X))
            P.emit("V", lambda e, k=k: e.reduce_sum(
                out=smalls[:, 128 + k:129 + k],
                in_=smalls[:, 88 + k:120 + k:8], axis=mybir.AxisListType.X))
        t_sums = P.emit("V", lambda e: e.tensor_scalar(
            out=smalls[:, 136:144], in0=smalls[:, 128:136],
            scalar1=consts[:, 1:2], scalar2=None, op0=ALU.mult),
            inc="sV", drain=True)
        t_mnegfree = st["l1b_lnmat_last"]
        tpairs = []
        for pc in range(8):
            w0 = []
            if pc == 0:
                w0 = [st.get(("tps", 0)), st.get(("tps", 4))]
            tpair = None
            for tb in range(16):
                tpair = P.emit("T", lambda e, tb=tb, pc=pc: e.matmul(
                    psP[:, pc, :], Ktm[:, tb, pc * 128:(pc + 1) * 128],
                    Vtm[:, tb, pc * 128:(pc + 1) * 128],
                    start=(tb == 0 and pc % 4 == 0),
                    stop=(tb == 15 and pc % 4 == 3)),
                    waits=w0 if tb == 0 else [],
                    inc="sT" if tb == 15 else None)
            tpairs.append(tpair)
        tpair = tpairs[-1]
        AB.mark()
        pA = AB.alloc([128, 8, 128], FP8E5)
        pB = AB.alloc([128, 8, 128], FP8E5)
        pC = AB.alloc([128, 8, 128], FP8E5)
        pD = AB.alloc([128, 8, 128], FP8E5)
        P.emit("V", lambda e: e.memset(pA[:, :, :], 0.0))
        P.emit("V", lambda e: e.memset(pB[:, :, :], 0.0))
        P.emit("V", lambda e: e.memset(pC[:, :, :], 0.0))
        P.emit("V", lambda e: e.memset(pD[:, :, :], 0.0))
        tpc = []
        I32 = mybir.dt.int32
        for pc in range(8):
            fm2 = mneg2[:, 0:64]
            im2 = mneg2[:, 64:128].bitcast(I32)
            ir5 = mneg2[:, 128:192].bitcast(I32)
            ir6 = mneg2[:, 192:256].bitcast(I32)
            ir8 = mneg2[:, 256:320].bitcast(I32)
            fr5 = mneg2[:, 320:384]
            fr6 = mneg2[:, 384:448]
            fr8 = mneg2[:, 448:512]
            P.emit("V", lambda e, pc=pc: e.tensor_scalar(
                out=mneg2[0:64, 0:64], in0=psP[0:64, pc, 0:64],
                scalar1=2.0, scalar2=None, op0=ALU.mult),
                waits=[tpairs[(pc // 4) * 4 + 3]])
            P.emit("V", lambda e, pc=pc: e.tensor_scalar(
                out=mneg2[64:128, 0:64], in0=psP[64:128, pc, 64:128],
                scalar1=2.0, scalar2=None, op0=ALU.mult),
                waits=[tpairs[(pc // 4) * 4 + 3]])
            P.emit("V", lambda e, fm2=fm2, im2=im2: e.tensor_copy(
                out=im2, in_=fm2))
            P.emit("V", lambda e, im2=im2, ir5=ir5: e.tensor_scalar(
                out=ir5, in0=im2, scalar1=511, scalar2=None,
                op0=ALU.bitwise_and))
            P.emit("V", lambda e, im2=im2, ir6=ir6: e.tensor_scalar(
                out=ir6, in0=im2, scalar1=63, scalar2=None,
                op0=ALU.bitwise_and))
            P.emit("V", lambda e, im2=im2, ir8=ir8: e.tensor_scalar(
                out=ir8, in0=im2, scalar1=7, scalar2=None,
                op0=ALU.bitwise_and))
            P.emit("V", lambda e, ir5=ir5, fr5=fr5: e.tensor_copy(
                out=fr5, in_=ir5))
            P.emit("V", lambda e, ir6=ir6, fr6=fr6: e.tensor_copy(
                out=fr6, in_=ir6))
            P.emit("V", lambda e, ir8=ir8, fr8=fr8: e.tensor_copy(
                out=fr8, in_=ir8))
            tk = None
            for lo, hi in ((0, 64), (64, 128)):
                P.emit("V", lambda e, pc=pc, lo=lo, hi=hi:
                       e.tensor_sub(pA[lo:hi, pc, lo:hi],
                                    mneg2[lo:hi, 0:64],
                                    mneg2[lo:hi, 320:384]))
                P.emit("V", lambda e, pc=pc, lo=lo, hi=hi:
                       e.tensor_sub(pB[lo:hi, pc, lo:hi],
                                    mneg2[lo:hi, 320:384],
                                    mneg2[lo:hi, 384:448]))
                P.emit("V", lambda e, pc=pc, lo=lo, hi=hi:
                       e.tensor_sub(pC[lo:hi, pc, lo:hi],
                                    mneg2[lo:hi, 384:448],
                                    mneg2[lo:hi, 448:512]))
                tk = P.emit("V", lambda e, pc=pc, lo=lo, hi=hi:
                            e.tensor_scalar(
                    pD[lo:hi, pc, lo:hi], mneg2[lo:hi, 448:512], 0.0, None,
                    op0=ALU.add), inc="sV" if hi == 128 else None,
                    drain=(hi == 128))
            tpc.append(tk)
        if L1BST < 3:
            AB.release()
            AB.release()
            return
        bias8 = mneg[:, :].bitcast(FP8).rearrange("p (mb l) -> p mb l",
                                                  mb=16, l=QT)
        wo, wot = load_weight(H["wob"], KC, D, wait0=st["l1b_wfree"])
        spkout = AB.alloc([128, KC, QT])
        opre = AB.alloc([128, KC, QT])
        st["bias8f"] = None
        for lq in range(4):
            c0 = lq * QT
            tb8 = P.emit("G", lambda e, c0=c0: e.dma_start(
                out=bias8[:, :, :], in_=H["biasmx"].rearrange(
                    "(mb p) l -> p mb l", p=128)[:, :, c0:c0 + QT]),
                waits=[st["bias8f"], t_mnegfree,
                       st.get("mneg_dma")], inc="dG")
            st["mneg_dma"] = tb8
            tqs = col_sums(lambda k: Q8[:, k, :], KC, 0,
                           lambda e, i, p: e.activation(
                               out=rowv(i), in_=p, func=ACT.Identity,
                               scale=consts[0:1, 1:2], bias=consts[0:1, 6:7]),
                           None, c0=c0, cols=QT, ones=consts85[:, 0:1])
            ta2 = bcast_row(0, lambda nt: ksb[:, :], 1.0, tqs, cols=QT)
            for ec in range(8):
                g = st["ppg"]; pg = pp[g % 2]
                w0 = [tb8]
                if st["ppfree"][g % 2] is not None:
                    w0.append(st["ppfree"][g % 2])
                tok = None
                for mb in range(16):
                    tok = P.emit("T", lambda e, mb=mb, ec=ec, pg=pg: e.matmul(
                        pg[:, 0, :], Vtm[:, mb, ec * 128:(ec + 1) * 128],
                        bias8[:, mb, :], start=(mb == 0), stop=False),
                        waits=w0 if mb == 0 else [], inc=None)
                for pi, piece in enumerate((pA, pB, pC, pD)):
                    tok = P.emit("T", lambda e, ec=ec, pg=pg, piece=piece,
                                 c0=c0: e.matmul(
                        pg[:, 0, :], piece[:, ec, :], Q8[:, ec, c0:c0 + QT],
                        start=False, stop=(piece is pD)),
                        waits=[tpc[ec], t_sums] if pi == 0 else [],
                        inc="sT" if pi == 3 else None)
                sc3 = ab2 if ec % 2 == 0 else mneg2
                t3 = P.emit("V", lambda e, ec=ec, pg=pg, sc3=sc3:
                            e.scalar_tensor_tensor(
                    out=sc3[:, :], in0=ksb[:, :],
                    scalar=smalls[:, 120 + ec:121 + ec], in1=pg[:, 0, :],
                    op0=ALU.mult, op1=ALU.add),
                    waits=[tok, ta2[-1], st.get(("ab2f", ec % 2)),
                           tpc[-1]], inc="sV", drain=True)
                tso = P.emit("V", lambda e, ec=ec, sc3=sc3: e.tensor_scalar(
                    out=spkout[:, ec, :], in0=sc3[:, :],
                    scalar1=smalls[:, 136 + ec:137 + ec],
                    scalar2=smalls[:, 112:113],
                    op0=ALU.add, op1=ALU.is_ge),
                    waits=[t_thr], inc="sV", drain=True)
                st[("ab2f", ec % 2)] = tso
                st[("so", ec)] = tso
                st["ppfree"][g % 2] = t3
                st["ppg"] += 1
                st["bias8f"] = tok

            def evac_ob(m, pg, waits):
                t_ = P.emit("S", lambda e, m=m, pg=pg: e.activation(
                    out=opre[:, m, :], in_=pg[:, 0, :], func=ACT.Identity,
                    bias=vecs[:, VEC["bo"] + m:VEC["bo"] + m + 1]),
                    waits=waits, inc="sS", drain=True)
                st[("pv", m)] = t_
                return t_

            gemm(wo, KC, list(range(KC)), lambda k: spkout[:, k, :], 0, QT,
                 evac_ob, waits0=[st[("so", m)] for m in range(8)], wk=wot)
            dsp = ln_spike(opre, [st[("pv", m)] for m in range(KC)],
                           VEC["go"], 48, lambda k: spkout[:, k, :], spkout)
            for k in range(KC):
                P.emit("V", lambda e, k=k, c0=c0: e.tensor_add(
                    R[:, k, c0:c0 + QT], R[:, k, c0:c0 + QT], spkout[:, k, :]),
                    waits=[dsp[k]])
        P.emit("V", lambda e: e.nop(), inc="sV", drain=True)
        AB.release()
        AB.release()

    subs = [sub_mamba, sub_ffn0, sub_attnb0, sub_ffn0,
            sub_l1a, sub_l1ffn, sub_l1b, sub_l1ffn]
    for f in subs[:NSUB]:
        f()
        st["sl_last_mm"] = st.get("last_mm")
        P.barrier()

    t_out = P.emit("G", lambda e: e.dma_start(
        out=outT.rearrange("(c p) t -> p c t", p=128), in_=R[:]), inc="dG")
    P.emit("G", lambda e: e.nop(), waits=[t_out])



def _pack_vec(v):
    v = np.asarray(v, np.float32)
    return v.reshape(v.shape[0] // 128, 128).T


def make_vecs(inp):
    cols = np.zeros((128, VEC_COLS), np.float32)

    def put(name, arr):
        a = _pack_vec(arr)
        cols[:, VEC[name]:VEC[name] + a.shape[1]] = a

    cw = np.asarray(inp["L0_attn_a_conv_w"], np.float32).reshape(E_INNER, 4)
    for c in range(16):
        for j in range(4):
            cols[:, VEC["convw"] + c * 4 + j] = cw[c * 128:(c + 1) * 128, j]
    put("convb", inp["L0_attn_a_conv_b"])
    put("mu_r", inp["L0_ffn_mu_r"]); put("mu_k", inp["L0_ffn_mu_k"])
    put("b_in", inp["L0_attn_b_b_in"]); put("A0", inp["L0_attn_b_A"])
    put("C0", inp["L0_attn_b_C"]); put("b_out", inp["L0_attn_b_b_out"])
    for nm, key in [("bq", "L1_attn_b_bq"), ("bk", "L1_attn_b_bk"),
                    ("bv", "L1_attn_b_bv"), ("bo", "L1_attn_b_bo"),
                    ("gq", "L1_attn_b_gq"), ("gk", "L1_attn_b_gk"),
                    ("gv", "L1_attn_b_gv"), ("go", "L1_attn_b_go"),
                    ("betaq", "L1_attn_b_betaq"), ("betak", "L1_attn_b_betak"),
                    ("betav", "L1_attn_b_betav"), ("betao", "L1_attn_b_betao")]:
        put(nm, inp[key])
    cols[:, VEC["scale_b"]] = np.float32(np.asarray(inp["L1_attn_b_scale"]))
    put("f1b", inp["L1_ffn_bias"])
    return cols


def make_biasmx():
    i = np.arange(T, dtype=np.float64)
    dist = np.abs(i[:, None] - i[None, :])
    b = np.maximum(np.ceil(np.log2((T - 1.0) / (dist + 1.0))), 0.0)
    r32 = (np.float32(T - 1.0) / (dist.astype(np.float32) + np.float32(1.0)))
    b32 = np.maximum(np.ceil(np.log2(r32)), 0.0)
    assert np.array_equal(b, b32.astype(np.float64))
    return b.astype(ml_dtypes.float8_e4m3)


_NC_CACHE = {}


def kernel(**inp):
    x = np.asarray(inp["L0_attn_a_hidden_states"], np.float32)
    B = x.shape[0]
    if "p" not in _NC_CACHE:
        _NC_CACHE["p"] = build_program()
    nc = _NC_CACHE["p"]

    def ct(a):
        return np.ascontiguousarray(np.asarray(a, np.float32))

    shared = {
        "w_inproj": ct(np.asarray(inp["L0_attn_a_in_proj_w"])[:E_INNER, :].T),
        "w_outproj": ct(np.asarray(inp["L0_attn_a_out_proj_w"]).T),
        "w0r": ct(np.asarray(inp["L0_ffn_W_r"]).T),
        "w0k": ct(np.asarray(inp["L0_ffn_W_k"]).T),
        "w0v": ct(np.asarray(inp["L0_ffn_W_v"]).T),
        "w_in0": ct(inp["L0_attn_b_W_in"]),
        "w_out0": ct(inp["L0_attn_b_W_out"]),
        "wqa": ct(np.asarray(inp["L1_attn_a_W_q"]).T),
        "wka": ct(np.asarray(inp["L1_attn_a_W_k"]).T),
        "wva": ct(np.asarray(inp["L1_attn_a_W_v"]).T),
        "woa": ct(np.asarray(inp["L1_attn_a_W_o"]).T),
        "wqb": ct(np.asarray(inp["L1_attn_b_Wq"]).T),
        "wkb": ct(np.asarray(inp["L1_attn_b_Wk"]).T),
        "wvb": ct(np.asarray(inp["L1_attn_b_Wv"]).T),
        "wob": ct(np.asarray(inp["L1_attn_b_Wo"]).T),
        "wf1": ct(inp["L1_ffn_weight"]),
        "wf2": ct(inp["L1_ffn_aux_weight1"]),
        "wf3": ct(inp["L1_ffn_aux_weight2"]),
        "vecs": make_vecs(inp),
        "ident": np.eye(128, dtype=np.float32),
        "biasmx": make_biasmx(),
    }
    in_maps = []
    for b in range(B):
        m = dict(shared)
        m["xT"] = ct(x[b].T)
        in_maps.append(m)
    res = run_bass_kernel_spmd(nc, in_maps, core_ids=list(range(B)))
    out = np.stack([np.asarray(r["outT"], np.float32).T for r in res.results])
    return out


# version 28
# speedup vs baseline: 1.0213x; 1.0213x over previous
import os

import numpy as np
import ml_dtypes

import concourse.bass as bass
import concourse.mybir as mybir
from concourse.bass_utils import run_bass_kernel_spmd

F32 = mybir.dt.float32
BF16 = mybir.dt.bfloat16
FP8 = mybir.dt.float8e4
FP8E5 = mybir.dt.float8e5
ALU = mybir.AluOpType
ACT = mybir.ActivationFunctionType

T = 2048
D = 1024
KC = 8
E_INNER = 2048
EPS = 1e-5
NSUB = int(os.environ.get("NSUB", "8"))
STAGE = int(os.environ.get("STAGE", "99"))
L1BST = int(os.environ.get("L1BST", "9"))

ENGINES = ("T", "V", "S", "G", "Y")
DMA_SEMS = ("dY0", "dY1", "dY2", "dY3", "dYC", "dG", "dS")


class Prog:
    def __init__(self):
        self.ops = {e: [] for e in ENGINES}
        self.cnt = {s: 0 for s in ("sT", "sV", "sS", "sG") + DMA_SEMS}
        self.waited = {e: {} for e in ENGINES}

    OWN = {"T": "sT", "V": "sV", "S": "sS", "G": "sG"}

    def emit(self, eng, fn, waits=(), inc=None, drain=None, nodrain=False):
        own = self.OWN.get(eng)
        waits = [w for w in waits if w is not None and w[0] != own]
        eff = []
        for s, v in waits:
            if eng != "G" and self.waited[eng].get(s, 0) >= v:
                continue
            self.waited[eng][s] = max(self.waited[eng].get(s, 0), v)
            eff.append((s, v))
        tok = None
        if inc is not None:
            amt = 16 if inc in DMA_SEMS else 1
            self.cnt[inc] += amt
            tok = (inc, self.cnt[inc])
        is_dma = inc in DMA_SEMS
        do_drain = ((eng in ("V", "S", "G")) or drain is True) and not is_dma \
            and not (nodrain and inc is None)

        def op(e, sems, eff=eff, fn=fn, inc=inc, do_drain=do_drain,
               is_dma=is_dma):
            for s, v in eff:
                e.wait_ge(sems[s], v)
            inst = fn(e)
            if do_drain:
                inst = e.drain()
            if inc is not None:
                inst.then_inc(sems[inc], 16 if is_dma else 1)

        import traceback
        if os.environ.get("PROV"):
            fr = traceback.extract_stack()[-3:-1]
            op._prov = " | ".join(f"{f.name}:{f.lineno}" for f in fr)
        self.ops[eng].append(op)
        return tok

    bar_toks = {}

    def barrier(self):
        pre = {s: self.cnt[s] for s in ("sT", "sV", "sS", "sG")}
        toks = {}
        for e, sem in (("T", "sT"), ("V", "sV"), ("S", "sS"), ("G", "sG")):
            w = [(s, v) for s, v in pre.items() if s != sem and v > 0]
            toks[e] = self.emit(e, lambda en: en.nop(), waits=w, inc=sem,
                                drain=True)
        for e in ENGINES:
            self.emit(e, lambda en: en.nop(),
                      waits=[toks[o] for o in toks if o != e])
        self.bar_toks = dict(toks)


class Arena:

    ESIZE = {}

    def __init__(self, handle, nbytes, name):
        self.h, self.n, self.name = handle, nbytes, name
        self.off = 0
        self.marks = []
        self.hwm = 0

    def alloc(self, shape, dt=BF16):
        assert shape[0] == 128
        es = mybir.dt.size(dt)
        n = int(np.prod(shape[1:])) * es
        self.off = (self.off + 63) // 64 * 64
        assert self.off + n <= self.n, \
            f"arena {self.name} overflow: {self.off}+{n} > {self.n}"
        ap = self.h[:, self.off:self.off + n].bitcast(dt)
        self.off += n
        self.hwm = max(self.hwm, self.off)
        if len(shape) > 2:
            dims = " ".join(f"d{i}" for i in range(1, len(shape)))
            kw = {f"d{i}": shape[i] for i in range(1, len(shape))}
            ap = ap.rearrange(f"p ({dims}) -> p {dims}", **kw)
        return ap

    def mark(self):
        self.marks.append(self.off)

    def release(self):
        self.off = self.marks.pop()


def ceil_div(a, b):
    return (a + b - 1) // b


VEC = {}
_vc = 0
for _n, _c in [
    ("convw", 64), ("convb", 16), ("mu_r", 8), ("mu_k", 8),
    ("b_in", 16), ("A0", 8), ("C0", 8), ("b_out", 8),
    ("bq", 8), ("bk", 8), ("bv", 8), ("bo", 8),
    ("gq", 8), ("gk", 8), ("gv", 8), ("go", 8),
    ("betaq", 8), ("betak", 8), ("betav", 8), ("betao", 8),
    ("scale_b", 1), ("f1b", 8),
]:
    VEC[_n] = _vc
    _vc += _c
VEC_COLS = _vc


def build_program():
    nc = bass.Bass("TRN2", target_bir_lowering=False)
    sbp = nc.SBUF_PARTITION_SIZE_BYTES

    def hin(name, shape, dt=F32):
        return nc.dram_tensor(name, list(shape), dt, kind="ExternalInput").ap()

    H = {}
    H["xT"] = hin("xT", [D, T])
    for n, sh in [("w_inproj", [D, E_INNER]), ("w_outproj", [E_INNER, D]),
                  ("w0r", [D, D]), ("w0k", [D, D]), ("w0v", [D, D]),
                  ("w_in0", [D, E_INNER]), ("w_out0", [D, D]),
                  ("wqa", [D, D]), ("wka", [D, D]), ("wva", [D, D]),
                  ("woa", [D, D]),
                  ("wqb", [D, D]), ("wkb", [D, D]), ("wvb", [D, D]),
                  ("wob", [D, D]),
                  ("wf1", [D, D]), ("wf2", [D, D]), ("wf3", [D, D])]:
        H[n] = hin(n, sh)
    H["vecs"] = hin("vecs", [128, VEC_COLS])
    H["ident"] = hin("ident", [128, 128])
    H["biasmx"] = hin("biasmx", [T, T], FP8)
    outT = nc.dram_tensor("outT", [D, T], F32, kind="ExternalOutput").ap()

    from contextlib import ExitStack
    with ExitStack() as ctx:
        E = {}
        def sbuf(name, shape, dt=F32):
            E[name] = ctx.enter_context(nc.sbuf_tensor(name, list(shape), dt))
            return E[name]

        R = sbuf("R", [128, KC, T])
        mneg = sbuf("mneg", [128, T])
        a_b = sbuf("a_b", [128, T], BF16)
        vecs = sbuf("vecs_s", [128, VEC_COLS])
        ident = sbuf("ident_s", [128, 128])
        identb = sbuf("identb_s", [128, 128], BF16)
        consts = sbuf("consts", [128, 8])
        constsb = sbuf("constsb", [128, 8], BF16)
        crowb = sbuf("crowb", [1, 2, 128], BF16)
        rows = sbuf("rows", [1, 8, 512])
        smalls = sbuf("smalls", [128, 160])
        stage = sbuf("stage", [128, 3, 1024])

        mneg2 = sbuf("mneg2", [128, 512])
        ksb = sbuf("ksb", [128, 512])
        ab2 = sbuf("ab2", [128, 512])
        consts8 = sbuf("consts8", [128, 8], FP8)
        consts85 = sbuf("consts85", [128, 8], FP8E5)
        ar_n = (nc.sbuf_top - nc.sbuf_base - 2048) // 64 * 64
        arena_b = ctx.enter_context(
            nc.sbuf_tensor("arena_b", [128, ar_n], mybir.dt.uint8))
        AB = Arena(arena_b, ar_n, "main")
        A8 = AB

        psA = ctx.enter_context(nc.psum_tensor("psA", [128, 2, 512], F32))
        psB = ctx.enter_context(nc.psum_tensor("psB", [128, 2, 512], F32))
        psC = ctx.enter_context(nc.psum_tensor("psC", [128, 512], F32))
        psD = ctx.enter_context(nc.psum_tensor("psD", [128, 512], F32))
        psP = ctx.enter_context(nc.psum_tensor("psP", [128, 8, 128], F32))

        sems = {s: ctx.enter_context(nc.semaphore(s))
                for s in ("sT", "sV", "sS", "sG") + DMA_SEMS}

        P = Prog()
        _build_body(P, nc, H, outT, locals())

        with nc.Block() as block:
            @block.tensor
            def _(e):
                for op in P.ops["T"]:
                    op(e, sems)

            @block.vector
            def _(e):
                for op in P.ops["V"]:
                    op(e, sems)

            @block.scalar
            def _(e):
                for op in P.ops["S"]:
                    op(e, sems)

            @block.gpsimd
            def _(e):
                for op in P.ops["G"]:
                    op(e, sems)

            @block.sync
            def _(e):
                for op in P.ops["Y"]:
                    op(e, sems)
    return nc


def _build_body(P, nc, H, outT, E):
    R = E["R"]; mneg = E["mneg"]; a_b = E["a_b"]; vecs = E["vecs"]
    crowb = E["crowb"]

    def rowv(i):
        return rows[0:1, i, 0:256].bitcast(BF16)
    mneg2 = E["mneg2"]; ab2 = E["ab2"]; consts8 = E["consts8"]
    ksb = E["ksb"]; consts85 = E["consts85"]
    ident = E["ident"]; identb = E["identb"]
    consts = E["consts"]; constsb = E["constsb"]
    rows = E["rows"]; smalls = E["smalls"]; stage = E["stage"]
    AB = E["AB"]; A8 = E["A8"]
    psA = E["psA"]; psB = E["psB"]; psC = E["psC"]; psD = E["psD"]
    psP = E["psP"]

    st = {}

    P.emit("Y", lambda e: e.dma_start(
        out=R[:], in_=H["xT"].rearrange("(c p) t -> p c t", p=128)), inc="dYC")
    P.emit("Y", lambda e: e.dma_start(out=vecs[:], in_=H["vecs"][:]), inc="dYC")
    P.emit("Y", lambda e: e.dma_start(out=ident[:], in_=H["ident"][:]),
           inc="dYC")
    t_init = ("dYC", 48)
    t_vec = t_init
    t_id = t_init
    st_init = t_init
    P.emit("V", lambda e: e.memset(crowb[0:1, 0, :], -1.0))
    P.emit("V", lambda e: e.memset(crowb[0:1, 1, :], 1.0))
    P.emit("V", lambda e: e.memset(consts[:, 1:2], -1.0))
    P.emit("V", lambda e: e.memset(consts[:, 2:3], 1.0))
    P.emit("V", lambda e: e.memset(consts[:, 4:5], EPS))
    P.emit("V", lambda e: e.memset(consts[:, 5:6], 1.0 / 1024.0))
    P.emit("V", lambda e: e.memset(consts[:, 6:7], 64.0))
    P.emit("V", lambda e: e.memset(consts[:, 7:8], 0.125))
    P.emit("V", lambda e: e.memset(consts8[:, 0:1], 1.0))
    P.emit("V", lambda e: e.memset(consts85[:, 0:1], 1.0))
    t_cb = P.emit("V", lambda e: e.memset(constsb[:, 0:1], 1.0 / 1024.0),
                  inc="sV", drain=True)
    t_idb = P.emit("S", lambda e: e.copy(out=identb[:], in_=ident[:]),
                   waits=[t_id], inc="sS", drain=True)
    st["tvec"] = t_init

    def load_weight(hbm, kk, mm, into=None, wait0=None):
        if into is None:
            wt = AB.alloc([128, kk, mm])
        else:
            wt = into.rearrange("p (k m) -> p k m", k=kk, m=mm) \
                if kk * mm == into.free_size() else into[:, 0:kk * mm].rearrange(
                    "p (k m) -> p k m", k=kk, m=mm)
        toks = []
        src = hbm.rearrange("(k p) m -> p k m", p=128)
        pieces = [(k, o) for k in range(kk) for o in range(0, mm, 1024)]
        st.setdefault("stgi", 0)
        for k, o in pieces:
            w = min(1024, mm - o)
            i = st["stgi"] % 3
            st["stgi"] += 1
            sl = stage[:, i, 0:w]
            prev = st.get(("stg", i))
            td = P.emit("Y", lambda e, k=k, o=o, w=w, sl=sl, src=src: e.dma_start(
                out=sl, in_=src[:, k, o:o + w]), waits=[prev] if prev else [],
                inc=f"dY{i}")
            cw = [td] + ([wait0] if wait0 else [])
            cw += list(P.bar_toks.values())
            cw.append(st.get("sl_last_mm"))
            tc = P.emit("S", lambda e, k=k, o=o, w=w, sl=sl, wt=wt: e.copy(
                out=wt[:, k, o:o + w], in_=sl), waits=cw, inc="sS", drain=True)
            st[("stg", i)] = tc
            toks.append(tc)
        ktoks = [None] * kk
        for (k, o), tc in zip(pieces, toks):
            ktoks[k] = tc
        return wt, ktoks

    pp = [psA, psB]
    st["ppg"] = 0
    st["ppfree"] = {0: None, 1: None}

    def gemm(wt, kk, m_list, rhs_of, c0, cols, evac, waits0=(), wk=None,
             rk=None):
        ntiles = ceil_div(cols, 512)
        assert ntiles <= 2
        first_m = True
        for m in m_list:
            g = st["ppg"]; pg = pp[g % 2]
            w0 = list(waits0)
            if st["ppfree"][g % 2] is not None:
                w0.append(st["ppfree"][g % 2])
            tok_mm = None
            for k in range(kk):
                for nt in range(ntiles):
                    nn = min(512, cols - nt * 512)
                    last = (k == kk - 1 and nt == ntiles - 1)
                    w = w0 if (k == 0 and nt == 0) else []
                    if wk is not None and nt == 0:
                        w = list(w) + [wk[k]]
                    if rk is not None and first_m and nt == 0:
                        w = list(w) + [rk[k]]
                    tok_mm = P.emit("T", lambda e, k=k, nt=nt, nn=nn, m=m,
                                    pg=pg, wt=wt, rhs_of=rhs_of, c0=c0: e.matmul(
                        pg[:, nt, 0:nn], wt[:, k, m * 128:(m + 1) * 128],
                        rhs_of(k)[:, c0 + nt * 512:c0 + nt * 512 + nn],
                        start=(k == 0), stop=(k == kk - 1)),
                        waits=w, inc="sT" if last else None)
            st["last_mm"] = tok_mm
            tfree = evac(m, pg, [tok_mm])
            st["ppfree"][g % 2] = tfree
            st["ppg"] += 1
            first_m = False

    def col_sums(src_of, kk, base, finish, toks_in, c0=0, cols=T, ones=None):
        ones = constsb[:, 0:1] if ones is None else ones
        out_toks = []
        for nt in range(cols // 512):
            ps, key = (psC, "psC") if nt % 2 == 0 else (psD, "psD")
            tok = None
            for k in range(kk):
                w = [t_cb, toks_in[k] if toks_in else None]
                if k == 0:
                    w.append(st.get(("aux", key)))
                    w.append(st.get(("row", base + nt)))
                tok = P.emit("T", lambda e, k=k, nt=nt, ps=ps, ones=ones: e.matmul(
                    ps[0:1, 0:512], ones,
                    src_of(k)[:, c0 + nt * 512:c0 + (nt + 1) * 512],
                    start=(k == 0), stop=(k == kk - 1)),
                    waits=w, inc="sT" if k == kk - 1 else None)
            te = P.emit("S", lambda e, nt=nt, ps=ps: finish(
                e, base + nt, ps[0:1, 0:512]),
                waits=[tok], inc="sS", drain=True)
            st[("aux", key)] = te
            out_toks.append(te)
        return out_toks

    def bcast_row(base, dst_of, sign, waits, cols=T):
        lhsT = crowb[0:1, 0, :] if sign < 0 else crowb[0:1, 1, :]
        toks = []
        for nt in range(cols // 512):
            ps, key = (psC, "psC") if nt % 2 == 0 else (psD, "psD")
            w = list(waits) + [st.get(("aux", key))]
            tok = P.emit("T", lambda e, nt=nt, ps=ps, lhsT=lhsT: e.matmul(
                ps[:, 0:512], lhsT, rowv(base + nt),
                start=True, stop=True), waits=w, inc="sT")
            st[("row", base + nt)] = tok
            te = P.emit("S", lambda e, nt=nt, ps=ps: e.copy(
                out=dst_of(nt), in_=ps[:, 0:512]),
                waits=[tok, st.get("mneg_dma")], inc="sS", drain=True)
            st[("aux", key)] = te
            toks.append(te)
        return toks

    def _ln_core(src_of, kk):
        accm = [psA[0:1, 0, :], psA[0:1, 1, :], psB[0:1, 0, :], psB[0:1, 1, :]]
        accv = [psC[0:1, 0:512], psD[0:1, 0:512],
                psP[0:1, 0:4, :].rearrange("p a b -> p (a b)"),
                psP[0:1, 4:8, :].rearrange("p a b -> p (a b)")]
        akm = ["psA0", "psA1", "psB0", "psB1"]
        akv = ["psC", "psD", "psP0", "psP1"]
        AB.mark()
        xb = AB.alloc([128, 2, T])
        xq = AB.alloc([128, 2, T])
        mtoks = []
        vtoks = []
        for k in range(kk):
            tc = P.emit("V", lambda e, k=k: e.tensor_copy(
                out=xb[:, k % 2, :], in_=src_of(k)),
                waits=[("dYC", 48), st.get(("lnm", k % 2))],
                inc="sV", drain=True)
            tq = P.emit("V", lambda e, k=k: e.tensor_mul(
                xq[:, k % 2, :], xb[:, k % 2, :], xb[:, k % 2, :]),
                waits=[st.get(("lnv", k % 2))], inc="sV", drain=True)
            tok = tok2 = None
            for nt in range(T // 512):
                w = [tc]
                if k == 0:
                    w += [t_cb, st.get(("aux", akm[nt]))]
                tok = P.emit("T", lambda e, k=k, nt=nt: e.matmul(
                    accm[nt], constsb[:, 0:1],
                    xb[:, k % 2, nt * 512:(nt + 1) * 512],
                    start=(k == 0), stop=(k == kk - 1)),
                    waits=w, inc="sT" if nt == T // 512 - 1 else None)
            st[("lnm", k % 2)] = tok
            mtoks.append(tok)
            for nt in range(T // 512):
                w = [tq]
                if k == 0:
                    w += [st.get(("aux", akv[nt])), st.get(("row", nt)),
                          st.get(("row", 4 + nt))]
                tok2 = P.emit("T", lambda e, k=k, nt=nt: e.matmul(
                    accv[nt], constsb[:, 0:1],
                    xq[:, k % 2, nt * 512:(nt + 1) * 512],
                    start=(k == 0), stop=(k == kk - 1)),
                    waits=w, inc="sT" if nt == T // 512 - 1 else None)
            st[("lnv", k % 2)] = tok2
            vtoks.append(tok2)
        tm = []
        tv = []
        for nt in range(T // 512):
            te = P.emit("S", lambda e, nt=nt: e.activation(
                out=rowv(nt), in_=accm[nt], func=ACT.Copy),
                waits=[mtoks[-1], st.get(("row", nt))], inc="sS", drain=True)
            st[("aux", akm[nt])] = te
            tm.append(te)
            t2 = P.emit("S", lambda e, nt=nt: e.activation(
                out=rows[0:1, 4 + nt, :], in_=accv[nt], func=ACT.Copy),
                waits=[vtoks[-1]], inc="sS", drain=True)
            st[("aux", akv[nt])] = t2
            tv.append(t2)
        t_mneg = bcast_row(0, lambda nt: mneg[:, nt * 512:(nt + 1) * 512],
                           -1.0, tm)
        NT = T // 512
        mrow = mneg[0:1, :].rearrange("p (a b) -> p a b", a=NT)
        tq1 = P.emit("V", lambda e: e.tensor_mul(
            rows[0:1, 0:NT, :], mrow, mrow),
            waits=[t_mneg[-1], tv[-1]] +
                  [st[("row", nt)] for nt in range(NT)],
            inc="sV", drain=True)
        tq2 = P.emit("V", lambda e: e.tensor_sub(
            rows[0:1, 4:4 + NT, :], rows[0:1, 4:4 + NT, :],
            rows[0:1, 0:NT, :]), waits=[tq1], inc="sV", drain=True)
        tq3 = P.emit("S", lambda e: e.activation(
            out=rows[0:1, 0:NT, :], in_=rows[0:1, 4:4 + NT, :],
            func=ACT.Sqrt, bias=consts[0:1, 4:5]),
            waits=[tq2], inc="sS", drain=True)
        tq4 = P.emit("V", lambda e: e.reciprocal(
            out=rows[0:1, 4:4 + NT, :], in_=rows[0:1, 0:NT, :]),
            waits=[tq3], inc="sV", drain=True)
        rtok = P.emit("S", lambda e: e.copy(
            out=rows[0:1, 0:NT, 0:256].bitcast(BF16),
            in_=rows[0:1, 4:4 + NT, :]),
            waits=[tq4], inc="sS", drain=True)
        rt = [rtok] * NT
        t_ab = bcast_row(0, lambda nt: a_b[:, nt * 512:(nt + 1) * 512],
                         1.0, rt)
        AB.release()
        return t_mneg, t_ab, tv[-1]

    def ln_stats(src_of, kk=KC):
        t_mneg, t_ab, t_scr = _ln_core(src_of, kk)
        return t_ab[-1], t_scr, t_mneg[-1]

    def ln_mat(dst_of, src_of, k_list, c0, cols, tok, tok_mul=None):
        mt = []
        w0 = tok if isinstance(tok, list) else [tok]
        wm = [tok_mul] if tok_mul is not None else []
        for k in k_list:
            ta = P.emit("V", lambda e, k=k: e.tensor_add(
                dst_of(k), src_of(k)[:, c0:c0 + cols],
                mneg[:, c0:c0 + cols]), waits=w0, inc="sV", drain=True)
            mt.append(P.emit("V", lambda e, k=k: e.tensor_mul(
                dst_of(k), dst_of(k), a_b[:, c0:c0 + cols]),
                waits=[ta] + wm, inc="sV", drain=True))
        return mt

    def sub_mamba():
        if STAGE < 1:
            return
        AB.mark()
        QT = 512
        halo = AB.alloc([128, 16, 4])
        slotA = AB.alloc([128, 8 * E_INNER])
        slotB = AB.alloc([128, 16 * D])
        t_ab, t_scr, t_mneg = ln_stats(lambda k: R[:, k, :])
        wt_in, tk_in = load_weight(H["w_inproj"], KC, E_INNER, into=slotA)
        wt_out, tk_out = load_weight(H["w_outproj"], 16, D, into=slotB)
        if STAGE < 2:
            AB.release()
            return
        for q in range(T // QT):
            c0 = q * QT
            AB.mark()
            ln = AB.alloc([128, KC, QT])
            s1 = AB.alloc([128, 16, QT])
            x1 = AB.alloc([128, 2, QT])
            acc = AB.alloc([128, 2, QT])
            lt = []
            for k in range(KC):
                lt.append(P.emit("V", lambda e, k=k, c0=c0, ln=ln: e.tensor_add(
                    ln[:, k, :], R[:, k, c0:c0 + QT], mneg[:, c0:c0 + QT]),
                    waits=[t_mneg], inc="sV"))

            def evac_conv(m, pg, waits, q=q, s1=s1, x1=x1, acc=acc, c0=c0):
                sl = m % 2
                w = list(waits) + [st.get(("x1f", sl)), st.get(("accf", sl)),
                                   t_ab]
                tcp = P.emit("V", lambda e, sl=sl, pg=pg, c0=c0: e.tensor_mul(
                    x1[:, sl, :], pg[:, 0, 0:QT], a_b[:, c0:c0 + QT]),
                    waits=w, inc="sV", drain=True)
                tap = VEC["convw"] + m * 4
                P.emit("V", lambda e, sl=sl, tap=tap, x1=x1, acc=acc:
                       e.tensor_scalar(
                           out=acc[:, sl, :], in0=x1[:, sl, :],
                           scalar1=vecs[:, tap + 3:tap + 4], scalar2=None,
                           op0=ALU.mult), waits=[st["tvec"], tcp])
                for j in (2, 1, 0):
                    sh = 3 - j
                    P.emit("V", lambda e, sl=sl, tap=tap, j=j, sh=sh, x1=x1,
                           acc=acc: e.scalar_tensor_tensor(
                               out=acc[:, sl, sh:QT], in0=x1[:, sl, 0:QT - sh],
                               scalar=vecs[:, tap + j:tap + j + 1],
                               in1=acc[:, sl, sh:QT],
                               op0=ALU.mult, op1=ALU.add))
                    if q > 0:
                        P.emit("V", lambda e, m=m, sl=sl, tap=tap, j=j, sh=sh,
                               acc=acc: e.scalar_tensor_tensor(
                                   out=acc[:, sl, 0:sh],
                                   in0=halo[:, m, 4 - sh:4],
                                   scalar=vecs[:, tap + j:tap + j + 1],
                                   in1=acc[:, sl, 0:sh],
                                   op0=ALU.mult, op1=ALU.add))
                tv = P.emit("V", lambda e, m=m, sl=sl, x1=x1: e.tensor_copy(
                    out=halo[:, m, 1:4], in_=x1[:, sl, QT - 3:QT]),
                    inc="sV", drain=True)
                st[("x1f", sl)] = tv
                ts = P.emit("S", lambda e, m=m, sl=sl, s1=s1, acc=acc:
                            e.activation(
                                out=s1[:, m, :], in_=acc[:, sl, :],
                                func=ACT.Silu,
                                bias=vecs[:, VEC["convb"] + m:
                                          VEC["convb"] + m + 1]),
                            waits=[tv], inc="sS", drain=True)
                st[("accf", sl)] = ts
                st[("s1", m)] = ts
                return tcp

            gemm(wt_in, KC, list(range(16)), lambda k, ln=ln: ln[:, k, :],
                 0, QT, evac_conv, wk=tk_in if q == 0 else None, rk=lt)

            def evac_res(m, pg, waits, c0=c0):
                return P.emit("V", lambda e, m=m, pg=pg, c0=c0: e.tensor_add(
                    R[:, m, c0:c0 + QT], R[:, m, c0:c0 + QT], pg[:, 0, 0:QT]),
                    waits=waits, inc="sV", drain=True)

            if STAGE < 3:
                AB.release()
                break
            w_s1 = [st[("s1", m)] for m in range(16)]
            gemm(wt_out, 16, list(range(KC)), lambda k, s1=s1: s1[:, k, :],
                 0, QT, evac_res, waits0=w_s1,
                 wk=tk_out if q == 0 else None)
            AB.release()
        AB.release()


    def sub_ffn0():
        QT = 512
        NQ = T // QT
        AB.mark()
        slot_r = AB.alloc([128, 8 * D])
        slot_k = AB.alloc([128, 8 * D])
        slot_v = AB.alloc([128, 8 * D])
        t_ab, t_scr, t_mneg = ln_stats(lambda k: R[:, k, :])
        wr, wrt = load_weight(H["w0r"], KC, D, into=slot_r)
        wk_, wkt = load_weight(H["w0k"], KC, D, into=slot_k)
        wv, wvt = load_weight(H["w0v"], KC, D, into=slot_v)
        lnp = AB.alloc([128, 2, QT + 1])
        d = AB.alloc([128, KC, QT])
        xr = AB.alloc([128, KC, QT])
        xk = AB.alloc([128, KC, QT])
        r_sb = AB.alloc([128, KC, QT])
        kact = AB.alloc([128, KC, QT])
        dtmp = AB.alloc([128, 2, QT])
        P.emit("V", lambda e: e.memset(smalls[:, 0:8], 0.0))
        st["ffn_xf"] = None
        st["ffn_xkf"] = None

        def prep_xr(q):
            c0 = q * QT
            tk = None
            for k in range(KC):
                sl = k % 2
                mur = vecs[:, VEC["mu_r"] + k:VEC["mu_r"] + k + 1]
                P.emit("V", lambda e, k=k, sl=sl: e.tensor_copy(
                    out=lnp[:, sl, 0:1], in_=smalls[:, k:k + 1]))
                P.emit("V", lambda e, k=k, sl=sl, c0=c0: e.tensor_add(
                    lnp[:, sl, 1:QT + 1], R[:, k, c0:c0 + QT],
                    mneg[:, c0:c0 + QT]),
                    waits=[t_mneg, st["tvec"], st.get("ffn_xf"),
                           st.get("ffn_dxf")])
                P.emit("V", lambda e, k=k, sl=sl, c0=c0: e.tensor_mul(
                    lnp[:, sl, 1:QT + 1], lnp[:, sl, 1:QT + 1],
                    a_b[:, c0:c0 + QT]),
                    waits=[t_ab])
                P.emit("V", lambda e, k=k, sl=sl: e.tensor_sub(
                    d[:, k, :], lnp[:, sl, 1:QT + 1], lnp[:, sl, 0:QT]))
                tk = P.emit("V", lambda e, k=k, sl=sl, mur=mur:
                            e.scalar_tensor_tensor(
                                out=xr[:, k, :], in0=d[:, k, :],
                                scalar=mur, in1=lnp[:, sl, 0:QT],
                                op0=ALU.mult, op1=ALU.add),
                            inc="sV" if k == KC - 1 else None, drain=False)
                P.emit("V", lambda e, k=k, sl=sl: e.tensor_copy(
                    out=smalls[:, k:k + 1], in_=lnp[:, sl, QT:QT + 1]))
            return tk

        def prep_xk(q):
            c0 = q * QT
            tk = None
            for k in range(KC):
                muk = vecs[:, VEC["mu_k"] + k:VEC["mu_k"] + k + 1]
                mur = vecs[:, VEC["mu_r"] + k:VEC["mu_r"] + k + 1]
                P.emit("V", lambda e, k=k, mur=mur: e.scalar_tensor_tensor(
                    out=xk[:, k, :], in0=d[:, k, :], scalar=mur,
                    in1=xr[:, k, :], op0=ALU.mult, op1=ALU.subtract),
                    waits=[st.get("ffn_xkf")])
                tk = P.emit("V", lambda e, k=k, muk=muk:
                            e.scalar_tensor_tensor(
                                out=xk[:, k, :], in0=d[:, k, :], scalar=muk,
                                in1=xk[:, k, :], op0=ALU.mult,
                                op1=ALU.subtract),
                            inc="sV" if k == KC - 1 else None, drain=False)
            return tk

        t_prep = prep_xr(0)
        t_prepk = prep_xk(0)
        for q in range(NQ):
            c0 = q * QT

            def evac_r(m, pg, waits):
                return P.emit("S", lambda e, m=m, pg=pg: e.activation(
                    out=r_sb[:, m, :], in_=pg[:, 0, :], func=ACT.Sigmoid),
                    waits=waits, inc="sS", drain=True)

            gemm(wr, KC, list(range(KC)), lambda k: xr[:, k, :], 0, QT,
                 evac_r, waits0=[t_prep], wk=wrt if q == 0 else None)
            tmm_r = st["last_mm"]
            st["ffn_xf"] = tmm_r
            if q + 1 < NQ:
                t_prep = prep_xr(q + 1)

            def evac_k(m, pg, waits):
                sl = m % 2
                tr_ = P.emit("V", lambda e, m=m, pg=pg, sl=sl: e.tensor_scalar(
                    out=dtmp[:, sl, :], in0=pg[:, 0, :],
                    scalar1=0.0, scalar2=None, op0=ALU.max),
                    waits=list(waits) + [st.get(("dt", sl))],
                    inc="sV", drain=True)
                tk_ = P.emit("S", lambda e, m=m, sl=sl: e.activation(
                    out=kact[:, m, :], in_=dtmp[:, sl, :], func=ACT.Square),
                    waits=[tr_], inc="sS", drain=True)
                st[("dt", sl)] = tk_
                st[("kact", m)] = tk_
                return tr_

            gemm(wk_, KC, list(range(KC)), lambda k: xk[:, k, :], 0, QT,
                 evac_k, waits0=[t_prepk], wk=wkt if q == 0 else None)
            st["ffn_xkf"] = st["last_mm"]
            st["ffn_dxf"] = st["last_mm"]

            def evac_v(m, pg, waits, c0=c0):
                sl = m % 2
                tm_ = P.emit("V", lambda e, m=m, pg=pg, sl=sl: e.tensor_mul(
                    dtmp[:, sl, :], pg[:, 0, :], r_sb[:, m, :]),
                    waits=list(waits) + [st.get(("dt", sl))],
                    inc="sV", drain=True)
                st[("dt", sl)] = P.emit("V", lambda e, m=m, sl=sl, c0=c0:
                                        e.tensor_add(
                    R[:, m, c0:c0 + QT], R[:, m, c0:c0 + QT], dtmp[:, sl, :]),
                    inc="sV", drain=True)
                return tm_

            gemm(wv, KC, list(range(KC)), lambda k: kact[:, k, :], 0, QT,
                 evac_v, waits0=[st[("kact", m)] for m in range(KC)],
                 wk=wvt if q == 0 else None)
            if q + 1 < NQ:
                t_prepk = prep_xk(q + 1)
        AB.release()

    def sub_attnb0():
        HT = T // 2
        AB.mark()
        st["wfree2"] = None
        wslot = AB.alloc([128, 8 * E_INNER])
        t_ab, t_scr, t_mneg = ln_stats(lambda k: R[:, k, :])
        ln = AB.alloc([128, KC, HT])
        gate = AB.alloc([128, KC, HT])
        y = AB.alloc([128, KC, HT])
        vtmp = AB.alloc([128, 2, HT])
        hs = AB.alloc([128, 2, HT])
        P.emit("V", lambda e: e.memset(smalls[:, 8:16], 0.0))
        t_dec = P.emit("S", lambda e: e.activation(
            out=smalls[:, 16:24], in_=vecs[:, VEC["A0"]:VEC["A0"] + 8],
            func=ACT.Sigmoid), waits=[st["tvec"]], inc="sS", drain=True)
        st["wfree2"] = None
        for h in range(2):
            c0 = h * HT
            lt = ln_mat(lambda k, ln=ln: ln[:, k, :], lambda k: R[:, k, :],
                        range(KC), c0, HT, t_mneg, t_ab)
            win, wint = load_weight(H["w_in0"], KC, E_INNER, into=wslot,
                                    wait0=st["wfree2"])

            def evac_g(m, pg, waits):
                return P.emit("S", lambda e, m=m, pg=pg: e.activation(
                    out=gate[:, m - 8, :], in_=pg[:, :, :].rearrange(
                        "p a b -> p (a b)"), func=ACT.Sigmoid,
                    bias=vecs[:, VEC["b_in"] + m:VEC["b_in"] + m + 1]),
                    waits=waits, inc="sS", drain=True)

            gemm(win, KC, list(range(8, 16)), lambda k: ln[:, k, :], 0, HT,
                 evac_g, wk=wint, rk=lt)

            def evac_vch(m, pg, waits, h=h):
                sl = m % 2
                tv_ = P.emit("S", lambda e, m=m, pg=pg, sl=sl: e.activation(
                    out=vtmp[:, sl, :], in_=pg[:, :, :].rearrange(
                        "p a b -> p (a b)"), func=ACT.Identity,
                    bias=vecs[:, VEC["b_in"] + m:VEC["b_in"] + m + 1]),
                    waits=list(waits) + [st.get(("vt", sl))],
                    inc="sS", drain=True)
                tsc = P.emit("V", lambda e, m=m, sl=sl: e.tensor_tensor_scan(
                    out=hs[:, sl, :],
                    data0=smalls[:, 16 + m:17 + m].broadcast_to([128, HT]),
                    data1=vtmp[:, sl, :],
                    initial=smalls[:, 8 + m:9 + m],
                    op0=ALU.mult, op1=ALU.add),
                    waits=[tv_, t_dec, st.get(("hs", sl))],
                    inc="sV", drain=True)
                st[("vt", sl)] = tsc
                P.emit("V", lambda e, m=m, sl=sl: e.tensor_copy(
                    out=smalls[:, 8 + m:9 + m], in_=hs[:, sl, HT - 1:HT]))
                st[("hs", sl)] = P.emit("V", lambda e, m=m, sl=sl:
                                        e.scalar_tensor_tensor(
                    out=y[:, m, :], in0=hs[:, sl, :],
                    scalar=vecs[:, VEC["C0"] + m:VEC["C0"] + m + 1],
                    in1=gate[:, m, :], op0=ALU.mult, op1=ALU.mult),
                    inc="sV", drain=True)
                return tv_

            gemm(win, KC, list(range(8)), lambda k: ln[:, k, :], 0, HT,
                 evac_vch, waits0=[], wk=None)
            tmm_in = st["last_mm"]

            wout, woutt = load_weight(H["w_out0"], KC, D, into=wslot,
                                      wait0=tmm_in)

            def evac_o(m, pg, waits, c0=c0):
                return P.emit("V", lambda e, m=m, pg=pg, c0=c0:
                              e.scalar_tensor_tensor(
                    out=R[:, m, c0:c0 + HT],
                    in0=pg[:, :, :].rearrange("p a b -> p (a b)"),
                    scalar=vecs[:, VEC["b_out"] + m:VEC["b_out"] + m + 1],
                    in1=R[:, m, c0:c0 + HT], op0=ALU.add, op1=ALU.add),
                    waits=waits, inc="sV", drain=True)

            w_y = [st[("hs", 0)], st[("hs", 1)]]
            gemm(wout, KC, list(range(KC)), lambda k: y[:, k, :], 0, HT,
                 evac_o, waits0=w_y, wk=woutt)
            st["wfree2"] = st["last_mm"]
        AB.release()


    def ln_spike(pre, prets, gbase, ombase, out_of, scratch):
        mt = vt = None
        sqt = []
        for k in range(KC):
            tq = P.emit("S", lambda e, k=k: e.activation(
                out=scratch[:, k, :], in_=pre[:, k, :], func=ACT.Square),
                waits=[prets[k]], inc="sS", drain=True)
            sqt.append(tq)
            w = [prets[k]]
            if k == 0:
                w += [t_cb, st.get(("aux", "psC")), st.get(("row", 0))]
            mt = P.emit("T", lambda e, k=k: e.matmul(
                psC[0:1, 0:512], constsb[:, 0:1], pre[:, k, :],
                start=(k == 0), stop=(k == KC - 1)),
                waits=w, inc="sT" if k == KC - 1 else None)
            w2 = [tq]
            if k == 0:
                w2 += [st.get(("aux", "psD")), st.get(("row", 4)),
                       st.get(("row", 1))]
            vt = P.emit("T", lambda e, k=k: e.matmul(
                psD[0:1, 0:512], constsb[:, 0:1], scratch[:, k, :],
                start=(k == 0), stop=(k == KC - 1)),
                waits=w2, inc="sT" if k == KC - 1 else None)
        tme = P.emit("S", lambda e: e.activation(
            out=rowv(0), in_=psC[0:1, 0:512], func=ACT.Copy),
            waits=[mt], inc="sS", drain=True)
        st[("aux", "psC")] = tme
        tve = P.emit("S", lambda e: e.activation(
            out=rows[0:1, 4, :], in_=psD[0:1, 0:512], func=ACT.Copy),
            waits=[vt], inc="sS", drain=True)
        st[("aux", "psD")] = tve
        tmn = bcast_row(0, lambda nt: mneg2[:, :], -1.0, [tme], cols=512)
        tq1 = P.emit("V", lambda e: e.tensor_mul(
            rows[0:1, 5, :], mneg2[0:1, :], mneg2[0:1, :]),
            waits=[tmn[-1], st.get(("row", 5))], inc="sV", drain=True)
        tq2 = P.emit("V", lambda e: e.tensor_sub(
            rows[0:1, 4, :], rows[0:1, 4, :], rows[0:1, 5, :]),
            waits=[tq1, tve], inc="sV", drain=True)
        tq3 = P.emit("S", lambda e: e.activation(
            out=rows[0:1, 5, :], in_=rows[0:1, 4, :],
            func=ACT.Sqrt, bias=consts[0:1, 4:5]),
            waits=[tq2], inc="sS", drain=True)
        st[("row", 5)] = tq3
        tq4 = P.emit("V", lambda e: e.reciprocal(
            out=rows[0:1, 4, :], in_=rows[0:1, 5, :]),
            waits=[tq3], inc="sV", drain=True)
        st[("row", 4)] = tq4
        tq5 = P.emit("S", lambda e: e.copy(
            out=rowv(1), in_=rows[0:1, 4, :]),
            waits=[tq4, st.get(("row", 1))], inc="sS", drain=True)
        tab = bcast_row(1, lambda nt: ab2[:, :], 1.0, [tq5], cols=512)
        out_toks = []
        for k in range(KC):
            P.emit("V", lambda e, k=k: e.tensor_add(
                pre[:, k, :], pre[:, k, :], mneg2[:]),
                waits=[tmn[-1], sqt[k]])
            P.emit("V", lambda e, k=k: e.tensor_mul(
                pre[:, k, :], pre[:, k, :], ab2[:]), waits=[tab[-1]])
            out_toks.append(P.emit("V", lambda e, k=k: e.tensor_scalar(
                out=out_of(k), in0=pre[:, k, :],
                scalar1=vecs[:, gbase + k:gbase + k + 1],
                scalar2=smalls[:, ombase + k:ombase + k + 1],
                op0=ALU.mult, op1=ALU.is_ge),
                inc="sV", drain=True))
        return out_toks

    def sub_l1a():
        HT = T // 2
        AB.mark()
        bd = AB.alloc([128, 8, 128])
        P.emit("V", lambda e: e.memset(bd[:, :, :], 0.0))
        ln = AB.alloc([128, KC, T])
        AB.mark()
        slot_ka = AB.alloc([128, 8 * D])
        slot_va = AB.alloc([128, 8 * D])
        t_ab, t_scr, t_mneg = ln_stats(lambda k: R[:, k, :])
        wk, wkt = load_weight(H["wka"], KC, D, into=slot_ka)
        wv, wvt = load_weight(H["wva"], KC, D, into=slot_va)
        kblk = AB.alloc([128, 2, D])
        vblk = AB.alloc([128, 2, D])
        tpair = None
        lt = None
        for tb in range(16):
            if tb % 4 == 0:
                qc0 = (tb // 4) * 512
                lt = ln_mat(lambda k, qc0=qc0: ln[:, k, qc0:qc0 + 512],
                            lambda k: R[:, k, :],
                            range(KC), qc0, 512, t_mneg, t_ab)
            sl = tb % 2
            for which, w_, wt_, blk in (("k", wk, wkt, kblk), ("v", wv, wvt, vblk)):
                ps = psA if which == "k" else psB
                tok = None
                for kc in range(KC):
                    for nt in range(2):
                        w0 = []
                        if kc == 0 and nt == 0:
                            w0 = [lt[-1], st.get(("blkf", which, sl))]
                        if nt == 0:
                            w0 = list(w0) + [wt_[kc]]
                        tok = P.emit("T", lambda e, kc=kc, nt=nt, tb=tb, ps=ps,
                                     w_=w_: e.matmul(
                            ps[:, nt, :], ln[:, kc, tb * 128:(tb + 1) * 128],
                            w_[:, kc, nt * 512:(nt + 1) * 512],
                            start=(kc == 0), stop=(kc == KC - 1)),
                            waits=w0,
                            inc="sT" if (kc == KC - 1 and nt == 1) else None)
                te = P.emit("S", lambda e, ps=ps, blk=blk, sl=sl: e.copy(
                    out=blk[:, sl, :], in_=ps[:, :, :].rearrange(
                        "p a b -> p (a b)")), waits=[tok], inc="sS", drain=True)
                st[("blkev", which, sl)] = te
            for pc in range(8):
                w0 = []
                if pc == 0:
                    w0 = [st[("blkev", "k", sl)], st[("blkev", "v", sl)]]
                tpair = P.emit("T", lambda e, pc=pc, sl=sl, tb=tb: e.matmul(
                    psP[:, pc, :], kblk[:, sl, pc * 128:(pc + 1) * 128],
                    vblk[:, sl, pc * 128:(pc + 1) * 128],
                    start=(tb == 0 and pc % 4 == 0),
                    stop=(tb == 15 and pc % 4 == 3)),
                    waits=w0, inc="sT" if pc == 7 else None)
            st[("blkf", "k", sl)] = tpair
            st[("blkf", "v", sl)] = tpair
        st["l1a_kv_done"] = tpair
        tbd = []
        for pc in range(8):
            P.emit("S", lambda e, pc=pc: e.mul(
                out=bd[0:64, pc, 0:64], in_=psP[0:64, pc, 0:64],
                mul=consts[0:64, 7:8]), waits=[tpair])
            tbd.append(P.emit("S", lambda e, pc=pc: e.mul(
                out=bd[64:128, pc, 64:128], in_=psP[64:128, pc, 64:128],
                mul=consts[64:128, 7:8]), inc="sS", drain=True))
        AB.release()
        for h in range(2):
            c0 = h * HT
            AB.mark()
            q = AB.alloc([128, KC, HT])
            wslot_a = AB.alloc([128, 8 * D])
            wq, wqt = load_weight(H["wqa"], KC, D, into=wslot_a,
                                  wait0=st.get("l1a_wq_free")
                                  or st["l1a_kv_done"])

            def evac_q(m, pg, waits, q=q):
                tq = P.emit("S", lambda e, m=m, pg=pg, q=q: e.copy(
                    out=q[:, m, :], in_=pg[:, :, :].rearrange(
                        "p a b -> p (a b)")), waits=waits, inc="sS", drain=True)
                st[("qv", m)] = tq
                return tq

            gemm(wq, KC, list(range(KC)), lambda k: ln[:, k, :], c0, HT,
                 evac_q, waits0=[lt[-1]], wk=wqt)
            st["l1a_wq_free"] = st["last_mm"]
            attn = AB.alloc([128, KC, HT])
            for ec in range(8):
                g = st["ppg"]; pg = pp[g % 2]
                w0 = [tbd[-1], st[("qv", ec)]]
                if st["ppfree"][g % 2] is not None:
                    w0.append(st["ppfree"][g % 2])
                tok = None
                for nt in range(2):
                    tok = P.emit("T", lambda e, ec=ec, nt=nt, pg=pg, q=q:
                                 e.matmul(
                        pg[:, nt, :], bd[:, ec, :],
                        q[:, ec, nt * 512:(nt + 1) * 512],
                        start=True, stop=True),
                        waits=w0 if nt == 0 else [],
                        inc="sT" if nt == 1 else None)
                ta = P.emit("S", lambda e, ec=ec, pg=pg, attn=attn: e.copy(
                    out=attn[:, ec, :], in_=pg[:, :, :].rearrange(
                        "p a b -> p (a b)")), waits=[tok], inc="sS", drain=True)
                st[("at", ec)] = ta
                st["ppfree"][g % 2] = ta
                st["ppg"] += 1
            wo, wot = load_weight(H["woa"], KC, D, into=wslot_a,
                                   wait0=st["last_mm"])

            def evac_oa(m, pg, waits, c0=c0):
                return P.emit("V", lambda e, m=m, pg=pg, c0=c0: e.tensor_add(
                    R[:, m, c0:c0 + HT], R[:, m, c0:c0 + HT],
                    pg[:, :, :].rearrange("p a b -> p (a b)")),
                    waits=waits, inc="sV", drain=True)

            gemm(wo, KC, list(range(KC)), lambda k, attn=attn: attn[:, k, :],
                 0, HT, evac_oa,
                 waits0=[st[("at", m)] for m in range(8)], wk=wot)
            st["l1a_wq_free"] = st["last_mm"]
            AB.release()
        AB.release()

    def sub_l1ffn():
        HT = T // 2
        AB.mark()
        st["l1f_wfree"] = None
        wslot = AB.alloc([128, 8 * D])
        t_ab, t_scr, t_mneg = ln_stats(lambda k: R[:, k, :])
        ln = AB.alloc([128, KC, HT])
        h1 = AB.alloc([128, KC, HT])
        h2 = AB.alloc([128, KC, HT])
        dtmp = AB.alloc([128, 2, HT])
        def emit_lt(c0, wtok):
            lts = []
            for k in range(KC):
                lts.append(P.emit("V", lambda e, k=k, c0=c0: e.tensor_add(
                    ln[:, k, :], R[:, k, c0:c0 + HT], mneg[:, c0:c0 + HT]),
                    waits=[t_mneg, wtok], inc="sV"))
            return lts

        lt = emit_lt(0, st.get("l1f_lnf"))
        for h in range(2):
            c0 = h * HT

            def evac_h(dst, scale_ab=False):
                def f(m, pg, waits, dst=dst, scale_ab=scale_ab):
                    if scale_ab:
                        t_ = P.emit("V", lambda e, m=m, pg=pg, dst=dst, c0=c0:
                                    e.tensor_mul(
                            dst[:, m, :], pg[:, :, :].rearrange(
                                "p a b -> p (a b)"), a_b[:, c0:c0 + HT]),
                            waits=list(waits) + [t_ab], inc="sV", drain=True)
                    else:
                        t_ = P.emit("S", lambda e, m=m, pg=pg, dst=dst: e.copy(
                            out=dst[:, m, :], in_=pg[:, :, :].rearrange(
                                "p a b -> p (a b)")), waits=waits,
                            inc="sS", drain=True)
                    st[("hv", m)] = t_
                    return t_
                return f

            w1, w1t = load_weight(H["wf1"], KC, D, into=wslot,
                                  wait0=st["l1f_wfree"])
            gemm(w1, KC, list(range(KC)), lambda k: ln[:, k, :], 0, HT,
                 evac_h(h1, True), wk=w1t, rk=lt)
            if h == 0:
                lt = emit_lt(HT, st["last_mm"])
            w2, w2t = load_weight(H["wf2"], KC, D, into=wslot,
                                  wait0=st["last_mm"])
            gemm(w2, KC, list(range(KC)), lambda k: h1[:, k, :], 0, HT,
                 evac_h(h2), waits0=[st[("hv", m)] for m in range(KC)], wk=w2t)
            w3, w3t = load_weight(H["wf3"], KC, D, into=wslot,
                                  wait0=st["last_mm"])

            def evac_t(m, pg, waits, c0=c0):
                sl = m % 2
                tt_ = P.emit("S", lambda e, m=m, pg=pg, sl=sl: e.activation(
                    out=dtmp[:, sl, :], in_=pg[:, :, :].rearrange(
                        "p a b -> p (a b)"), func=ACT.Tanh,
                    bias=vecs[:, VEC["f1b"] + m:VEC["f1b"] + m + 1]),
                    waits=list(waits) + [st.get(("dt2", sl))],
                    inc="sS", drain=True)
                st[("dt2", sl)] = P.emit("V", lambda e, m=m, sl=sl, c0=c0:
                                         e.tensor_add(
                    R[:, m, c0:c0 + HT], R[:, m, c0:c0 + HT], dtmp[:, sl, :]),
                    waits=[tt_], inc="sV", drain=True)
                return tt_

            gemm(w3, KC, list(range(KC)), lambda k: h2[:, k, :], 0, HT,
                 evac_t, waits0=[st[("hv", m)] for m in range(KC)], wk=w3t)
            st["l1f_wfree"] = st["last_mm"]
            st["l1f_lnf"] = st["last_mm"]
        AB.release()

    def sub_l1b():
        QT = 512
        AB.mark()
        t_ab, t_scr, t_mneg = ln_stats(lambda k: R[:, k, :])
        st["l1b_wfree"] = None
        for xi, nm in enumerate(("betaq", "betak", "betav", "betao")):
            P.emit("V", lambda e, xi=xi, nm=nm: e.tensor_scalar(
                out=smalls[:, 24 + 8 * xi:32 + 8 * xi],
                in0=vecs[:, VEC[nm]:VEC[nm] + 8],
                scalar1=consts[:, 1:2], scalar2=consts[:, 2:3],
                op0=ALU.mult, op1=ALU.add), waits=[st["tvec"], t_cb])
        P.emit("V", lambda e: e.tensor_scalar(
            out=smalls[:, 113:114], in0=vecs[:, VEC["scale_b"]:VEC["scale_b"] + 1],
            scalar1=2.0, scalar2=None, op0=ALU.mult))
        t_thr = P.emit("V", lambda e: e.reciprocal(
            out=smalls[:, 112:113], in_=smalls[:, 113:114]),
            inc="sV", drain=True)
        Q8 = AB.alloc([128, KC, T], FP8E5)
        Vtm = AB.alloc([128, 16, D], FP8)
        Ktm = AB.alloc([128, 16, D], FP8)
        AB.mark()
        wslot = AB.alloc([128, 8 * D])
        lnq = AB.alloc([128, KC, QT])
        pre = AB.alloc([128, KC, QT])
        spk = AB.alloc([128, KC, QT])
        for qt in range(4):
            c0 = qt * QT
            lt = ln_mat(lambda k, lnq=lnq: lnq[:, k, :], lambda k: R[:, k, :],
                        range(KC), c0, QT,
                        [t_mneg, st.get("l1b_wfree")], t_ab)
            st["l1b_lnmat_last"] = lt[-1]
            fin_prev = st.pop("l1b_fin", None)
            if fin_prev:
                fin_prev()
            for X, wh, bbase, gbase, ombase in (
                    ("k", "wkb", VEC["bk"], VEC["gk"], 32),
                    ("v", "wvb", VEC["bv"], VEC["gv"], 40),
                    ("q", "wqb", VEC["bq"], VEC["gq"], 24)):
                wx, wxt = load_weight(H[wh], KC, D, into=wslot,
                                      wait0=st["l1b_wfree"])

                def evac_p(m, pg, waits, bbase=bbase):
                    t_ = P.emit("S", lambda e, m=m, pg=pg, bbase=bbase:
                                e.activation(
                        out=pre[:, m, :], in_=pg[:, 0, :], func=ACT.Identity,
                        bias=vecs[:, bbase + m:bbase + m + 1]),
                        waits=list(waits) + [st.get("prejunk")],
                        inc="sS", drain=True)
                    st[("pv", m)] = t_
                    return t_

                gemm(wx, KC, list(range(KC)), lambda k, lnq=lnq: lnq[:, k, :],
                     0, QT, evac_p, wk=wxt, rk=lt)
                st["l1b_wfree"] = st["last_mm"]
                if X == "q":
                    pv = [st[("pv", m)] for m in range(KC)]

                    def fin(c0=c0, pv=pv, gbase=gbase, ombase=ombase):
                        spk_toks = ln_spike(pre, pv, gbase, ombase,
                                            lambda k: spk[:, k, :], spk)
                        st["prejunk"] = spk_toks[-1]
                        for k in range(KC):
                            P.emit("S", lambda e, k=k, c0=c0: e.copy(
                                out=Q8[:, k, c0:c0 + QT], in_=spk[:, k, :]),
                                waits=[spk_toks[k]], inc="sS", drain=True)
                    st["l1b_fin"] = fin
                else:
                    spk_toks = ln_spike(pre, [st[("pv", m)] for m in range(KC)],
                                        gbase, ombase, lambda k: spk[:, k, :],
                                        spk)
                    dst = Ktm if X == "k" else Vtm
                    for ec in range(KC):
                        for lb in range(4):
                            slot = ((ec * 4 + lb) % 2) * 4
                            tb = qt * 4 + lb
                            ttr = P.emit("T", lambda e, ec=ec, lb=lb, slot=slot:
                                         e.transpose(
                                psP[:, slot, 0:64].bitcast(BF16),
                                spk[:, ec, lb * 128:(lb + 1) * 128],
                                identb[:]),
                                waits=[spk_toks[ec], t_idb,
                                       st.get(("tps", slot))],
                                inc="sT")
                            st[("tps", slot)] = P.emit(
                                "S", lambda e, slot=slot, tb=tb, ec=ec, dst=dst:
                                e.copy(out=dst[:, tb, ec * 128:(ec + 1) * 128],
                                       in_=psP[:, slot, 0:64].bitcast(BF16)),
                                waits=[ttr], inc="sS", drain=True)
                if X == "k":
                    tks = col_sums(lambda k: spk[:, k, :], KC, 0,
                                   lambda e, i, p: e.activation(
                                       out=rowv(i), in_=p, func=ACT.Copy),
                                   spk_toks, cols=QT)
                    st["tksb"] = bcast_row(0, lambda nt: ksb[:, :], 1.0, tks,
                                           cols=QT)[-1]
                if X == "v":
                    for k in range(KC):
                        P.emit("V", lambda e, k=k, qt=qt: e.scalar_tensor_tensor(
                            out=pre[:, k, :], in0=spk[:, k, :], scalar=0.0,
                            in1=ksb[:], op0=ALU.bypass, op1=ALU.mult,
                            accum_out=smalls[:, 88 + qt * 8 + k:89 + qt * 8 + k]),
                            waits=[st["tksb"], spk_toks[k]])
                        st["prejunk"] = P.emit(
                            "V", lambda e, k=k, qt=qt: e.scalar_tensor_tensor(
                                out=pre[:, k, :], in0=spk[:, k, :], scalar=0.0,
                                in1=spk[:, k, :], op0=ALU.bypass, op1=ALU.max,
                                accum_out=smalls[:, 56 + qt * 8 + k:
                                                 57 + qt * 8 + k]),
                            inc="sV", drain=True)
        st.pop("l1b_fin")()
        AB.release()
        if L1BST < 2:
            AB.release()
            return
        for k in range(KC):
            P.emit("V", lambda e, k=k: e.reduce_sum(
                out=smalls[:, 120 + k:121 + k],
                in_=smalls[:, 56 + k:88 + k:8], axis=mybir.AxisListType.X))
            P.emit("V", lambda e, k=k: e.reduce_sum(
                out=smalls[:, 128 + k:129 + k],
                in_=smalls[:, 88 + k:120 + k:8], axis=mybir.AxisListType.X))
        t_sums = P.emit("V", lambda e: e.tensor_scalar(
            out=smalls[:, 136:144], in0=smalls[:, 128:136],
            scalar1=consts[:, 1:2], scalar2=None, op0=ALU.mult),
            inc="sV", drain=True)
        t_mnegfree = st["l1b_lnmat_last"]
        tpairs = []
        for pc in range(8):
            w0 = []
            if pc == 0:
                w0 = [st.get(("tps", 0)), st.get(("tps", 4))]
            tpair = None
            for tb in range(16):
                tpair = P.emit("T", lambda e, tb=tb, pc=pc: e.matmul(
                    psP[:, pc, :], Ktm[:, tb, pc * 128:(pc + 1) * 128],
                    Vtm[:, tb, pc * 128:(pc + 1) * 128],
                    start=(tb == 0 and pc % 4 == 0),
                    stop=(tb == 15 and pc % 4 == 3)),
                    waits=w0 if tb == 0 else [],
                    inc="sT" if tb == 15 else None)
            tpairs.append(tpair)
        tpair = tpairs[-1]
        AB.mark()
        pA = AB.alloc([128, 8, 128], FP8E5)
        pB = AB.alloc([128, 8, 128], FP8E5)
        pC = AB.alloc([128, 8, 128], FP8E5)
        pD = AB.alloc([128, 8, 128], FP8E5)
        P.emit("V", lambda e: e.memset(pA[:, :, :], 0.0))
        P.emit("V", lambda e: e.memset(pB[:, :, :], 0.0))
        P.emit("V", lambda e: e.memset(pC[:, :, :], 0.0))
        P.emit("V", lambda e: e.memset(pD[:, :, :], 0.0))
        tpc = []
        I32 = mybir.dt.int32
        for pc in range(8):
            fm2 = mneg2[:, 0:64]
            im2 = mneg2[:, 64:128].bitcast(I32)
            ir5 = mneg2[:, 128:192].bitcast(I32)
            ir6 = mneg2[:, 192:256].bitcast(I32)
            ir8 = mneg2[:, 256:320].bitcast(I32)
            fr5 = mneg2[:, 320:384]
            fr6 = mneg2[:, 384:448]
            fr8 = mneg2[:, 448:512]
            P.emit("V", lambda e, pc=pc: e.tensor_scalar(
                out=mneg2[0:64, 0:64], in0=psP[0:64, pc, 0:64],
                scalar1=2.0, scalar2=None, op0=ALU.mult),
                waits=[tpairs[(pc // 4) * 4 + 3]])
            P.emit("V", lambda e, pc=pc: e.tensor_scalar(
                out=mneg2[64:128, 0:64], in0=psP[64:128, pc, 64:128],
                scalar1=2.0, scalar2=None, op0=ALU.mult),
                waits=[tpairs[(pc // 4) * 4 + 3]])
            P.emit("V", lambda e, fm2=fm2, im2=im2: e.tensor_copy(
                out=im2, in_=fm2))
            P.emit("V", lambda e, im2=im2, ir5=ir5: e.tensor_scalar(
                out=ir5, in0=im2, scalar1=511, scalar2=None,
                op0=ALU.bitwise_and))
            P.emit("V", lambda e, im2=im2, ir6=ir6: e.tensor_scalar(
                out=ir6, in0=im2, scalar1=63, scalar2=None,
                op0=ALU.bitwise_and))
            P.emit("V", lambda e, im2=im2, ir8=ir8: e.tensor_scalar(
                out=ir8, in0=im2, scalar1=7, scalar2=None,
                op0=ALU.bitwise_and))
            P.emit("V", lambda e, ir5=ir5, fr5=fr5: e.tensor_copy(
                out=fr5, in_=ir5))
            P.emit("V", lambda e, ir6=ir6, fr6=fr6: e.tensor_copy(
                out=fr6, in_=ir6))
            P.emit("V", lambda e, ir8=ir8, fr8=fr8: e.tensor_copy(
                out=fr8, in_=ir8))
            tk = None
            for lo, hi in ((0, 64), (64, 128)):
                P.emit("V", lambda e, pc=pc, lo=lo, hi=hi:
                       e.tensor_sub(pA[lo:hi, pc, lo:hi],
                                    mneg2[lo:hi, 0:64],
                                    mneg2[lo:hi, 320:384]))
                P.emit("V", lambda e, pc=pc, lo=lo, hi=hi:
                       e.tensor_sub(pB[lo:hi, pc, lo:hi],
                                    mneg2[lo:hi, 320:384],
                                    mneg2[lo:hi, 384:448]))
                P.emit("V", lambda e, pc=pc, lo=lo, hi=hi:
                       e.tensor_sub(pC[lo:hi, pc, lo:hi],
                                    mneg2[lo:hi, 384:448],
                                    mneg2[lo:hi, 448:512]))
                tk = P.emit("V", lambda e, pc=pc, lo=lo, hi=hi:
                            e.tensor_scalar(
                    pD[lo:hi, pc, lo:hi], mneg2[lo:hi, 448:512], 0.0, None,
                    op0=ALU.add), inc="sV" if hi == 128 else None,
                    drain=(hi == 128))
            tpc.append(tk)
        if L1BST < 3:
            AB.release()
            AB.release()
            return
        bias8 = mneg[:, :].bitcast(FP8).rearrange("p (mb l) -> p mb l",
                                                  mb=16, l=QT)
        wo, wot = load_weight(H["wob"], KC, D, wait0=st["l1b_wfree"])
        spkout = AB.alloc([128, KC, QT])
        opre = AB.alloc([128, KC, QT])
        st["bias8f"] = None
        for lq in range(4):
            c0 = lq * QT
            tb8 = P.emit("G", lambda e, c0=c0: e.dma_start(
                out=bias8[:, :, :], in_=H["biasmx"].rearrange(
                    "(mb p) l -> p mb l", p=128)[:, :, c0:c0 + QT]),
                waits=[st["bias8f"], t_mnegfree,
                       st.get("mneg_dma")], inc="dG")
            st["mneg_dma"] = tb8
            tqs = col_sums(lambda k: Q8[:, k, :], KC, 0,
                           lambda e, i, p: e.activation(
                               out=rowv(i), in_=p, func=ACT.Identity,
                               scale=consts[0:1, 1:2], bias=consts[0:1, 6:7]),
                           None, c0=c0, cols=QT, ones=consts85[:, 0:1])
            ta2 = bcast_row(0, lambda nt: ksb[:, :], 1.0, tqs, cols=QT)
            for ec in range(8):
                g = st["ppg"]; pg = pp[g % 2]
                w0 = [tb8]
                if st["ppfree"][g % 2] is not None:
                    w0.append(st["ppfree"][g % 2])
                tok = None
                for mb in range(16):
                    tok = P.emit("T", lambda e, mb=mb, ec=ec, pg=pg: e.matmul(
                        pg[:, 0, :], Vtm[:, mb, ec * 128:(ec + 1) * 128],
                        bias8[:, mb, :], start=(mb == 0), stop=False),
                        waits=w0 if mb == 0 else [], inc=None)
                for pi, piece in enumerate((pA, pB, pC, pD)):
                    tok = P.emit("T", lambda e, ec=ec, pg=pg, piece=piece,
                                 c0=c0: e.matmul(
                        pg[:, 0, :], piece[:, ec, :], Q8[:, ec, c0:c0 + QT],
                        start=False, stop=(piece is pD)),
                        waits=[tpc[ec], t_sums] if pi == 0 else [],
                        inc="sT" if pi == 3 else None)
                sc3 = ab2 if ec % 2 == 0 else mneg2
                t3 = P.emit("V", lambda e, ec=ec, pg=pg, sc3=sc3:
                            e.scalar_tensor_tensor(
                    out=sc3[:, :], in0=ksb[:, :],
                    scalar=smalls[:, 120 + ec:121 + ec], in1=pg[:, 0, :],
                    op0=ALU.mult, op1=ALU.add),
                    waits=[tok, ta2[-1], st.get(("ab2f", ec % 2)),
                           tpc[-1]], inc="sV", drain=True)
                tso = P.emit("V", lambda e, ec=ec, sc3=sc3: e.tensor_scalar(
                    out=spkout[:, ec, :], in0=sc3[:, :],
                    scalar1=smalls[:, 136 + ec:137 + ec],
                    scalar2=smalls[:, 112:113],
                    op0=ALU.add, op1=ALU.is_ge),
                    waits=[t_thr], inc="sV", drain=True)
                st[("ab2f", ec % 2)] = tso
                st[("so", ec)] = tso
                st["ppfree"][g % 2] = t3
                st["ppg"] += 1
                st["bias8f"] = tok

            def evac_ob(m, pg, waits):
                t_ = P.emit("S", lambda e, m=m, pg=pg: e.activation(
                    out=opre[:, m, :], in_=pg[:, 0, :], func=ACT.Identity,
                    bias=vecs[:, VEC["bo"] + m:VEC["bo"] + m + 1]),
                    waits=waits, inc="sS", drain=True)
                st[("pv", m)] = t_
                return t_

            gemm(wo, KC, list(range(KC)), lambda k: spkout[:, k, :], 0, QT,
                 evac_ob, waits0=[st[("so", m)] for m in range(8)], wk=wot)
            dsp = ln_spike(opre, [st[("pv", m)] for m in range(KC)],
                           VEC["go"], 48, lambda k: spkout[:, k, :], spkout)
            for k in range(KC):
                P.emit("V", lambda e, k=k, c0=c0: e.tensor_add(
                    R[:, k, c0:c0 + QT], R[:, k, c0:c0 + QT], spkout[:, k, :]),
                    waits=[dsp[k]])
        P.emit("V", lambda e: e.nop(), inc="sV", drain=True)
        AB.release()
        AB.release()

    subs = [sub_mamba, sub_ffn0, sub_attnb0, sub_ffn0,
            sub_l1a, sub_l1ffn, sub_l1b, sub_l1ffn]
    for f in subs[:NSUB]:
        f()
        st["sl_last_mm"] = st.get("last_mm")
        P.barrier()

    t_out = P.emit("G", lambda e: e.dma_start(
        out=outT.rearrange("(c p) t -> p c t", p=128), in_=R[:]), inc="dG")
    P.emit("G", lambda e: e.nop(), waits=[t_out])



def _pack_vec(v):
    v = np.asarray(v, np.float32)
    return v.reshape(v.shape[0] // 128, 128).T


def make_vecs(inp):
    cols = np.zeros((128, VEC_COLS), np.float32)

    def put(name, arr):
        a = _pack_vec(arr)
        cols[:, VEC[name]:VEC[name] + a.shape[1]] = a

    cw = np.asarray(inp["L0_attn_a_conv_w"], np.float32).reshape(E_INNER, 4)
    for c in range(16):
        for j in range(4):
            cols[:, VEC["convw"] + c * 4 + j] = cw[c * 128:(c + 1) * 128, j]
    put("convb", inp["L0_attn_a_conv_b"])
    put("mu_r", inp["L0_ffn_mu_r"]); put("mu_k", inp["L0_ffn_mu_k"])
    put("b_in", inp["L0_attn_b_b_in"]); put("A0", inp["L0_attn_b_A"])
    put("C0", inp["L0_attn_b_C"]); put("b_out", inp["L0_attn_b_b_out"])
    for nm, key in [("bq", "L1_attn_b_bq"), ("bk", "L1_attn_b_bk"),
                    ("bv", "L1_attn_b_bv"), ("bo", "L1_attn_b_bo"),
                    ("gq", "L1_attn_b_gq"), ("gk", "L1_attn_b_gk"),
                    ("gv", "L1_attn_b_gv"), ("go", "L1_attn_b_go"),
                    ("betaq", "L1_attn_b_betaq"), ("betak", "L1_attn_b_betak"),
                    ("betav", "L1_attn_b_betav"), ("betao", "L1_attn_b_betao")]:
        put(nm, inp[key])
    cols[:, VEC["scale_b"]] = np.float32(np.asarray(inp["L1_attn_b_scale"]))
    put("f1b", inp["L1_ffn_bias"])
    return cols


def make_biasmx():
    i = np.arange(T, dtype=np.float64)
    dist = np.abs(i[:, None] - i[None, :])
    b = np.maximum(np.ceil(np.log2((T - 1.0) / (dist + 1.0))), 0.0)
    r32 = (np.float32(T - 1.0) / (dist.astype(np.float32) + np.float32(1.0)))
    b32 = np.maximum(np.ceil(np.log2(r32)), 0.0)
    assert np.array_equal(b, b32.astype(np.float64))
    return b.astype(ml_dtypes.float8_e4m3)


_NC_CACHE = {}


def kernel(**inp):
    x = np.asarray(inp["L0_attn_a_hidden_states"], np.float32)
    B = x.shape[0]
    if "p" not in _NC_CACHE:
        _NC_CACHE["p"] = build_program()
    nc = _NC_CACHE["p"]

    def ct(a):
        return np.ascontiguousarray(np.asarray(a, np.float32))

    shared = {
        "w_inproj": ct(np.asarray(inp["L0_attn_a_in_proj_w"])[:E_INNER, :].T),
        "w_outproj": ct(np.asarray(inp["L0_attn_a_out_proj_w"]).T),
        "w0r": ct(np.asarray(inp["L0_ffn_W_r"]).T),
        "w0k": ct(np.asarray(inp["L0_ffn_W_k"]).T),
        "w0v": ct(np.asarray(inp["L0_ffn_W_v"]).T),
        "w_in0": ct(inp["L0_attn_b_W_in"]),
        "w_out0": ct(inp["L0_attn_b_W_out"]),
        "wqa": ct(np.asarray(inp["L1_attn_a_W_q"]).T),
        "wka": ct(np.asarray(inp["L1_attn_a_W_k"]).T),
        "wva": ct(np.asarray(inp["L1_attn_a_W_v"]).T),
        "woa": ct(np.asarray(inp["L1_attn_a_W_o"]).T),
        "wqb": ct(np.asarray(inp["L1_attn_b_Wq"]).T),
        "wkb": ct(np.asarray(inp["L1_attn_b_Wk"]).T),
        "wvb": ct(np.asarray(inp["L1_attn_b_Wv"]).T),
        "wob": ct(np.asarray(inp["L1_attn_b_Wo"]).T),
        "wf1": ct(inp["L1_ffn_weight"]),
        "wf2": ct(inp["L1_ffn_aux_weight1"]),
        "wf3": ct(inp["L1_ffn_aux_weight2"]),
        "vecs": make_vecs(inp),
        "ident": np.eye(128, dtype=np.float32),
        "biasmx": make_biasmx(),
    }
    in_maps = []
    for b in range(B):
        m = dict(shared)
        m["xT"] = ct(x[b].T)
        in_maps.append(m)
    res = run_bass_kernel_spmd(nc, in_maps, core_ids=list(range(B)))
    out = np.stack([np.asarray(r["outT"], np.float32).T for r in res.results])
    return out


# version 29
# speedup vs baseline: 1.0213x; 1.0000x over previous
import os

import numpy as np
import ml_dtypes

import concourse.bass as bass
import concourse.mybir as mybir
from concourse.bass_utils import run_bass_kernel_spmd

F32 = mybir.dt.float32
BF16 = mybir.dt.bfloat16
FP8 = mybir.dt.float8e4
FP8E5 = mybir.dt.float8e5
ALU = mybir.AluOpType
ACT = mybir.ActivationFunctionType

T = 2048
D = 1024
KC = 8
E_INNER = 2048
EPS = 1e-5
NSUB = int(os.environ.get("NSUB", "8"))
STAGE = int(os.environ.get("STAGE", "99"))
L1BST = int(os.environ.get("L1BST", "9"))

ENGINES = ("T", "V", "S", "G", "Y")
DMA_SEMS = ("dY0", "dY1", "dY2", "dY3", "dYC", "dG", "dS")


class Prog:
    def __init__(self):
        self.ops = {e: [] for e in ENGINES}
        self.cnt = {s: 0 for s in ("sT", "sV", "sS", "sG") + DMA_SEMS}
        self.waited = {e: {} for e in ENGINES}

    OWN = {"T": "sT", "V": "sV", "S": "sS", "G": "sG"}

    def emit(self, eng, fn, waits=(), inc=None, drain=None, nodrain=False):
        own = self.OWN.get(eng)
        waits = [w for w in waits if w is not None and w[0] != own]
        eff = []
        for s, v in waits:
            if eng != "G" and self.waited[eng].get(s, 0) >= v:
                continue
            self.waited[eng][s] = max(self.waited[eng].get(s, 0), v)
            eff.append((s, v))
        tok = None
        if inc is not None:
            amt = 16 if inc in DMA_SEMS else 1
            self.cnt[inc] += amt
            tok = (inc, self.cnt[inc])
        is_dma = inc in DMA_SEMS
        do_drain = ((eng in ("V", "S", "G")) or drain is True) and not is_dma \
            and not (nodrain and inc is None)

        def op(e, sems, eff=eff, fn=fn, inc=inc, do_drain=do_drain,
               is_dma=is_dma):
            for s, v in eff:
                e.wait_ge(sems[s], v)
            inst = fn(e)
            if do_drain:
                inst = e.drain()
            if inc is not None:
                inst.then_inc(sems[inc], 16 if is_dma else 1)

        import traceback
        if os.environ.get("PROV"):
            fr = traceback.extract_stack()[-3:-1]
            op._prov = " | ".join(f"{f.name}:{f.lineno}" for f in fr)
        self.ops[eng].append(op)
        return tok

    bar_toks = {}

    def barrier(self):
        pre = {s: self.cnt[s] for s in ("sT", "sV", "sS", "sG")}
        toks = {}
        for e, sem in (("T", "sT"), ("V", "sV"), ("S", "sS"), ("G", "sG")):
            w = [(s, v) for s, v in pre.items() if s != sem and v > 0]
            toks[e] = self.emit(e, lambda en: en.nop(), waits=w, inc=sem,
                                drain=True)
        for e in ENGINES:
            self.emit(e, lambda en: en.nop(),
                      waits=[toks[o] for o in toks if o != e])
        self.bar_toks = dict(toks)


class Arena:

    ESIZE = {}

    def __init__(self, handle, nbytes, name):
        self.h, self.n, self.name = handle, nbytes, name
        self.off = 0
        self.marks = []
        self.hwm = 0

    def alloc(self, shape, dt=BF16):
        assert shape[0] == 128
        es = mybir.dt.size(dt)
        n = int(np.prod(shape[1:])) * es
        self.off = (self.off + 63) // 64 * 64
        assert self.off + n <= self.n, \
            f"arena {self.name} overflow: {self.off}+{n} > {self.n}"
        ap = self.h[:, self.off:self.off + n].bitcast(dt)
        self.off += n
        self.hwm = max(self.hwm, self.off)
        if len(shape) > 2:
            dims = " ".join(f"d{i}" for i in range(1, len(shape)))
            kw = {f"d{i}": shape[i] for i in range(1, len(shape))}
            ap = ap.rearrange(f"p ({dims}) -> p {dims}", **kw)
        return ap

    def mark(self):
        self.marks.append(self.off)

    def release(self):
        self.off = self.marks.pop()


def ceil_div(a, b):
    return (a + b - 1) // b


VEC = {}
_vc = 0
for _n, _c in [
    ("convw", 64), ("convb", 16), ("mu_r", 8), ("mu_k", 8),
    ("b_in", 16), ("A0", 8), ("C0", 8), ("b_out", 8),
    ("bq", 8), ("bk", 8), ("bv", 8), ("bo", 8),
    ("gq", 8), ("gk", 8), ("gv", 8), ("go", 8),
    ("betaq", 8), ("betak", 8), ("betav", 8), ("betao", 8),
    ("scale_b", 1), ("f1b", 8),
]:
    VEC[_n] = _vc
    _vc += _c
VEC_COLS = _vc


def build_program():
    nc = bass.Bass("TRN2", target_bir_lowering=False)
    sbp = nc.SBUF_PARTITION_SIZE_BYTES

    def hin(name, shape, dt=F32):
        return nc.dram_tensor(name, list(shape), dt, kind="ExternalInput").ap()

    H = {}
    H["xT"] = hin("xT", [D, T])
    for n, sh in [("w_inproj", [D, E_INNER]), ("w_outproj", [E_INNER, D]),
                  ("w0r", [D, D]), ("w0k", [D, D]), ("w0v", [D, D]),
                  ("w_in0", [D, E_INNER]), ("w_out0", [D, D]),
                  ("wqa", [D, D]), ("wka", [D, D]), ("wva", [D, D]),
                  ("woa", [D, D]),
                  ("wqb", [D, D]), ("wkb", [D, D]), ("wvb", [D, D]),
                  ("wob", [D, D]),
                  ("wf1", [D, D]), ("wf2", [D, D]), ("wf3", [D, D])]:
        H[n] = hin(n, sh)
    H["vecs"] = hin("vecs", [128, VEC_COLS])
    H["ident"] = hin("ident", [128, 128])
    H["biasmx"] = hin("biasmx", [T, T], FP8)
    outT = nc.dram_tensor("outT", [D, T], F32, kind="ExternalOutput").ap()

    from contextlib import ExitStack
    with ExitStack() as ctx:
        E = {}
        def sbuf(name, shape, dt=F32):
            E[name] = ctx.enter_context(nc.sbuf_tensor(name, list(shape), dt))
            return E[name]

        R = sbuf("R", [128, KC, T])
        mneg = sbuf("mneg", [128, T])
        a_b = sbuf("a_b", [128, T], BF16)
        vecs = sbuf("vecs_s", [128, VEC_COLS])
        ident = sbuf("ident_s", [128, 128])
        identb = sbuf("identb_s", [128, 128], BF16)
        consts = sbuf("consts", [128, 8])
        constsb = sbuf("constsb", [128, 8], BF16)
        crowb = sbuf("crowb", [1, 2, 128], BF16)
        rows = sbuf("rows", [1, 8, 512])
        smalls = sbuf("smalls", [128, 160])
        stage = sbuf("stage", [128, 3, 1024])

        mneg2 = sbuf("mneg2", [128, 512])
        ksb = sbuf("ksb", [128, 512])
        ab2 = sbuf("ab2", [128, 512])
        consts8 = sbuf("consts8", [128, 8], FP8)
        consts85 = sbuf("consts85", [128, 8], FP8E5)
        ar_n = (nc.sbuf_top - nc.sbuf_base - 2048) // 64 * 64
        arena_b = ctx.enter_context(
            nc.sbuf_tensor("arena_b", [128, ar_n], mybir.dt.uint8))
        AB = Arena(arena_b, ar_n, "main")
        A8 = AB

        psA = ctx.enter_context(nc.psum_tensor("psA", [128, 2, 512], F32))
        psB = ctx.enter_context(nc.psum_tensor("psB", [128, 2, 512], F32))
        psC = ctx.enter_context(nc.psum_tensor("psC", [128, 512], F32))
        psD = ctx.enter_context(nc.psum_tensor("psD", [128, 512], F32))
        psP = ctx.enter_context(nc.psum_tensor("psP", [128, 8, 128], F32))

        sems = {s: ctx.enter_context(nc.semaphore(s))
                for s in ("sT", "sV", "sS", "sG") + DMA_SEMS}

        P = Prog()
        _build_body(P, nc, H, outT, locals())

        with nc.Block() as block:
            @block.tensor
            def _(e):
                for op in P.ops["T"]:
                    op(e, sems)

            @block.vector
            def _(e):
                for op in P.ops["V"]:
                    op(e, sems)

            @block.scalar
            def _(e):
                for op in P.ops["S"]:
                    op(e, sems)

            @block.gpsimd
            def _(e):
                for op in P.ops["G"]:
                    op(e, sems)

            @block.sync
            def _(e):
                for op in P.ops["Y"]:
                    op(e, sems)
    return nc


def _build_body(P, nc, H, outT, E):
    R = E["R"]; mneg = E["mneg"]; a_b = E["a_b"]; vecs = E["vecs"]
    crowb = E["crowb"]

    def rowv(i):
        return rows[0:1, i, 0:256].bitcast(BF16)
    mneg2 = E["mneg2"]; ab2 = E["ab2"]; consts8 = E["consts8"]
    ksb = E["ksb"]; consts85 = E["consts85"]
    ident = E["ident"]; identb = E["identb"]
    consts = E["consts"]; constsb = E["constsb"]
    rows = E["rows"]; smalls = E["smalls"]; stage = E["stage"]
    AB = E["AB"]; A8 = E["A8"]
    psA = E["psA"]; psB = E["psB"]; psC = E["psC"]; psD = E["psD"]
    psP = E["psP"]

    st = {}

    P.emit("Y", lambda e: e.dma_start(
        out=R[:], in_=H["xT"].rearrange("(c p) t -> p c t", p=128)), inc="dYC")
    P.emit("Y", lambda e: e.dma_start(out=vecs[:], in_=H["vecs"][:]), inc="dYC")
    P.emit("Y", lambda e: e.dma_start(out=ident[:], in_=H["ident"][:]),
           inc="dYC")
    t_init = ("dYC", 48)
    t_vec = t_init
    t_id = t_init
    st_init = t_init
    P.emit("V", lambda e: e.memset(crowb[0:1, 0, :], -1.0))
    P.emit("V", lambda e: e.memset(crowb[0:1, 1, :], 1.0))
    P.emit("V", lambda e: e.memset(consts[:, 1:2], -1.0))
    P.emit("V", lambda e: e.memset(consts[:, 2:3], 1.0))
    P.emit("V", lambda e: e.memset(consts[:, 4:5], EPS))
    P.emit("V", lambda e: e.memset(consts[:, 5:6], 1.0 / 1024.0))
    P.emit("V", lambda e: e.memset(consts[:, 6:7], 64.0))
    P.emit("V", lambda e: e.memset(consts[:, 7:8], 0.125))
    P.emit("V", lambda e: e.memset(consts8[:, 0:1], 1.0))
    P.emit("V", lambda e: e.memset(consts85[:, 0:1], 1.0))
    t_cb = P.emit("V", lambda e: e.memset(constsb[:, 0:1], 1.0 / 1024.0),
                  inc="sV", drain=True)
    t_idb = P.emit("S", lambda e: e.copy(out=identb[:], in_=ident[:]),
                   waits=[t_id], inc="sS", drain=True)
    st["tvec"] = t_init

    def load_weight(hbm, kk, mm, into=None, wait0=None):
        if into is None:
            wt = AB.alloc([128, kk, mm])
        else:
            wt = into.rearrange("p (k m) -> p k m", k=kk, m=mm) \
                if kk * mm == into.free_size() else into[:, 0:kk * mm].rearrange(
                    "p (k m) -> p k m", k=kk, m=mm)
        toks = []
        src = hbm.rearrange("(k p) m -> p k m", p=128)
        pieces = [(k, o) for k in range(kk) for o in range(0, mm, 1024)]
        st.setdefault("stgi", 0)
        for k, o in pieces:
            w = min(1024, mm - o)
            i = st["stgi"] % 3
            st["stgi"] += 1
            sl = stage[:, i, 0:w]
            prev = st.get(("stg", i))
            td = P.emit("Y", lambda e, k=k, o=o, w=w, sl=sl, src=src: e.dma_start(
                out=sl, in_=src[:, k, o:o + w]), waits=[prev] if prev else [],
                inc=f"dY{i}")
            cw = [td] + ([wait0] if wait0 else [])
            cw += list(P.bar_toks.values())
            cw.append(st.get("sl_last_mm"))
            tc = P.emit("S", lambda e, k=k, o=o, w=w, sl=sl, wt=wt: e.copy(
                out=wt[:, k, o:o + w], in_=sl), waits=cw, inc="sS", drain=True)
            st[("stg", i)] = tc
            toks.append(tc)
        ktoks = [None] * kk
        for (k, o), tc in zip(pieces, toks):
            ktoks[k] = tc
        return wt, ktoks

    pp = [psA, psB]
    st["ppg"] = 0
    st["ppfree"] = {0: None, 1: None}

    def gemm(wt, kk, m_list, rhs_of, c0, cols, evac, waits0=(), wk=None,
             rk=None):
        ntiles = ceil_div(cols, 512)
        assert ntiles <= 2
        first_m = True
        for m in m_list:
            g = st["ppg"]; pg = pp[g % 2]
            w0 = list(waits0)
            if st["ppfree"][g % 2] is not None:
                w0.append(st["ppfree"][g % 2])
            tok_mm = None
            for k in range(kk):
                for nt in range(ntiles):
                    nn = min(512, cols - nt * 512)
                    last = (k == kk - 1 and nt == ntiles - 1)
                    w = w0 if (k == 0 and nt == 0) else []
                    if wk is not None and nt == 0:
                        w = list(w) + [wk[k]]
                    if rk is not None and first_m and nt == 0:
                        w = list(w) + [rk[k]]
                    tok_mm = P.emit("T", lambda e, k=k, nt=nt, nn=nn, m=m,
                                    pg=pg, wt=wt, rhs_of=rhs_of, c0=c0: e.matmul(
                        pg[:, nt, 0:nn], wt[:, k, m * 128:(m + 1) * 128],
                        rhs_of(k)[:, c0 + nt * 512:c0 + nt * 512 + nn],
                        start=(k == 0), stop=(k == kk - 1)),
                        waits=w, inc="sT" if last else None)
            st["last_mm"] = tok_mm
            tfree = evac(m, pg, [tok_mm])
            st["ppfree"][g % 2] = tfree
            st["ppg"] += 1
            first_m = False

    def col_sums(src_of, kk, base, finish, toks_in, c0=0, cols=T, ones=None):
        ones = constsb[:, 0:1] if ones is None else ones
        out_toks = []
        for nt in range(cols // 512):
            ps, key = (psC, "psC") if nt % 2 == 0 else (psD, "psD")
            tok = None
            for k in range(kk):
                w = [t_cb, toks_in[k] if toks_in else None]
                if k == 0:
                    w.append(st.get(("aux", key)))
                    w.append(st.get(("row", base + nt)))
                tok = P.emit("T", lambda e, k=k, nt=nt, ps=ps, ones=ones: e.matmul(
                    ps[0:1, 0:512], ones,
                    src_of(k)[:, c0 + nt * 512:c0 + (nt + 1) * 512],
                    start=(k == 0), stop=(k == kk - 1)),
                    waits=w, inc="sT" if k == kk - 1 else None)
            te = P.emit("S", lambda e, nt=nt, ps=ps: finish(
                e, base + nt, ps[0:1, 0:512]),
                waits=[tok], inc="sS", drain=True)
            st[("aux", key)] = te
            out_toks.append(te)
        return out_toks

    def bcast_row(base, dst_of, sign, waits, cols=T):
        lhsT = crowb[0:1, 0, :] if sign < 0 else crowb[0:1, 1, :]
        toks = []
        for nt in range(cols // 512):
            ps, key = (psC, "psC") if nt % 2 == 0 else (psD, "psD")
            w = list(waits) + [st.get(("aux", key))]
            tok = P.emit("T", lambda e, nt=nt, ps=ps, lhsT=lhsT: e.matmul(
                ps[:, 0:512], lhsT, rowv(base + nt),
                start=True, stop=True), waits=w, inc="sT")
            st[("row", base + nt)] = tok
            te = P.emit("S", lambda e, nt=nt, ps=ps: e.copy(
                out=dst_of(nt), in_=ps[:, 0:512]),
                waits=[tok, st.get("mneg_dma")], inc="sS", drain=True)
            st[("aux", key)] = te
            toks.append(te)
        return toks

    def _ln_core(src_of, kk):
        accm = [psA[0:1, 0, :], psA[0:1, 1, :], psB[0:1, 0, :], psB[0:1, 1, :]]
        accv = [psC[0:1, 0:512], psD[0:1, 0:512],
                psP[0:1, 0:4, :].rearrange("p a b -> p (a b)"),
                psP[0:1, 4:8, :].rearrange("p a b -> p (a b)")]
        akm = ["psA0", "psA1", "psB0", "psB1"]
        akv = ["psC", "psD", "psP0", "psP1"]
        AB.mark()
        xb = AB.alloc([128, 2, T])
        xq = AB.alloc([128, 2, T])
        mtoks = []
        vtoks = []
        for k in range(kk):
            tc = P.emit("V", lambda e, k=k: e.tensor_copy(
                out=xb[:, k % 2, :], in_=src_of(k)),
                waits=[("dYC", 48), st.get(("lnm", k % 2))],
                inc="sV", drain=True)
            tq = P.emit("V", lambda e, k=k: e.tensor_mul(
                xq[:, k % 2, :], xb[:, k % 2, :], xb[:, k % 2, :]),
                waits=[st.get(("lnv", k % 2))], inc="sV", drain=True)
            tok = tok2 = None
            for nt in range(T // 512):
                w = [tc]
                if k == 0:
                    w += [t_cb, st.get(("aux", akm[nt]))]
                tok = P.emit("T", lambda e, k=k, nt=nt: e.matmul(
                    accm[nt], constsb[:, 0:1],
                    xb[:, k % 2, nt * 512:(nt + 1) * 512],
                    start=(k == 0), stop=(k == kk - 1)),
                    waits=w, inc="sT" if nt == T // 512 - 1 else None)
            st[("lnm", k % 2)] = tok
            mtoks.append(tok)
            for nt in range(T // 512):
                w = [tq]
                if k == 0:
                    w += [st.get(("aux", akv[nt])), st.get(("row", nt)),
                          st.get(("row", 4 + nt))]
                tok2 = P.emit("T", lambda e, k=k, nt=nt: e.matmul(
                    accv[nt], constsb[:, 0:1],
                    xq[:, k % 2, nt * 512:(nt + 1) * 512],
                    start=(k == 0), stop=(k == kk - 1)),
                    waits=w, inc="sT" if nt == T // 512 - 1 else None)
            st[("lnv", k % 2)] = tok2
            vtoks.append(tok2)
        tm = []
        tv = []
        for nt in range(T // 512):
            te = P.emit("S", lambda e, nt=nt: e.activation(
                out=rowv(nt), in_=accm[nt], func=ACT.Copy),
                waits=[mtoks[-1], st.get(("row", nt))], inc="sS", drain=True)
            st[("aux", akm[nt])] = te
            tm.append(te)
            t2 = P.emit("S", lambda e, nt=nt: e.activation(
                out=rows[0:1, 4 + nt, :], in_=accv[nt], func=ACT.Copy),
                waits=[vtoks[-1]], inc="sS", drain=True)
            st[("aux", akv[nt])] = t2
            tv.append(t2)
        t_mneg = bcast_row(0, lambda nt: mneg[:, nt * 512:(nt + 1) * 512],
                           -1.0, tm)
        NT = T // 512
        mrow = mneg[0:1, :].rearrange("p (a b) -> p a b", a=NT)
        tq1 = P.emit("V", lambda e: e.tensor_mul(
            rows[0:1, 0:NT, :], mrow, mrow),
            waits=[t_mneg[-1], tv[-1]] +
                  [st[("row", nt)] for nt in range(NT)],
            inc="sV", drain=True)
        tq2 = P.emit("V", lambda e: e.tensor_sub(
            rows[0:1, 4:4 + NT, :], rows[0:1, 4:4 + NT, :],
            rows[0:1, 0:NT, :]), waits=[tq1], inc="sV", drain=True)
        tq3 = P.emit("S", lambda e: e.activation(
            out=rows[0:1, 0:NT, :], in_=rows[0:1, 4:4 + NT, :],
            func=ACT.Sqrt, bias=consts[0:1, 4:5]),
            waits=[tq2], inc="sS", drain=True)
        rtok = P.emit("S", lambda e: e.copy(
            out=rows[0:1, 0:NT, 0:256].bitcast(BF16),
            in_=rows[0:1, 0:NT, :]),
            waits=[tq3], inc="sS", drain=True)
        rsc = [mneg2[:, :], ab2[:, :]]
        t_ab = []
        for nt in range(NT):
            ps, key = (psC, "psC") if nt % 2 == 0 else (psD, "psD")
            tok = P.emit("T", lambda e, nt=nt, ps=ps: e.matmul(
                ps[:, 0:512], crowb[0:1, 1, :], rowv(nt),
                start=True, stop=True),
                waits=[rtok, st.get(("aux", key)), st.get(("row", nt))],
                inc="sT")
            st[("row", nt)] = tok
            tv_ = P.emit("V", lambda e, nt=nt, ps=ps, r=rsc[nt % 2]:
                         e.reciprocal(
                out=r, in_=ps[:, 0:512]),
                waits=[tok], inc="sV", drain=True)
            te = P.emit("S", lambda e, nt=nt, r=rsc[nt % 2]: e.copy(
                out=a_b[:, nt * 512:(nt + 1) * 512], in_=r),
                waits=[tv_], inc="sS", drain=True)
            st[("aux", key)] = te
            t_ab.append(te)
        AB.release()
        return t_mneg, t_ab, tv[-1]

    def ln_stats(src_of, kk=KC):
        t_mneg, t_ab, t_scr = _ln_core(src_of, kk)
        return t_ab[-1], t_scr, t_mneg[-1]

    def ln_mat(dst_of, src_of, k_list, c0, cols, tok, tok_mul=None):
        mt = []
        w0 = tok if isinstance(tok, list) else [tok]
        wm = [tok_mul] if tok_mul is not None else []
        for k in k_list:
            ta = P.emit("V", lambda e, k=k: e.tensor_add(
                dst_of(k), src_of(k)[:, c0:c0 + cols],
                mneg[:, c0:c0 + cols]), waits=w0, inc="sV", drain=True)
            mt.append(P.emit("V", lambda e, k=k: e.tensor_mul(
                dst_of(k), dst_of(k), a_b[:, c0:c0 + cols]),
                waits=[ta] + wm, inc="sV", drain=True))
        return mt

    def sub_mamba():
        if STAGE < 1:
            return
        AB.mark()
        QT = 512
        halo = AB.alloc([128, 16, 4])
        slotA = AB.alloc([128, 8 * E_INNER])
        slotB = AB.alloc([128, 16 * D])
        t_ab, t_scr, t_mneg = ln_stats(lambda k: R[:, k, :])
        wt_in, tk_in = load_weight(H["w_inproj"], KC, E_INNER, into=slotA)
        wt_out, tk_out = load_weight(H["w_outproj"], 16, D, into=slotB)
        if STAGE < 2:
            AB.release()
            return
        for q in range(T // QT):
            c0 = q * QT
            AB.mark()
            ln = AB.alloc([128, KC, QT])
            s1 = AB.alloc([128, 16, QT])
            x1 = AB.alloc([128, 2, QT])
            acc = AB.alloc([128, 2, QT])
            lt = []
            for k in range(KC):
                lt.append(P.emit("V", lambda e, k=k, c0=c0, ln=ln: e.tensor_add(
                    ln[:, k, :], R[:, k, c0:c0 + QT], mneg[:, c0:c0 + QT]),
                    waits=[t_mneg], inc="sV"))

            def evac_conv(m, pg, waits, q=q, s1=s1, x1=x1, acc=acc, c0=c0):
                sl = m % 2
                w = list(waits) + [st.get(("x1f", sl)), st.get(("accf", sl)),
                                   t_ab]
                tcp = P.emit("V", lambda e, sl=sl, pg=pg, c0=c0: e.tensor_mul(
                    x1[:, sl, :], pg[:, 0, 0:QT], a_b[:, c0:c0 + QT]),
                    waits=w, inc="sV", drain=True)
                tap = VEC["convw"] + m * 4
                P.emit("V", lambda e, sl=sl, tap=tap, x1=x1, acc=acc:
                       e.tensor_scalar(
                           out=acc[:, sl, :], in0=x1[:, sl, :],
                           scalar1=vecs[:, tap + 3:tap + 4], scalar2=None,
                           op0=ALU.mult), waits=[st["tvec"], tcp])
                for j in (2, 1, 0):
                    sh = 3 - j
                    P.emit("V", lambda e, sl=sl, tap=tap, j=j, sh=sh, x1=x1,
                           acc=acc: e.scalar_tensor_tensor(
                               out=acc[:, sl, sh:QT], in0=x1[:, sl, 0:QT - sh],
                               scalar=vecs[:, tap + j:tap + j + 1],
                               in1=acc[:, sl, sh:QT],
                               op0=ALU.mult, op1=ALU.add))
                    if q > 0:
                        P.emit("V", lambda e, m=m, sl=sl, tap=tap, j=j, sh=sh,
                               acc=acc: e.scalar_tensor_tensor(
                                   out=acc[:, sl, 0:sh],
                                   in0=halo[:, m, 4 - sh:4],
                                   scalar=vecs[:, tap + j:tap + j + 1],
                                   in1=acc[:, sl, 0:sh],
                                   op0=ALU.mult, op1=ALU.add))
                tv = P.emit("V", lambda e, m=m, sl=sl, x1=x1: e.tensor_copy(
                    out=halo[:, m, 1:4], in_=x1[:, sl, QT - 3:QT]),
                    inc="sV", drain=True)
                st[("x1f", sl)] = tv
                ts = P.emit("S", lambda e, m=m, sl=sl, s1=s1, acc=acc:
                            e.activation(
                                out=s1[:, m, :], in_=acc[:, sl, :],
                                func=ACT.Silu,
                                bias=vecs[:, VEC["convb"] + m:
                                          VEC["convb"] + m + 1]),
                            waits=[tv], inc="sS", drain=True)
                st[("accf", sl)] = ts
                st[("s1", m)] = ts
                return tcp

            gemm(wt_in, KC, list(range(16)), lambda k, ln=ln: ln[:, k, :],
                 0, QT, evac_conv, wk=tk_in if q == 0 else None, rk=lt)

            def evac_res(m, pg, waits, c0=c0):
                return P.emit("V", lambda e, m=m, pg=pg, c0=c0: e.tensor_add(
                    R[:, m, c0:c0 + QT], R[:, m, c0:c0 + QT], pg[:, 0, 0:QT]),
                    waits=waits, inc="sV", drain=True)

            if STAGE < 3:
                AB.release()
                break
            w_s1 = [st[("s1", m)] for m in range(16)]
            gemm(wt_out, 16, list(range(KC)), lambda k, s1=s1: s1[:, k, :],
                 0, QT, evac_res, waits0=w_s1,
                 wk=tk_out if q == 0 else None)
            AB.release()
        AB.release()


    def sub_ffn0():
        QT = 512
        NQ = T // QT
        AB.mark()
        slot_r = AB.alloc([128, 8 * D])
        slot_k = AB.alloc([128, 8 * D])
        slot_v = AB.alloc([128, 8 * D])
        t_ab, t_scr, t_mneg = ln_stats(lambda k: R[:, k, :])
        wr, wrt = load_weight(H["w0r"], KC, D, into=slot_r)
        wk_, wkt = load_weight(H["w0k"], KC, D, into=slot_k)
        wv, wvt = load_weight(H["w0v"], KC, D, into=slot_v)
        lnp = AB.alloc([128, 2, QT + 1])
        d = AB.alloc([128, KC, QT])
        xr = AB.alloc([128, KC, QT])
        xk = AB.alloc([128, KC, QT])
        r_sb = AB.alloc([128, KC, QT])
        kact = AB.alloc([128, KC, QT])
        dtmp = AB.alloc([128, 2, QT])
        P.emit("V", lambda e: e.memset(smalls[:, 0:8], 0.0))
        st["ffn_xf"] = None
        st["ffn_xkf"] = None

        def prep_xr(q):
            c0 = q * QT
            tk = None
            for k in range(KC):
                sl = k % 2
                mur = vecs[:, VEC["mu_r"] + k:VEC["mu_r"] + k + 1]
                P.emit("V", lambda e, k=k, sl=sl: e.tensor_copy(
                    out=lnp[:, sl, 0:1], in_=smalls[:, k:k + 1]))
                P.emit("V", lambda e, k=k, sl=sl, c0=c0: e.tensor_add(
                    lnp[:, sl, 1:QT + 1], R[:, k, c0:c0 + QT],
                    mneg[:, c0:c0 + QT]),
                    waits=[t_mneg, st["tvec"], st.get("ffn_xf"),
                           st.get("ffn_dxf")])
                P.emit("V", lambda e, k=k, sl=sl, c0=c0: e.tensor_mul(
                    lnp[:, sl, 1:QT + 1], lnp[:, sl, 1:QT + 1],
                    a_b[:, c0:c0 + QT]),
                    waits=[t_ab])
                P.emit("V", lambda e, k=k, sl=sl: e.tensor_sub(
                    d[:, k, :], lnp[:, sl, 1:QT + 1], lnp[:, sl, 0:QT]))
                tk = P.emit("V", lambda e, k=k, sl=sl, mur=mur:
                            e.scalar_tensor_tensor(
                                out=xr[:, k, :], in0=d[:, k, :],
                                scalar=mur, in1=lnp[:, sl, 0:QT],
                                op0=ALU.mult, op1=ALU.add),
                            inc="sV" if k == KC - 1 else None, drain=False)
                P.emit("V", lambda e, k=k, sl=sl: e.tensor_copy(
                    out=smalls[:, k:k + 1], in_=lnp[:, sl, QT:QT + 1]))
            return tk

        def prep_xk(q):
            c0 = q * QT
            tk = None
            for k in range(KC):
                muk = vecs[:, VEC["mu_k"] + k:VEC["mu_k"] + k + 1]
                mur = vecs[:, VEC["mu_r"] + k:VEC["mu_r"] + k + 1]
                P.emit("V", lambda e, k=k, mur=mur: e.scalar_tensor_tensor(
                    out=xk[:, k, :], in0=d[:, k, :], scalar=mur,
                    in1=xr[:, k, :], op0=ALU.mult, op1=ALU.subtract),
                    waits=[st.get("ffn_xkf")])
                tk = P.emit("V", lambda e, k=k, muk=muk:
                            e.scalar_tensor_tensor(
                                out=xk[:, k, :], in0=d[:, k, :], scalar=muk,
                                in1=xk[:, k, :], op0=ALU.mult,
                                op1=ALU.subtract),
                            inc="sV" if k == KC - 1 else None, drain=False)
            return tk

        t_prep = prep_xr(0)
        t_prepk = prep_xk(0)
        for q in range(NQ):
            c0 = q * QT

            def evac_r(m, pg, waits):
                return P.emit("S", lambda e, m=m, pg=pg: e.activation(
                    out=r_sb[:, m, :], in_=pg[:, 0, :], func=ACT.Sigmoid),
                    waits=waits, inc="sS", drain=True)

            gemm(wr, KC, list(range(KC)), lambda k: xr[:, k, :], 0, QT,
                 evac_r, waits0=[t_prep], wk=wrt if q == 0 else None)
            tmm_r = st["last_mm"]
            st["ffn_xf"] = tmm_r
            if q + 1 < NQ:
                t_prep = prep_xr(q + 1)

            def evac_k(m, pg, waits):
                sl = m % 2
                tr_ = P.emit("V", lambda e, m=m, pg=pg, sl=sl: e.tensor_scalar(
                    out=dtmp[:, sl, :], in0=pg[:, 0, :],
                    scalar1=0.0, scalar2=None, op0=ALU.max),
                    waits=list(waits) + [st.get(("dt", sl))],
                    inc="sV", drain=True)
                tk_ = P.emit("S", lambda e, m=m, sl=sl: e.activation(
                    out=kact[:, m, :], in_=dtmp[:, sl, :], func=ACT.Square),
                    waits=[tr_], inc="sS", drain=True)
                st[("dt", sl)] = tk_
                st[("kact", m)] = tk_
                return tr_

            gemm(wk_, KC, list(range(KC)), lambda k: xk[:, k, :], 0, QT,
                 evac_k, waits0=[t_prepk], wk=wkt if q == 0 else None)
            st["ffn_xkf"] = st["last_mm"]
            st["ffn_dxf"] = st["last_mm"]

            def evac_v(m, pg, waits, c0=c0):
                sl = m % 2
                tm_ = P.emit("V", lambda e, m=m, pg=pg, sl=sl: e.tensor_mul(
                    dtmp[:, sl, :], pg[:, 0, :], r_sb[:, m, :]),
                    waits=list(waits) + [st.get(("dt", sl))],
                    inc="sV", drain=True)
                st[("dt", sl)] = P.emit("V", lambda e, m=m, sl=sl, c0=c0:
                                        e.tensor_add(
                    R[:, m, c0:c0 + QT], R[:, m, c0:c0 + QT], dtmp[:, sl, :]),
                    inc="sV", drain=True)
                return tm_

            gemm(wv, KC, list(range(KC)), lambda k: kact[:, k, :], 0, QT,
                 evac_v, waits0=[st[("kact", m)] for m in range(KC)],
                 wk=wvt if q == 0 else None)
            if q + 1 < NQ:
                t_prepk = prep_xk(q + 1)
        AB.release()

    def sub_attnb0():
        HT = T // 2
        AB.mark()
        st["wfree2"] = None
        wslot = AB.alloc([128, 8 * E_INNER])
        t_ab, t_scr, t_mneg = ln_stats(lambda k: R[:, k, :])
        ln = AB.alloc([128, KC, HT])
        gate = AB.alloc([128, KC, HT])
        y = AB.alloc([128, KC, HT])
        vtmp = AB.alloc([128, 2, HT])
        hs = AB.alloc([128, 2, HT])
        P.emit("V", lambda e: e.memset(smalls[:, 8:16], 0.0))
        t_dec = P.emit("S", lambda e: e.activation(
            out=smalls[:, 16:24], in_=vecs[:, VEC["A0"]:VEC["A0"] + 8],
            func=ACT.Sigmoid), waits=[st["tvec"]], inc="sS", drain=True)
        st["wfree2"] = None
        for h in range(2):
            c0 = h * HT
            lt = ln_mat(lambda k, ln=ln: ln[:, k, :], lambda k: R[:, k, :],
                        range(KC), c0, HT, t_mneg, t_ab)
            win, wint = load_weight(H["w_in0"], KC, E_INNER, into=wslot,
                                    wait0=st["wfree2"])

            def evac_g(m, pg, waits):
                return P.emit("S", lambda e, m=m, pg=pg: e.activation(
                    out=gate[:, m - 8, :], in_=pg[:, :, :].rearrange(
                        "p a b -> p (a b)"), func=ACT.Sigmoid,
                    bias=vecs[:, VEC["b_in"] + m:VEC["b_in"] + m + 1]),
                    waits=waits, inc="sS", drain=True)

            gemm(win, KC, list(range(8, 16)), lambda k: ln[:, k, :], 0, HT,
                 evac_g, wk=wint, rk=lt)

            def evac_vch(m, pg, waits, h=h):
                sl = m % 2
                tv_ = P.emit("S", lambda e, m=m, pg=pg, sl=sl: e.activation(
                    out=vtmp[:, sl, :], in_=pg[:, :, :].rearrange(
                        "p a b -> p (a b)"), func=ACT.Identity,
                    bias=vecs[:, VEC["b_in"] + m:VEC["b_in"] + m + 1]),
                    waits=list(waits) + [st.get(("vt", sl))],
                    inc="sS", drain=True)
                tsc = P.emit("V", lambda e, m=m, sl=sl: e.tensor_tensor_scan(
                    out=hs[:, sl, :],
                    data0=smalls[:, 16 + m:17 + m].broadcast_to([128, HT]),
                    data1=vtmp[:, sl, :],
                    initial=smalls[:, 8 + m:9 + m],
                    op0=ALU.mult, op1=ALU.add),
                    waits=[tv_, t_dec, st.get(("hs", sl))],
                    inc="sV", drain=True)
                st[("vt", sl)] = tsc
                P.emit("V", lambda e, m=m, sl=sl: e.tensor_copy(
                    out=smalls[:, 8 + m:9 + m], in_=hs[:, sl, HT - 1:HT]))
                st[("hs", sl)] = P.emit("V", lambda e, m=m, sl=sl:
                                        e.scalar_tensor_tensor(
                    out=y[:, m, :], in0=hs[:, sl, :],
                    scalar=vecs[:, VEC["C0"] + m:VEC["C0"] + m + 1],
                    in1=gate[:, m, :], op0=ALU.mult, op1=ALU.mult),
                    inc="sV", drain=True)
                return tv_

            gemm(win, KC, list(range(8)), lambda k: ln[:, k, :], 0, HT,
                 evac_vch, waits0=[], wk=None)
            tmm_in = st["last_mm"]

            wout, woutt = load_weight(H["w_out0"], KC, D, into=wslot,
                                      wait0=tmm_in)

            def evac_o(m, pg, waits, c0=c0):
                return P.emit("V", lambda e, m=m, pg=pg, c0=c0:
                              e.scalar_tensor_tensor(
                    out=R[:, m, c0:c0 + HT],
                    in0=pg[:, :, :].rearrange("p a b -> p (a b)"),
                    scalar=vecs[:, VEC["b_out"] + m:VEC["b_out"] + m + 1],
                    in1=R[:, m, c0:c0 + HT], op0=ALU.add, op1=ALU.add),
                    waits=waits, inc="sV", drain=True)

            w_y = [st[("hs", 0)], st[("hs", 1)]]
            gemm(wout, KC, list(range(KC)), lambda k: y[:, k, :], 0, HT,
                 evac_o, waits0=w_y, wk=woutt)
            st["wfree2"] = st["last_mm"]
        AB.release()


    def ln_spike(pre, prets, gbase, ombase, out_of, scratch):
        mt = vt = None
        sqt = []
        for k in range(KC):
            tq = P.emit("S", lambda e, k=k: e.activation(
                out=scratch[:, k, :], in_=pre[:, k, :], func=ACT.Square),
                waits=[prets[k]], inc="sS", drain=True)
            sqt.append(tq)
            w = [prets[k]]
            if k == 0:
                w += [t_cb, st.get(("aux", "psC")), st.get(("row", 0))]
            mt = P.emit("T", lambda e, k=k: e.matmul(
                psC[0:1, 0:512], constsb[:, 0:1], pre[:, k, :],
                start=(k == 0), stop=(k == KC - 1)),
                waits=w, inc="sT" if k == KC - 1 else None)
            w2 = [tq]
            if k == 0:
                w2 += [st.get(("aux", "psD")), st.get(("row", 4)),
                       st.get(("row", 1))]
            vt = P.emit("T", lambda e, k=k: e.matmul(
                psD[0:1, 0:512], constsb[:, 0:1], scratch[:, k, :],
                start=(k == 0), stop=(k == KC - 1)),
                waits=w2, inc="sT" if k == KC - 1 else None)
        tme = P.emit("S", lambda e: e.activation(
            out=rowv(0), in_=psC[0:1, 0:512], func=ACT.Copy),
            waits=[mt], inc="sS", drain=True)
        st[("aux", "psC")] = tme
        tve = P.emit("S", lambda e: e.activation(
            out=rows[0:1, 4, :], in_=psD[0:1, 0:512], func=ACT.Copy),
            waits=[vt], inc="sS", drain=True)
        st[("aux", "psD")] = tve
        tmn = bcast_row(0, lambda nt: mneg2[:, :], -1.0, [tme], cols=512)
        tq1 = P.emit("V", lambda e: e.tensor_mul(
            rows[0:1, 5, :], mneg2[0:1, :], mneg2[0:1, :]),
            waits=[tmn[-1], st.get(("row", 5))], inc="sV", drain=True)
        tq2 = P.emit("V", lambda e: e.tensor_sub(
            rows[0:1, 4, :], rows[0:1, 4, :], rows[0:1, 5, :]),
            waits=[tq1, tve], inc="sV", drain=True)
        tq3 = P.emit("S", lambda e: e.activation(
            out=rows[0:1, 5, :], in_=rows[0:1, 4, :],
            func=ACT.Sqrt, bias=consts[0:1, 4:5]),
            waits=[tq2], inc="sS", drain=True)
        st[("row", 5)] = tq3
        tq4 = P.emit("V", lambda e: e.reciprocal(
            out=rows[0:1, 4, :], in_=rows[0:1, 5, :]),
            waits=[tq3], inc="sV", drain=True)
        st[("row", 4)] = tq4
        tq5 = P.emit("S", lambda e: e.copy(
            out=rowv(1), in_=rows[0:1, 4, :]),
            waits=[tq4, st.get(("row", 1))], inc="sS", drain=True)
        tab = bcast_row(1, lambda nt: ab2[:, :], 1.0, [tq5], cols=512)
        out_toks = []
        for k in range(KC):
            P.emit("V", lambda e, k=k: e.tensor_add(
                pre[:, k, :], pre[:, k, :], mneg2[:]),
                waits=[tmn[-1], sqt[k]])
            P.emit("V", lambda e, k=k: e.tensor_mul(
                pre[:, k, :], pre[:, k, :], ab2[:]), waits=[tab[-1]])
            out_toks.append(P.emit("V", lambda e, k=k: e.tensor_scalar(
                out=out_of(k), in0=pre[:, k, :],
                scalar1=vecs[:, gbase + k:gbase + k + 1],
                scalar2=smalls[:, ombase + k:ombase + k + 1],
                op0=ALU.mult, op1=ALU.is_ge),
                inc="sV", drain=True))
        return out_toks

    def sub_l1a():
        HT = T // 2
        AB.mark()
        bd = AB.alloc([128, 8, 128])
        P.emit("V", lambda e: e.memset(bd[:, :, :], 0.0))
        ln = AB.alloc([128, KC, T])
        AB.mark()
        slot_ka = AB.alloc([128, 8 * D])
        slot_va = AB.alloc([128, 8 * D])
        t_ab, t_scr, t_mneg = ln_stats(lambda k: R[:, k, :])
        wk, wkt = load_weight(H["wka"], KC, D, into=slot_ka)
        wv, wvt = load_weight(H["wva"], KC, D, into=slot_va)
        kblk = AB.alloc([128, 2, D])
        vblk = AB.alloc([128, 2, D])
        tpair = None
        lt = None
        for tb in range(16):
            if tb % 4 == 0:
                qc0 = (tb // 4) * 512
                lt = ln_mat(lambda k, qc0=qc0: ln[:, k, qc0:qc0 + 512],
                            lambda k: R[:, k, :],
                            range(KC), qc0, 512, t_mneg, t_ab)
            sl = tb % 2
            for which, w_, wt_, blk in (("k", wk, wkt, kblk), ("v", wv, wvt, vblk)):
                ps = psA if which == "k" else psB
                tok = None
                for kc in range(KC):
                    for nt in range(2):
                        w0 = []
                        if kc == 0 and nt == 0:
                            w0 = [lt[-1], st.get(("blkf", which, sl))]
                        if nt == 0:
                            w0 = list(w0) + [wt_[kc]]
                        tok = P.emit("T", lambda e, kc=kc, nt=nt, tb=tb, ps=ps,
                                     w_=w_: e.matmul(
                            ps[:, nt, :], ln[:, kc, tb * 128:(tb + 1) * 128],
                            w_[:, kc, nt * 512:(nt + 1) * 512],
                            start=(kc == 0), stop=(kc == KC - 1)),
                            waits=w0,
                            inc="sT" if (kc == KC - 1 and nt == 1) else None)
                te = P.emit("S", lambda e, ps=ps, blk=blk, sl=sl: e.copy(
                    out=blk[:, sl, :], in_=ps[:, :, :].rearrange(
                        "p a b -> p (a b)")), waits=[tok], inc="sS", drain=True)
                st[("blkev", which, sl)] = te
            for pc in range(8):
                w0 = []
                if pc == 0:
                    w0 = [st[("blkev", "k", sl)], st[("blkev", "v", sl)]]
                tpair = P.emit("T", lambda e, pc=pc, sl=sl, tb=tb: e.matmul(
                    psP[:, pc, :], kblk[:, sl, pc * 128:(pc + 1) * 128],
                    vblk[:, sl, pc * 128:(pc + 1) * 128],
                    start=(tb == 0 and pc % 4 == 0),
                    stop=(tb == 15 and pc % 4 == 3)),
                    waits=w0, inc="sT" if pc == 7 else None)
            st[("blkf", "k", sl)] = tpair
            st[("blkf", "v", sl)] = tpair
        st["l1a_kv_done"] = tpair
        tbd = []
        for pc in range(8):
            P.emit("S", lambda e, pc=pc: e.mul(
                out=bd[0:64, pc, 0:64], in_=psP[0:64, pc, 0:64],
                mul=consts[0:64, 7:8]), waits=[tpair])
            tbd.append(P.emit("S", lambda e, pc=pc: e.mul(
                out=bd[64:128, pc, 64:128], in_=psP[64:128, pc, 64:128],
                mul=consts[64:128, 7:8]), inc="sS", drain=True))
        AB.release()
        for h in range(2):
            c0 = h * HT
            AB.mark()
            q = AB.alloc([128, KC, HT])
            wslot_a = AB.alloc([128, 8 * D])
            wq, wqt = load_weight(H["wqa"], KC, D, into=wslot_a,
                                  wait0=st.get("l1a_wq_free")
                                  or st["l1a_kv_done"])

            def evac_q(m, pg, waits, q=q):
                tq = P.emit("S", lambda e, m=m, pg=pg, q=q: e.copy(
                    out=q[:, m, :], in_=pg[:, :, :].rearrange(
                        "p a b -> p (a b)")), waits=waits, inc="sS", drain=True)
                st[("qv", m)] = tq
                return tq

            gemm(wq, KC, list(range(KC)), lambda k: ln[:, k, :], c0, HT,
                 evac_q, waits0=[lt[-1]], wk=wqt)
            st["l1a_wq_free"] = st["last_mm"]
            attn = AB.alloc([128, KC, HT])
            for ec in range(8):
                g = st["ppg"]; pg = pp[g % 2]
                w0 = [tbd[-1], st[("qv", ec)]]
                if st["ppfree"][g % 2] is not None:
                    w0.append(st["ppfree"][g % 2])
                tok = None
                for nt in range(2):
                    tok = P.emit("T", lambda e, ec=ec, nt=nt, pg=pg, q=q:
                                 e.matmul(
                        pg[:, nt, :], bd[:, ec, :],
                        q[:, ec, nt * 512:(nt + 1) * 512],
                        start=True, stop=True),
                        waits=w0 if nt == 0 else [],
                        inc="sT" if nt == 1 else None)
                ta = P.emit("S", lambda e, ec=ec, pg=pg, attn=attn: e.copy(
                    out=attn[:, ec, :], in_=pg[:, :, :].rearrange(
                        "p a b -> p (a b)")), waits=[tok], inc="sS", drain=True)
                st[("at", ec)] = ta
                st["ppfree"][g % 2] = ta
                st["ppg"] += 1
            wo, wot = load_weight(H["woa"], KC, D, into=wslot_a,
                                   wait0=st["last_mm"])

            def evac_oa(m, pg, waits, c0=c0):
                return P.emit("V", lambda e, m=m, pg=pg, c0=c0: e.tensor_add(
                    R[:, m, c0:c0 + HT], R[:, m, c0:c0 + HT],
                    pg[:, :, :].rearrange("p a b -> p (a b)")),
                    waits=waits, inc="sV", drain=True)

            gemm(wo, KC, list(range(KC)), lambda k, attn=attn: attn[:, k, :],
                 0, HT, evac_oa,
                 waits0=[st[("at", m)] for m in range(8)], wk=wot)
            st["l1a_wq_free"] = st["last_mm"]
            AB.release()
        AB.release()

    def sub_l1ffn():
        HT = T // 2
        AB.mark()
        st["l1f_wfree"] = None
        wslot = AB.alloc([128, 8 * D])
        t_ab, t_scr, t_mneg = ln_stats(lambda k: R[:, k, :])
        ln = AB.alloc([128, KC, HT])
        h1 = AB.alloc([128, KC, HT])
        h2 = AB.alloc([128, KC, HT])
        dtmp = AB.alloc([128, 2, HT])
        def emit_lt(c0, wtok):
            lts = []
            for k in range(KC):
                lts.append(P.emit("V", lambda e, k=k, c0=c0: e.tensor_add(
                    ln[:, k, :], R[:, k, c0:c0 + HT], mneg[:, c0:c0 + HT]),
                    waits=[t_mneg, wtok], inc="sV"))
            return lts

        lt = emit_lt(0, st.get("l1f_lnf"))
        for h in range(2):
            c0 = h * HT

            def evac_h(dst, scale_ab=False):
                def f(m, pg, waits, dst=dst, scale_ab=scale_ab):
                    if scale_ab:
                        t_ = P.emit("V", lambda e, m=m, pg=pg, dst=dst, c0=c0:
                                    e.tensor_mul(
                            dst[:, m, :], pg[:, :, :].rearrange(
                                "p a b -> p (a b)"), a_b[:, c0:c0 + HT]),
                            waits=list(waits) + [t_ab], inc="sV", drain=True)
                    else:
                        t_ = P.emit("S", lambda e, m=m, pg=pg, dst=dst: e.copy(
                            out=dst[:, m, :], in_=pg[:, :, :].rearrange(
                                "p a b -> p (a b)")), waits=waits,
                            inc="sS", drain=True)
                    st[("hv", m)] = t_
                    return t_
                return f

            w1, w1t = load_weight(H["wf1"], KC, D, into=wslot,
                                  wait0=st["l1f_wfree"])
            gemm(w1, KC, list(range(KC)), lambda k: ln[:, k, :], 0, HT,
                 evac_h(h1, True), wk=w1t, rk=lt)
            if h == 0:
                lt = emit_lt(HT, st["last_mm"])
            w2, w2t = load_weight(H["wf2"], KC, D, into=wslot,
                                  wait0=st["last_mm"])
            gemm(w2, KC, list(range(KC)), lambda k: h1[:, k, :], 0, HT,
                 evac_h(h2), waits0=[st[("hv", m)] for m in range(KC)], wk=w2t)
            w3, w3t = load_weight(H["wf3"], KC, D, into=wslot,
                                  wait0=st["last_mm"])

            def evac_t(m, pg, waits, c0=c0):
                sl = m % 2
                tt_ = P.emit("S", lambda e, m=m, pg=pg, sl=sl: e.activation(
                    out=dtmp[:, sl, :], in_=pg[:, :, :].rearrange(
                        "p a b -> p (a b)"), func=ACT.Tanh,
                    bias=vecs[:, VEC["f1b"] + m:VEC["f1b"] + m + 1]),
                    waits=list(waits) + [st.get(("dt2", sl))],
                    inc="sS", drain=True)
                st[("dt2", sl)] = P.emit("V", lambda e, m=m, sl=sl, c0=c0:
                                         e.tensor_add(
                    R[:, m, c0:c0 + HT], R[:, m, c0:c0 + HT], dtmp[:, sl, :]),
                    waits=[tt_], inc="sV", drain=True)
                return tt_

            gemm(w3, KC, list(range(KC)), lambda k: h2[:, k, :], 0, HT,
                 evac_t, waits0=[st[("hv", m)] for m in range(KC)], wk=w3t)
            st["l1f_wfree"] = st["last_mm"]
            st["l1f_lnf"] = st["last_mm"]
        AB.release()

    def sub_l1b():
        QT = 512
        AB.mark()
        t_ab, t_scr, t_mneg = ln_stats(lambda k: R[:, k, :])
        st["l1b_wfree"] = None
        for xi, nm in enumerate(("betaq", "betak", "betav", "betao")):
            P.emit("V", lambda e, xi=xi, nm=nm: e.tensor_scalar(
                out=smalls[:, 24 + 8 * xi:32 + 8 * xi],
                in0=vecs[:, VEC[nm]:VEC[nm] + 8],
                scalar1=consts[:, 1:2], scalar2=consts[:, 2:3],
                op0=ALU.mult, op1=ALU.add), waits=[st["tvec"], t_cb])
        P.emit("V", lambda e: e.tensor_scalar(
            out=smalls[:, 113:114], in0=vecs[:, VEC["scale_b"]:VEC["scale_b"] + 1],
            scalar1=2.0, scalar2=None, op0=ALU.mult))
        t_thr = P.emit("V", lambda e: e.reciprocal(
            out=smalls[:, 112:113], in_=smalls[:, 113:114]),
            inc="sV", drain=True)
        Q8 = AB.alloc([128, KC, T], FP8E5)
        Vtm = AB.alloc([128, 16, D], FP8)
        Ktm = AB.alloc([128, 16, D], FP8)
        AB.mark()
        wslot = AB.alloc([128, 8 * D])
        lnq = AB.alloc([128, KC, QT])
        pre = AB.alloc([128, KC, QT])
        spk = AB.alloc([128, KC, QT])
        for qt in range(4):
            c0 = qt * QT
            lt = ln_mat(lambda k, lnq=lnq: lnq[:, k, :], lambda k: R[:, k, :],
                        range(KC), c0, QT,
                        [t_mneg, st.get("l1b_wfree")], t_ab)
            st["l1b_lnmat_last"] = lt[-1]
            fin_prev = st.pop("l1b_fin", None)
            if fin_prev:
                fin_prev()
            for X, wh, bbase, gbase, ombase in (
                    ("k", "wkb", VEC["bk"], VEC["gk"], 32),
                    ("v", "wvb", VEC["bv"], VEC["gv"], 40),
                    ("q", "wqb", VEC["bq"], VEC["gq"], 24)):
                wx, wxt = load_weight(H[wh], KC, D, into=wslot,
                                      wait0=st["l1b_wfree"])

                def evac_p(m, pg, waits, bbase=bbase):
                    t_ = P.emit("S", lambda e, m=m, pg=pg, bbase=bbase:
                                e.activation(
                        out=pre[:, m, :], in_=pg[:, 0, :], func=ACT.Identity,
                        bias=vecs[:, bbase + m:bbase + m + 1]),
                        waits=list(waits) + [st.get("prejunk")],
                        inc="sS", drain=True)
                    st[("pv", m)] = t_
                    return t_

                gemm(wx, KC, list(range(KC)), lambda k, lnq=lnq: lnq[:, k, :],
                     0, QT, evac_p, wk=wxt, rk=lt)
                st["l1b_wfree"] = st["last_mm"]
                if X == "q":
                    pv = [st[("pv", m)] for m in range(KC)]

                    def fin(c0=c0, pv=pv, gbase=gbase, ombase=ombase):
                        spk_toks = ln_spike(pre, pv, gbase, ombase,
                                            lambda k: spk[:, k, :], spk)
                        st["prejunk"] = spk_toks[-1]
                        for k in range(KC):
                            P.emit("S", lambda e, k=k, c0=c0: e.copy(
                                out=Q8[:, k, c0:c0 + QT], in_=spk[:, k, :]),
                                waits=[spk_toks[k]], inc="sS", drain=True)
                    st["l1b_fin"] = fin
                else:
                    spk_toks = ln_spike(pre, [st[("pv", m)] for m in range(KC)],
                                        gbase, ombase, lambda k: spk[:, k, :],
                                        spk)
                    dst = Ktm if X == "k" else Vtm
                    for ec in range(KC):
                        for lb in range(4):
                            slot = ((ec * 4 + lb) % 2) * 4
                            tb = qt * 4 + lb
                            ttr = P.emit("T", lambda e, ec=ec, lb=lb, slot=slot:
                                         e.transpose(
                                psP[:, slot, 0:64].bitcast(BF16),
                                spk[:, ec, lb * 128:(lb + 1) * 128],
                                identb[:]),
                                waits=[spk_toks[ec], t_idb,
                                       st.get(("tps", slot))],
                                inc="sT")
                            st[("tps", slot)] = P.emit(
                                "S", lambda e, slot=slot, tb=tb, ec=ec, dst=dst:
                                e.copy(out=dst[:, tb, ec * 128:(ec + 1) * 128],
                                       in_=psP[:, slot, 0:64].bitcast(BF16)),
                                waits=[ttr], inc="sS", drain=True)
                if X == "k":
                    tks = col_sums(lambda k: spk[:, k, :], KC, 0,
                                   lambda e, i, p: e.activation(
                                       out=rowv(i), in_=p, func=ACT.Copy),
                                   spk_toks, cols=QT)
                    st["tksb"] = bcast_row(0, lambda nt: ksb[:, :], 1.0, tks,
                                           cols=QT)[-1]
                if X == "v":
                    for k in range(KC):
                        P.emit("V", lambda e, k=k, qt=qt: e.scalar_tensor_tensor(
                            out=pre[:, k, :], in0=spk[:, k, :], scalar=0.0,
                            in1=ksb[:], op0=ALU.bypass, op1=ALU.mult,
                            accum_out=smalls[:, 88 + qt * 8 + k:89 + qt * 8 + k]),
                            waits=[st["tksb"], spk_toks[k]])
                        st["prejunk"] = P.emit(
                            "V", lambda e, k=k, qt=qt: e.scalar_tensor_tensor(
                                out=pre[:, k, :], in0=spk[:, k, :], scalar=0.0,
                                in1=spk[:, k, :], op0=ALU.bypass, op1=ALU.max,
                                accum_out=smalls[:, 56 + qt * 8 + k:
                                                 57 + qt * 8 + k]),
                            inc="sV", drain=True)
        st.pop("l1b_fin")()
        AB.release()
        if L1BST < 2:
            AB.release()
            return
        for k in range(KC):
            P.emit("V", lambda e, k=k: e.reduce_sum(
                out=smalls[:, 120 + k:121 + k],
                in_=smalls[:, 56 + k:88 + k:8], axis=mybir.AxisListType.X))
            P.emit("V", lambda e, k=k: e.reduce_sum(
                out=smalls[:, 128 + k:129 + k],
                in_=smalls[:, 88 + k:120 + k:8], axis=mybir.AxisListType.X))
        t_sums = P.emit("V", lambda e: e.tensor_scalar(
            out=smalls[:, 136:144], in0=smalls[:, 128:136],
            scalar1=consts[:, 1:2], scalar2=None, op0=ALU.mult),
            inc="sV", drain=True)
        t_mnegfree = st["l1b_lnmat_last"]
        tpairs = []
        for pc in range(8):
            w0 = []
            if pc == 0:
                w0 = [st.get(("tps", 0)), st.get(("tps", 4))]
            tpair = None
            for tb in range(16):
                tpair = P.emit("T", lambda e, tb=tb, pc=pc: e.matmul(
                    psP[:, pc, :], Ktm[:, tb, pc * 128:(pc + 1) * 128],
                    Vtm[:, tb, pc * 128:(pc + 1) * 128],
                    start=(tb == 0 and pc % 4 == 0),
                    stop=(tb == 15 and pc % 4 == 3)),
                    waits=w0 if tb == 0 else [],
                    inc="sT" if tb == 15 else None)
            tpairs.append(tpair)
        tpair = tpairs[-1]
        AB.mark()
        pA = AB.alloc([128, 8, 128], FP8E5)
        pB = AB.alloc([128, 8, 128], FP8E5)
        pC = AB.alloc([128, 8, 128], FP8E5)
        pD = AB.alloc([128, 8, 128], FP8E5)
        P.emit("V", lambda e: e.memset(pA[:, :, :], 0.0))
        P.emit("V", lambda e: e.memset(pB[:, :, :], 0.0))
        P.emit("V", lambda e: e.memset(pC[:, :, :], 0.0))
        P.emit("V", lambda e: e.memset(pD[:, :, :], 0.0))
        tpc = []
        I32 = mybir.dt.int32
        for pc in range(8):
            fm2 = mneg2[:, 0:64]
            im2 = mneg2[:, 64:128].bitcast(I32)
            ir5 = mneg2[:, 128:192].bitcast(I32)
            ir6 = mneg2[:, 192:256].bitcast(I32)
            ir8 = mneg2[:, 256:320].bitcast(I32)
            fr5 = mneg2[:, 320:384]
            fr6 = mneg2[:, 384:448]
            fr8 = mneg2[:, 448:512]
            P.emit("V", lambda e, pc=pc: e.tensor_scalar(
                out=mneg2[0:64, 0:64], in0=psP[0:64, pc, 0:64],
                scalar1=2.0, scalar2=None, op0=ALU.mult),
                waits=[tpairs[(pc // 4) * 4 + 3]])
            P.emit("V", lambda e, pc=pc: e.tensor_scalar(
                out=mneg2[64:128, 0:64], in0=psP[64:128, pc, 64:128],
                scalar1=2.0, scalar2=None, op0=ALU.mult),
                waits=[tpairs[(pc // 4) * 4 + 3]])
            P.emit("V", lambda e, fm2=fm2, im2=im2: e.tensor_copy(
                out=im2, in_=fm2))
            P.emit("V", lambda e, im2=im2, ir5=ir5: e.tensor_scalar(
                out=ir5, in0=im2, scalar1=511, scalar2=None,
                op0=ALU.bitwise_and))
            P.emit("V", lambda e, im2=im2, ir6=ir6: e.tensor_scalar(
                out=ir6, in0=im2, scalar1=63, scalar2=None,
                op0=ALU.bitwise_and))
            P.emit("V", lambda e, im2=im2, ir8=ir8: e.tensor_scalar(
                out=ir8, in0=im2, scalar1=7, scalar2=None,
                op0=ALU.bitwise_and))
            P.emit("V", lambda e, ir5=ir5, fr5=fr5: e.tensor_copy(
                out=fr5, in_=ir5))
            P.emit("V", lambda e, ir6=ir6, fr6=fr6: e.tensor_copy(
                out=fr6, in_=ir6))
            P.emit("V", lambda e, ir8=ir8, fr8=fr8: e.tensor_copy(
                out=fr8, in_=ir8))
            tk = None
            for lo, hi in ((0, 64), (64, 128)):
                P.emit("V", lambda e, pc=pc, lo=lo, hi=hi:
                       e.tensor_sub(pA[lo:hi, pc, lo:hi],
                                    mneg2[lo:hi, 0:64],
                                    mneg2[lo:hi, 320:384]))
                P.emit("V", lambda e, pc=pc, lo=lo, hi=hi:
                       e.tensor_sub(pB[lo:hi, pc, lo:hi],
                                    mneg2[lo:hi, 320:384],
                                    mneg2[lo:hi, 384:448]))
                P.emit("V", lambda e, pc=pc, lo=lo, hi=hi:
                       e.tensor_sub(pC[lo:hi, pc, lo:hi],
                                    mneg2[lo:hi, 384:448],
                                    mneg2[lo:hi, 448:512]))
                tk = P.emit("V", lambda e, pc=pc, lo=lo, hi=hi:
                            e.tensor_scalar(
                    pD[lo:hi, pc, lo:hi], mneg2[lo:hi, 448:512], 0.0, None,
                    op0=ALU.add), inc="sV" if hi == 128 else None,
                    drain=(hi == 128))
            tpc.append(tk)
        if L1BST < 3:
            AB.release()
            AB.release()
            return
        bias8 = mneg[:, :].bitcast(FP8).rearrange("p (mb l) -> p mb l",
                                                  mb=16, l=QT)
        wo, wot = load_weight(H["wob"], KC, D, wait0=st["l1b_wfree"])
        spkout = AB.alloc([128, KC, QT])
        opre = AB.alloc([128, KC, QT])
        st["bias8f"] = None
        for lq in range(4):
            c0 = lq * QT
            tb8 = P.emit("G", lambda e, c0=c0: e.dma_start(
                out=bias8[:, :, :], in_=H["biasmx"].rearrange(
                    "(mb p) l -> p mb l", p=128)[:, :, c0:c0 + QT]),
                waits=[st["bias8f"], t_mnegfree,
                       st.get("mneg_dma")], inc="dG")
            st["mneg_dma"] = tb8
            tqs = col_sums(lambda k: Q8[:, k, :], KC, 0,
                           lambda e, i, p: e.activation(
                               out=rowv(i), in_=p, func=ACT.Identity,
                               scale=consts[0:1, 1:2], bias=consts[0:1, 6:7]),
                           None, c0=c0, cols=QT, ones=consts85[:, 0:1])
            ta2 = bcast_row(0, lambda nt: ksb[:, :], 1.0, tqs, cols=QT)
            for ec in range(8):
                g = st["ppg"]; pg = pp[g % 2]
                w0 = [tb8]
                if st["ppfree"][g % 2] is not None:
                    w0.append(st["ppfree"][g % 2])
                tok = None
                for mb in range(16):
                    tok = P.emit("T", lambda e, mb=mb, ec=ec, pg=pg: e.matmul(
                        pg[:, 0, :], Vtm[:, mb, ec * 128:(ec + 1) * 128],
                        bias8[:, mb, :], start=(mb == 0), stop=False),
                        waits=w0 if mb == 0 else [], inc=None)
                for pi, piece in enumerate((pA, pB, pC, pD)):
                    tok = P.emit("T", lambda e, ec=ec, pg=pg, piece=piece,
                                 c0=c0: e.matmul(
                        pg[:, 0, :], piece[:, ec, :], Q8[:, ec, c0:c0 + QT],
                        start=False, stop=(piece is pD)),
                        waits=[tpc[ec], t_sums] if pi == 0 else [],
                        inc="sT" if pi == 3 else None)
                sc3 = ab2 if ec % 2 == 0 else mneg2
                t3 = P.emit("V", lambda e, ec=ec, pg=pg, sc3=sc3:
                            e.scalar_tensor_tensor(
                    out=sc3[:, :], in0=ksb[:, :],
                    scalar=smalls[:, 120 + ec:121 + ec], in1=pg[:, 0, :],
                    op0=ALU.mult, op1=ALU.add),
                    waits=[tok, ta2[-1], st.get(("ab2f", ec % 2)),
                           tpc[-1]], inc="sV", drain=True)
                tso = P.emit("V", lambda e, ec=ec, sc3=sc3: e.tensor_scalar(
                    out=spkout[:, ec, :], in0=sc3[:, :],
                    scalar1=smalls[:, 136 + ec:137 + ec],
                    scalar2=smalls[:, 112:113],
                    op0=ALU.add, op1=ALU.is_ge),
                    waits=[t_thr], inc="sV", drain=True)
                st[("ab2f", ec % 2)] = tso
                st[("so", ec)] = tso
                st["ppfree"][g % 2] = t3
                st["ppg"] += 1
                st["bias8f"] = tok

            def evac_ob(m, pg, waits):
                t_ = P.emit("S", lambda e, m=m, pg=pg: e.activation(
                    out=opre[:, m, :], in_=pg[:, 0, :], func=ACT.Identity,
                    bias=vecs[:, VEC["bo"] + m:VEC["bo"] + m + 1]),
                    waits=waits, inc="sS", drain=True)
                st[("pv", m)] = t_
                return t_

            gemm(wo, KC, list(range(KC)), lambda k: spkout[:, k, :], 0, QT,
                 evac_ob, waits0=[st[("so", m)] for m in range(8)], wk=wot)
            dsp = ln_spike(opre, [st[("pv", m)] for m in range(KC)],
                           VEC["go"], 48, lambda k: spkout[:, k, :], spkout)
            for k in range(KC):
                P.emit("V", lambda e, k=k, c0=c0: e.tensor_add(
                    R[:, k, c0:c0 + QT], R[:, k, c0:c0 + QT], spkout[:, k, :]),
                    waits=[dsp[k]])
        P.emit("V", lambda e: e.nop(), inc="sV", drain=True)
        AB.release()
        AB.release()

    subs = [sub_mamba, sub_ffn0, sub_attnb0, sub_ffn0,
            sub_l1a, sub_l1ffn, sub_l1b, sub_l1ffn]
    for f in subs[:NSUB]:
        f()
        st["sl_last_mm"] = st.get("last_mm")
        P.barrier()

    t_out = P.emit("G", lambda e: e.dma_start(
        out=outT.rearrange("(c p) t -> p c t", p=128), in_=R[:]), inc="dG")
    P.emit("G", lambda e: e.nop(), waits=[t_out])



def _pack_vec(v):
    v = np.asarray(v, np.float32)
    return v.reshape(v.shape[0] // 128, 128).T


def make_vecs(inp):
    cols = np.zeros((128, VEC_COLS), np.float32)

    def put(name, arr):
        a = _pack_vec(arr)
        cols[:, VEC[name]:VEC[name] + a.shape[1]] = a

    cw = np.asarray(inp["L0_attn_a_conv_w"], np.float32).reshape(E_INNER, 4)
    for c in range(16):
        for j in range(4):
            cols[:, VEC["convw"] + c * 4 + j] = cw[c * 128:(c + 1) * 128, j]
    put("convb", inp["L0_attn_a_conv_b"])
    put("mu_r", inp["L0_ffn_mu_r"]); put("mu_k", inp["L0_ffn_mu_k"])
    put("b_in", inp["L0_attn_b_b_in"]); put("A0", inp["L0_attn_b_A"])
    put("C0", inp["L0_attn_b_C"]); put("b_out", inp["L0_attn_b_b_out"])
    for nm, key in [("bq", "L1_attn_b_bq"), ("bk", "L1_attn_b_bk"),
                    ("bv", "L1_attn_b_bv"), ("bo", "L1_attn_b_bo"),
                    ("gq", "L1_attn_b_gq"), ("gk", "L1_attn_b_gk"),
                    ("gv", "L1_attn_b_gv"), ("go", "L1_attn_b_go"),
                    ("betaq", "L1_attn_b_betaq"), ("betak", "L1_attn_b_betak"),
                    ("betav", "L1_attn_b_betav"), ("betao", "L1_attn_b_betao")]:
        put(nm, inp[key])
    cols[:, VEC["scale_b"]] = np.float32(np.asarray(inp["L1_attn_b_scale"]))
    put("f1b", inp["L1_ffn_bias"])
    return cols


def make_biasmx():
    i = np.arange(T, dtype=np.float64)
    dist = np.abs(i[:, None] - i[None, :])
    b = np.maximum(np.ceil(np.log2((T - 1.0) / (dist + 1.0))), 0.0)
    r32 = (np.float32(T - 1.0) / (dist.astype(np.float32) + np.float32(1.0)))
    b32 = np.maximum(np.ceil(np.log2(r32)), 0.0)
    assert np.array_equal(b, b32.astype(np.float64))
    return b.astype(ml_dtypes.float8_e4m3)


_NC_CACHE = {}


def kernel(**inp):
    x = np.asarray(inp["L0_attn_a_hidden_states"], np.float32)
    B = x.shape[0]
    if "p" not in _NC_CACHE:
        _NC_CACHE["p"] = build_program()
    nc = _NC_CACHE["p"]

    def ct(a):
        return np.ascontiguousarray(np.asarray(a, np.float32))

    shared = {
        "w_inproj": ct(np.asarray(inp["L0_attn_a_in_proj_w"])[:E_INNER, :].T),
        "w_outproj": ct(np.asarray(inp["L0_attn_a_out_proj_w"]).T),
        "w0r": ct(np.asarray(inp["L0_ffn_W_r"]).T),
        "w0k": ct(np.asarray(inp["L0_ffn_W_k"]).T),
        "w0v": ct(np.asarray(inp["L0_ffn_W_v"]).T),
        "w_in0": ct(inp["L0_attn_b_W_in"]),
        "w_out0": ct(inp["L0_attn_b_W_out"]),
        "wqa": ct(np.asarray(inp["L1_attn_a_W_q"]).T),
        "wka": ct(np.asarray(inp["L1_attn_a_W_k"]).T),
        "wva": ct(np.asarray(inp["L1_attn_a_W_v"]).T),
        "woa": ct(np.asarray(inp["L1_attn_a_W_o"]).T),
        "wqb": ct(np.asarray(inp["L1_attn_b_Wq"]).T),
        "wkb": ct(np.asarray(inp["L1_attn_b_Wk"]).T),
        "wvb": ct(np.asarray(inp["L1_attn_b_Wv"]).T),
        "wob": ct(np.asarray(inp["L1_attn_b_Wo"]).T),
        "wf1": ct(inp["L1_ffn_weight"]),
        "wf2": ct(inp["L1_ffn_aux_weight1"]),
        "wf3": ct(inp["L1_ffn_aux_weight2"]),
        "vecs": make_vecs(inp),
        "ident": np.eye(128, dtype=np.float32),
        "biasmx": make_biasmx(),
    }
    in_maps = []
    for b in range(B):
        m = dict(shared)
        m["xT"] = ct(x[b].T)
        in_maps.append(m)
    res = run_bass_kernel_spmd(nc, in_maps, core_ids=list(range(B)))
    out = np.stack([np.asarray(r["outT"], np.float32).T for r in res.results])
    return out


# version 30
# speedup vs baseline: 1.0219x; 1.0006x over previous
import os

import numpy as np
import ml_dtypes

import concourse.bass as bass
import concourse.mybir as mybir
from concourse.bass_utils import run_bass_kernel_spmd

F32 = mybir.dt.float32
BF16 = mybir.dt.bfloat16
FP8 = mybir.dt.float8e4
FP8E5 = mybir.dt.float8e5
ALU = mybir.AluOpType
ACT = mybir.ActivationFunctionType

T = 2048
D = 1024
KC = 8
E_INNER = 2048
EPS = 1e-5
NSUB = int(os.environ.get("NSUB", "8"))
STAGE = int(os.environ.get("STAGE", "99"))
L1BST = int(os.environ.get("L1BST", "9"))

ENGINES = ("T", "V", "S", "G", "Y")
DMA_SEMS = ("dY0", "dY1", "dY2", "dY3", "dYC", "dG", "dS")


class Prog:
    def __init__(self):
        self.ops = {e: [] for e in ENGINES}
        self.cnt = {s: 0 for s in ("sT", "sV", "sS", "sG") + DMA_SEMS}
        self.waited = {e: {} for e in ENGINES}

    OWN = {"T": "sT", "V": "sV", "S": "sS", "G": "sG"}

    def emit(self, eng, fn, waits=(), inc=None, drain=None, nodrain=False):
        own = self.OWN.get(eng)
        waits = [w for w in waits if w is not None and w[0] != own]
        eff = []
        for s, v in waits:
            if eng != "G" and self.waited[eng].get(s, 0) >= v:
                continue
            self.waited[eng][s] = max(self.waited[eng].get(s, 0), v)
            eff.append((s, v))
        tok = None
        if inc is not None:
            amt = 16 if inc in DMA_SEMS else 1
            self.cnt[inc] += amt
            tok = (inc, self.cnt[inc])
        is_dma = inc in DMA_SEMS
        do_drain = ((eng in ("V", "S", "G")) or drain is True) and not is_dma \
            and not (nodrain and inc is None)

        def op(e, sems, eff=eff, fn=fn, inc=inc, do_drain=do_drain,
               is_dma=is_dma):
            for s, v in eff:
                e.wait_ge(sems[s], v)
            inst = fn(e)
            if do_drain:
                inst = e.drain()
            if inc is not None:
                inst.then_inc(sems[inc], 16 if is_dma else 1)

        import traceback
        if os.environ.get("PROV"):
            fr = traceback.extract_stack()[-3:-1]
            op._prov = " | ".join(f"{f.name}:{f.lineno}" for f in fr)
        self.ops[eng].append(op)
        return tok

    bar_toks = {}

    def barrier(self):
        pre = {s: self.cnt[s] for s in ("sT", "sV", "sS", "sG")}
        toks = {}
        for e, sem in (("T", "sT"), ("V", "sV"), ("S", "sS"), ("G", "sG")):
            w = [(s, v) for s, v in pre.items() if s != sem and v > 0]
            toks[e] = self.emit(e, lambda en: en.nop(), waits=w, inc=sem,
                                drain=True)
        for e in ENGINES:
            self.emit(e, lambda en: en.nop(),
                      waits=[toks[o] for o in toks if o != e])
        self.bar_toks = dict(toks)


class Arena:

    ESIZE = {}

    def __init__(self, handle, nbytes, name):
        self.h, self.n, self.name = handle, nbytes, name
        self.off = 0
        self.marks = []
        self.hwm = 0

    def alloc(self, shape, dt=BF16):
        assert shape[0] == 128
        es = mybir.dt.size(dt)
        n = int(np.prod(shape[1:])) * es
        self.off = (self.off + 63) // 64 * 64
        assert self.off + n <= self.n, \
            f"arena {self.name} overflow: {self.off}+{n} > {self.n}"
        ap = self.h[:, self.off:self.off + n].bitcast(dt)
        self.off += n
        self.hwm = max(self.hwm, self.off)
        if len(shape) > 2:
            dims = " ".join(f"d{i}" for i in range(1, len(shape)))
            kw = {f"d{i}": shape[i] for i in range(1, len(shape))}
            ap = ap.rearrange(f"p ({dims}) -> p {dims}", **kw)
        return ap

    def mark(self):
        self.marks.append(self.off)

    def release(self):
        self.off = self.marks.pop()


def ceil_div(a, b):
    return (a + b - 1) // b


VEC = {}
_vc = 0
for _n, _c in [
    ("convw", 64), ("convb", 16), ("mu_r", 8), ("mu_k", 8),
    ("b_in", 16), ("A0", 8), ("C0", 8), ("b_out", 8),
    ("bq", 8), ("bk", 8), ("bv", 8), ("bo", 8),
    ("gq", 8), ("gk", 8), ("gv", 8), ("go", 8),
    ("betaq", 8), ("betak", 8), ("betav", 8), ("betao", 8),
    ("scale_b", 1), ("f1b", 8),
]:
    VEC[_n] = _vc
    _vc += _c
VEC_COLS = _vc


def build_program():
    nc = bass.Bass("TRN2", target_bir_lowering=False)
    sbp = nc.SBUF_PARTITION_SIZE_BYTES

    def hin(name, shape, dt=F32):
        return nc.dram_tensor(name, list(shape), dt, kind="ExternalInput").ap()

    H = {}
    H["xT"] = hin("xT", [D, T])
    for n, sh in [("w_inproj", [D, E_INNER]), ("w_outproj", [E_INNER, D]),
                  ("w0r", [D, D]), ("w0k", [D, D]), ("w0v", [D, D]),
                  ("w_in0", [D, E_INNER]), ("w_out0", [D, D]),
                  ("wqa", [D, D]), ("wka", [D, D]), ("wva", [D, D]),
                  ("woa", [D, D]),
                  ("wqb", [D, D]), ("wkb", [D, D]), ("wvb", [D, D]),
                  ("wob", [D, D]),
                  ("wf1", [D, D]), ("wf2", [D, D]), ("wf3", [D, D])]:
        H[n] = hin(n, sh)
    H["vecs"] = hin("vecs", [128, VEC_COLS])
    H["ident"] = hin("ident", [128, 128])
    H["biasmx"] = hin("biasmx", [T, T], FP8)
    outT = nc.dram_tensor("outT", [D, T], F32, kind="ExternalOutput").ap()

    from contextlib import ExitStack
    with ExitStack() as ctx:
        E = {}
        def sbuf(name, shape, dt=F32):
            E[name] = ctx.enter_context(nc.sbuf_tensor(name, list(shape), dt))
            return E[name]

        R = sbuf("R", [128, KC, T])
        mneg = sbuf("mneg", [128, T])
        a_b = sbuf("a_b", [128, T], BF16)
        vecs = sbuf("vecs_s", [128, VEC_COLS])
        ident = sbuf("ident_s", [128, 128])
        identb = sbuf("identb_s", [128, 128], BF16)
        consts = sbuf("consts", [128, 8])
        constsb = sbuf("constsb", [128, 8], BF16)
        crowb = sbuf("crowb", [1, 2, 128], BF16)
        rows = sbuf("rows", [1, 8, 512])
        smalls = sbuf("smalls", [128, 160])
        stage = sbuf("stage", [128, 3, 1024])

        mneg2 = sbuf("mneg2", [128, 512])
        ksb = sbuf("ksb", [128, 512])
        ab2 = sbuf("ab2", [128, 512])
        consts8 = sbuf("consts8", [128, 8], FP8)
        consts85 = sbuf("consts85", [128, 8], FP8E5)
        ar_n = (nc.sbuf_top - nc.sbuf_base - 2048) // 64 * 64
        arena_b = ctx.enter_context(
            nc.sbuf_tensor("arena_b", [128, ar_n], mybir.dt.uint8))
        AB = Arena(arena_b, ar_n, "main")
        A8 = AB

        psA = ctx.enter_context(nc.psum_tensor("psA", [128, 2, 512], F32))
        psB = ctx.enter_context(nc.psum_tensor("psB", [128, 2, 512], F32))
        psC = ctx.enter_context(nc.psum_tensor("psC", [128, 512], F32))
        psD = ctx.enter_context(nc.psum_tensor("psD", [128, 512], F32))
        psP = ctx.enter_context(nc.psum_tensor("psP", [128, 8, 128], F32))

        sems = {s: ctx.enter_context(nc.semaphore(s))
                for s in ("sT", "sV", "sS", "sG") + DMA_SEMS}

        P = Prog()
        _build_body(P, nc, H, outT, locals())

        with nc.Block() as block:
            @block.tensor
            def _(e):
                for op in P.ops["T"]:
                    op(e, sems)

            @block.vector
            def _(e):
                for op in P.ops["V"]:
                    op(e, sems)

            @block.scalar
            def _(e):
                for op in P.ops["S"]:
                    op(e, sems)

            @block.gpsimd
            def _(e):
                for op in P.ops["G"]:
                    op(e, sems)

            @block.sync
            def _(e):
                for op in P.ops["Y"]:
                    op(e, sems)
    return nc


def _build_body(P, nc, H, outT, E):
    R = E["R"]; mneg = E["mneg"]; a_b = E["a_b"]; vecs = E["vecs"]
    crowb = E["crowb"]

    def rowv(i):
        return rows[0:1, i, 0:256].bitcast(BF16)
    mneg2 = E["mneg2"]; ab2 = E["ab2"]; consts8 = E["consts8"]
    ksb = E["ksb"]; consts85 = E["consts85"]
    ident = E["ident"]; identb = E["identb"]
    consts = E["consts"]; constsb = E["constsb"]
    rows = E["rows"]; smalls = E["smalls"]; stage = E["stage"]
    AB = E["AB"]; A8 = E["A8"]
    psA = E["psA"]; psB = E["psB"]; psC = E["psC"]; psD = E["psD"]
    psP = E["psP"]

    st = {}

    P.emit("Y", lambda e: e.dma_start(
        out=R[:], in_=H["xT"].rearrange("(c p) t -> p c t", p=128)), inc="dYC")
    P.emit("Y", lambda e: e.dma_start(out=vecs[:], in_=H["vecs"][:]), inc="dYC")
    P.emit("Y", lambda e: e.dma_start(out=ident[:], in_=H["ident"][:]),
           inc="dYC")
    t_init = ("dYC", 48)
    t_vec = t_init
    t_id = t_init
    st_init = t_init
    P.emit("V", lambda e: e.memset(crowb[0:1, 0, :], -1.0))
    P.emit("V", lambda e: e.memset(crowb[0:1, 1, :], 1.0))
    P.emit("V", lambda e: e.memset(consts[:, 1:2], -1.0))
    P.emit("V", lambda e: e.memset(consts[:, 2:3], 1.0))
    P.emit("V", lambda e: e.memset(consts[:, 4:5], EPS))
    P.emit("V", lambda e: e.memset(consts[:, 5:6], 1.0 / 1024.0))
    P.emit("V", lambda e: e.memset(consts[:, 6:7], 64.0))
    P.emit("V", lambda e: e.memset(consts[:, 7:8], 0.125))
    P.emit("V", lambda e: e.memset(consts8[:, 0:1], 1.0))
    P.emit("V", lambda e: e.memset(consts85[:, 0:1], 1.0))
    t_cb = P.emit("V", lambda e: e.memset(constsb[:, 0:1], 1.0 / 1024.0),
                  inc="sV", drain=True)
    t_idb = P.emit("S", lambda e: e.copy(out=identb[:], in_=ident[:]),
                   waits=[t_id], inc="sS", drain=True)
    st["tvec"] = t_init

    def load_weight(hbm, kk, mm, into=None, wait0=None):
        if into is None:
            wt = AB.alloc([128, kk, mm])
        else:
            wt = into.rearrange("p (k m) -> p k m", k=kk, m=mm) \
                if kk * mm == into.free_size() else into[:, 0:kk * mm].rearrange(
                    "p (k m) -> p k m", k=kk, m=mm)
        toks = []
        src = hbm.rearrange("(k p) m -> p k m", p=128)
        pieces = [(k, o) for k in range(kk) for o in range(0, mm, 1024)]
        st.setdefault("stgi", 0)
        for k, o in pieces:
            w = min(1024, mm - o)
            i = st["stgi"] % 3
            st["stgi"] += 1
            sl = stage[:, i, 0:w]
            prev = st.get(("stg", i))
            td = P.emit("Y", lambda e, k=k, o=o, w=w, sl=sl, src=src: e.dma_start(
                out=sl, in_=src[:, k, o:o + w]), waits=[prev] if prev else [],
                inc=f"dY{i}")
            cw = [td] + ([wait0] if wait0 else [])
            cw += list(P.bar_toks.values())
            cw.append(st.get("sl_last_mm"))
            tc = P.emit("S", lambda e, k=k, o=o, w=w, sl=sl, wt=wt: e.copy(
                out=wt[:, k, o:o + w], in_=sl), waits=cw, inc="sS", drain=True)
            st[("stg", i)] = tc
            toks.append(tc)
        ktoks = [None] * kk
        for (k, o), tc in zip(pieces, toks):
            ktoks[k] = tc
        return wt, ktoks

    pp = [psA, psB]
    st["ppg"] = 0
    st["ppfree"] = {0: None, 1: None}

    def gemm(wt, kk, m_list, rhs_of, c0, cols, evac, waits0=(), wk=None,
             rk=None):
        ntiles = ceil_div(cols, 512)
        assert ntiles <= 2
        first_m = True
        for m in m_list:
            g = st["ppg"]; pg = pp[g % 2]
            w0 = list(waits0)
            if st["ppfree"][g % 2] is not None:
                w0.append(st["ppfree"][g % 2])
            tok_mm = None
            for k in range(kk):
                for nt in range(ntiles):
                    nn = min(512, cols - nt * 512)
                    last = (k == kk - 1 and nt == ntiles - 1)
                    w = w0 if (k == 0 and nt == 0) else []
                    if wk is not None and nt == 0:
                        w = list(w) + [wk[k]]
                    if rk is not None and first_m and nt == 0:
                        w = list(w) + [rk[k]]
                    tok_mm = P.emit("T", lambda e, k=k, nt=nt, nn=nn, m=m,
                                    pg=pg, wt=wt, rhs_of=rhs_of, c0=c0: e.matmul(
                        pg[:, nt, 0:nn], wt[:, k, m * 128:(m + 1) * 128],
                        rhs_of(k)[:, c0 + nt * 512:c0 + nt * 512 + nn],
                        start=(k == 0), stop=(k == kk - 1)),
                        waits=w, inc="sT" if last else None)
            st["last_mm"] = tok_mm
            tfree = evac(m, pg, [tok_mm])
            st["ppfree"][g % 2] = tfree
            st["ppg"] += 1
            first_m = False

    def col_sums(src_of, kk, base, finish, toks_in, c0=0, cols=T, ones=None):
        ones = constsb[:, 0:1] if ones is None else ones
        out_toks = []
        for nt in range(cols // 512):
            ps, key = (psC, "psC") if nt % 2 == 0 else (psD, "psD")
            tok = None
            for k in range(kk):
                w = [t_cb, toks_in[k] if toks_in else None]
                if k == 0:
                    w.append(st.get(("aux", key)))
                    w.append(st.get(("row", base + nt)))
                tok = P.emit("T", lambda e, k=k, nt=nt, ps=ps, ones=ones: e.matmul(
                    ps[0:1, 0:512], ones,
                    src_of(k)[:, c0 + nt * 512:c0 + (nt + 1) * 512],
                    start=(k == 0), stop=(k == kk - 1)),
                    waits=w, inc="sT" if k == kk - 1 else None)
            te = P.emit("S", lambda e, nt=nt, ps=ps: finish(
                e, base + nt, ps[0:1, 0:512]),
                waits=[tok], inc="sS", drain=True)
            st[("aux", key)] = te
            out_toks.append(te)
        return out_toks

    def bcast_row(base, dst_of, sign, waits, cols=T):
        lhsT = crowb[0:1, 0, :] if sign < 0 else crowb[0:1, 1, :]
        toks = []
        for nt in range(cols // 512):
            ps, key = (psC, "psC") if nt % 2 == 0 else (psD, "psD")
            w = list(waits) + [st.get(("aux", key))]
            tok = P.emit("T", lambda e, nt=nt, ps=ps, lhsT=lhsT: e.matmul(
                ps[:, 0:512], lhsT, rowv(base + nt),
                start=True, stop=True), waits=w, inc="sT")
            st[("row", base + nt)] = tok
            te = P.emit("S", lambda e, nt=nt, ps=ps: e.copy(
                out=dst_of(nt), in_=ps[:, 0:512]),
                waits=[tok, st.get("mneg_dma")], inc="sS", drain=True)
            st[("aux", key)] = te
            toks.append(te)
        return toks

    def _ln_core(src_of, kk):
        accm = [psA[0:1, 0, :], psA[0:1, 1, :], psB[0:1, 0, :], psB[0:1, 1, :]]
        accv = [psC[0:1, 0:512], psD[0:1, 0:512],
                psP[0:1, 0:4, :].rearrange("p a b -> p (a b)"),
                psP[0:1, 4:8, :].rearrange("p a b -> p (a b)")]
        akm = ["psA0", "psA1", "psB0", "psB1"]
        akv = ["psC", "psD", "psP0", "psP1"]
        AB.mark()
        xb = AB.alloc([128, 2, T])
        xq = AB.alloc([128, 2, T])
        mtoks = []
        vtoks = []
        for k in range(kk):
            tc = P.emit("V", lambda e, k=k: e.tensor_copy(
                out=xb[:, k % 2, :], in_=src_of(k)),
                waits=[("dYC", 48), st.get(("lnm", k % 2))],
                inc="sV", drain=True)
            tq = P.emit("V", lambda e, k=k: e.tensor_mul(
                xq[:, k % 2, :], xb[:, k % 2, :], xb[:, k % 2, :]),
                waits=[st.get(("lnv", k % 2))], inc="sV", drain=True)
            tok = tok2 = None
            for nt in range(T // 512):
                w = [tc]
                if k == 0:
                    w += [t_cb, st.get(("aux", akm[nt]))]
                tok = P.emit("T", lambda e, k=k, nt=nt: e.matmul(
                    accm[nt], constsb[:, 0:1],
                    xb[:, k % 2, nt * 512:(nt + 1) * 512],
                    start=(k == 0), stop=(k == kk - 1)),
                    waits=w, inc="sT" if nt == T // 512 - 1 else None)
            st[("lnm", k % 2)] = tok
            mtoks.append(tok)
            for nt in range(T // 512):
                w = [tq]
                if k == 0:
                    w += [st.get(("aux", akv[nt])), st.get(("row", nt)),
                          st.get(("row", 4 + nt))]
                tok2 = P.emit("T", lambda e, k=k, nt=nt: e.matmul(
                    accv[nt], constsb[:, 0:1],
                    xq[:, k % 2, nt * 512:(nt + 1) * 512],
                    start=(k == 0), stop=(k == kk - 1)),
                    waits=w, inc="sT" if nt == T // 512 - 1 else None)
            st[("lnv", k % 2)] = tok2
            vtoks.append(tok2)
        tm = []
        tv = []
        for nt in range(T // 512):
            te = P.emit("S", lambda e, nt=nt: e.activation(
                out=rowv(nt), in_=accm[nt], func=ACT.Copy),
                waits=[mtoks[-1], st.get(("row", nt))], inc="sS", drain=True)
            st[("aux", akm[nt])] = te
            tm.append(te)
            t2 = P.emit("S", lambda e, nt=nt: e.activation(
                out=rows[0:1, 4 + nt, :], in_=accv[nt], func=ACT.Copy),
                waits=[vtoks[-1]], inc="sS", drain=True)
            st[("aux", akv[nt])] = t2
            tv.append(t2)
        t_mneg = bcast_row(0, lambda nt: mneg[:, nt * 512:(nt + 1) * 512],
                           -1.0, tm)
        NT = T // 512
        mrow = mneg[0:1, :].rearrange("p (a b) -> p a b", a=NT)
        tq1 = P.emit("V", lambda e: e.tensor_mul(
            rows[0:1, 0:NT, :], mrow, mrow),
            waits=[t_mneg[-1], tv[-1]] +
                  [st[("row", nt)] for nt in range(NT)],
            inc="sV", drain=True)
        tq2 = P.emit("V", lambda e: e.tensor_sub(
            rows[0:1, 4:4 + NT, :], rows[0:1, 4:4 + NT, :],
            rows[0:1, 0:NT, :]), waits=[tq1], inc="sV", drain=True)
        tq3 = P.emit("S", lambda e: e.activation(
            out=rows[0:1, 0:NT, :], in_=rows[0:1, 4:4 + NT, :],
            func=ACT.Sqrt, bias=consts[0:1, 4:5]),
            waits=[tq2], inc="sS", drain=True)
        rtok = P.emit("S", lambda e: e.copy(
            out=rows[0:1, 0:NT, 0:256].bitcast(BF16),
            in_=rows[0:1, 0:NT, :]),
            waits=[tq3], inc="sS", drain=True)
        rsc = [mneg2[:, :], ab2[:, :]]
        t_ab = []
        for nt in range(NT):
            ps, key = (psC, "psC") if nt % 2 == 0 else (psD, "psD")
            tok = P.emit("T", lambda e, nt=nt, ps=ps: e.matmul(
                ps[:, 0:512], crowb[0:1, 1, :], rowv(nt),
                start=True, stop=True),
                waits=[rtok, st.get(("aux", key)), st.get(("row", nt))],
                inc="sT")
            st[("row", nt)] = tok
            tv_ = P.emit("V", lambda e, nt=nt, ps=ps, r=rsc[nt % 2]:
                         e.reciprocal(
                out=r, in_=ps[:, 0:512]),
                waits=[tok], inc="sV", drain=True)
            te = P.emit("S", lambda e, nt=nt, r=rsc[nt % 2]: e.copy(
                out=a_b[:, nt * 512:(nt + 1) * 512], in_=r),
                waits=[tv_], inc="sS", drain=True)
            st[("aux", key)] = te
            t_ab.append(te)
        AB.release()
        return t_mneg, t_ab, tv[-1]

    def ln_stats(src_of, kk=KC):
        t_mneg, t_ab, t_scr = _ln_core(src_of, kk)
        return t_ab[-1], t_scr, t_mneg[-1]

    def ln_mat(dst_of, src_of, k_list, c0, cols, tok, tok_mul=None):
        mt = []
        w0 = tok if isinstance(tok, list) else [tok]
        wm = [tok_mul] if tok_mul is not None else []
        for k in k_list:
            ta = P.emit("V", lambda e, k=k: e.tensor_add(
                dst_of(k), src_of(k)[:, c0:c0 + cols],
                mneg[:, c0:c0 + cols]), waits=w0, inc="sV", drain=True)
            mt.append(P.emit("V", lambda e, k=k: e.tensor_mul(
                dst_of(k), dst_of(k), a_b[:, c0:c0 + cols]),
                waits=[ta] + wm, inc="sV", drain=True))
        return mt

    def sub_mamba():
        if STAGE < 1:
            return
        AB.mark()
        QT = 512
        halo = AB.alloc([128, 16, 4])
        slotA = AB.alloc([128, 8 * E_INNER])
        slotB = AB.alloc([128, 16 * D])
        t_ab, t_scr, t_mneg = ln_stats(lambda k: R[:, k, :])
        wt_in, tk_in = load_weight(H["w_inproj"], KC, E_INNER, into=slotA)
        wt_out, tk_out = load_weight(H["w_outproj"], 16, D, into=slotB)
        if STAGE < 2:
            AB.release()
            return
        for q in range(T // QT):
            c0 = q * QT
            AB.mark()
            ln = AB.alloc([128, KC, QT])
            s1 = AB.alloc([128, 16, QT])
            x1 = AB.alloc([128, 2, QT])
            acc = AB.alloc([128, 2, QT])
            lt = []
            for k in range(KC):
                lt.append(P.emit("V", lambda e, k=k, c0=c0, ln=ln: e.tensor_add(
                    ln[:, k, :], R[:, k, c0:c0 + QT], mneg[:, c0:c0 + QT]),
                    waits=[t_mneg], inc="sV"))

            def evac_conv(m, pg, waits, q=q, s1=s1, x1=x1, acc=acc, c0=c0):
                sl = m % 2
                w = list(waits) + [st.get(("x1f", sl)), st.get(("accf", sl)),
                                   t_ab]
                tcp = P.emit("V", lambda e, sl=sl, pg=pg, c0=c0: e.tensor_mul(
                    x1[:, sl, :], pg[:, 0, 0:QT], a_b[:, c0:c0 + QT]),
                    waits=w, inc="sV", drain=True)
                tap = VEC["convw"] + m * 4
                P.emit("V", lambda e, sl=sl, tap=tap, x1=x1, acc=acc:
                       e.tensor_scalar(
                           out=acc[:, sl, :], in0=x1[:, sl, :],
                           scalar1=vecs[:, tap + 3:tap + 4], scalar2=None,
                           op0=ALU.mult), waits=[st["tvec"], tcp])
                for j in (2, 1, 0):
                    sh = 3 - j
                    P.emit("V", lambda e, sl=sl, tap=tap, j=j, sh=sh, x1=x1,
                           acc=acc: e.scalar_tensor_tensor(
                               out=acc[:, sl, sh:QT], in0=x1[:, sl, 0:QT - sh],
                               scalar=vecs[:, tap + j:tap + j + 1],
                               in1=acc[:, sl, sh:QT],
                               op0=ALU.mult, op1=ALU.add))
                    if q > 0:
                        P.emit("V", lambda e, m=m, sl=sl, tap=tap, j=j, sh=sh,
                               acc=acc: e.scalar_tensor_tensor(
                                   out=acc[:, sl, 0:sh],
                                   in0=halo[:, m, 4 - sh:4],
                                   scalar=vecs[:, tap + j:tap + j + 1],
                                   in1=acc[:, sl, 0:sh],
                                   op0=ALU.mult, op1=ALU.add))
                tv = P.emit("V", lambda e, m=m, sl=sl, x1=x1: e.tensor_copy(
                    out=halo[:, m, 1:4], in_=x1[:, sl, QT - 3:QT]),
                    inc="sV", drain=True)
                st[("x1f", sl)] = tv
                ts = P.emit("S", lambda e, m=m, sl=sl, s1=s1, acc=acc:
                            e.activation(
                                out=s1[:, m, :], in_=acc[:, sl, :],
                                func=ACT.Silu,
                                bias=vecs[:, VEC["convb"] + m:
                                          VEC["convb"] + m + 1]),
                            waits=[tv], inc="sS", drain=True)
                st[("accf", sl)] = ts
                st[("s1", m)] = ts
                return tcp

            gemm(wt_in, KC, list(range(16)), lambda k, ln=ln: ln[:, k, :],
                 0, QT, evac_conv, wk=tk_in if q == 0 else None, rk=lt)

            def evac_res(m, pg, waits, c0=c0):
                return P.emit("V", lambda e, m=m, pg=pg, c0=c0: e.tensor_add(
                    R[:, m, c0:c0 + QT], R[:, m, c0:c0 + QT], pg[:, 0, 0:QT]),
                    waits=waits, inc="sV", drain=True)

            if STAGE < 3:
                AB.release()
                break
            w_s1 = [st[("s1", m)] for m in range(16)]
            gemm(wt_out, 16, list(range(KC)), lambda k, s1=s1: s1[:, k, :],
                 0, QT, evac_res, waits0=w_s1,
                 wk=tk_out if q == 0 else None)
            AB.release()
        AB.release()


    def sub_ffn0():
        QT = 512
        NQ = T // QT
        AB.mark()
        slot_r = AB.alloc([128, 8 * D])
        slot_k = AB.alloc([128, 8 * D])
        slot_v = AB.alloc([128, 8 * D])
        t_ab, t_scr, t_mneg = ln_stats(lambda k: R[:, k, :])
        wr, wrt = load_weight(H["w0r"], KC, D, into=slot_r)
        wk_, wkt = load_weight(H["w0k"], KC, D, into=slot_k)
        wv, wvt = load_weight(H["w0v"], KC, D, into=slot_v)
        lnp = AB.alloc([128, 2, QT + 1])
        d = AB.alloc([128, KC, QT])
        xr = AB.alloc([128, KC, QT])
        xk = AB.alloc([128, KC, QT])
        r_sb = AB.alloc([128, KC, QT])
        kact = AB.alloc([128, KC, QT])
        dtmp = AB.alloc([128, 2, QT])
        P.emit("V", lambda e: e.memset(smalls[:, 0:8], 0.0))
        st["ffn_xf"] = None
        st["ffn_xkf"] = None

        def prep_xr(q, ks=range(KC)):
            c0 = q * QT
            tk = None
            for k in ks:
                sl = k % 2
                mur = vecs[:, VEC["mu_r"] + k:VEC["mu_r"] + k + 1]
                P.emit("V", lambda e, k=k, sl=sl: e.tensor_copy(
                    out=lnp[:, sl, 0:1], in_=smalls[:, k:k + 1]))
                P.emit("V", lambda e, k=k, sl=sl, c0=c0: e.tensor_add(
                    lnp[:, sl, 1:QT + 1], R[:, k, c0:c0 + QT],
                    mneg[:, c0:c0 + QT]),
                    waits=[t_mneg, st["tvec"], st.get("ffn_xf"),
                           st.get("ffn_dxf")])
                P.emit("V", lambda e, k=k, sl=sl, c0=c0: e.tensor_mul(
                    lnp[:, sl, 1:QT + 1], lnp[:, sl, 1:QT + 1],
                    a_b[:, c0:c0 + QT]),
                    waits=[t_ab])
                P.emit("V", lambda e, k=k, sl=sl: e.tensor_sub(
                    d[:, k, :], lnp[:, sl, 1:QT + 1], lnp[:, sl, 0:QT]))
                tk = P.emit("V", lambda e, k=k, sl=sl, mur=mur:
                            e.scalar_tensor_tensor(
                                out=xr[:, k, :], in0=d[:, k, :],
                                scalar=mur, in1=lnp[:, sl, 0:QT],
                                op0=ALU.mult, op1=ALU.add),
                            inc="sV" if k == ks[-1] else None, drain=False)
                P.emit("V", lambda e, k=k, sl=sl: e.tensor_copy(
                    out=smalls[:, k:k + 1], in_=lnp[:, sl, QT:QT + 1]))
            return tk

        def prep_xk(q):
            c0 = q * QT
            tk = None
            for k in range(KC):
                muk = vecs[:, VEC["mu_k"] + k:VEC["mu_k"] + k + 1]
                mur = vecs[:, VEC["mu_r"] + k:VEC["mu_r"] + k + 1]
                P.emit("V", lambda e, k=k, mur=mur: e.scalar_tensor_tensor(
                    out=xk[:, k, :], in0=d[:, k, :], scalar=mur,
                    in1=xr[:, k, :], op0=ALU.mult, op1=ALU.subtract),
                    waits=[st.get("ffn_xkf")])
                tk = P.emit("V", lambda e, k=k, muk=muk:
                            e.scalar_tensor_tensor(
                                out=xk[:, k, :], in0=d[:, k, :], scalar=muk,
                                in1=xk[:, k, :], op0=ALU.mult,
                                op1=ALU.subtract),
                            inc="sV" if k == KC - 1 else None, drain=False)
            return tk

        t_prep = prep_xr(0)
        t_prepk = prep_xk(0)
        for q in range(NQ):
            c0 = q * QT

            def evac_r(m, pg, waits):
                return P.emit("S", lambda e, m=m, pg=pg: e.activation(
                    out=r_sb[:, m, :], in_=pg[:, 0, :], func=ACT.Sigmoid),
                    waits=waits, inc="sS", drain=True)

            gemm(wr, KC, list(range(KC)), lambda k: xr[:, k, :], 0, QT,
                 evac_r, waits0=[t_prep], wk=wrt if q == 0 else None)
            tmm_r = st["last_mm"]
            st["ffn_xf"] = tmm_r
            if q + 1 < NQ:
                prep_xr(q + 1, range(0, 4))

            def evac_k(m, pg, waits):
                sl = m % 2
                tr_ = P.emit("V", lambda e, m=m, pg=pg, sl=sl: e.tensor_scalar(
                    out=dtmp[:, sl, :], in0=pg[:, 0, :],
                    scalar1=0.0, scalar2=None, op0=ALU.max),
                    waits=list(waits) + [st.get(("dt", sl))],
                    inc="sV", drain=True)
                tk_ = P.emit("S", lambda e, m=m, sl=sl: e.activation(
                    out=kact[:, m, :], in_=dtmp[:, sl, :], func=ACT.Square),
                    waits=[tr_], inc="sS", drain=True)
                st[("dt", sl)] = tk_
                st[("kact", m)] = tk_
                return tr_

            gemm(wk_, KC, list(range(KC)), lambda k: xk[:, k, :], 0, QT,
                 evac_k, waits0=[t_prepk], wk=wkt if q == 0 else None)
            st["ffn_xkf"] = st["last_mm"]
            st["ffn_dxf"] = st["last_mm"]
            if q + 1 < NQ:
                t_prep = prep_xr(q + 1, range(4, KC))

            def evac_v(m, pg, waits, c0=c0):
                sl = m % 2
                tm_ = P.emit("V", lambda e, m=m, pg=pg, sl=sl: e.tensor_mul(
                    dtmp[:, sl, :], pg[:, 0, :], r_sb[:, m, :]),
                    waits=list(waits) + [st.get(("dt", sl))],
                    inc="sV", drain=True)
                st[("dt", sl)] = P.emit("V", lambda e, m=m, sl=sl, c0=c0:
                                        e.tensor_add(
                    R[:, m, c0:c0 + QT], R[:, m, c0:c0 + QT], dtmp[:, sl, :]),
                    inc="sV", drain=True)
                return tm_

            gemm(wv, KC, list(range(KC)), lambda k: kact[:, k, :], 0, QT,
                 evac_v, waits0=[st[("kact", m)] for m in range(KC)],
                 wk=wvt if q == 0 else None)
            if q + 1 < NQ:
                t_prepk = prep_xk(q + 1)
        AB.release()

    def sub_attnb0():
        HT = T // 2
        AB.mark()
        st["wfree2"] = None
        wslot = AB.alloc([128, 8 * E_INNER])
        t_ab, t_scr, t_mneg = ln_stats(lambda k: R[:, k, :])
        ln = AB.alloc([128, KC, HT])
        gate = AB.alloc([128, KC, HT])
        y = AB.alloc([128, KC, HT])
        vtmp = AB.alloc([128, 2, HT])
        hs = AB.alloc([128, 2, HT])
        P.emit("V", lambda e: e.memset(smalls[:, 8:16], 0.0))
        t_dec = P.emit("S", lambda e: e.activation(
            out=smalls[:, 16:24], in_=vecs[:, VEC["A0"]:VEC["A0"] + 8],
            func=ACT.Sigmoid), waits=[st["tvec"]], inc="sS", drain=True)
        st["wfree2"] = None
        for h in range(2):
            c0 = h * HT
            lt = ln_mat(lambda k, ln=ln: ln[:, k, :], lambda k: R[:, k, :],
                        range(KC), c0, HT, t_mneg, t_ab)
            win, wint = load_weight(H["w_in0"], KC, E_INNER, into=wslot,
                                    wait0=st["wfree2"])

            def evac_g(m, pg, waits):
                return P.emit("S", lambda e, m=m, pg=pg: e.activation(
                    out=gate[:, m - 8, :], in_=pg[:, :, :].rearrange(
                        "p a b -> p (a b)"), func=ACT.Sigmoid,
                    bias=vecs[:, VEC["b_in"] + m:VEC["b_in"] + m + 1]),
                    waits=waits, inc="sS", drain=True)

            gemm(win, KC, list(range(8, 16)), lambda k: ln[:, k, :], 0, HT,
                 evac_g, wk=wint, rk=lt)

            def evac_vch(m, pg, waits, h=h):
                sl = m % 2
                tv_ = P.emit("S", lambda e, m=m, pg=pg, sl=sl: e.activation(
                    out=vtmp[:, sl, :], in_=pg[:, :, :].rearrange(
                        "p a b -> p (a b)"), func=ACT.Identity,
                    bias=vecs[:, VEC["b_in"] + m:VEC["b_in"] + m + 1]),
                    waits=list(waits) + [st.get(("vt", sl))],
                    inc="sS", drain=True)
                tsc = P.emit("V", lambda e, m=m, sl=sl: e.tensor_tensor_scan(
                    out=hs[:, sl, :],
                    data0=smalls[:, 16 + m:17 + m].broadcast_to([128, HT]),
                    data1=vtmp[:, sl, :],
                    initial=smalls[:, 8 + m:9 + m],
                    op0=ALU.mult, op1=ALU.add),
                    waits=[tv_, t_dec, st.get(("hs", sl))],
                    inc="sV", drain=True)
                st[("vt", sl)] = tsc
                P.emit("V", lambda e, m=m, sl=sl: e.tensor_copy(
                    out=smalls[:, 8 + m:9 + m], in_=hs[:, sl, HT - 1:HT]))
                st[("hs", sl)] = P.emit("V", lambda e, m=m, sl=sl:
                                        e.scalar_tensor_tensor(
                    out=y[:, m, :], in0=hs[:, sl, :],
                    scalar=vecs[:, VEC["C0"] + m:VEC["C0"] + m + 1],
                    in1=gate[:, m, :], op0=ALU.mult, op1=ALU.mult),
                    inc="sV", drain=True)
                return tv_

            gemm(win, KC, list(range(8)), lambda k: ln[:, k, :], 0, HT,
                 evac_vch, waits0=[], wk=None)
            tmm_in = st["last_mm"]

            wout, woutt = load_weight(H["w_out0"], KC, D, into=wslot,
                                      wait0=tmm_in)

            def evac_o(m, pg, waits, c0=c0):
                return P.emit("V", lambda e, m=m, pg=pg, c0=c0:
                              e.scalar_tensor_tensor(
                    out=R[:, m, c0:c0 + HT],
                    in0=pg[:, :, :].rearrange("p a b -> p (a b)"),
                    scalar=vecs[:, VEC["b_out"] + m:VEC["b_out"] + m + 1],
                    in1=R[:, m, c0:c0 + HT], op0=ALU.add, op1=ALU.add),
                    waits=waits, inc="sV", drain=True)

            w_y = [st[("hs", 0)], st[("hs", 1)]]
            gemm(wout, KC, list(range(KC)), lambda k: y[:, k, :], 0, HT,
                 evac_o, waits0=w_y, wk=woutt)
            st["wfree2"] = st["last_mm"]
        AB.release()


    def ln_spike(pre, prets, gbase, ombase, out_of, scratch):
        mt = vt = None
        sqt = []
        for k in range(KC):
            tq = P.emit("S", lambda e, k=k: e.activation(
                out=scratch[:, k, :], in_=pre[:, k, :], func=ACT.Square),
                waits=[prets[k]], inc="sS", drain=True)
            sqt.append(tq)
            w = [prets[k]]
            if k == 0:
                w += [t_cb, st.get(("aux", "psC")), st.get(("row", 0))]
            mt = P.emit("T", lambda e, k=k: e.matmul(
                psC[0:1, 0:512], constsb[:, 0:1], pre[:, k, :],
                start=(k == 0), stop=(k == KC - 1)),
                waits=w, inc="sT" if k == KC - 1 else None)
            w2 = [tq]
            if k == 0:
                w2 += [st.get(("aux", "psD")), st.get(("row", 4)),
                       st.get(("row", 1))]
            vt = P.emit("T", lambda e, k=k: e.matmul(
                psD[0:1, 0:512], constsb[:, 0:1], scratch[:, k, :],
                start=(k == 0), stop=(k == KC - 1)),
                waits=w2, inc="sT" if k == KC - 1 else None)
        tme = P.emit("S", lambda e: e.activation(
            out=rowv(0), in_=psC[0:1, 0:512], func=ACT.Copy),
            waits=[mt], inc="sS", drain=True)
        st[("aux", "psC")] = tme
        tve = P.emit("S", lambda e: e.activation(
            out=rows[0:1, 4, :], in_=psD[0:1, 0:512], func=ACT.Copy),
            waits=[vt], inc="sS", drain=True)
        st[("aux", "psD")] = tve
        tmn = bcast_row(0, lambda nt: mneg2[:, :], -1.0, [tme], cols=512)
        tq1 = P.emit("V", lambda e: e.tensor_mul(
            rows[0:1, 5, :], mneg2[0:1, :], mneg2[0:1, :]),
            waits=[tmn[-1], st.get(("row", 5))], inc="sV", drain=True)
        tq2 = P.emit("V", lambda e: e.tensor_sub(
            rows[0:1, 4, :], rows[0:1, 4, :], rows[0:1, 5, :]),
            waits=[tq1, tve], inc="sV", drain=True)
        tq3 = P.emit("S", lambda e: e.activation(
            out=rows[0:1, 5, :], in_=rows[0:1, 4, :],
            func=ACT.Sqrt, bias=consts[0:1, 4:5]),
            waits=[tq2], inc="sS", drain=True)
        st[("row", 5)] = tq3
        tq4 = P.emit("V", lambda e: e.reciprocal(
            out=rows[0:1, 4, :], in_=rows[0:1, 5, :]),
            waits=[tq3], inc="sV", drain=True)
        st[("row", 4)] = tq4
        tq5 = P.emit("S", lambda e: e.copy(
            out=rowv(1), in_=rows[0:1, 4, :]),
            waits=[tq4, st.get(("row", 1))], inc="sS", drain=True)
        tab = bcast_row(1, lambda nt: ab2[:, :], 1.0, [tq5], cols=512)
        out_toks = []
        for k in range(KC):
            P.emit("V", lambda e, k=k: e.tensor_add(
                pre[:, k, :], pre[:, k, :], mneg2[:]),
                waits=[tmn[-1], sqt[k]])
            P.emit("V", lambda e, k=k: e.tensor_mul(
                pre[:, k, :], pre[:, k, :], ab2[:]), waits=[tab[-1]])
            out_toks.append(P.emit("V", lambda e, k=k: e.tensor_scalar(
                out=out_of(k), in0=pre[:, k, :],
                scalar1=vecs[:, gbase + k:gbase + k + 1],
                scalar2=smalls[:, ombase + k:ombase + k + 1],
                op0=ALU.mult, op1=ALU.is_ge),
                inc="sV", drain=True))
        return out_toks

    def sub_l1a():
        HT = T // 2
        AB.mark()
        bd = AB.alloc([128, 8, 128])
        P.emit("V", lambda e: e.memset(bd[:, :, :], 0.0))
        ln = AB.alloc([128, KC, T])
        AB.mark()
        slot_ka = AB.alloc([128, 8 * D])
        slot_va = AB.alloc([128, 8 * D])
        t_ab, t_scr, t_mneg = ln_stats(lambda k: R[:, k, :])
        wk, wkt = load_weight(H["wka"], KC, D, into=slot_ka)
        wv, wvt = load_weight(H["wva"], KC, D, into=slot_va)
        kblk = AB.alloc([128, 2, D])
        vblk = AB.alloc([128, 2, D])
        tpair = None
        lt = None
        for tb in range(16):
            if tb % 4 == 0:
                qc0 = (tb // 4) * 512
                lt = ln_mat(lambda k, qc0=qc0: ln[:, k, qc0:qc0 + 512],
                            lambda k: R[:, k, :],
                            range(KC), qc0, 512, t_mneg, t_ab)
            sl = tb % 2
            for which, w_, wt_, blk in (("k", wk, wkt, kblk), ("v", wv, wvt, vblk)):
                ps = psA if which == "k" else psB
                tok = None
                for kc in range(KC):
                    for nt in range(2):
                        w0 = []
                        if kc == 0 and nt == 0:
                            w0 = [lt[-1], st.get(("blkf", which, sl))]
                        if nt == 0:
                            w0 = list(w0) + [wt_[kc]]
                        tok = P.emit("T", lambda e, kc=kc, nt=nt, tb=tb, ps=ps,
                                     w_=w_: e.matmul(
                            ps[:, nt, :], ln[:, kc, tb * 128:(tb + 1) * 128],
                            w_[:, kc, nt * 512:(nt + 1) * 512],
                            start=(kc == 0), stop=(kc == KC - 1)),
                            waits=w0,
                            inc="sT" if (kc == KC - 1 and nt == 1) else None)
                te = P.emit("S", lambda e, ps=ps, blk=blk, sl=sl: e.copy(
                    out=blk[:, sl, :], in_=ps[:, :, :].rearrange(
                        "p a b -> p (a b)")), waits=[tok], inc="sS", drain=True)
                st[("blkev", which, sl)] = te
            for pc in range(8):
                w0 = []
                if pc == 0:
                    w0 = [st[("blkev", "k", sl)], st[("blkev", "v", sl)]]
                tpair = P.emit("T", lambda e, pc=pc, sl=sl, tb=tb: e.matmul(
                    psP[:, pc, :], kblk[:, sl, pc * 128:(pc + 1) * 128],
                    vblk[:, sl, pc * 128:(pc + 1) * 128],
                    start=(tb == 0 and pc % 4 == 0),
                    stop=(tb == 15 and pc % 4 == 3)),
                    waits=w0, inc="sT" if pc == 7 else None)
            st[("blkf", "k", sl)] = tpair
            st[("blkf", "v", sl)] = tpair
        st["l1a_kv_done"] = tpair
        tbd = []
        for pc in range(8):
            P.emit("S", lambda e, pc=pc: e.mul(
                out=bd[0:64, pc, 0:64], in_=psP[0:64, pc, 0:64],
                mul=consts[0:64, 7:8]), waits=[tpair])
            tbd.append(P.emit("S", lambda e, pc=pc: e.mul(
                out=bd[64:128, pc, 64:128], in_=psP[64:128, pc, 64:128],
                mul=consts[64:128, 7:8]), inc="sS", drain=True))
        AB.release()
        for h in range(2):
            c0 = h * HT
            AB.mark()
            q = AB.alloc([128, KC, HT])
            wslot_a = AB.alloc([128, 8 * D])
            wq, wqt = load_weight(H["wqa"], KC, D, into=wslot_a,
                                  wait0=st.get("l1a_wq_free")
                                  or st["l1a_kv_done"])

            def evac_q(m, pg, waits, q=q):
                tq = P.emit("S", lambda e, m=m, pg=pg, q=q: e.copy(
                    out=q[:, m, :], in_=pg[:, :, :].rearrange(
                        "p a b -> p (a b)")), waits=waits, inc="sS", drain=True)
                st[("qv", m)] = tq
                return tq

            gemm(wq, KC, list(range(KC)), lambda k: ln[:, k, :], c0, HT,
                 evac_q, waits0=[lt[-1]], wk=wqt)
            st["l1a_wq_free"] = st["last_mm"]
            attn = AB.alloc([128, KC, HT])
            for ec in range(8):
                g = st["ppg"]; pg = pp[g % 2]
                w0 = [tbd[-1], st[("qv", ec)]]
                if st["ppfree"][g % 2] is not None:
                    w0.append(st["ppfree"][g % 2])
                tok = None
                for nt in range(2):
                    tok = P.emit("T", lambda e, ec=ec, nt=nt, pg=pg, q=q:
                                 e.matmul(
                        pg[:, nt, :], bd[:, ec, :],
                        q[:, ec, nt * 512:(nt + 1) * 512],
                        start=True, stop=True),
                        waits=w0 if nt == 0 else [],
                        inc="sT" if nt == 1 else None)
                ta = P.emit("S", lambda e, ec=ec, pg=pg, attn=attn: e.copy(
                    out=attn[:, ec, :], in_=pg[:, :, :].rearrange(
                        "p a b -> p (a b)")), waits=[tok], inc="sS", drain=True)
                st[("at", ec)] = ta
                st["ppfree"][g % 2] = ta
                st["ppg"] += 1
            wo, wot = load_weight(H["woa"], KC, D, into=wslot_a,
                                   wait0=st["last_mm"])

            def evac_oa(m, pg, waits, c0=c0):
                return P.emit("V", lambda e, m=m, pg=pg, c0=c0: e.tensor_add(
                    R[:, m, c0:c0 + HT], R[:, m, c0:c0 + HT],
                    pg[:, :, :].rearrange("p a b -> p (a b)")),
                    waits=waits, inc="sV", drain=True)

            gemm(wo, KC, list(range(KC)), lambda k, attn=attn: attn[:, k, :],
                 0, HT, evac_oa,
                 waits0=[st[("at", m)] for m in range(8)], wk=wot)
            st["l1a_wq_free"] = st["last_mm"]
            AB.release()
        AB.release()

    def sub_l1ffn():
        HT = T // 2
        AB.mark()
        st["l1f_wfree"] = None
        wslot = AB.alloc([128, 8 * D])
        t_ab, t_scr, t_mneg = ln_stats(lambda k: R[:, k, :])
        ln = AB.alloc([128, KC, HT])
        h1 = AB.alloc([128, KC, HT])
        h2 = AB.alloc([128, KC, HT])
        dtmp = AB.alloc([128, 2, HT])
        def emit_lt(c0, wtok):
            lts = []
            for k in range(KC):
                lts.append(P.emit("V", lambda e, k=k, c0=c0: e.tensor_add(
                    ln[:, k, :], R[:, k, c0:c0 + HT], mneg[:, c0:c0 + HT]),
                    waits=[t_mneg, wtok], inc="sV"))
            return lts

        lt = emit_lt(0, st.get("l1f_lnf"))
        for h in range(2):
            c0 = h * HT

            def evac_h(dst, scale_ab=False):
                def f(m, pg, waits, dst=dst, scale_ab=scale_ab):
                    if scale_ab:
                        t_ = P.emit("V", lambda e, m=m, pg=pg, dst=dst, c0=c0:
                                    e.tensor_mul(
                            dst[:, m, :], pg[:, :, :].rearrange(
                                "p a b -> p (a b)"), a_b[:, c0:c0 + HT]),
                            waits=list(waits) + [t_ab], inc="sV", drain=True)
                    else:
                        t_ = P.emit("S", lambda e, m=m, pg=pg, dst=dst: e.copy(
                            out=dst[:, m, :], in_=pg[:, :, :].rearrange(
                                "p a b -> p (a b)")), waits=waits,
                            inc="sS", drain=True)
                    st[("hv", m)] = t_
                    return t_
                return f

            w1, w1t = load_weight(H["wf1"], KC, D, into=wslot,
                                  wait0=st["l1f_wfree"])
            gemm(w1, KC, list(range(KC)), lambda k: ln[:, k, :], 0, HT,
                 evac_h(h1, True), wk=w1t, rk=lt)
            if h == 0:
                lt = emit_lt(HT, st["last_mm"])
            w2, w2t = load_weight(H["wf2"], KC, D, into=wslot,
                                  wait0=st["last_mm"])
            gemm(w2, KC, list(range(KC)), lambda k: h1[:, k, :], 0, HT,
                 evac_h(h2), waits0=[st[("hv", m)] for m in range(KC)], wk=w2t)
            w3, w3t = load_weight(H["wf3"], KC, D, into=wslot,
                                  wait0=st["last_mm"])

            def evac_t(m, pg, waits, c0=c0):
                sl = m % 2
                tt_ = P.emit("S", lambda e, m=m, pg=pg, sl=sl: e.activation(
                    out=dtmp[:, sl, :], in_=pg[:, :, :].rearrange(
                        "p a b -> p (a b)"), func=ACT.Tanh,
                    bias=vecs[:, VEC["f1b"] + m:VEC["f1b"] + m + 1]),
                    waits=list(waits) + [st.get(("dt2", sl))],
                    inc="sS", drain=True)
                st[("dt2", sl)] = P.emit("V", lambda e, m=m, sl=sl, c0=c0:
                                         e.tensor_add(
                    R[:, m, c0:c0 + HT], R[:, m, c0:c0 + HT], dtmp[:, sl, :]),
                    waits=[tt_], inc="sV", drain=True)
                return tt_

            gemm(w3, KC, list(range(KC)), lambda k: h2[:, k, :], 0, HT,
                 evac_t, waits0=[st[("hv", m)] for m in range(KC)], wk=w3t)
            st["l1f_wfree"] = st["last_mm"]
            st["l1f_lnf"] = st["last_mm"]
        AB.release()

    def sub_l1b():
        QT = 512
        AB.mark()
        t_ab, t_scr, t_mneg = ln_stats(lambda k: R[:, k, :])
        st["l1b_wfree"] = None
        for xi, nm in enumerate(("betaq", "betak", "betav", "betao")):
            P.emit("V", lambda e, xi=xi, nm=nm: e.tensor_scalar(
                out=smalls[:, 24 + 8 * xi:32 + 8 * xi],
                in0=vecs[:, VEC[nm]:VEC[nm] + 8],
                scalar1=consts[:, 1:2], scalar2=consts[:, 2:3],
                op0=ALU.mult, op1=ALU.add), waits=[st["tvec"], t_cb])
        P.emit("V", lambda e: e.tensor_scalar(
            out=smalls[:, 113:114], in0=vecs[:, VEC["scale_b"]:VEC["scale_b"] + 1],
            scalar1=2.0, scalar2=None, op0=ALU.mult))
        t_thr = P.emit("V", lambda e: e.reciprocal(
            out=smalls[:, 112:113], in_=smalls[:, 113:114]),
            inc="sV", drain=True)
        Q8 = AB.alloc([128, KC, T], FP8E5)
        Vtm = AB.alloc([128, 16, D], FP8)
        Ktm = AB.alloc([128, 16, D], FP8)
        AB.mark()
        wslot = AB.alloc([128, 8 * D])
        lnq = AB.alloc([128, KC, QT])
        pre = AB.alloc([128, KC, QT])
        spk = AB.alloc([128, KC, QT])
        for qt in range(4):
            c0 = qt * QT
            lt = ln_mat(lambda k, lnq=lnq: lnq[:, k, :], lambda k: R[:, k, :],
                        range(KC), c0, QT,
                        [t_mneg, st.get("l1b_wfree")], t_ab)
            st["l1b_lnmat_last"] = lt[-1]
            fin_prev = st.pop("l1b_fin", None)
            if fin_prev:
                fin_prev()
            for X, wh, bbase, gbase, ombase in (
                    ("k", "wkb", VEC["bk"], VEC["gk"], 32),
                    ("v", "wvb", VEC["bv"], VEC["gv"], 40),
                    ("q", "wqb", VEC["bq"], VEC["gq"], 24)):
                wx, wxt = load_weight(H[wh], KC, D, into=wslot,
                                      wait0=st["l1b_wfree"])

                def evac_p(m, pg, waits, bbase=bbase):
                    t_ = P.emit("S", lambda e, m=m, pg=pg, bbase=bbase:
                                e.activation(
                        out=pre[:, m, :], in_=pg[:, 0, :], func=ACT.Identity,
                        bias=vecs[:, bbase + m:bbase + m + 1]),
                        waits=list(waits) + [st.get("prejunk")],
                        inc="sS", drain=True)
                    st[("pv", m)] = t_
                    return t_

                gemm(wx, KC, list(range(KC)), lambda k, lnq=lnq: lnq[:, k, :],
                     0, QT, evac_p, wk=wxt, rk=lt)
                st["l1b_wfree"] = st["last_mm"]
                if X == "q":
                    pv = [st[("pv", m)] for m in range(KC)]

                    def fin(c0=c0, pv=pv, gbase=gbase, ombase=ombase):
                        spk_toks = ln_spike(pre, pv, gbase, ombase,
                                            lambda k: spk[:, k, :], spk)
                        st["prejunk"] = spk_toks[-1]
                        for k in range(KC):
                            P.emit("S", lambda e, k=k, c0=c0: e.copy(
                                out=Q8[:, k, c0:c0 + QT], in_=spk[:, k, :]),
                                waits=[spk_toks[k]], inc="sS", drain=True)
                    st["l1b_fin"] = fin
                else:
                    spk_toks = ln_spike(pre, [st[("pv", m)] for m in range(KC)],
                                        gbase, ombase, lambda k: spk[:, k, :],
                                        spk)
                    dst = Ktm if X == "k" else Vtm
                    for ec in range(KC):
                        for lb in range(4):
                            slot = ((ec * 4 + lb) % 2) * 4
                            tb = qt * 4 + lb
                            ttr = P.emit("T", lambda e, ec=ec, lb=lb, slot=slot:
                                         e.transpose(
                                psP[:, slot, 0:64].bitcast(BF16),
                                spk[:, ec, lb * 128:(lb + 1) * 128],
                                identb[:]),
                                waits=[spk_toks[ec], t_idb,
                                       st.get(("tps", slot))],
                                inc="sT")
                            st[("tps", slot)] = P.emit(
                                "S", lambda e, slot=slot, tb=tb, ec=ec, dst=dst:
                                e.copy(out=dst[:, tb, ec * 128:(ec + 1) * 128],
                                       in_=psP[:, slot, 0:64].bitcast(BF16)),
                                waits=[ttr], inc="sS", drain=True)
                if X == "k":
                    tks = col_sums(lambda k: spk[:, k, :], KC, 0,
                                   lambda e, i, p: e.activation(
                                       out=rowv(i), in_=p, func=ACT.Copy),
                                   spk_toks, cols=QT)
                    st["tksb"] = bcast_row(0, lambda nt: ksb[:, :], 1.0, tks,
                                           cols=QT)[-1]
                if X == "v":
                    for k in range(KC):
                        P.emit("V", lambda e, k=k, qt=qt: e.scalar_tensor_tensor(
                            out=pre[:, k, :], in0=spk[:, k, :], scalar=0.0,
                            in1=ksb[:], op0=ALU.bypass, op1=ALU.mult,
                            accum_out=smalls[:, 88 + qt * 8 + k:89 + qt * 8 + k]),
                            waits=[st["tksb"], spk_toks[k]])
                        st["prejunk"] = P.emit(
                            "V", lambda e, k=k, qt=qt: e.scalar_tensor_tensor(
                                out=pre[:, k, :], in0=spk[:, k, :], scalar=0.0,
                                in1=spk[:, k, :], op0=ALU.bypass, op1=ALU.max,
                                accum_out=smalls[:, 56 + qt * 8 + k:
                                                 57 + qt * 8 + k]),
                            inc="sV", drain=True)
        st.pop("l1b_fin")()
        AB.release()
        if L1BST < 2:
            AB.release()
            return
        for k in range(KC):
            P.emit("V", lambda e, k=k: e.reduce_sum(
                out=smalls[:, 120 + k:121 + k],
                in_=smalls[:, 56 + k:88 + k:8], axis=mybir.AxisListType.X))
            P.emit("V", lambda e, k=k: e.reduce_sum(
                out=smalls[:, 128 + k:129 + k],
                in_=smalls[:, 88 + k:120 + k:8], axis=mybir.AxisListType.X))
        t_sums = P.emit("V", lambda e: e.tensor_scalar(
            out=smalls[:, 136:144], in0=smalls[:, 128:136],
            scalar1=consts[:, 1:2], scalar2=None, op0=ALU.mult),
            inc="sV", drain=True)
        t_mnegfree = st["l1b_lnmat_last"]
        tpairs = []
        for pc in range(8):
            w0 = []
            if pc == 0:
                w0 = [st.get(("tps", 0)), st.get(("tps", 4))]
            tpair = None
            for tb in range(16):
                tpair = P.emit("T", lambda e, tb=tb, pc=pc: e.matmul(
                    psP[:, pc, :], Ktm[:, tb, pc * 128:(pc + 1) * 128],
                    Vtm[:, tb, pc * 128:(pc + 1) * 128],
                    start=(tb == 0 and pc % 4 == 0),
                    stop=(tb == 15 and pc % 4 == 3)),
                    waits=w0 if tb == 0 else [],
                    inc="sT" if tb == 15 else None)
            tpairs.append(tpair)
        tpair = tpairs[-1]
        AB.mark()
        pA = AB.alloc([128, 8, 128], FP8E5)
        pB = AB.alloc([128, 8, 128], FP8E5)
        pC = AB.alloc([128, 8, 128], FP8E5)
        pD = AB.alloc([128, 8, 128], FP8E5)
        P.emit("V", lambda e: e.memset(pA[:, :, :], 0.0))
        P.emit("V", lambda e: e.memset(pB[:, :, :], 0.0))
        P.emit("V", lambda e: e.memset(pC[:, :, :], 0.0))
        P.emit("V", lambda e: e.memset(pD[:, :, :], 0.0))
        tpc = []
        I32 = mybir.dt.int32
        for pc in range(8):
            fm2 = mneg2[:, 0:64]
            im2 = mneg2[:, 64:128].bitcast(I32)
            ir5 = mneg2[:, 128:192].bitcast(I32)
            ir6 = mneg2[:, 192:256].bitcast(I32)
            ir8 = mneg2[:, 256:320].bitcast(I32)
            fr5 = mneg2[:, 320:384]
            fr6 = mneg2[:, 384:448]
            fr8 = mneg2[:, 448:512]
            P.emit("V", lambda e, pc=pc: e.tensor_scalar(
                out=mneg2[0:64, 0:64], in0=psP[0:64, pc, 0:64],
                scalar1=2.0, scalar2=None, op0=ALU.mult),
                waits=[tpairs[(pc // 4) * 4 + 3]])
            P.emit("V", lambda e, pc=pc: e.tensor_scalar(
                out=mneg2[64:128, 0:64], in0=psP[64:128, pc, 64:128],
                scalar1=2.0, scalar2=None, op0=ALU.mult),
                waits=[tpairs[(pc // 4) * 4 + 3]])
            P.emit("V", lambda e, fm2=fm2, im2=im2: e.tensor_copy(
                out=im2, in_=fm2))
            P.emit("V", lambda e, im2=im2, ir5=ir5: e.tensor_scalar(
                out=ir5, in0=im2, scalar1=511, scalar2=None,
                op0=ALU.bitwise_and))
            P.emit("V", lambda e, im2=im2, ir6=ir6: e.tensor_scalar(
                out=ir6, in0=im2, scalar1=63, scalar2=None,
                op0=ALU.bitwise_and))
            P.emit("V", lambda e, im2=im2, ir8=ir8: e.tensor_scalar(
                out=ir8, in0=im2, scalar1=7, scalar2=None,
                op0=ALU.bitwise_and))
            P.emit("V", lambda e, ir5=ir5, fr5=fr5: e.tensor_copy(
                out=fr5, in_=ir5))
            P.emit("V", lambda e, ir6=ir6, fr6=fr6: e.tensor_copy(
                out=fr6, in_=ir6))
            P.emit("V", lambda e, ir8=ir8, fr8=fr8: e.tensor_copy(
                out=fr8, in_=ir8))
            tk = None
            for lo, hi in ((0, 64), (64, 128)):
                P.emit("V", lambda e, pc=pc, lo=lo, hi=hi:
                       e.tensor_sub(pA[lo:hi, pc, lo:hi],
                                    mneg2[lo:hi, 0:64],
                                    mneg2[lo:hi, 320:384]))
                P.emit("V", lambda e, pc=pc, lo=lo, hi=hi:
                       e.tensor_sub(pB[lo:hi, pc, lo:hi],
                                    mneg2[lo:hi, 320:384],
                                    mneg2[lo:hi, 384:448]))
                P.emit("V", lambda e, pc=pc, lo=lo, hi=hi:
                       e.tensor_sub(pC[lo:hi, pc, lo:hi],
                                    mneg2[lo:hi, 384:448],
                                    mneg2[lo:hi, 448:512]))
                tk = P.emit("V", lambda e, pc=pc, lo=lo, hi=hi:
                            e.tensor_scalar(
                    pD[lo:hi, pc, lo:hi], mneg2[lo:hi, 448:512], 0.0, None,
                    op0=ALU.add), inc="sV" if hi == 128 else None,
                    drain=(hi == 128))
            tpc.append(tk)
        if L1BST < 3:
            AB.release()
            AB.release()
            return
        bias8 = mneg[:, :].bitcast(FP8).rearrange("p (mb l) -> p mb l",
                                                  mb=16, l=QT)
        wo, wot = load_weight(H["wob"], KC, D, wait0=st["l1b_wfree"])
        spkout = AB.alloc([128, KC, QT])
        opre = AB.alloc([128, KC, QT])
        st["bias8f"] = None
        for lq in range(4):
            c0 = lq * QT
            tb8 = P.emit("G", lambda e, c0=c0: e.dma_start(
                out=bias8[:, :, :], in_=H["biasmx"].rearrange(
                    "(mb p) l -> p mb l", p=128)[:, :, c0:c0 + QT]),
                waits=[st["bias8f"], t_mnegfree,
                       st.get("mneg_dma")], inc="dG")
            st["mneg_dma"] = tb8
            tqs = col_sums(lambda k: Q8[:, k, :], KC, 0,
                           lambda e, i, p: e.activation(
                               out=rowv(i), in_=p, func=ACT.Identity,
                               scale=consts[0:1, 1:2], bias=consts[0:1, 6:7]),
                           None, c0=c0, cols=QT, ones=consts85[:, 0:1])
            ta2 = bcast_row(0, lambda nt: ksb[:, :], 1.0, tqs, cols=QT)
            for ec in range(8):
                g = st["ppg"]; pg = pp[g % 2]
                w0 = [tb8]
                if st["ppfree"][g % 2] is not None:
                    w0.append(st["ppfree"][g % 2])
                tok = None
                for mb in range(16):
                    tok = P.emit("T", lambda e, mb=mb, ec=ec, pg=pg: e.matmul(
                        pg[:, 0, :], Vtm[:, mb, ec * 128:(ec + 1) * 128],
                        bias8[:, mb, :], start=(mb == 0), stop=False),
                        waits=w0 if mb == 0 else [], inc=None)
                for pi, piece in enumerate((pA, pB, pC, pD)):
                    tok = P.emit("T", lambda e, ec=ec, pg=pg, piece=piece,
                                 c0=c0: e.matmul(
                        pg[:, 0, :], piece[:, ec, :], Q8[:, ec, c0:c0 + QT],
                        start=False, stop=(piece is pD)),
                        waits=[tpc[ec], t_sums] if pi == 0 else [],
                        inc="sT" if pi == 3 else None)
                sc3 = ab2 if ec % 2 == 0 else mneg2
                t3 = P.emit("V", lambda e, ec=ec, pg=pg, sc3=sc3:
                            e.scalar_tensor_tensor(
                    out=sc3[:, :], in0=ksb[:, :],
                    scalar=smalls[:, 120 + ec:121 + ec], in1=pg[:, 0, :],
                    op0=ALU.mult, op1=ALU.add),
                    waits=[tok, ta2[-1], st.get(("ab2f", ec % 2)),
                           tpc[-1]], inc="sV", drain=True)
                tso = P.emit("V", lambda e, ec=ec, sc3=sc3: e.tensor_scalar(
                    out=spkout[:, ec, :], in0=sc3[:, :],
                    scalar1=smalls[:, 136 + ec:137 + ec],
                    scalar2=smalls[:, 112:113],
                    op0=ALU.add, op1=ALU.is_ge),
                    waits=[t_thr], inc="sV", drain=True)
                st[("ab2f", ec % 2)] = tso
                st[("so", ec)] = tso
                st["ppfree"][g % 2] = t3
                st["ppg"] += 1
                st["bias8f"] = tok

            def evac_ob(m, pg, waits):
                t_ = P.emit("S", lambda e, m=m, pg=pg: e.activation(
                    out=opre[:, m, :], in_=pg[:, 0, :], func=ACT.Identity,
                    bias=vecs[:, VEC["bo"] + m:VEC["bo"] + m + 1]),
                    waits=waits, inc="sS", drain=True)
                st[("pv", m)] = t_
                return t_

            gemm(wo, KC, list(range(KC)), lambda k: spkout[:, k, :], 0, QT,
                 evac_ob, waits0=[st[("so", m)] for m in range(8)], wk=wot)
            dsp = ln_spike(opre, [st[("pv", m)] for m in range(KC)],
                           VEC["go"], 48, lambda k: spkout[:, k, :], spkout)
            for k in range(KC):
                P.emit("V", lambda e, k=k, c0=c0: e.tensor_add(
                    R[:, k, c0:c0 + QT], R[:, k, c0:c0 + QT], spkout[:, k, :]),
                    waits=[dsp[k]])
        P.emit("V", lambda e: e.nop(), inc="sV", drain=True)
        AB.release()
        AB.release()

    subs = [sub_mamba, sub_ffn0, sub_attnb0, sub_ffn0,
            sub_l1a, sub_l1ffn, sub_l1b, sub_l1ffn]
    for f in subs[:NSUB]:
        f()
        st["sl_last_mm"] = st.get("last_mm")
        P.barrier()

    t_out = P.emit("G", lambda e: e.dma_start(
        out=outT.rearrange("(c p) t -> p c t", p=128), in_=R[:]), inc="dG")
    P.emit("G", lambda e: e.nop(), waits=[t_out])



def _pack_vec(v):
    v = np.asarray(v, np.float32)
    return v.reshape(v.shape[0] // 128, 128).T


def make_vecs(inp):
    cols = np.zeros((128, VEC_COLS), np.float32)

    def put(name, arr):
        a = _pack_vec(arr)
        cols[:, VEC[name]:VEC[name] + a.shape[1]] = a

    cw = np.asarray(inp["L0_attn_a_conv_w"], np.float32).reshape(E_INNER, 4)
    for c in range(16):
        for j in range(4):
            cols[:, VEC["convw"] + c * 4 + j] = cw[c * 128:(c + 1) * 128, j]
    put("convb", inp["L0_attn_a_conv_b"])
    put("mu_r", inp["L0_ffn_mu_r"]); put("mu_k", inp["L0_ffn_mu_k"])
    put("b_in", inp["L0_attn_b_b_in"]); put("A0", inp["L0_attn_b_A"])
    put("C0", inp["L0_attn_b_C"]); put("b_out", inp["L0_attn_b_b_out"])
    for nm, key in [("bq", "L1_attn_b_bq"), ("bk", "L1_attn_b_bk"),
                    ("bv", "L1_attn_b_bv"), ("bo", "L1_attn_b_bo"),
                    ("gq", "L1_attn_b_gq"), ("gk", "L1_attn_b_gk"),
                    ("gv", "L1_attn_b_gv"), ("go", "L1_attn_b_go"),
                    ("betaq", "L1_attn_b_betaq"), ("betak", "L1_attn_b_betak"),
                    ("betav", "L1_attn_b_betav"), ("betao", "L1_attn_b_betao")]:
        put(nm, inp[key])
    cols[:, VEC["scale_b"]] = np.float32(np.asarray(inp["L1_attn_b_scale"]))
    put("f1b", inp["L1_ffn_bias"])
    return cols


def make_biasmx():
    i = np.arange(T, dtype=np.float64)
    dist = np.abs(i[:, None] - i[None, :])
    b = np.maximum(np.ceil(np.log2((T - 1.0) / (dist + 1.0))), 0.0)
    r32 = (np.float32(T - 1.0) / (dist.astype(np.float32) + np.float32(1.0)))
    b32 = np.maximum(np.ceil(np.log2(r32)), 0.0)
    assert np.array_equal(b, b32.astype(np.float64))
    return b.astype(ml_dtypes.float8_e4m3)


_NC_CACHE = {}


def kernel(**inp):
    x = np.asarray(inp["L0_attn_a_hidden_states"], np.float32)
    B = x.shape[0]
    if "p" not in _NC_CACHE:
        _NC_CACHE["p"] = build_program()
    nc = _NC_CACHE["p"]

    def ct(a):
        return np.ascontiguousarray(np.asarray(a, np.float32))

    shared = {
        "w_inproj": ct(np.asarray(inp["L0_attn_a_in_proj_w"])[:E_INNER, :].T),
        "w_outproj": ct(np.asarray(inp["L0_attn_a_out_proj_w"]).T),
        "w0r": ct(np.asarray(inp["L0_ffn_W_r"]).T),
        "w0k": ct(np.asarray(inp["L0_ffn_W_k"]).T),
        "w0v": ct(np.asarray(inp["L0_ffn_W_v"]).T),
        "w_in0": ct(inp["L0_attn_b_W_in"]),
        "w_out0": ct(inp["L0_attn_b_W_out"]),
        "wqa": ct(np.asarray(inp["L1_attn_a_W_q"]).T),
        "wka": ct(np.asarray(inp["L1_attn_a_W_k"]).T),
        "wva": ct(np.asarray(inp["L1_attn_a_W_v"]).T),
        "woa": ct(np.asarray(inp["L1_attn_a_W_o"]).T),
        "wqb": ct(np.asarray(inp["L1_attn_b_Wq"]).T),
        "wkb": ct(np.asarray(inp["L1_attn_b_Wk"]).T),
        "wvb": ct(np.asarray(inp["L1_attn_b_Wv"]).T),
        "wob": ct(np.asarray(inp["L1_attn_b_Wo"]).T),
        "wf1": ct(inp["L1_ffn_weight"]),
        "wf2": ct(inp["L1_ffn_aux_weight1"]),
        "wf3": ct(inp["L1_ffn_aux_weight2"]),
        "vecs": make_vecs(inp),
        "ident": np.eye(128, dtype=np.float32),
        "biasmx": make_biasmx(),
    }
    in_maps = []
    for b in range(B):
        m = dict(shared)
        m["xT"] = ct(x[b].T)
        in_maps.append(m)
    res = run_bass_kernel_spmd(nc, in_maps, core_ids=list(range(B)))
    out = np.stack([np.asarray(r["outT"], np.float32).T for r in res.results])
    return out


# version 31
# speedup vs baseline: 1.0254x; 1.0034x over previous
import os

import numpy as np
import ml_dtypes

import concourse.bass as bass
import concourse.mybir as mybir
from concourse.bass_utils import run_bass_kernel_spmd

F32 = mybir.dt.float32
BF16 = mybir.dt.bfloat16
FP8 = mybir.dt.float8e4
FP8E5 = mybir.dt.float8e5
ALU = mybir.AluOpType
ACT = mybir.ActivationFunctionType

T = 2048
D = 1024
KC = 8
E_INNER = 2048
EPS = 1e-5
NSUB = int(os.environ.get("NSUB", "8"))
STAGE = int(os.environ.get("STAGE", "99"))
L1BST = int(os.environ.get("L1BST", "9"))

ENGINES = ("T", "V", "S", "G", "Y")
DMA_SEMS = ("dY0", "dY1", "dY2", "dY3", "dYC", "dG", "dS")


class Prog:
    def __init__(self):
        self.ops = {e: [] for e in ENGINES}
        self.cnt = {s: 0 for s in ("sT", "sV", "sS", "sG") + DMA_SEMS}
        self.waited = {e: {} for e in ENGINES}

    OWN = {"T": "sT", "V": "sV", "S": "sS", "G": "sG"}

    def emit(self, eng, fn, waits=(), inc=None, drain=None, nodrain=False):
        own = self.OWN.get(eng)
        waits = [w for w in waits if w is not None and w[0] != own]
        eff = []
        for s, v in waits:
            if eng != "G" and self.waited[eng].get(s, 0) >= v:
                continue
            self.waited[eng][s] = max(self.waited[eng].get(s, 0), v)
            eff.append((s, v))
        tok = None
        if inc is not None:
            amt = 16 if inc in DMA_SEMS else 1
            self.cnt[inc] += amt
            tok = (inc, self.cnt[inc])
        is_dma = inc in DMA_SEMS
        do_drain = ((eng in ("V", "S", "G")) or drain is True) and not is_dma \
            and not (nodrain and inc is None)

        def op(e, sems, eff=eff, fn=fn, inc=inc, do_drain=do_drain,
               is_dma=is_dma):
            for s, v in eff:
                e.wait_ge(sems[s], v)
            inst = fn(e)
            if do_drain:
                inst = e.drain()
            if inc is not None:
                inst.then_inc(sems[inc], 16 if is_dma else 1)

        import traceback
        if os.environ.get("PROV"):
            fr = traceback.extract_stack()[-3:-1]
            op._prov = " | ".join(f"{f.name}:{f.lineno}" for f in fr)
        self.ops[eng].append(op)
        return tok

    bar_toks = {}

    def barrier(self):
        pre = {s: self.cnt[s] for s in ("sT", "sV", "sS", "sG")}
        toks = {}
        for e, sem in (("T", "sT"), ("V", "sV"), ("S", "sS"), ("G", "sG")):
            w = [(s, v) for s, v in pre.items() if s != sem and v > 0]
            toks[e] = self.emit(e, lambda en: en.nop(), waits=w, inc=sem,
                                drain=True)
        for e in ENGINES:
            self.emit(e, lambda en: en.nop(),
                      waits=[toks[o] for o in toks if o != e])
        self.bar_toks = dict(toks)


class Arena:

    ESIZE = {}

    def __init__(self, handle, nbytes, name):
        self.h, self.n, self.name = handle, nbytes, name
        self.off = 0
        self.marks = []
        self.hwm = 0

    def alloc(self, shape, dt=BF16):
        assert shape[0] == 128
        es = mybir.dt.size(dt)
        n = int(np.prod(shape[1:])) * es
        self.off = (self.off + 63) // 64 * 64
        assert self.off + n <= self.n, \
            f"arena {self.name} overflow: {self.off}+{n} > {self.n}"
        ap = self.h[:, self.off:self.off + n].bitcast(dt)
        self.off += n
        self.hwm = max(self.hwm, self.off)
        if len(shape) > 2:
            dims = " ".join(f"d{i}" for i in range(1, len(shape)))
            kw = {f"d{i}": shape[i] for i in range(1, len(shape))}
            ap = ap.rearrange(f"p ({dims}) -> p {dims}", **kw)
        return ap

    def mark(self):
        self.marks.append(self.off)

    def release(self):
        self.off = self.marks.pop()


def ceil_div(a, b):
    return (a + b - 1) // b


VEC = {}
_vc = 0
for _n, _c in [
    ("convw", 64), ("convb", 16), ("mu_r", 8), ("mu_k", 8),
    ("b_in", 16), ("A0", 8), ("C0", 8), ("b_out", 8),
    ("bq", 8), ("bk", 8), ("bv", 8), ("bo", 8),
    ("gq", 8), ("gk", 8), ("gv", 8), ("go", 8),
    ("betaq", 8), ("betak", 8), ("betav", 8), ("betao", 8),
    ("scale_b", 1), ("f1b", 8),
]:
    VEC[_n] = _vc
    _vc += _c
VEC_COLS = _vc


def build_program():
    nc = bass.Bass("TRN2", target_bir_lowering=False)
    sbp = nc.SBUF_PARTITION_SIZE_BYTES

    def hin(name, shape, dt=F32):
        return nc.dram_tensor(name, list(shape), dt, kind="ExternalInput").ap()

    H = {}
    H["xT"] = hin("xT", [D, T])
    for n, sh in [("w_inproj", [D, E_INNER]), ("w_outproj", [E_INNER, D]),
                  ("w0r", [D, D]), ("w0k", [D, D]), ("w0v", [D, D]),
                  ("w_in0", [D, E_INNER]), ("w_out0", [D, D]),
                  ("wqa", [D, D]), ("wka", [D, D]), ("wva", [D, D]),
                  ("woa", [D, D]),
                  ("wqb", [D, D]), ("wkb", [D, D]), ("wvb", [D, D]),
                  ("wob", [D, D]),
                  ("wf1", [D, D]), ("wf2", [D, D]), ("wf3", [D, D])]:
        H[n] = hin(n, sh)
    H["vecs"] = hin("vecs", [128, VEC_COLS])
    H["ident"] = hin("ident", [128, 128])
    H["biasmx"] = hin("biasmx", [T, T], FP8)
    outT = nc.dram_tensor("outT", [D, T], F32, kind="ExternalOutput").ap()

    from contextlib import ExitStack
    with ExitStack() as ctx:
        E = {}
        def sbuf(name, shape, dt=F32):
            E[name] = ctx.enter_context(nc.sbuf_tensor(name, list(shape), dt))
            return E[name]

        R = sbuf("R", [128, KC, T])
        mneg = sbuf("mneg", [128, T])
        a_b = sbuf("a_b", [128, T], BF16)
        vecs = sbuf("vecs_s", [128, VEC_COLS])
        ident = sbuf("ident_s", [128, 128])
        identb = sbuf("identb_s", [128, 128], BF16)
        consts = sbuf("consts", [128, 8])
        constsb = sbuf("constsb", [128, 8], BF16)
        crowb = sbuf("crowb", [1, 2, 128], BF16)
        rows = sbuf("rows", [1, 8, 512])
        smalls = sbuf("smalls", [128, 160])
        stage = sbuf("stage", [128, 3, 1024])

        mneg2 = sbuf("mneg2", [128, 512])
        ksb = sbuf("ksb", [128, 512])
        ab2 = sbuf("ab2", [128, 512])
        consts8 = sbuf("consts8", [128, 8], FP8)
        consts85 = sbuf("consts85", [128, 8], FP8E5)
        ar_n = (nc.sbuf_top - nc.sbuf_base - 2048) // 64 * 64
        arena_b = ctx.enter_context(
            nc.sbuf_tensor("arena_b", [128, ar_n], mybir.dt.uint8))
        AB = Arena(arena_b, ar_n, "main")
        A8 = AB

        psA = ctx.enter_context(nc.psum_tensor("psA", [128, 2, 512], F32))
        psB = ctx.enter_context(nc.psum_tensor("psB", [128, 2, 512], F32))
        psC = ctx.enter_context(nc.psum_tensor("psC", [128, 512], F32))
        psD = ctx.enter_context(nc.psum_tensor("psD", [128, 512], F32))
        psP = ctx.enter_context(nc.psum_tensor("psP", [128, 8, 128], F32))

        sems = {s: ctx.enter_context(nc.semaphore(s))
                for s in ("sT", "sV", "sS", "sG") + DMA_SEMS}

        P = Prog()
        _build_body(P, nc, H, outT, locals())

        with nc.Block() as block:
            @block.tensor
            def _(e):
                for op in P.ops["T"]:
                    op(e, sems)

            @block.vector
            def _(e):
                for op in P.ops["V"]:
                    op(e, sems)

            @block.scalar
            def _(e):
                for op in P.ops["S"]:
                    op(e, sems)

            @block.gpsimd
            def _(e):
                for op in P.ops["G"]:
                    op(e, sems)

            @block.sync
            def _(e):
                for op in P.ops["Y"]:
                    op(e, sems)
    return nc


def _build_body(P, nc, H, outT, E):
    R = E["R"]; mneg = E["mneg"]; a_b = E["a_b"]; vecs = E["vecs"]
    crowb = E["crowb"]

    def rowv(i):
        return rows[0:1, i, 0:256].bitcast(BF16)
    mneg2 = E["mneg2"]; ab2 = E["ab2"]; consts8 = E["consts8"]
    ksb = E["ksb"]; consts85 = E["consts85"]
    ident = E["ident"]; identb = E["identb"]
    consts = E["consts"]; constsb = E["constsb"]
    rows = E["rows"]; smalls = E["smalls"]; stage = E["stage"]
    AB = E["AB"]; A8 = E["A8"]
    psA = E["psA"]; psB = E["psB"]; psC = E["psC"]; psD = E["psD"]
    psP = E["psP"]

    st = {}

    P.emit("Y", lambda e: e.dma_start(
        out=R[:], in_=H["xT"].rearrange("(c p) t -> p c t", p=128)), inc="dYC")
    P.emit("Y", lambda e: e.dma_start(out=vecs[:], in_=H["vecs"][:]), inc="dYC")
    P.emit("Y", lambda e: e.dma_start(out=ident[:], in_=H["ident"][:]),
           inc="dYC")
    t_init = ("dYC", 48)
    t_vec = t_init
    t_id = t_init
    st_init = t_init
    P.emit("V", lambda e: e.memset(crowb[0:1, 0, :], -1.0))
    P.emit("V", lambda e: e.memset(crowb[0:1, 1, :], 1.0))
    P.emit("V", lambda e: e.memset(consts[:, 1:2], -1.0))
    P.emit("V", lambda e: e.memset(consts[:, 2:3], 1.0))
    P.emit("V", lambda e: e.memset(consts[:, 4:5], EPS))
    P.emit("V", lambda e: e.memset(consts[:, 5:6], 1.0 / 1024.0))
    P.emit("V", lambda e: e.memset(consts[:, 6:7], 64.0))
    P.emit("V", lambda e: e.memset(consts[:, 7:8], 0.125))
    P.emit("V", lambda e: e.memset(consts8[:, 0:1], 1.0))
    P.emit("V", lambda e: e.memset(consts85[:, 0:1], 1.0))
    t_cb = P.emit("V", lambda e: e.memset(constsb[:, 0:1], 1.0 / 1024.0),
                  inc="sV", drain=True)
    t_idb = P.emit("S", lambda e: e.copy(out=identb[:], in_=ident[:]),
                   waits=[t_id], inc="sS", drain=True)
    st["tvec"] = t_init

    def load_weight(hbm, kk, mm, into=None, wait0=None):
        if into is None:
            wt = AB.alloc([128, kk, mm])
        else:
            wt = into.rearrange("p (k m) -> p k m", k=kk, m=mm) \
                if kk * mm == into.free_size() else into[:, 0:kk * mm].rearrange(
                    "p (k m) -> p k m", k=kk, m=mm)
        toks = []
        src = hbm.rearrange("(k p) m -> p k m", p=128)
        pieces = [(k, o) for k in range(kk) for o in range(0, mm, 1024)]
        st.setdefault("stgi", 0)
        for k, o in pieces:
            w = min(1024, mm - o)
            i = st["stgi"] % 3
            st["stgi"] += 1
            sl = stage[:, i, 0:w]
            prev = st.get(("stg", i))
            td = P.emit("Y", lambda e, k=k, o=o, w=w, sl=sl, src=src: e.dma_start(
                out=sl, in_=src[:, k, o:o + w]), waits=[prev] if prev else [],
                inc=f"dY{i}")
            cw = [td] + ([wait0] if wait0 else [])
            cw += list(P.bar_toks.values())
            cw.append(st.get("sl_last_mm"))
            tc = P.emit("S", lambda e, k=k, o=o, w=w, sl=sl, wt=wt: e.copy(
                out=wt[:, k, o:o + w], in_=sl), waits=cw, inc="sS", drain=True)
            st[("stg", i)] = tc
            toks.append(tc)
        ktoks = [None] * kk
        for (k, o), tc in zip(pieces, toks):
            ktoks[k] = tc
        return wt, ktoks

    pp = [psA, psB]
    st["ppg"] = 0
    st["ppfree"] = {0: None, 1: None}

    def gemm(wt, kk, m_list, rhs_of, c0, cols, evac, waits0=(), wk=None,
             rk=None):
        ntiles = ceil_div(cols, 512)
        assert ntiles <= 2
        first_m = True
        for m in m_list:
            g = st["ppg"]; pg = pp[g % 2]
            w0 = list(waits0)
            if st["ppfree"][g % 2] is not None:
                w0.append(st["ppfree"][g % 2])
            tok_mm = None
            for k in range(kk):
                for nt in range(ntiles):
                    nn = min(512, cols - nt * 512)
                    last = (k == kk - 1 and nt == ntiles - 1)
                    w = w0 if (k == 0 and nt == 0) else []
                    if wk is not None and nt == 0:
                        w = list(w) + [wk[k]]
                    if rk is not None and first_m and nt == 0:
                        w = list(w) + [rk[k]]
                    tok_mm = P.emit("T", lambda e, k=k, nt=nt, nn=nn, m=m,
                                    pg=pg, wt=wt, rhs_of=rhs_of, c0=c0: e.matmul(
                        pg[:, nt, 0:nn], wt[:, k, m * 128:(m + 1) * 128],
                        rhs_of(k)[:, c0 + nt * 512:c0 + nt * 512 + nn],
                        start=(k == 0), stop=(k == kk - 1)),
                        waits=w, inc="sT" if last else None)
            st["last_mm"] = tok_mm
            tfree = evac(m, pg, [tok_mm])
            st["ppfree"][g % 2] = tfree
            st["ppg"] += 1
            first_m = False

    def col_sums(src_of, kk, base, finish, toks_in, c0=0, cols=T, ones=None):
        ones = constsb[:, 0:1] if ones is None else ones
        out_toks = []
        for nt in range(cols // 512):
            ps, key = (psC, "psC") if nt % 2 == 0 else (psD, "psD")
            tok = None
            for k in range(kk):
                w = [t_cb, toks_in[k] if toks_in else None]
                if k == 0:
                    w.append(st.get(("aux", key)))
                    w.append(st.get(("row", base + nt)))
                tok = P.emit("T", lambda e, k=k, nt=nt, ps=ps, ones=ones: e.matmul(
                    ps[0:1, 0:512], ones,
                    src_of(k)[:, c0 + nt * 512:c0 + (nt + 1) * 512],
                    start=(k == 0), stop=(k == kk - 1)),
                    waits=w, inc="sT" if k == kk - 1 else None)
            te = P.emit("S", lambda e, nt=nt, ps=ps: finish(
                e, base + nt, ps[0:1, 0:512]),
                waits=[tok], inc="sS", drain=True)
            st[("aux", key)] = te
            out_toks.append(te)
        return out_toks

    def bcast_row(base, dst_of, sign, waits, cols=T):
        lhsT = crowb[0:1, 0, :] if sign < 0 else crowb[0:1, 1, :]
        toks = []
        for nt in range(cols // 512):
            ps, key = (psC, "psC") if nt % 2 == 0 else (psD, "psD")
            w = list(waits) + [st.get(("aux", key))]
            tok = P.emit("T", lambda e, nt=nt, ps=ps, lhsT=lhsT: e.matmul(
                ps[:, 0:512], lhsT, rowv(base + nt),
                start=True, stop=True), waits=w, inc="sT")
            st[("row", base + nt)] = tok
            te = P.emit("S", lambda e, nt=nt, ps=ps: e.copy(
                out=dst_of(nt), in_=ps[:, 0:512]),
                waits=[tok, st.get("mneg_dma")], inc="sS", drain=True)
            st[("aux", key)] = te
            toks.append(te)
        return toks

    def _ln_core(src_of, kk):
        accm = [psA[0:1, 0, :], psA[0:1, 1, :], psB[0:1, 0, :], psB[0:1, 1, :]]
        accv = [psC[0:1, 0:512], psD[0:1, 0:512],
                psP[0:1, 0:4, :].rearrange("p a b -> p (a b)"),
                psP[0:1, 4:8, :].rearrange("p a b -> p (a b)")]
        akm = ["psA0", "psA1", "psB0", "psB1"]
        akv = ["psC", "psD", "psP0", "psP1"]
        AB.mark()
        xb = AB.alloc([128, 2, T])
        xq = AB.alloc([128, 2, T])
        mtoks = []
        vtoks = []
        for k in range(kk):
            tc = P.emit("S", lambda e, k=k: e.copy(
                out=xb[:, k % 2, :], in_=src_of(k)),
                waits=[("dYC", 48), st.get(("lnm", k % 2)),
                       st.get(("lnq", k % 2))],
                inc="sS", drain=True)
            tq = P.emit("V", lambda e, k=k: e.tensor_mul(
                xq[:, k % 2, :], xb[:, k % 2, :], xb[:, k % 2, :]),
                waits=[tc, st.get(("lnv", k % 2))], inc="sV", drain=True)
            st[("lnq", k % 2)] = tq
            tok = tok2 = None
            for nt in range(T // 512):
                w = [tc]
                if k == 0:
                    w += [t_cb, st.get(("aux", akm[nt]))]
                tok = P.emit("T", lambda e, k=k, nt=nt: e.matmul(
                    accm[nt], constsb[:, 0:1],
                    xb[:, k % 2, nt * 512:(nt + 1) * 512],
                    start=(k == 0), stop=(k == kk - 1)),
                    waits=w, inc="sT" if nt == T // 512 - 1 else None)
            st[("lnm", k % 2)] = tok
            mtoks.append(tok)
            for nt in range(T // 512):
                w = [tq]
                if k == 0:
                    w += [st.get(("aux", akv[nt])), st.get(("row", nt)),
                          st.get(("row", 4 + nt))]
                tok2 = P.emit("T", lambda e, k=k, nt=nt: e.matmul(
                    accv[nt], constsb[:, 0:1],
                    xq[:, k % 2, nt * 512:(nt + 1) * 512],
                    start=(k == 0), stop=(k == kk - 1)),
                    waits=w, inc="sT" if nt == T // 512 - 1 else None)
            st[("lnv", k % 2)] = tok2
            vtoks.append(tok2)
        tm = []
        tv = []
        for nt in range(T // 512):
            te = P.emit("S", lambda e, nt=nt: e.activation(
                out=rowv(nt), in_=accm[nt], func=ACT.Copy),
                waits=[mtoks[-1], st.get(("row", nt))], inc="sS", drain=True)
            st[("aux", akm[nt])] = te
            tm.append(te)
            t2 = P.emit("S", lambda e, nt=nt: e.activation(
                out=rows[0:1, 4 + nt, :], in_=accv[nt], func=ACT.Copy),
                waits=[vtoks[-1]], inc="sS", drain=True)
            st[("aux", akv[nt])] = t2
            tv.append(t2)
        t_mneg = bcast_row(0, lambda nt: mneg[:, nt * 512:(nt + 1) * 512],
                           -1.0, tm)
        NT = T // 512
        mrow = mneg[0:1, :].rearrange("p (a b) -> p a b", a=NT)
        tq1 = P.emit("V", lambda e: e.tensor_mul(
            rows[0:1, 0:NT, :], mrow, mrow),
            waits=[t_mneg[-1], tv[-1]] +
                  [st[("row", nt)] for nt in range(NT)],
            inc="sV", drain=True)
        tq2 = P.emit("V", lambda e: e.tensor_sub(
            rows[0:1, 4:4 + NT, :], rows[0:1, 4:4 + NT, :],
            rows[0:1, 0:NT, :]), waits=[tq1], inc="sV", drain=True)
        tq3 = P.emit("S", lambda e: e.activation(
            out=rows[0:1, 0:NT, :], in_=rows[0:1, 4:4 + NT, :],
            func=ACT.Sqrt, bias=consts[0:1, 4:5]),
            waits=[tq2], inc="sS", drain=True)
        rtok = P.emit("S", lambda e: e.copy(
            out=rows[0:1, 0:NT, 0:256].bitcast(BF16),
            in_=rows[0:1, 0:NT, :]),
            waits=[tq3], inc="sS", drain=True)
        rsc = [mneg2[:, :], ab2[:, :]]
        t_ab = []
        for nt in range(NT):
            ps, key = (psC, "psC") if nt % 2 == 0 else (psD, "psD")
            tok = P.emit("T", lambda e, nt=nt, ps=ps: e.matmul(
                ps[:, 0:512], crowb[0:1, 1, :], rowv(nt),
                start=True, stop=True),
                waits=[rtok, st.get(("aux", key)), st.get(("row", nt))],
                inc="sT")
            st[("row", nt)] = tok
            tv_ = P.emit("V", lambda e, nt=nt, ps=ps, r=rsc[nt % 2]:
                         e.reciprocal(
                out=r, in_=ps[:, 0:512]),
                waits=[tok], inc="sV", drain=True)
            te = P.emit("S", lambda e, nt=nt, r=rsc[nt % 2]: e.copy(
                out=a_b[:, nt * 512:(nt + 1) * 512], in_=r),
                waits=[tv_], inc="sS", drain=True)
            st[("aux", key)] = te
            t_ab.append(te)
        AB.release()
        return t_mneg, t_ab, tv[-1]

    def ln_stats(src_of, kk=KC):
        t_mneg, t_ab, t_scr = _ln_core(src_of, kk)
        return t_ab[-1], t_scr, t_mneg[-1]

    def ln_mat(dst_of, src_of, k_list, c0, cols, tok, tok_mul=None):
        mt = []
        w0 = tok if isinstance(tok, list) else [tok]
        wm = [tok_mul] if tok_mul is not None else []
        for k in k_list:
            ta = P.emit("V", lambda e, k=k: e.tensor_add(
                dst_of(k), src_of(k)[:, c0:c0 + cols],
                mneg[:, c0:c0 + cols]), waits=w0, inc="sV", drain=True)
            mt.append(P.emit("V", lambda e, k=k: e.tensor_mul(
                dst_of(k), dst_of(k), a_b[:, c0:c0 + cols]),
                waits=[ta] + wm, inc="sV", drain=True))
        return mt

    def sub_mamba():
        if STAGE < 1:
            return
        AB.mark()
        QT = 512
        halo = AB.alloc([128, 16, 4])
        slotA = AB.alloc([128, 8 * E_INNER])
        slotB = AB.alloc([128, 16 * D])
        t_ab, t_scr, t_mneg = ln_stats(lambda k: R[:, k, :])
        wt_in, tk_in = load_weight(H["w_inproj"], KC, E_INNER, into=slotA)
        wt_out, tk_out = load_weight(H["w_outproj"], 16, D, into=slotB)
        if STAGE < 2:
            AB.release()
            return
        for q in range(T // QT):
            c0 = q * QT
            AB.mark()
            ln = AB.alloc([128, KC, QT])
            s1 = AB.alloc([128, 16, QT])
            x1 = AB.alloc([128, 2, QT])
            acc = AB.alloc([128, 2, QT])
            lt = []
            for k in range(KC):
                lt.append(P.emit("V", lambda e, k=k, c0=c0, ln=ln: e.tensor_add(
                    ln[:, k, :], R[:, k, c0:c0 + QT], mneg[:, c0:c0 + QT]),
                    waits=[t_mneg], inc="sV"))

            def evac_conv(m, pg, waits, q=q, s1=s1, x1=x1, acc=acc, c0=c0):
                sl = m % 2
                w = list(waits) + [st.get(("x1f", sl)), st.get(("accf", sl)),
                                   t_ab]
                tcp = P.emit("V", lambda e, sl=sl, pg=pg, c0=c0: e.tensor_mul(
                    x1[:, sl, :], pg[:, 0, 0:QT], a_b[:, c0:c0 + QT]),
                    waits=w, inc="sV", drain=True)
                tap = VEC["convw"] + m * 4
                P.emit("V", lambda e, sl=sl, tap=tap, x1=x1, acc=acc:
                       e.tensor_scalar(
                           out=acc[:, sl, :], in0=x1[:, sl, :],
                           scalar1=vecs[:, tap + 3:tap + 4], scalar2=None,
                           op0=ALU.mult), waits=[st["tvec"], tcp])
                for j in (2, 1, 0):
                    sh = 3 - j
                    P.emit("V", lambda e, sl=sl, tap=tap, j=j, sh=sh, x1=x1,
                           acc=acc: e.scalar_tensor_tensor(
                               out=acc[:, sl, sh:QT], in0=x1[:, sl, 0:QT - sh],
                               scalar=vecs[:, tap + j:tap + j + 1],
                               in1=acc[:, sl, sh:QT],
                               op0=ALU.mult, op1=ALU.add))
                    if q > 0:
                        P.emit("V", lambda e, m=m, sl=sl, tap=tap, j=j, sh=sh,
                               acc=acc: e.scalar_tensor_tensor(
                                   out=acc[:, sl, 0:sh],
                                   in0=halo[:, m, 4 - sh:4],
                                   scalar=vecs[:, tap + j:tap + j + 1],
                                   in1=acc[:, sl, 0:sh],
                                   op0=ALU.mult, op1=ALU.add))
                tv = P.emit("V", lambda e, m=m, sl=sl, x1=x1: e.tensor_copy(
                    out=halo[:, m, 1:4], in_=x1[:, sl, QT - 3:QT]),
                    inc="sV", drain=True)
                st[("x1f", sl)] = tv
                ts = P.emit("S", lambda e, m=m, sl=sl, s1=s1, acc=acc:
                            e.activation(
                                out=s1[:, m, :], in_=acc[:, sl, :],
                                func=ACT.Silu,
                                bias=vecs[:, VEC["convb"] + m:
                                          VEC["convb"] + m + 1]),
                            waits=[tv], inc="sS", drain=True)
                st[("accf", sl)] = ts
                st[("s1", m)] = ts
                return tcp

            gemm(wt_in, KC, list(range(16)), lambda k, ln=ln: ln[:, k, :],
                 0, QT, evac_conv, wk=tk_in if q == 0 else None, rk=lt)

            def evac_res(m, pg, waits, c0=c0):
                return P.emit("V", lambda e, m=m, pg=pg, c0=c0: e.tensor_add(
                    R[:, m, c0:c0 + QT], R[:, m, c0:c0 + QT], pg[:, 0, 0:QT]),
                    waits=waits, inc="sV", drain=True)

            if STAGE < 3:
                AB.release()
                break
            w_s1 = [st[("s1", m)] for m in range(16)]
            gemm(wt_out, 16, list(range(KC)), lambda k, s1=s1: s1[:, k, :],
                 0, QT, evac_res, waits0=w_s1,
                 wk=tk_out if q == 0 else None)
            AB.release()
        AB.release()


    def sub_ffn0():
        QT = 512
        NQ = T // QT
        AB.mark()
        slot_r = AB.alloc([128, 8 * D])
        slot_k = AB.alloc([128, 8 * D])
        slot_v = AB.alloc([128, 8 * D])
        t_ab, t_scr, t_mneg = ln_stats(lambda k: R[:, k, :])
        wr, wrt = load_weight(H["w0r"], KC, D, into=slot_r)
        wk_, wkt = load_weight(H["w0k"], KC, D, into=slot_k)
        wv, wvt = load_weight(H["w0v"], KC, D, into=slot_v)
        lnp = AB.alloc([128, 2, QT + 1])
        d = AB.alloc([128, KC, QT])
        xr = AB.alloc([128, KC, QT])
        xk = AB.alloc([128, KC, QT])
        r_sb = AB.alloc([128, KC, QT])
        kact = AB.alloc([128, KC, QT])
        dtmp = AB.alloc([128, 2, QT])
        P.emit("V", lambda e: e.memset(smalls[:, 0:8], 0.0))
        st["ffn_xf"] = None
        st["ffn_xkf"] = None

        def prep_xr(q, ks=range(KC)):
            c0 = q * QT
            tk = None
            for k in ks:
                sl = k % 2
                mur = vecs[:, VEC["mu_r"] + k:VEC["mu_r"] + k + 1]
                P.emit("V", lambda e, k=k, sl=sl: e.tensor_copy(
                    out=lnp[:, sl, 0:1], in_=smalls[:, k:k + 1]))
                P.emit("V", lambda e, k=k, sl=sl, c0=c0: e.tensor_add(
                    lnp[:, sl, 1:QT + 1], R[:, k, c0:c0 + QT],
                    mneg[:, c0:c0 + QT]),
                    waits=[t_mneg, st["tvec"], st.get("ffn_xf"),
                           st.get("ffn_dxf")])
                P.emit("V", lambda e, k=k, sl=sl, c0=c0: e.tensor_mul(
                    lnp[:, sl, 1:QT + 1], lnp[:, sl, 1:QT + 1],
                    a_b[:, c0:c0 + QT]),
                    waits=[t_ab])
                P.emit("V", lambda e, k=k, sl=sl: e.tensor_sub(
                    d[:, k, :], lnp[:, sl, 1:QT + 1], lnp[:, sl, 0:QT]))
                tk = P.emit("V", lambda e, k=k, sl=sl, mur=mur:
                            e.scalar_tensor_tensor(
                                out=xr[:, k, :], in0=d[:, k, :],
                                scalar=mur, in1=lnp[:, sl, 0:QT],
                                op0=ALU.mult, op1=ALU.add),
                            inc="sV" if k == ks[-1] else None, drain=False)
                P.emit("V", lambda e, k=k, sl=sl: e.tensor_copy(
                    out=smalls[:, k:k + 1], in_=lnp[:, sl, QT:QT + 1]))
            return tk

        def prep_xk(q):
            c0 = q * QT
            tk = None
            for k in range(KC):
                muk = vecs[:, VEC["mu_k"] + k:VEC["mu_k"] + k + 1]
                mur = vecs[:, VEC["mu_r"] + k:VEC["mu_r"] + k + 1]
                P.emit("V", lambda e, k=k, mur=mur: e.scalar_tensor_tensor(
                    out=xk[:, k, :], in0=d[:, k, :], scalar=mur,
                    in1=xr[:, k, :], op0=ALU.mult, op1=ALU.subtract),
                    waits=[st.get("ffn_xkf")])
                tk = P.emit("V", lambda e, k=k, muk=muk:
                            e.scalar_tensor_tensor(
                                out=xk[:, k, :], in0=d[:, k, :], scalar=muk,
                                in1=xk[:, k, :], op0=ALU.mult,
                                op1=ALU.subtract),
                            inc="sV" if k == KC - 1 else None, drain=False)
            return tk

        t_prep = prep_xr(0)
        t_prepk = prep_xk(0)
        for q in range(NQ):
            c0 = q * QT

            def evac_r(m, pg, waits):
                return P.emit("S", lambda e, m=m, pg=pg: e.activation(
                    out=r_sb[:, m, :], in_=pg[:, 0, :], func=ACT.Sigmoid),
                    waits=waits, inc="sS", drain=True)

            gemm(wr, KC, list(range(KC)), lambda k: xr[:, k, :], 0, QT,
                 evac_r, waits0=[t_prep], wk=wrt if q == 0 else None)
            tmm_r = st["last_mm"]
            st["ffn_xf"] = tmm_r
            if q + 1 < NQ:
                prep_xr(q + 1, range(0, 4))

            def evac_k(m, pg, waits):
                sl = m % 2
                tr_ = P.emit("V", lambda e, m=m, pg=pg, sl=sl: e.tensor_scalar(
                    out=dtmp[:, sl, :], in0=pg[:, 0, :],
                    scalar1=0.0, scalar2=None, op0=ALU.max),
                    waits=list(waits) + [st.get(("dt", sl))],
                    inc="sV", drain=True)
                tk_ = P.emit("S", lambda e, m=m, sl=sl: e.activation(
                    out=kact[:, m, :], in_=dtmp[:, sl, :], func=ACT.Square),
                    waits=[tr_], inc="sS", drain=True)
                st[("dt", sl)] = tk_
                st[("kact", m)] = tk_
                return tr_

            gemm(wk_, KC, list(range(KC)), lambda k: xk[:, k, :], 0, QT,
                 evac_k, waits0=[t_prepk], wk=wkt if q == 0 else None)
            st["ffn_xkf"] = st["last_mm"]
            st["ffn_dxf"] = st["last_mm"]
            if q + 1 < NQ:
                t_prep = prep_xr(q + 1, range(4, KC))

            def evac_v(m, pg, waits, c0=c0):
                sl = m % 2
                tm_ = P.emit("V", lambda e, m=m, pg=pg, sl=sl: e.tensor_mul(
                    dtmp[:, sl, :], pg[:, 0, :], r_sb[:, m, :]),
                    waits=list(waits) + [st.get(("dt", sl))],
                    inc="sV", drain=True)
                st[("dt", sl)] = P.emit("V", lambda e, m=m, sl=sl, c0=c0:
                                        e.tensor_add(
                    R[:, m, c0:c0 + QT], R[:, m, c0:c0 + QT], dtmp[:, sl, :]),
                    inc="sV", drain=True)
                return tm_

            gemm(wv, KC, list(range(KC)), lambda k: kact[:, k, :], 0, QT,
                 evac_v, waits0=[st[("kact", m)] for m in range(KC)],
                 wk=wvt if q == 0 else None)
            if q + 1 < NQ:
                t_prepk = prep_xk(q + 1)
        AB.release()

    def sub_attnb0():
        HT = T // 2
        AB.mark()
        st["wfree2"] = None
        wslot = AB.alloc([128, 8 * E_INNER])
        t_ab, t_scr, t_mneg = ln_stats(lambda k: R[:, k, :])
        ln = AB.alloc([128, KC, HT])
        gate = AB.alloc([128, KC, HT])
        y = AB.alloc([128, KC, HT])
        vtmp = AB.alloc([128, 2, HT])
        hs = AB.alloc([128, 2, HT])
        P.emit("V", lambda e: e.memset(smalls[:, 8:16], 0.0))
        t_dec = P.emit("S", lambda e: e.activation(
            out=smalls[:, 16:24], in_=vecs[:, VEC["A0"]:VEC["A0"] + 8],
            func=ACT.Sigmoid), waits=[st["tvec"]], inc="sS", drain=True)
        st["wfree2"] = None
        for h in range(2):
            c0 = h * HT
            lt = ln_mat(lambda k, ln=ln: ln[:, k, :], lambda k: R[:, k, :],
                        range(KC), c0, HT, t_mneg, t_ab)
            win, wint = load_weight(H["w_in0"], KC, E_INNER, into=wslot,
                                    wait0=st["wfree2"])

            def evac_g(m, pg, waits):
                return P.emit("S", lambda e, m=m, pg=pg: e.activation(
                    out=gate[:, m - 8, :], in_=pg[:, :, :].rearrange(
                        "p a b -> p (a b)"), func=ACT.Sigmoid,
                    bias=vecs[:, VEC["b_in"] + m:VEC["b_in"] + m + 1]),
                    waits=waits, inc="sS", drain=True)

            gemm(win, KC, list(range(8, 16)), lambda k: ln[:, k, :], 0, HT,
                 evac_g, wk=wint, rk=lt)

            def evac_vch(m, pg, waits, h=h):
                sl = m % 2
                tv_ = P.emit("S", lambda e, m=m, pg=pg, sl=sl: e.activation(
                    out=vtmp[:, sl, :], in_=pg[:, :, :].rearrange(
                        "p a b -> p (a b)"), func=ACT.Identity,
                    bias=vecs[:, VEC["b_in"] + m:VEC["b_in"] + m + 1]),
                    waits=list(waits) + [st.get(("vt", sl))],
                    inc="sS", drain=True)
                tsc = P.emit("V", lambda e, m=m, sl=sl: e.tensor_tensor_scan(
                    out=hs[:, sl, :],
                    data0=smalls[:, 16 + m:17 + m].broadcast_to([128, HT]),
                    data1=vtmp[:, sl, :],
                    initial=smalls[:, 8 + m:9 + m],
                    op0=ALU.mult, op1=ALU.add),
                    waits=[tv_, t_dec, st.get(("hs", sl))],
                    inc="sV", drain=True)
                st[("vt", sl)] = tsc
                P.emit("V", lambda e, m=m, sl=sl: e.tensor_copy(
                    out=smalls[:, 8 + m:9 + m], in_=hs[:, sl, HT - 1:HT]))
                st[("hs", sl)] = P.emit("V", lambda e, m=m, sl=sl:
                                        e.scalar_tensor_tensor(
                    out=y[:, m, :], in0=hs[:, sl, :],
                    scalar=vecs[:, VEC["C0"] + m:VEC["C0"] + m + 1],
                    in1=gate[:, m, :], op0=ALU.mult, op1=ALU.mult),
                    inc="sV", drain=True)
                return tv_

            gemm(win, KC, list(range(8)), lambda k: ln[:, k, :], 0, HT,
                 evac_vch, waits0=[], wk=None)
            tmm_in = st["last_mm"]

            wout, woutt = load_weight(H["w_out0"], KC, D, into=wslot,
                                      wait0=tmm_in)

            def evac_o(m, pg, waits, c0=c0):
                return P.emit("V", lambda e, m=m, pg=pg, c0=c0:
                              e.scalar_tensor_tensor(
                    out=R[:, m, c0:c0 + HT],
                    in0=pg[:, :, :].rearrange("p a b -> p (a b)"),
                    scalar=vecs[:, VEC["b_out"] + m:VEC["b_out"] + m + 1],
                    in1=R[:, m, c0:c0 + HT], op0=ALU.add, op1=ALU.add),
                    waits=waits, inc="sV", drain=True)

            w_y = [st[("hs", 0)], st[("hs", 1)]]
            gemm(wout, KC, list(range(KC)), lambda k: y[:, k, :], 0, HT,
                 evac_o, waits0=w_y, wk=woutt)
            st["wfree2"] = st["last_mm"]
        AB.release()


    def ln_spike(pre, prets, gbase, ombase, out_of, scratch):
        mt = vt = None
        sqt = []
        for k in range(KC):
            tq = P.emit("S", lambda e, k=k: e.activation(
                out=scratch[:, k, :], in_=pre[:, k, :], func=ACT.Square),
                waits=[prets[k]], inc="sS", drain=True)
            sqt.append(tq)
            w = [prets[k]]
            if k == 0:
                w += [t_cb, st.get(("aux", "psC")), st.get(("row", 0))]
            mt = P.emit("T", lambda e, k=k: e.matmul(
                psC[0:1, 0:512], constsb[:, 0:1], pre[:, k, :],
                start=(k == 0), stop=(k == KC - 1)),
                waits=w, inc="sT" if k == KC - 1 else None)
            w2 = [tq]
            if k == 0:
                w2 += [st.get(("aux", "psD")), st.get(("row", 4)),
                       st.get(("row", 1))]
            vt = P.emit("T", lambda e, k=k: e.matmul(
                psD[0:1, 0:512], constsb[:, 0:1], scratch[:, k, :],
                start=(k == 0), stop=(k == KC - 1)),
                waits=w2, inc="sT" if k == KC - 1 else None)
        tme = P.emit("S", lambda e: e.activation(
            out=rowv(0), in_=psC[0:1, 0:512], func=ACT.Copy),
            waits=[mt], inc="sS", drain=True)
        st[("aux", "psC")] = tme
        tve = P.emit("S", lambda e: e.activation(
            out=rows[0:1, 4, :], in_=psD[0:1, 0:512], func=ACT.Copy),
            waits=[vt], inc="sS", drain=True)
        st[("aux", "psD")] = tve
        tmn = bcast_row(0, lambda nt: mneg2[:, :], -1.0, [tme], cols=512)
        tq1 = P.emit("V", lambda e: e.tensor_mul(
            rows[0:1, 5, :], mneg2[0:1, :], mneg2[0:1, :]),
            waits=[tmn[-1], st.get(("row", 5))], inc="sV", drain=True)
        tq2 = P.emit("V", lambda e: e.tensor_sub(
            rows[0:1, 4, :], rows[0:1, 4, :], rows[0:1, 5, :]),
            waits=[tq1, tve], inc="sV", drain=True)
        tq3 = P.emit("S", lambda e: e.activation(
            out=rows[0:1, 5, :], in_=rows[0:1, 4, :],
            func=ACT.Sqrt, bias=consts[0:1, 4:5]),
            waits=[tq2], inc="sS", drain=True)
        st[("row", 5)] = tq3
        tq4 = P.emit("V", lambda e: e.reciprocal(
            out=rows[0:1, 4, :], in_=rows[0:1, 5, :]),
            waits=[tq3], inc="sV", drain=True)
        st[("row", 4)] = tq4
        tq5 = P.emit("S", lambda e: e.copy(
            out=rowv(1), in_=rows[0:1, 4, :]),
            waits=[tq4, st.get(("row", 1))], inc="sS", drain=True)
        tab = bcast_row(1, lambda nt: ab2[:, :], 1.0, [tq5], cols=512)
        out_toks = []
        for k in range(KC):
            P.emit("V", lambda e, k=k: e.tensor_add(
                pre[:, k, :], pre[:, k, :], mneg2[:]),
                waits=[tmn[-1], sqt[k]])
            P.emit("V", lambda e, k=k: e.tensor_mul(
                pre[:, k, :], pre[:, k, :], ab2[:]), waits=[tab[-1]])
            out_toks.append(P.emit("V", lambda e, k=k: e.tensor_scalar(
                out=out_of(k), in0=pre[:, k, :],
                scalar1=vecs[:, gbase + k:gbase + k + 1],
                scalar2=smalls[:, ombase + k:ombase + k + 1],
                op0=ALU.mult, op1=ALU.is_ge),
                inc="sV", drain=True))
        return out_toks

    def sub_l1a():
        HT = T // 2
        AB.mark()
        bd = AB.alloc([128, 8, 128])
        P.emit("V", lambda e: e.memset(bd[:, :, :], 0.0))
        ln = AB.alloc([128, KC, T])
        AB.mark()
        slot_ka = AB.alloc([128, 8 * D])
        slot_va = AB.alloc([128, 8 * D])
        t_ab, t_scr, t_mneg = ln_stats(lambda k: R[:, k, :])
        wk, wkt = load_weight(H["wka"], KC, D, into=slot_ka)
        wv, wvt = load_weight(H["wva"], KC, D, into=slot_va)
        kblk = AB.alloc([128, 2, D])
        vblk = AB.alloc([128, 2, D])
        tpair = None
        lt = None
        for tb in range(16):
            if tb % 4 == 0:
                qc0 = (tb // 4) * 512
                lt = ln_mat(lambda k, qc0=qc0: ln[:, k, qc0:qc0 + 512],
                            lambda k: R[:, k, :],
                            range(KC), qc0, 512, t_mneg, t_ab)
            sl = tb % 2
            for which, w_, wt_, blk in (("k", wk, wkt, kblk), ("v", wv, wvt, vblk)):
                ps = psA if which == "k" else psB
                tok = None
                for kc in range(KC):
                    for nt in range(2):
                        w0 = []
                        if kc == 0 and nt == 0:
                            w0 = [lt[-1], st.get(("blkf", which, sl))]
                        if nt == 0:
                            w0 = list(w0) + [wt_[kc]]
                        tok = P.emit("T", lambda e, kc=kc, nt=nt, tb=tb, ps=ps,
                                     w_=w_: e.matmul(
                            ps[:, nt, :], ln[:, kc, tb * 128:(tb + 1) * 128],
                            w_[:, kc, nt * 512:(nt + 1) * 512],
                            start=(kc == 0), stop=(kc == KC - 1)),
                            waits=w0,
                            inc="sT" if (kc == KC - 1 and nt == 1) else None)
                te = P.emit("S", lambda e, ps=ps, blk=blk, sl=sl: e.copy(
                    out=blk[:, sl, :], in_=ps[:, :, :].rearrange(
                        "p a b -> p (a b)")), waits=[tok], inc="sS", drain=True)
                st[("blkev", which, sl)] = te
            for pc in range(8):
                w0 = []
                if pc == 0:
                    w0 = [st[("blkev", "k", sl)], st[("blkev", "v", sl)]]
                tpair = P.emit("T", lambda e, pc=pc, sl=sl, tb=tb: e.matmul(
                    psP[:, pc, :], kblk[:, sl, pc * 128:(pc + 1) * 128],
                    vblk[:, sl, pc * 128:(pc + 1) * 128],
                    start=(tb == 0 and pc % 4 == 0),
                    stop=(tb == 15 and pc % 4 == 3)),
                    waits=w0, inc="sT" if pc == 7 else None)
            st[("blkf", "k", sl)] = tpair
            st[("blkf", "v", sl)] = tpair
        st["l1a_kv_done"] = tpair
        tbd = []
        for pc in range(8):
            P.emit("S", lambda e, pc=pc: e.mul(
                out=bd[0:64, pc, 0:64], in_=psP[0:64, pc, 0:64],
                mul=consts[0:64, 7:8]), waits=[tpair])
            tbd.append(P.emit("S", lambda e, pc=pc: e.mul(
                out=bd[64:128, pc, 64:128], in_=psP[64:128, pc, 64:128],
                mul=consts[64:128, 7:8]), inc="sS", drain=True))
        AB.release()
        for h in range(2):
            c0 = h * HT
            AB.mark()
            q = AB.alloc([128, KC, HT])
            wslot_a = AB.alloc([128, 8 * D])
            wq, wqt = load_weight(H["wqa"], KC, D, into=wslot_a,
                                  wait0=st.get("l1a_wq_free")
                                  or st["l1a_kv_done"])

            def evac_q(m, pg, waits, q=q):
                tq = P.emit("S", lambda e, m=m, pg=pg, q=q: e.copy(
                    out=q[:, m, :], in_=pg[:, :, :].rearrange(
                        "p a b -> p (a b)")), waits=waits, inc="sS", drain=True)
                st[("qv", m)] = tq
                return tq

            gemm(wq, KC, list(range(KC)), lambda k: ln[:, k, :], c0, HT,
                 evac_q, waits0=[lt[-1]], wk=wqt)
            st["l1a_wq_free"] = st["last_mm"]
            attn = AB.alloc([128, KC, HT])
            for ec in range(8):
                g = st["ppg"]; pg = pp[g % 2]
                w0 = [tbd[-1], st[("qv", ec)]]
                if st["ppfree"][g % 2] is not None:
                    w0.append(st["ppfree"][g % 2])
                tok = None
                for nt in range(2):
                    tok = P.emit("T", lambda e, ec=ec, nt=nt, pg=pg, q=q:
                                 e.matmul(
                        pg[:, nt, :], bd[:, ec, :],
                        q[:, ec, nt * 512:(nt + 1) * 512],
                        start=True, stop=True),
                        waits=w0 if nt == 0 else [],
                        inc="sT" if nt == 1 else None)
                ta = P.emit("S", lambda e, ec=ec, pg=pg, attn=attn: e.copy(
                    out=attn[:, ec, :], in_=pg[:, :, :].rearrange(
                        "p a b -> p (a b)")), waits=[tok], inc="sS", drain=True)
                st[("at", ec)] = ta
                st["ppfree"][g % 2] = ta
                st["ppg"] += 1
            wo, wot = load_weight(H["woa"], KC, D, into=wslot_a,
                                   wait0=st["last_mm"])

            def evac_oa(m, pg, waits, c0=c0):
                return P.emit("V", lambda e, m=m, pg=pg, c0=c0: e.tensor_add(
                    R[:, m, c0:c0 + HT], R[:, m, c0:c0 + HT],
                    pg[:, :, :].rearrange("p a b -> p (a b)")),
                    waits=waits, inc="sV", drain=True)

            gemm(wo, KC, list(range(KC)), lambda k, attn=attn: attn[:, k, :],
                 0, HT, evac_oa,
                 waits0=[st[("at", m)] for m in range(8)], wk=wot)
            st["l1a_wq_free"] = st["last_mm"]
            AB.release()
        AB.release()

    def sub_l1ffn():
        HT = T // 2
        AB.mark()
        st["l1f_wfree"] = None
        wslot = AB.alloc([128, 8 * D])
        t_ab, t_scr, t_mneg = ln_stats(lambda k: R[:, k, :])
        ln = AB.alloc([128, KC, HT])
        h1 = AB.alloc([128, KC, HT])
        h2 = AB.alloc([128, KC, HT])
        dtmp = AB.alloc([128, 2, HT])
        def emit_lt(c0, wtok):
            lts = []
            for k in range(KC):
                lts.append(P.emit("V", lambda e, k=k, c0=c0: e.tensor_add(
                    ln[:, k, :], R[:, k, c0:c0 + HT], mneg[:, c0:c0 + HT]),
                    waits=[t_mneg, wtok], inc="sV"))
            return lts

        lt = emit_lt(0, st.get("l1f_lnf"))
        for h in range(2):
            c0 = h * HT

            def evac_h(dst, scale_ab=False):
                def f(m, pg, waits, dst=dst, scale_ab=scale_ab):
                    if scale_ab:
                        t_ = P.emit("V", lambda e, m=m, pg=pg, dst=dst, c0=c0:
                                    e.tensor_mul(
                            dst[:, m, :], pg[:, :, :].rearrange(
                                "p a b -> p (a b)"), a_b[:, c0:c0 + HT]),
                            waits=list(waits) + [t_ab], inc="sV", drain=True)
                    else:
                        t_ = P.emit("S", lambda e, m=m, pg=pg, dst=dst: e.copy(
                            out=dst[:, m, :], in_=pg[:, :, :].rearrange(
                                "p a b -> p (a b)")), waits=waits,
                            inc="sS", drain=True)
                    st[("hv", m)] = t_
                    return t_
                return f

            w1, w1t = load_weight(H["wf1"], KC, D, into=wslot,
                                  wait0=st["l1f_wfree"])
            gemm(w1, KC, list(range(KC)), lambda k: ln[:, k, :], 0, HT,
                 evac_h(h1, True), wk=w1t, rk=lt)
            if h == 0:
                lt = emit_lt(HT, st["last_mm"])
            w2, w2t = load_weight(H["wf2"], KC, D, into=wslot,
                                  wait0=st["last_mm"])
            gemm(w2, KC, list(range(KC)), lambda k: h1[:, k, :], 0, HT,
                 evac_h(h2), waits0=[st[("hv", m)] for m in range(KC)], wk=w2t)
            w3, w3t = load_weight(H["wf3"], KC, D, into=wslot,
                                  wait0=st["last_mm"])

            def evac_t(m, pg, waits, c0=c0):
                sl = m % 2
                tt_ = P.emit("S", lambda e, m=m, pg=pg, sl=sl: e.activation(
                    out=dtmp[:, sl, :], in_=pg[:, :, :].rearrange(
                        "p a b -> p (a b)"), func=ACT.Tanh,
                    bias=vecs[:, VEC["f1b"] + m:VEC["f1b"] + m + 1]),
                    waits=list(waits) + [st.get(("dt2", sl))],
                    inc="sS", drain=True)
                st[("dt2", sl)] = P.emit("V", lambda e, m=m, sl=sl, c0=c0:
                                         e.tensor_add(
                    R[:, m, c0:c0 + HT], R[:, m, c0:c0 + HT], dtmp[:, sl, :]),
                    waits=[tt_], inc="sV", drain=True)
                return tt_

            gemm(w3, KC, list(range(KC)), lambda k: h2[:, k, :], 0, HT,
                 evac_t, waits0=[st[("hv", m)] for m in range(KC)], wk=w3t)
            st["l1f_wfree"] = st["last_mm"]
            st["l1f_lnf"] = st["last_mm"]
        AB.release()

    def sub_l1b():
        QT = 512
        AB.mark()
        t_ab, t_scr, t_mneg = ln_stats(lambda k: R[:, k, :])
        st["l1b_wfree"] = None
        for xi, nm in enumerate(("betaq", "betak", "betav", "betao")):
            P.emit("V", lambda e, xi=xi, nm=nm: e.tensor_scalar(
                out=smalls[:, 24 + 8 * xi:32 + 8 * xi],
                in0=vecs[:, VEC[nm]:VEC[nm] + 8],
                scalar1=consts[:, 1:2], scalar2=consts[:, 2:3],
                op0=ALU.mult, op1=ALU.add), waits=[st["tvec"], t_cb])
        P.emit("V", lambda e: e.tensor_scalar(
            out=smalls[:, 113:114], in0=vecs[:, VEC["scale_b"]:VEC["scale_b"] + 1],
            scalar1=2.0, scalar2=None, op0=ALU.mult))
        t_thr = P.emit("V", lambda e: e.reciprocal(
            out=smalls[:, 112:113], in_=smalls[:, 113:114]),
            inc="sV", drain=True)
        Q8 = AB.alloc([128, KC, T], FP8E5)
        Vtm = AB.alloc([128, 16, D], FP8)
        Ktm = AB.alloc([128, 16, D], FP8)
        AB.mark()
        wslot = AB.alloc([128, 8 * D])
        lnq = AB.alloc([128, KC, QT])
        pre = AB.alloc([128, KC, QT])
        spk = AB.alloc([128, KC, QT])
        for qt in range(4):
            c0 = qt * QT
            lt = ln_mat(lambda k, lnq=lnq: lnq[:, k, :], lambda k: R[:, k, :],
                        range(KC), c0, QT,
                        [t_mneg, st.get("l1b_wfree")], t_ab)
            st["l1b_lnmat_last"] = lt[-1]
            fin_prev = st.pop("l1b_fin", None)
            if fin_prev:
                fin_prev()
            for X, wh, bbase, gbase, ombase in (
                    ("k", "wkb", VEC["bk"], VEC["gk"], 32),
                    ("v", "wvb", VEC["bv"], VEC["gv"], 40),
                    ("q", "wqb", VEC["bq"], VEC["gq"], 24)):
                wx, wxt = load_weight(H[wh], KC, D, into=wslot,
                                      wait0=st["l1b_wfree"])

                def evac_p(m, pg, waits, bbase=bbase):
                    t_ = P.emit("S", lambda e, m=m, pg=pg, bbase=bbase:
                                e.activation(
                        out=pre[:, m, :], in_=pg[:, 0, :], func=ACT.Identity,
                        bias=vecs[:, bbase + m:bbase + m + 1]),
                        waits=list(waits) + [st.get("prejunk")],
                        inc="sS", drain=True)
                    st[("pv", m)] = t_
                    return t_

                gemm(wx, KC, list(range(KC)), lambda k, lnq=lnq: lnq[:, k, :],
                     0, QT, evac_p, wk=wxt, rk=lt)
                st["l1b_wfree"] = st["last_mm"]
                if X == "q":
                    pv = [st[("pv", m)] for m in range(KC)]

                    def fin(c0=c0, pv=pv, gbase=gbase, ombase=ombase):
                        spk_toks = ln_spike(pre, pv, gbase, ombase,
                                            lambda k: spk[:, k, :], spk)
                        st["prejunk"] = spk_toks[-1]
                        for k in range(KC):
                            P.emit("S", lambda e, k=k, c0=c0: e.copy(
                                out=Q8[:, k, c0:c0 + QT], in_=spk[:, k, :]),
                                waits=[spk_toks[k]], inc="sS", drain=True)
                    st["l1b_fin"] = fin
                else:
                    spk_toks = ln_spike(pre, [st[("pv", m)] for m in range(KC)],
                                        gbase, ombase, lambda k: spk[:, k, :],
                                        spk)
                    dst = Ktm if X == "k" else Vtm
                    for ec in range(KC):
                        for lb in range(4):
                            slot = ((ec * 4 + lb) % 2) * 4
                            tb = qt * 4 + lb
                            ttr = P.emit("T", lambda e, ec=ec, lb=lb, slot=slot:
                                         e.transpose(
                                psP[:, slot, 0:64].bitcast(BF16),
                                spk[:, ec, lb * 128:(lb + 1) * 128],
                                identb[:]),
                                waits=[spk_toks[ec], t_idb,
                                       st.get(("tps", slot))],
                                inc="sT")
                            st[("tps", slot)] = P.emit(
                                "S", lambda e, slot=slot, tb=tb, ec=ec, dst=dst:
                                e.copy(out=dst[:, tb, ec * 128:(ec + 1) * 128],
                                       in_=psP[:, slot, 0:64].bitcast(BF16)),
                                waits=[ttr], inc="sS", drain=True)
                if X == "k":
                    tks = col_sums(lambda k: spk[:, k, :], KC, 0,
                                   lambda e, i, p: e.activation(
                                       out=rowv(i), in_=p, func=ACT.Copy),
                                   spk_toks, cols=QT)
                    st["tksb"] = bcast_row(0, lambda nt: ksb[:, :], 1.0, tks,
                                           cols=QT)[-1]
                if X == "v":
                    for k in range(KC):
                        P.emit("V", lambda e, k=k, qt=qt: e.scalar_tensor_tensor(
                            out=pre[:, k, :], in0=spk[:, k, :], scalar=0.0,
                            in1=ksb[:], op0=ALU.bypass, op1=ALU.mult,
                            accum_out=smalls[:, 88 + qt * 8 + k:89 + qt * 8 + k]),
                            waits=[st["tksb"], spk_toks[k]])
                        st["prejunk"] = P.emit(
                            "V", lambda e, k=k, qt=qt: e.scalar_tensor_tensor(
                                out=pre[:, k, :], in0=spk[:, k, :], scalar=0.0,
                                in1=spk[:, k, :], op0=ALU.bypass, op1=ALU.max,
                                accum_out=smalls[:, 56 + qt * 8 + k:
                                                 57 + qt * 8 + k]),
                            inc="sV", drain=True)
        st.pop("l1b_fin")()
        AB.release()
        if L1BST < 2:
            AB.release()
            return
        for k in range(KC):
            P.emit("V", lambda e, k=k: e.reduce_sum(
                out=smalls[:, 120 + k:121 + k],
                in_=smalls[:, 56 + k:88 + k:8], axis=mybir.AxisListType.X))
            P.emit("V", lambda e, k=k: e.reduce_sum(
                out=smalls[:, 128 + k:129 + k],
                in_=smalls[:, 88 + k:120 + k:8], axis=mybir.AxisListType.X))
        t_sums = P.emit("V", lambda e: e.tensor_scalar(
            out=smalls[:, 136:144], in0=smalls[:, 128:136],
            scalar1=consts[:, 1:2], scalar2=None, op0=ALU.mult),
            inc="sV", drain=True)
        t_mnegfree = st["l1b_lnmat_last"]
        tpairs = []
        for pc in range(8):
            w0 = []
            if pc == 0:
                w0 = [st.get(("tps", 0)), st.get(("tps", 4))]
            tpair = None
            for tb in range(16):
                tpair = P.emit("T", lambda e, tb=tb, pc=pc: e.matmul(
                    psP[:, pc, :], Ktm[:, tb, pc * 128:(pc + 1) * 128],
                    Vtm[:, tb, pc * 128:(pc + 1) * 128],
                    start=(tb == 0 and pc % 4 == 0),
                    stop=(tb == 15 and pc % 4 == 3)),
                    waits=w0 if tb == 0 else [],
                    inc="sT" if tb == 15 else None)
            tpairs.append(tpair)
        tpair = tpairs[-1]
        AB.mark()
        pA = AB.alloc([128, 8, 128], FP8E5)
        pB = AB.alloc([128, 8, 128], FP8E5)
        pC = AB.alloc([128, 8, 128], FP8E5)
        pD = AB.alloc([128, 8, 128], FP8E5)
        P.emit("V", lambda e: e.memset(pA[:, :, :], 0.0))
        P.emit("V", lambda e: e.memset(pB[:, :, :], 0.0))
        P.emit("V", lambda e: e.memset(pC[:, :, :], 0.0))
        P.emit("V", lambda e: e.memset(pD[:, :, :], 0.0))
        tpc = []
        I32 = mybir.dt.int32
        for pc in range(8):
            fm2 = mneg2[:, 0:64]
            im2 = mneg2[:, 64:128].bitcast(I32)
            ir5 = mneg2[:, 128:192].bitcast(I32)
            ir6 = mneg2[:, 192:256].bitcast(I32)
            ir8 = mneg2[:, 256:320].bitcast(I32)
            fr5 = mneg2[:, 320:384]
            fr6 = mneg2[:, 384:448]
            fr8 = mneg2[:, 448:512]
            P.emit("V", lambda e, pc=pc: e.tensor_scalar(
                out=mneg2[0:64, 0:64], in0=psP[0:64, pc, 0:64],
                scalar1=2.0, scalar2=None, op0=ALU.mult),
                waits=[tpairs[(pc // 4) * 4 + 3]])
            P.emit("V", lambda e, pc=pc: e.tensor_scalar(
                out=mneg2[64:128, 0:64], in0=psP[64:128, pc, 64:128],
                scalar1=2.0, scalar2=None, op0=ALU.mult),
                waits=[tpairs[(pc // 4) * 4 + 3]])
            P.emit("V", lambda e, fm2=fm2, im2=im2: e.tensor_copy(
                out=im2, in_=fm2))
            P.emit("V", lambda e, im2=im2, ir5=ir5: e.tensor_scalar(
                out=ir5, in0=im2, scalar1=511, scalar2=None,
                op0=ALU.bitwise_and))
            P.emit("V", lambda e, im2=im2, ir6=ir6: e.tensor_scalar(
                out=ir6, in0=im2, scalar1=63, scalar2=None,
                op0=ALU.bitwise_and))
            P.emit("V", lambda e, im2=im2, ir8=ir8: e.tensor_scalar(
                out=ir8, in0=im2, scalar1=7, scalar2=None,
                op0=ALU.bitwise_and))
            P.emit("V", lambda e, ir5=ir5, fr5=fr5: e.tensor_copy(
                out=fr5, in_=ir5))
            P.emit("V", lambda e, ir6=ir6, fr6=fr6: e.tensor_copy(
                out=fr6, in_=ir6))
            P.emit("V", lambda e, ir8=ir8, fr8=fr8: e.tensor_copy(
                out=fr8, in_=ir8))
            tk = None
            for lo, hi in ((0, 64), (64, 128)):
                P.emit("V", lambda e, pc=pc, lo=lo, hi=hi:
                       e.tensor_sub(pA[lo:hi, pc, lo:hi],
                                    mneg2[lo:hi, 0:64],
                                    mneg2[lo:hi, 320:384]))
                P.emit("V", lambda e, pc=pc, lo=lo, hi=hi:
                       e.tensor_sub(pB[lo:hi, pc, lo:hi],
                                    mneg2[lo:hi, 320:384],
                                    mneg2[lo:hi, 384:448]))
                P.emit("V", lambda e, pc=pc, lo=lo, hi=hi:
                       e.tensor_sub(pC[lo:hi, pc, lo:hi],
                                    mneg2[lo:hi, 384:448],
                                    mneg2[lo:hi, 448:512]))
                tk = P.emit("V", lambda e, pc=pc, lo=lo, hi=hi:
                            e.tensor_scalar(
                    pD[lo:hi, pc, lo:hi], mneg2[lo:hi, 448:512], 0.0, None,
                    op0=ALU.add), inc="sV" if hi == 128 else None,
                    drain=(hi == 128))
            tpc.append(tk)
        if L1BST < 3:
            AB.release()
            AB.release()
            return
        bias8 = mneg[:, :].bitcast(FP8).rearrange("p (mb l) -> p mb l",
                                                  mb=16, l=QT)
        wo, wot = load_weight(H["wob"], KC, D, wait0=st["l1b_wfree"])
        spkout = AB.alloc([128, KC, QT])
        opre = AB.alloc([128, KC, QT])
        st["bias8f"] = None
        for lq in range(4):
            c0 = lq * QT
            tb8 = P.emit("G", lambda e, c0=c0: e.dma_start(
                out=bias8[:, :, :], in_=H["biasmx"].rearrange(
                    "(mb p) l -> p mb l", p=128)[:, :, c0:c0 + QT]),
                waits=[st["bias8f"], t_mnegfree,
                       st.get("mneg_dma")], inc="dG")
            st["mneg_dma"] = tb8
            tqs = col_sums(lambda k: Q8[:, k, :], KC, 0,
                           lambda e, i, p: e.activation(
                               out=rowv(i), in_=p, func=ACT.Identity,
                               scale=consts[0:1, 1:2], bias=consts[0:1, 6:7]),
                           None, c0=c0, cols=QT, ones=consts85[:, 0:1])
            ta2 = bcast_row(0, lambda nt: ksb[:, :], 1.0, tqs, cols=QT)
            for ec in range(8):
                g = st["ppg"]; pg = pp[g % 2]
                w0 = [tb8]
                if st["ppfree"][g % 2] is not None:
                    w0.append(st["ppfree"][g % 2])
                tok = None
                for mb in range(16):
                    tok = P.emit("T", lambda e, mb=mb, ec=ec, pg=pg: e.matmul(
                        pg[:, 0, :], Vtm[:, mb, ec * 128:(ec + 1) * 128],
                        bias8[:, mb, :], start=(mb == 0), stop=False),
                        waits=w0 if mb == 0 else [], inc=None)
                for pi, piece in enumerate((pA, pB, pC, pD)):
                    tok = P.emit("T", lambda e, ec=ec, pg=pg, piece=piece,
                                 c0=c0: e.matmul(
                        pg[:, 0, :], piece[:, ec, :], Q8[:, ec, c0:c0 + QT],
                        start=False, stop=(piece is pD)),
                        waits=[tpc[ec], t_sums] if pi == 0 else [],
                        inc="sT" if pi == 3 else None)
                sc3 = ab2 if ec % 2 == 0 else mneg2
                t3 = P.emit("V", lambda e, ec=ec, pg=pg, sc3=sc3:
                            e.scalar_tensor_tensor(
                    out=sc3[:, :], in0=ksb[:, :],
                    scalar=smalls[:, 120 + ec:121 + ec], in1=pg[:, 0, :],
                    op0=ALU.mult, op1=ALU.add),
                    waits=[tok, ta2[-1], st.get(("ab2f", ec % 2)),
                           tpc[-1]], inc="sV", drain=True)
                tso = P.emit("V", lambda e, ec=ec, sc3=sc3: e.tensor_scalar(
                    out=spkout[:, ec, :], in0=sc3[:, :],
                    scalar1=smalls[:, 136 + ec:137 + ec],
                    scalar2=smalls[:, 112:113],
                    op0=ALU.add, op1=ALU.is_ge),
                    waits=[t_thr], inc="sV", drain=True)
                st[("ab2f", ec % 2)] = tso
                st[("so", ec)] = tso
                st["ppfree"][g % 2] = t3
                st["ppg"] += 1
                st["bias8f"] = tok

            def evac_ob(m, pg, waits):
                t_ = P.emit("S", lambda e, m=m, pg=pg: e.activation(
                    out=opre[:, m, :], in_=pg[:, 0, :], func=ACT.Identity,
                    bias=vecs[:, VEC["bo"] + m:VEC["bo"] + m + 1]),
                    waits=waits, inc="sS", drain=True)
                st[("pv", m)] = t_
                return t_

            gemm(wo, KC, list(range(KC)), lambda k: spkout[:, k, :], 0, QT,
                 evac_ob, waits0=[st[("so", m)] for m in range(8)], wk=wot)
            dsp = ln_spike(opre, [st[("pv", m)] for m in range(KC)],
                           VEC["go"], 48, lambda k: spkout[:, k, :], spkout)
            for k in range(KC):
                P.emit("V", lambda e, k=k, c0=c0: e.tensor_add(
                    R[:, k, c0:c0 + QT], R[:, k, c0:c0 + QT], spkout[:, k, :]),
                    waits=[dsp[k]])
        P.emit("V", lambda e: e.nop(), inc="sV", drain=True)
        AB.release()
        AB.release()

    subs = [sub_mamba, sub_ffn0, sub_attnb0, sub_ffn0,
            sub_l1a, sub_l1ffn, sub_l1b, sub_l1ffn]
    for f in subs[:NSUB]:
        f()
        st["sl_last_mm"] = st.get("last_mm")
        P.barrier()

    t_out = P.emit("G", lambda e: e.dma_start(
        out=outT.rearrange("(c p) t -> p c t", p=128), in_=R[:]), inc="dG")
    P.emit("G", lambda e: e.nop(), waits=[t_out])



def _pack_vec(v):
    v = np.asarray(v, np.float32)
    return v.reshape(v.shape[0] // 128, 128).T


def make_vecs(inp):
    cols = np.zeros((128, VEC_COLS), np.float32)

    def put(name, arr):
        a = _pack_vec(arr)
        cols[:, VEC[name]:VEC[name] + a.shape[1]] = a

    cw = np.asarray(inp["L0_attn_a_conv_w"], np.float32).reshape(E_INNER, 4)
    for c in range(16):
        for j in range(4):
            cols[:, VEC["convw"] + c * 4 + j] = cw[c * 128:(c + 1) * 128, j]
    put("convb", inp["L0_attn_a_conv_b"])
    put("mu_r", inp["L0_ffn_mu_r"]); put("mu_k", inp["L0_ffn_mu_k"])
    put("b_in", inp["L0_attn_b_b_in"]); put("A0", inp["L0_attn_b_A"])
    put("C0", inp["L0_attn_b_C"]); put("b_out", inp["L0_attn_b_b_out"])
    for nm, key in [("bq", "L1_attn_b_bq"), ("bk", "L1_attn_b_bk"),
                    ("bv", "L1_attn_b_bv"), ("bo", "L1_attn_b_bo"),
                    ("gq", "L1_attn_b_gq"), ("gk", "L1_attn_b_gk"),
                    ("gv", "L1_attn_b_gv"), ("go", "L1_attn_b_go"),
                    ("betaq", "L1_attn_b_betaq"), ("betak", "L1_attn_b_betak"),
                    ("betav", "L1_attn_b_betav"), ("betao", "L1_attn_b_betao")]:
        put(nm, inp[key])
    cols[:, VEC["scale_b"]] = np.float32(np.asarray(inp["L1_attn_b_scale"]))
    put("f1b", inp["L1_ffn_bias"])
    return cols


def make_biasmx():
    i = np.arange(T, dtype=np.float64)
    dist = np.abs(i[:, None] - i[None, :])
    b = np.maximum(np.ceil(np.log2((T - 1.0) / (dist + 1.0))), 0.0)
    r32 = (np.float32(T - 1.0) / (dist.astype(np.float32) + np.float32(1.0)))
    b32 = np.maximum(np.ceil(np.log2(r32)), 0.0)
    assert np.array_equal(b, b32.astype(np.float64))
    return b.astype(ml_dtypes.float8_e4m3)


_NC_CACHE = {}


def kernel(**inp):
    x = np.asarray(inp["L0_attn_a_hidden_states"], np.float32)
    B = x.shape[0]
    if "p" not in _NC_CACHE:
        _NC_CACHE["p"] = build_program()
    nc = _NC_CACHE["p"]

    def ct(a):
        return np.ascontiguousarray(np.asarray(a, np.float32))

    shared = {
        "w_inproj": ct(np.asarray(inp["L0_attn_a_in_proj_w"])[:E_INNER, :].T),
        "w_outproj": ct(np.asarray(inp["L0_attn_a_out_proj_w"]).T),
        "w0r": ct(np.asarray(inp["L0_ffn_W_r"]).T),
        "w0k": ct(np.asarray(inp["L0_ffn_W_k"]).T),
        "w0v": ct(np.asarray(inp["L0_ffn_W_v"]).T),
        "w_in0": ct(inp["L0_attn_b_W_in"]),
        "w_out0": ct(inp["L0_attn_b_W_out"]),
        "wqa": ct(np.asarray(inp["L1_attn_a_W_q"]).T),
        "wka": ct(np.asarray(inp["L1_attn_a_W_k"]).T),
        "wva": ct(np.asarray(inp["L1_attn_a_W_v"]).T),
        "woa": ct(np.asarray(inp["L1_attn_a_W_o"]).T),
        "wqb": ct(np.asarray(inp["L1_attn_b_Wq"]).T),
        "wkb": ct(np.asarray(inp["L1_attn_b_Wk"]).T),
        "wvb": ct(np.asarray(inp["L1_attn_b_Wv"]).T),
        "wob": ct(np.asarray(inp["L1_attn_b_Wo"]).T),
        "wf1": ct(inp["L1_ffn_weight"]),
        "wf2": ct(inp["L1_ffn_aux_weight1"]),
        "wf3": ct(inp["L1_ffn_aux_weight2"]),
        "vecs": make_vecs(inp),
        "ident": np.eye(128, dtype=np.float32),
        "biasmx": make_biasmx(),
    }
    in_maps = []
    for b in range(B):
        m = dict(shared)
        m["xT"] = ct(x[b].T)
        in_maps.append(m)
    res = run_bass_kernel_spmd(nc, in_maps, core_ids=list(range(B)))
    out = np.stack([np.asarray(r["outT"], np.float32).T for r in res.results])
    return out


# version 32 (final)
# speedup vs baseline: 1.0396x; 1.0138x over previous
import os

import numpy as np
import ml_dtypes

import concourse.bass as bass
import concourse.mybir as mybir
from concourse.bass_utils import run_bass_kernel_spmd

F32 = mybir.dt.float32
BF16 = mybir.dt.bfloat16
FP8 = mybir.dt.float8e4
FP8E5 = mybir.dt.float8e5
ALU = mybir.AluOpType
ACT = mybir.ActivationFunctionType

T = 2048
D = 1024
KC = 8
E_INNER = 2048
EPS = 1e-5
NSUB = int(os.environ.get("NSUB", "8"))
STAGE = int(os.environ.get("STAGE", "99"))
L1BST = int(os.environ.get("L1BST", "9"))

ENGINES = ("T", "V", "S", "G", "Y")
DMA_SEMS = ("dY0", "dY1", "dY2", "dY3", "dYC", "dG", "dS")


class Prog:
    def __init__(self):
        self.ops = {e: [] for e in ENGINES}
        self.cnt = {s: 0 for s in ("sT", "sV", "sS", "sG") + DMA_SEMS}
        self.waited = {e: {} for e in ENGINES}

    OWN = {"T": "sT", "V": "sV", "S": "sS", "G": "sG"}

    def emit(self, eng, fn, waits=(), inc=None, drain=None, nodrain=False):
        own = self.OWN.get(eng)
        waits = [w for w in waits if w is not None and w[0] != own]
        eff = []
        for s, v in waits:
            if eng != "G" and self.waited[eng].get(s, 0) >= v:
                continue
            self.waited[eng][s] = max(self.waited[eng].get(s, 0), v)
            eff.append((s, v))
        tok = None
        if inc is not None:
            amt = 16 if inc in DMA_SEMS else 1
            self.cnt[inc] += amt
            tok = (inc, self.cnt[inc])
        is_dma = inc in DMA_SEMS
        do_drain = ((eng in ("V", "S", "G")) or drain is True) and not is_dma \
            and not (nodrain and inc is None)

        def op(e, sems, eff=eff, fn=fn, inc=inc, do_drain=do_drain,
               is_dma=is_dma):
            for s, v in eff:
                e.wait_ge(sems[s], v)
            inst = fn(e)
            if do_drain:
                inst = e.drain()
            if inc is not None:
                inst.then_inc(sems[inc], 16 if is_dma else 1)

        import traceback
        if os.environ.get("PROV"):
            fr = traceback.extract_stack()[-3:-1]
            op._prov = " | ".join(f"{f.name}:{f.lineno}" for f in fr)
        self.ops[eng].append(op)
        return tok

    bar_toks = {}

    def barrier(self):
        pre = {s: self.cnt[s] for s in ("sT", "sV", "sS", "sG")}
        toks = {}
        for e, sem in (("T", "sT"), ("V", "sV"), ("S", "sS"), ("G", "sG")):
            w = [(s, v) for s, v in pre.items() if s != sem and v > 0]
            toks[e] = self.emit(e, lambda en: en.nop(), waits=w, inc=sem,
                                drain=True)
        for e in ENGINES:
            self.emit(e, lambda en: en.nop(),
                      waits=[toks[o] for o in toks if o != e])
        self.bar_toks = dict(toks)


class Arena:

    ESIZE = {}

    def __init__(self, handle, nbytes, name):
        self.h, self.n, self.name = handle, nbytes, name
        self.off = 0
        self.marks = []
        self.hwm = 0

    def alloc(self, shape, dt=BF16):
        assert shape[0] == 128
        es = mybir.dt.size(dt)
        n = int(np.prod(shape[1:])) * es
        self.off = (self.off + 63) // 64 * 64
        assert self.off + n <= self.n, \
            f"arena {self.name} overflow: {self.off}+{n} > {self.n}"
        ap = self.h[:, self.off:self.off + n].bitcast(dt)
        self.off += n
        self.hwm = max(self.hwm, self.off)
        if len(shape) > 2:
            dims = " ".join(f"d{i}" for i in range(1, len(shape)))
            kw = {f"d{i}": shape[i] for i in range(1, len(shape))}
            ap = ap.rearrange(f"p ({dims}) -> p {dims}", **kw)
        return ap

    def mark(self):
        self.marks.append(self.off)

    def release(self):
        self.off = self.marks.pop()


def ceil_div(a, b):
    return (a + b - 1) // b


VEC = {}
_vc = 0
for _n, _c in [
    ("convw", 64), ("convb", 16), ("mu_r", 8), ("mu_k", 8),
    ("b_in", 16), ("A0", 8), ("C0", 8), ("b_out", 8),
    ("bq", 8), ("bk", 8), ("bv", 8), ("bo", 8),
    ("gq", 8), ("gk", 8), ("gv", 8), ("go", 8),
    ("betaq", 8), ("betak", 8), ("betav", 8), ("betao", 8),
    ("scale_b", 1), ("f1b", 8),
]:
    VEC[_n] = _vc
    _vc += _c
VEC_COLS = _vc


def build_program():
    nc = bass.Bass("TRN2", target_bir_lowering=False)
    sbp = nc.SBUF_PARTITION_SIZE_BYTES

    def hin(name, shape, dt=F32):
        return nc.dram_tensor(name, list(shape), dt, kind="ExternalInput").ap()

    H = {}
    H["xT"] = hin("xT", [D, T])
    for n, sh in [("w_inproj", [D, E_INNER]), ("w_outproj", [E_INNER, D]),
                  ("w0r", [D, D]), ("w0k", [D, D]), ("w0v", [D, D]),
                  ("w_in0", [D, E_INNER]), ("w_out0", [D, D]),
                  ("wqa", [D, D]), ("wka", [D, D]), ("wva", [D, D]),
                  ("woa", [D, D]),
                  ("wqb", [D, D]), ("wkb", [D, D]), ("wvb", [D, D]),
                  ("wob", [D, D]),
                  ("wf1", [D, D]), ("wf2", [D, D]), ("wf3", [D, D])]:
        H[n] = hin(n, sh)
    H["vecs"] = hin("vecs", [128, VEC_COLS])
    H["ident"] = hin("ident", [128, 128])
    H["biasmx"] = hin("biasmx", [T, T], FP8)
    outT = nc.dram_tensor("outT", [D, T], F32, kind="ExternalOutput").ap()

    from contextlib import ExitStack
    with ExitStack() as ctx:
        E = {}
        def sbuf(name, shape, dt=F32):
            E[name] = ctx.enter_context(nc.sbuf_tensor(name, list(shape), dt))
            return E[name]

        R = sbuf("R", [128, KC, T])
        mneg = sbuf("mneg", [128, T])
        a_b = sbuf("a_b", [128, T], BF16)
        vecs = sbuf("vecs_s", [128, VEC_COLS])
        ident = sbuf("ident_s", [128, 128])
        identb = sbuf("identb_s", [128, 128], BF16)
        consts = sbuf("consts", [128, 8])
        constsb = sbuf("constsb", [128, 8], BF16)
        crowb = sbuf("crowb", [1, 2, 128], BF16)
        rows = sbuf("rows", [1, 8, 512])
        smalls = sbuf("smalls", [128, 160])
        stage = sbuf("stage", [128, 3, 1024])

        mneg2 = sbuf("mneg2", [128, 512])
        ksb = sbuf("ksb", [128, 512])
        ab2 = sbuf("ab2", [128, 512])
        consts8 = sbuf("consts8", [128, 8], FP8)
        consts85 = sbuf("consts85", [128, 8], FP8E5)
        ar_n = (nc.sbuf_top - nc.sbuf_base - 704) // 64 * 64
        arena_b = ctx.enter_context(
            nc.sbuf_tensor("arena_b", [128, ar_n], mybir.dt.uint8))
        AB = Arena(arena_b, ar_n, "main")
        A8 = AB

        psA = ctx.enter_context(nc.psum_tensor("psA", [128, 2, 512], F32))
        psB = ctx.enter_context(nc.psum_tensor("psB", [128, 2, 512], F32))
        psC = ctx.enter_context(nc.psum_tensor("psC", [128, 512], F32))
        psD = ctx.enter_context(nc.psum_tensor("psD", [128, 512], F32))
        psP = ctx.enter_context(nc.psum_tensor("psP", [128, 8, 128], F32))

        sems = {s: ctx.enter_context(nc.semaphore(s))
                for s in ("sT", "sV", "sS", "sG") + DMA_SEMS}

        P = Prog()
        _build_body(P, nc, H, outT, locals())

        with nc.Block() as block:
            @block.tensor
            def _(e):
                for op in P.ops["T"]:
                    op(e, sems)

            @block.vector
            def _(e):
                for op in P.ops["V"]:
                    op(e, sems)

            @block.scalar
            def _(e):
                for op in P.ops["S"]:
                    op(e, sems)

            @block.gpsimd
            def _(e):
                for op in P.ops["G"]:
                    op(e, sems)

            @block.sync
            def _(e):
                for op in P.ops["Y"]:
                    op(e, sems)
    return nc


def _build_body(P, nc, H, outT, E):
    R = E["R"]; mneg = E["mneg"]; a_b = E["a_b"]; vecs = E["vecs"]
    crowb = E["crowb"]

    def rowv(i):
        return rows[0:1, i, 0:256].bitcast(BF16)
    mneg2 = E["mneg2"]; ab2 = E["ab2"]; consts8 = E["consts8"]
    ksb = E["ksb"]; consts85 = E["consts85"]
    ident = E["ident"]; identb = E["identb"]
    consts = E["consts"]; constsb = E["constsb"]
    rows = E["rows"]; smalls = E["smalls"]; stage = E["stage"]
    AB = E["AB"]; A8 = E["A8"]
    psA = E["psA"]; psB = E["psB"]; psC = E["psC"]; psD = E["psD"]
    psP = E["psP"]

    st = {}

    P.emit("Y", lambda e: e.dma_start(
        out=R[:], in_=H["xT"].rearrange("(c p) t -> p c t", p=128)), inc="dYC")
    P.emit("Y", lambda e: e.dma_start(out=vecs[:], in_=H["vecs"][:]), inc="dYC")
    P.emit("Y", lambda e: e.dma_start(out=ident[:], in_=H["ident"][:]),
           inc="dYC")
    t_init = ("dYC", 48)
    t_vec = t_init
    t_id = t_init
    st_init = t_init
    P.emit("V", lambda e: e.memset(crowb[0:1, 0, :], -1.0))
    P.emit("V", lambda e: e.memset(crowb[0:1, 1, :], 1.0))
    P.emit("V", lambda e: e.memset(consts[:, 1:2], -1.0))
    P.emit("V", lambda e: e.memset(consts[:, 2:3], 1.0))
    P.emit("V", lambda e: e.memset(consts[:, 4:5], EPS))
    P.emit("V", lambda e: e.memset(consts[:, 5:6], 1.0 / 1024.0))
    P.emit("V", lambda e: e.memset(consts[:, 6:7], 64.0))
    P.emit("V", lambda e: e.memset(consts[:, 7:8], 0.125))
    P.emit("V", lambda e: e.memset(consts8[:, 0:1], 1.0))
    P.emit("V", lambda e: e.memset(consts85[:, 0:1], 1.0))
    t_cb = P.emit("V", lambda e: e.memset(constsb[:, 0:1], 1.0 / 1024.0),
                  inc="sV", drain=True)
    t_idb = P.emit("S", lambda e: e.copy(out=identb[:], in_=ident[:]),
                   waits=[t_id], inc="sS", drain=True)
    st["tvec"] = t_init

    def load_weight(hbm, kk, mm, into=None, wait0=None):
        if into is None:
            wt = AB.alloc([128, kk, mm])
        else:
            wt = into.rearrange("p (k m) -> p k m", k=kk, m=mm) \
                if kk * mm == into.free_size() else into[:, 0:kk * mm].rearrange(
                    "p (k m) -> p k m", k=kk, m=mm)
        toks = []
        src = hbm.rearrange("(k p) m -> p k m", p=128)
        pieces = [(k, o) for k in range(kk) for o in range(0, mm, 1024)]
        st.setdefault("stgi", 0)
        for k, o in pieces:
            w = min(1024, mm - o)
            i = st["stgi"] % 3
            st["stgi"] += 1
            sl = stage[:, i, 0:w]
            prev = st.get(("stg", i))
            td = P.emit("Y", lambda e, k=k, o=o, w=w, sl=sl, src=src: e.dma_start(
                out=sl, in_=src[:, k, o:o + w]), waits=[prev] if prev else [],
                inc=f"dY{i}")
            cw = [td] + ([wait0] if wait0 else [])
            cw += list(P.bar_toks.values())
            cw.append(st.get("sl_last_mm"))
            tc = P.emit("S", lambda e, k=k, o=o, w=w, sl=sl, wt=wt: e.copy(
                out=wt[:, k, o:o + w], in_=sl), waits=cw, inc="sS", drain=True)
            st[("stg", i)] = tc
            toks.append(tc)
        ktoks = [None] * kk
        for (k, o), tc in zip(pieces, toks):
            ktoks[k] = tc
        return wt, ktoks

    pp = [psA, psB]
    st["ppg"] = 0
    st["ppfree"] = {0: None, 1: None}

    def gemm(wt, kk, m_list, rhs_of, c0, cols, evac, waits0=(), wk=None,
             rk=None):
        ntiles = ceil_div(cols, 512)
        assert ntiles <= 2
        first_m = True
        for m in m_list:
            g = st["ppg"]; pg = pp[g % 2]
            w0 = list(waits0)
            if st["ppfree"][g % 2] is not None:
                w0.append(st["ppfree"][g % 2])
            tok_mm = None
            for k in range(kk):
                for nt in range(ntiles):
                    nn = min(512, cols - nt * 512)
                    last = (k == kk - 1 and nt == ntiles - 1)
                    w = w0 if (k == 0 and nt == 0) else []
                    if wk is not None and nt == 0:
                        w = list(w) + [wk[k]]
                    if rk is not None and first_m and nt == 0:
                        w = list(w) + [rk[k]]
                    tok_mm = P.emit("T", lambda e, k=k, nt=nt, nn=nn, m=m,
                                    pg=pg, wt=wt, rhs_of=rhs_of, c0=c0: e.matmul(
                        pg[:, nt, 0:nn], wt[:, k, m * 128:(m + 1) * 128],
                        rhs_of(k)[:, c0 + nt * 512:c0 + nt * 512 + nn],
                        start=(k == 0), stop=(k == kk - 1)),
                        waits=w, inc="sT" if last else None)
            st["last_mm"] = tok_mm
            tfree = evac(m, pg, [tok_mm])
            st["ppfree"][g % 2] = tfree
            st["ppg"] += 1
            first_m = False

    def col_sums(src_of, kk, base, finish, toks_in, c0=0, cols=T, ones=None):
        ones = constsb[:, 0:1] if ones is None else ones
        out_toks = []
        for nt in range(cols // 512):
            ps, key = (psC, "psC") if nt % 2 == 0 else (psD, "psD")
            tok = None
            for k in range(kk):
                w = [t_cb, toks_in[k] if toks_in else None]
                if k == 0:
                    w.append(st.get(("aux", key)))
                    w.append(st.get(("row", base + nt)))
                tok = P.emit("T", lambda e, k=k, nt=nt, ps=ps, ones=ones: e.matmul(
                    ps[0:1, 0:512], ones,
                    src_of(k)[:, c0 + nt * 512:c0 + (nt + 1) * 512],
                    start=(k == 0), stop=(k == kk - 1)),
                    waits=w, inc="sT" if k == kk - 1 else None)
            te = P.emit("S", lambda e, nt=nt, ps=ps: finish(
                e, base + nt, ps[0:1, 0:512]),
                waits=[tok], inc="sS", drain=True)
            st[("aux", key)] = te
            out_toks.append(te)
        return out_toks

    def bcast_row(base, dst_of, sign, waits, cols=T):
        lhsT = crowb[0:1, 0, :] if sign < 0 else crowb[0:1, 1, :]
        toks = []
        for nt in range(cols // 512):
            ps, key = (psC, "psC") if nt % 2 == 0 else (psD, "psD")
            w = list(waits) + [st.get(("aux", key))]
            tok = P.emit("T", lambda e, nt=nt, ps=ps, lhsT=lhsT: e.matmul(
                ps[:, 0:512], lhsT, rowv(base + nt),
                start=True, stop=True), waits=w, inc="sT")
            st[("row", base + nt)] = tok
            te = P.emit("S", lambda e, nt=nt, ps=ps: e.copy(
                out=dst_of(nt), in_=ps[:, 0:512]),
                waits=[tok, st.get("mneg_dma")], inc="sS", drain=True)
            st[("aux", key)] = te
            toks.append(te)
        return toks

    def _ln_core(src_of, kk):
        accm = [psA[0:1, 0, :], psA[0:1, 1, :], psB[0:1, 0, :], psB[0:1, 1, :]]
        accv = [psC[0:1, 0:512], psD[0:1, 0:512],
                psP[0:1, 0:4, :].rearrange("p a b -> p (a b)"),
                psP[0:1, 4:8, :].rearrange("p a b -> p (a b)")]
        akm = ["psA0", "psA1", "psB0", "psB1"]
        akv = ["psC", "psD", "psP0", "psP1"]
        AB.mark()
        xb = AB.alloc([128, 2, T])
        xq = AB.alloc([128, 2, T])
        mtoks = []
        vtoks = []
        for k in range(kk):
            tc = P.emit("S", lambda e, k=k: e.copy(
                out=xb[:, k % 2, :], in_=src_of(k)),
                waits=[("dYC", 48), st.get(("lnm", k % 2)),
                       st.get(("lnq", k % 2))],
                inc="sS", drain=True)
            tq = P.emit("V", lambda e, k=k: e.tensor_mul(
                xq[:, k % 2, :], xb[:, k % 2, :], xb[:, k % 2, :]),
                waits=[tc, st.get(("lnv", k % 2))], inc="sV", drain=True)
            st[("lnq", k % 2)] = tq
            tok = tok2 = None
            for nt in range(T // 512):
                w = [tc]
                if k == 0:
                    w += [t_cb, st.get(("aux", akm[nt]))]
                tok = P.emit("T", lambda e, k=k, nt=nt: e.matmul(
                    accm[nt], constsb[:, 0:1],
                    xb[:, k % 2, nt * 512:(nt + 1) * 512],
                    start=(k == 0), stop=(k == kk - 1)),
                    waits=w, inc="sT" if nt == T // 512 - 1 else None)
            st[("lnm", k % 2)] = tok
            mtoks.append(tok)
            for nt in range(T // 512):
                w = [tq]
                if k == 0:
                    w += [st.get(("aux", akv[nt])), st.get(("row", nt)),
                          st.get(("row", 4 + nt))]
                tok2 = P.emit("T", lambda e, k=k, nt=nt: e.matmul(
                    accv[nt], constsb[:, 0:1],
                    xq[:, k % 2, nt * 512:(nt + 1) * 512],
                    start=(k == 0), stop=(k == kk - 1)),
                    waits=w, inc="sT" if nt == T // 512 - 1 else None)
            st[("lnv", k % 2)] = tok2
            vtoks.append(tok2)
        tm = []
        tv = []
        for nt in range(T // 512):
            te = P.emit("S", lambda e, nt=nt: e.activation(
                out=rowv(nt), in_=accm[nt], func=ACT.Copy),
                waits=[mtoks[-1], st.get(("row", nt))], inc="sS", drain=True)
            st[("aux", akm[nt])] = te
            tm.append(te)
            t2 = P.emit("S", lambda e, nt=nt: e.activation(
                out=rows[0:1, 4 + nt, :], in_=accv[nt], func=ACT.Copy),
                waits=[vtoks[-1]], inc="sS", drain=True)
            st[("aux", akv[nt])] = t2
            tv.append(t2)
        t_mneg = bcast_row(0, lambda nt: mneg[:, nt * 512:(nt + 1) * 512],
                           -1.0, tm)
        NT = T // 512
        mrow = mneg[0:1, :].rearrange("p (a b) -> p a b", a=NT)
        tq1 = P.emit("V", lambda e: e.tensor_mul(
            rows[0:1, 0:NT, :], mrow, mrow),
            waits=[t_mneg[-1], tv[-1]] +
                  [st[("row", nt)] for nt in range(NT)],
            inc="sV", drain=True)
        tq2 = P.emit("V", lambda e: e.tensor_sub(
            rows[0:1, 4:4 + NT, :], rows[0:1, 4:4 + NT, :],
            rows[0:1, 0:NT, :]), waits=[tq1], inc="sV", drain=True)
        tq3 = P.emit("S", lambda e: e.activation(
            out=rows[0:1, 0:NT, :], in_=rows[0:1, 4:4 + NT, :],
            func=ACT.Sqrt, bias=consts[0:1, 4:5]),
            waits=[tq2], inc="sS", drain=True)
        rtok = P.emit("S", lambda e: e.copy(
            out=rows[0:1, 0:NT, 0:256].bitcast(BF16),
            in_=rows[0:1, 0:NT, :]),
            waits=[tq3], inc="sS", drain=True)
        rsc = [mneg2[:, :], ab2[:, :]]
        t_ab = []
        for nt in range(NT):
            ps, key = (psC, "psC") if nt % 2 == 0 else (psD, "psD")
            tok = P.emit("T", lambda e, nt=nt, ps=ps: e.matmul(
                ps[:, 0:512], crowb[0:1, 1, :], rowv(nt),
                start=True, stop=True),
                waits=[rtok, st.get(("aux", key)), st.get(("row", nt))],
                inc="sT")
            st[("row", nt)] = tok
            tv_ = P.emit("V", lambda e, nt=nt, ps=ps, r=rsc[nt % 2]:
                         e.reciprocal(
                out=r, in_=ps[:, 0:512]),
                waits=[tok], inc="sV", drain=True)
            te = P.emit("S", lambda e, nt=nt, r=rsc[nt % 2]: e.copy(
                out=a_b[:, nt * 512:(nt + 1) * 512], in_=r),
                waits=[tv_], inc="sS", drain=True)
            st[("aux", key)] = te
            t_ab.append(te)
        AB.release()
        return t_mneg, t_ab, tv[-1]

    def ln_stats(src_of, kk=KC):
        t_mneg, t_ab, t_scr = _ln_core(src_of, kk)
        return t_ab[-1], t_scr, t_mneg[-1]

    def ln_mat(dst_of, src_of, k_list, c0, cols, tok, tok_mul=None):
        mt = []
        w0 = tok if isinstance(tok, list) else [tok]
        wm = [tok_mul] if tok_mul is not None else []
        for k in k_list:
            ta = P.emit("V", lambda e, k=k: e.tensor_add(
                dst_of(k), src_of(k)[:, c0:c0 + cols],
                mneg[:, c0:c0 + cols]), waits=w0, inc="sV", drain=True)
            mt.append(P.emit("V", lambda e, k=k: e.tensor_mul(
                dst_of(k), dst_of(k), a_b[:, c0:c0 + cols]),
                waits=[ta] + wm, inc="sV", drain=True))
        return mt

    def sub_mamba():
        if STAGE < 1:
            return
        AB.mark()
        QT = 512
        halo = AB.alloc([128, 16, 4])
        slotA = AB.alloc([128, 8 * E_INNER])
        slotB = AB.alloc([128, 16 * D])
        t_ab, t_scr, t_mneg = ln_stats(lambda k: R[:, k, :])
        wt_in, tk_in = load_weight(H["w_inproj"], KC, E_INNER, into=slotA)
        wt_out, tk_out = load_weight(H["w_outproj"], 16, D, into=slotB)
        if STAGE < 2:
            AB.release()
            return
        for q in range(T // QT):
            c0 = q * QT
            AB.mark()
            ln = AB.alloc([128, KC, QT])
            s1 = AB.alloc([128, 16, QT])
            x1 = AB.alloc([128, 2, QT])
            acc = AB.alloc([128, 2, QT])
            lt = []
            for k in range(KC):
                lt.append(P.emit("V", lambda e, k=k, c0=c0, ln=ln: e.tensor_add(
                    ln[:, k, :], R[:, k, c0:c0 + QT], mneg[:, c0:c0 + QT]),
                    waits=[t_mneg], inc="sV"))

            def evac_conv(m, pg, waits, q=q, s1=s1, x1=x1, acc=acc, c0=c0):
                sl = m % 2
                w = list(waits) + [st.get(("x1f", sl)), st.get(("accf", sl)),
                                   t_ab]
                tcp = P.emit("V", lambda e, sl=sl, pg=pg, c0=c0: e.tensor_mul(
                    x1[:, sl, :], pg[:, 0, 0:QT], a_b[:, c0:c0 + QT]),
                    waits=w, inc="sV", drain=True)
                tap = VEC["convw"] + m * 4
                P.emit("V", lambda e, sl=sl, tap=tap, x1=x1, acc=acc:
                       e.tensor_scalar(
                           out=acc[:, sl, :], in0=x1[:, sl, :],
                           scalar1=vecs[:, tap + 3:tap + 4], scalar2=None,
                           op0=ALU.mult), waits=[st["tvec"], tcp])
                for j in (2, 1, 0):
                    sh = 3 - j
                    P.emit("V", lambda e, sl=sl, tap=tap, j=j, sh=sh, x1=x1,
                           acc=acc: e.scalar_tensor_tensor(
                               out=acc[:, sl, sh:QT], in0=x1[:, sl, 0:QT - sh],
                               scalar=vecs[:, tap + j:tap + j + 1],
                               in1=acc[:, sl, sh:QT],
                               op0=ALU.mult, op1=ALU.add))
                    if q > 0:
                        P.emit("V", lambda e, m=m, sl=sl, tap=tap, j=j, sh=sh,
                               acc=acc: e.scalar_tensor_tensor(
                                   out=acc[:, sl, 0:sh],
                                   in0=halo[:, m, 4 - sh:4],
                                   scalar=vecs[:, tap + j:tap + j + 1],
                                   in1=acc[:, sl, 0:sh],
                                   op0=ALU.mult, op1=ALU.add))
                tv = P.emit("V", lambda e, m=m, sl=sl, x1=x1: e.tensor_copy(
                    out=halo[:, m, 1:4], in_=x1[:, sl, QT - 3:QT]),
                    inc="sV", drain=True)
                st[("x1f", sl)] = tv
                ts = P.emit("S", lambda e, m=m, sl=sl, s1=s1, acc=acc:
                            e.activation(
                                out=s1[:, m, :], in_=acc[:, sl, :],
                                func=ACT.Silu,
                                bias=vecs[:, VEC["convb"] + m:
                                          VEC["convb"] + m + 1]),
                            waits=[tv], inc="sS", drain=True)
                st[("accf", sl)] = ts
                st[("s1", m)] = ts
                return tcp

            gemm(wt_in, KC, list(range(16)), lambda k, ln=ln: ln[:, k, :],
                 0, QT, evac_conv, wk=tk_in if q == 0 else None, rk=lt)

            def evac_res(m, pg, waits, c0=c0):
                return P.emit("V", lambda e, m=m, pg=pg, c0=c0: e.tensor_add(
                    R[:, m, c0:c0 + QT], R[:, m, c0:c0 + QT], pg[:, 0, 0:QT]),
                    waits=waits, inc="sV", drain=True)

            if STAGE < 3:
                AB.release()
                break
            w_s1 = [st[("s1", m)] for m in range(16)]
            gemm(wt_out, 16, list(range(KC)), lambda k, s1=s1: s1[:, k, :],
                 0, QT, evac_res, waits0=w_s1,
                 wk=tk_out if q == 0 else None)
            AB.release()
        AB.release()


    def sub_ffn0():
        QT = 512
        NQ = T // QT
        AB.mark()
        slot_r = AB.alloc([128, 8 * D])
        slot_k = AB.alloc([128, 8 * D])
        slot_v = AB.alloc([128, 8 * D])
        t_ab, t_scr, t_mneg = ln_stats(lambda k: R[:, k, :])
        wr, wrt = load_weight(H["w0r"], KC, D, into=slot_r)
        wk_, wkt = load_weight(H["w0k"], KC, D, into=slot_k)
        wv, wvt = load_weight(H["w0v"], KC, D, into=slot_v)
        lnp = AB.alloc([128, 4, QT + 1])
        d = AB.alloc([128, KC, QT])
        xr = AB.alloc([128, KC, QT])
        xk = AB.alloc([128, KC, QT])
        r_sb = AB.alloc([128, KC, QT])
        kact = AB.alloc([128, KC, QT])
        dtmp = AB.alloc([128, 2, QT])
        P.emit("V", lambda e: e.memset(smalls[:, 0:8], 0.0))
        st["ffn_xf"] = None
        st["ffn_xkf"] = None

        def prep_xr(q, ks=range(KC)):
            c0 = q * QT
            for k in ks:
                sl = k % 4
                P.emit("V", lambda e, k=k, sl=sl: e.tensor_copy(
                    out=lnp[:, sl, 0:1], in_=smalls[:, k:k + 1]))
                P.emit("V", lambda e, k=k, sl=sl, c0=c0: e.tensor_add(
                    lnp[:, sl, 1:QT + 1], R[:, k, c0:c0 + QT],
                    mneg[:, c0:c0 + QT]),
                    waits=[t_mneg, st["tvec"], st.get("ffn_dxf")])
                P.emit("V", lambda e, k=k, sl=sl, c0=c0: e.tensor_mul(
                    lnp[:, sl, 1:QT + 1], lnp[:, sl, 1:QT + 1],
                    a_b[:, c0:c0 + QT]),
                    waits=[t_ab])
                P.emit("V", lambda e, k=k, sl=sl: e.tensor_sub(
                    d[:, k, :], lnp[:, sl, 1:QT + 1], lnp[:, sl, 0:QT]))
                P.emit("V", lambda e, k=k, sl=sl: e.tensor_copy(
                    out=smalls[:, k:k + 1], in_=lnp[:, sl, QT:QT + 1]))

        def prep_xr_fin(q, ks=range(KC)):
            tk = None
            for k in ks:
                mur = vecs[:, VEC["mu_r"] + k:VEC["mu_r"] + k + 1]
                tk = P.emit("V", lambda e, k=k, mur=mur:
                            e.scalar_tensor_tensor(
                                out=xr[:, k, :], in0=d[:, k, :],
                                scalar=mur, in1=lnp[:, k % 4, 0:QT],
                                op0=ALU.mult, op1=ALU.add),
                            waits=[st.get("ffn_xf")] if k == ks[0] else [],
                            inc="sV" if k == ks[-1] else None, drain=False)
            return tk

        def prep_xk(q):
            c0 = q * QT
            tk = None
            for k in range(KC):
                muk = vecs[:, VEC["mu_k"] + k:VEC["mu_k"] + k + 1]
                mur = vecs[:, VEC["mu_r"] + k:VEC["mu_r"] + k + 1]
                P.emit("V", lambda e, k=k, mur=mur: e.scalar_tensor_tensor(
                    out=xk[:, k, :], in0=d[:, k, :], scalar=mur,
                    in1=xr[:, k, :], op0=ALU.mult, op1=ALU.subtract),
                    waits=[st.get("ffn_xkf")])
                tk = P.emit("V", lambda e, k=k, muk=muk:
                            e.scalar_tensor_tensor(
                                out=xk[:, k, :], in0=d[:, k, :], scalar=muk,
                                in1=xk[:, k, :], op0=ALU.mult,
                                op1=ALU.subtract),
                            inc="sV" if k == KC - 1 else None, drain=False)
            return tk

        prep_xr(0, range(0, 4))
        prep_xr_fin(0, range(0, 4))
        prep_xr(0, range(4, KC))
        t_prep = prep_xr_fin(0, range(4, KC))
        t_prepk = prep_xk(0)
        for q in range(NQ):
            c0 = q * QT

            def evac_r(m, pg, waits):
                return P.emit("S", lambda e, m=m, pg=pg: e.activation(
                    out=r_sb[:, m, :], in_=pg[:, 0, :], func=ACT.Sigmoid),
                    waits=waits, inc="sS", drain=True)

            gemm(wr, KC, list(range(KC)), lambda k: xr[:, k, :], 0, QT,
                 evac_r, waits0=[t_prep], wk=wrt if q == 0 else None)
            tmm_r = st["last_mm"]
            st["ffn_xf"] = tmm_r
            if q + 1 < NQ:
                prep_xr(q + 1, range(0, 4))

            def evac_k(m, pg, waits):
                sl = m % 2
                tr_ = P.emit("V", lambda e, m=m, pg=pg, sl=sl: e.tensor_scalar(
                    out=dtmp[:, sl, :], in0=pg[:, 0, :],
                    scalar1=0.0, scalar2=None, op0=ALU.max),
                    waits=list(waits) + [st.get(("dt", sl))],
                    inc="sV", drain=True)
                tk_ = P.emit("S", lambda e, m=m, sl=sl: e.activation(
                    out=kact[:, m, :], in_=dtmp[:, sl, :], func=ACT.Square),
                    waits=[tr_], inc="sS", drain=True)
                st[("dt", sl)] = tk_
                st[("kact", m)] = tk_
                return tr_

            gemm(wk_, KC, list(range(KC)), lambda k: xk[:, k, :], 0, QT,
                 evac_k, waits0=[t_prepk], wk=wkt if q == 0 else None)
            st["ffn_xkf"] = st["last_mm"]
            st["ffn_dxf"] = st["last_mm"]
            if q + 1 < NQ:
                prep_xr_fin(q + 1, range(0, 4))
                prep_xr(q + 1, range(4, KC))
                t_prep = prep_xr_fin(q + 1, range(4, KC))

            def evac_v(m, pg, waits, c0=c0):
                sl = m % 2
                tm_ = P.emit("V", lambda e, m=m, pg=pg, sl=sl: e.tensor_mul(
                    dtmp[:, sl, :], pg[:, 0, :], r_sb[:, m, :]),
                    waits=list(waits) + [st.get(("dt", sl))],
                    inc="sV", drain=True)
                st[("dt", sl)] = P.emit("V", lambda e, m=m, sl=sl, c0=c0:
                                        e.tensor_add(
                    R[:, m, c0:c0 + QT], R[:, m, c0:c0 + QT], dtmp[:, sl, :]),
                    inc="sV", drain=True)
                return tm_

            gemm(wv, KC, list(range(KC)), lambda k: kact[:, k, :], 0, QT,
                 evac_v, waits0=[st[("kact", m)] for m in range(KC)],
                 wk=wvt if q == 0 else None)
            if q + 1 < NQ:
                t_prepk = prep_xk(q + 1)
        AB.release()

    def sub_attnb0():
        HT = T // 2
        AB.mark()
        st["wfree2"] = None
        wslot = AB.alloc([128, 8 * E_INNER])
        t_ab, t_scr, t_mneg = ln_stats(lambda k: R[:, k, :])
        ln = AB.alloc([128, KC, HT])
        gate = AB.alloc([128, KC, HT])
        y = AB.alloc([128, KC, HT])
        vtmp = AB.alloc([128, 2, HT])
        hs = AB.alloc([128, 2, HT])
        P.emit("V", lambda e: e.memset(smalls[:, 8:16], 0.0))
        t_dec = P.emit("S", lambda e: e.activation(
            out=smalls[:, 16:24], in_=vecs[:, VEC["A0"]:VEC["A0"] + 8],
            func=ACT.Sigmoid), waits=[st["tvec"]], inc="sS", drain=True)
        st["wfree2"] = None
        for h in range(2):
            c0 = h * HT
            lt = ln_mat(lambda k, ln=ln: ln[:, k, :], lambda k: R[:, k, :],
                        range(KC), c0, HT, t_mneg, t_ab)
            win, wint = load_weight(H["w_in0"], KC, E_INNER, into=wslot,
                                    wait0=st["wfree2"])

            def evac_g(m, pg, waits):
                return P.emit("S", lambda e, m=m, pg=pg: e.activation(
                    out=gate[:, m - 8, :], in_=pg[:, :, :].rearrange(
                        "p a b -> p (a b)"), func=ACT.Sigmoid,
                    bias=vecs[:, VEC["b_in"] + m:VEC["b_in"] + m + 1]),
                    waits=waits, inc="sS", drain=True)

            gemm(win, KC, list(range(8, 16)), lambda k: ln[:, k, :], 0, HT,
                 evac_g, wk=wint, rk=lt)

            def evac_vch(m, pg, waits, h=h):
                sl = m % 2
                tv_ = P.emit("S", lambda e, m=m, pg=pg, sl=sl: e.activation(
                    out=vtmp[:, sl, :], in_=pg[:, :, :].rearrange(
                        "p a b -> p (a b)"), func=ACT.Identity,
                    bias=vecs[:, VEC["b_in"] + m:VEC["b_in"] + m + 1]),
                    waits=list(waits) + [st.get(("vt", sl))],
                    inc="sS", drain=True)
                tsc = P.emit("V", lambda e, m=m, sl=sl: e.tensor_tensor_scan(
                    out=hs[:, sl, :],
                    data0=smalls[:, 16 + m:17 + m].broadcast_to([128, HT]),
                    data1=vtmp[:, sl, :],
                    initial=smalls[:, 8 + m:9 + m],
                    op0=ALU.mult, op1=ALU.add),
                    waits=[tv_, t_dec, st.get(("hs", sl))],
                    inc="sV", drain=True)
                st[("vt", sl)] = tsc
                P.emit("V", lambda e, m=m, sl=sl: e.tensor_copy(
                    out=smalls[:, 8 + m:9 + m], in_=hs[:, sl, HT - 1:HT]))
                st[("hs", sl)] = P.emit("V", lambda e, m=m, sl=sl:
                                        e.scalar_tensor_tensor(
                    out=y[:, m, :], in0=hs[:, sl, :],
                    scalar=vecs[:, VEC["C0"] + m:VEC["C0"] + m + 1],
                    in1=gate[:, m, :], op0=ALU.mult, op1=ALU.mult),
                    inc="sV", drain=True)
                return tv_

            gemm(win, KC, list(range(8)), lambda k: ln[:, k, :], 0, HT,
                 evac_vch, waits0=[], wk=None)
            tmm_in = st["last_mm"]

            wout, woutt = load_weight(H["w_out0"], KC, D, into=wslot,
                                      wait0=tmm_in)

            def evac_o(m, pg, waits, c0=c0):
                return P.emit("V", lambda e, m=m, pg=pg, c0=c0:
                              e.scalar_tensor_tensor(
                    out=R[:, m, c0:c0 + HT],
                    in0=pg[:, :, :].rearrange("p a b -> p (a b)"),
                    scalar=vecs[:, VEC["b_out"] + m:VEC["b_out"] + m + 1],
                    in1=R[:, m, c0:c0 + HT], op0=ALU.add, op1=ALU.add),
                    waits=waits, inc="sV", drain=True)

            w_y = [st[("hs", 0)], st[("hs", 1)]]
            gemm(wout, KC, list(range(KC)), lambda k: y[:, k, :], 0, HT,
                 evac_o, waits0=w_y, wk=woutt)
            st["wfree2"] = st["last_mm"]
        AB.release()


    def ln_spike(pre, prets, gbase, ombase, out_of, scratch):
        mt = vt = None
        sqt = []
        for k in range(KC):
            tq = P.emit("S", lambda e, k=k: e.activation(
                out=scratch[:, k, :], in_=pre[:, k, :], func=ACT.Square),
                waits=[prets[k]], inc="sS", drain=True)
            sqt.append(tq)
            w = [prets[k]]
            if k == 0:
                w += [t_cb, st.get(("aux", "psC")), st.get(("row", 0))]
            mt = P.emit("T", lambda e, k=k: e.matmul(
                psC[0:1, 0:512], constsb[:, 0:1], pre[:, k, :],
                start=(k == 0), stop=(k == KC - 1)),
                waits=w, inc="sT" if k == KC - 1 else None)
            w2 = [tq]
            if k == 0:
                w2 += [st.get(("aux", "psD")), st.get(("row", 4)),
                       st.get(("row", 1))]
            vt = P.emit("T", lambda e, k=k: e.matmul(
                psD[0:1, 0:512], constsb[:, 0:1], scratch[:, k, :],
                start=(k == 0), stop=(k == KC - 1)),
                waits=w2, inc="sT" if k == KC - 1 else None)
        tme = P.emit("S", lambda e: e.activation(
            out=rowv(0), in_=psC[0:1, 0:512], func=ACT.Copy),
            waits=[mt], inc="sS", drain=True)
        st[("aux", "psC")] = tme
        tve = P.emit("S", lambda e: e.activation(
            out=rows[0:1, 4, :], in_=psD[0:1, 0:512], func=ACT.Copy),
            waits=[vt], inc="sS", drain=True)
        st[("aux", "psD")] = tve
        tmn = bcast_row(0, lambda nt: mneg2[:, :], -1.0, [tme], cols=512)
        tq1 = P.emit("V", lambda e: e.tensor_mul(
            rows[0:1, 5, :], mneg2[0:1, :], mneg2[0:1, :]),
            waits=[tmn[-1], st.get(("row", 5))], inc="sV", drain=True)
        tq2 = P.emit("V", lambda e: e.tensor_sub(
            rows[0:1, 4, :], rows[0:1, 4, :], rows[0:1, 5, :]),
            waits=[tq1, tve], inc="sV", drain=True)
        tq3 = P.emit("S", lambda e: e.activation(
            out=rows[0:1, 5, :], in_=rows[0:1, 4, :],
            func=ACT.Sqrt, bias=consts[0:1, 4:5]),
            waits=[tq2], inc="sS", drain=True)
        st[("row", 5)] = tq3
        tq4 = P.emit("V", lambda e: e.reciprocal(
            out=rows[0:1, 4, :], in_=rows[0:1, 5, :]),
            waits=[tq3], inc="sV", drain=True)
        st[("row", 4)] = tq4
        tq5 = P.emit("S", lambda e: e.copy(
            out=rowv(1), in_=rows[0:1, 4, :]),
            waits=[tq4, st.get(("row", 1))], inc="sS", drain=True)
        tab = bcast_row(1, lambda nt: ab2[:, :], 1.0, [tq5], cols=512)
        out_toks = []
        for k in range(KC):
            P.emit("V", lambda e, k=k: e.tensor_add(
                pre[:, k, :], pre[:, k, :], mneg2[:]),
                waits=[tmn[-1], sqt[k]])
            P.emit("V", lambda e, k=k: e.tensor_mul(
                pre[:, k, :], pre[:, k, :], ab2[:]), waits=[tab[-1]])
            out_toks.append(P.emit("V", lambda e, k=k: e.tensor_scalar(
                out=out_of(k), in0=pre[:, k, :],
                scalar1=vecs[:, gbase + k:gbase + k + 1],
                scalar2=smalls[:, ombase + k:ombase + k + 1],
                op0=ALU.mult, op1=ALU.is_ge),
                inc="sV", drain=True))
        return out_toks

    def sub_l1a():
        HT = T // 2
        AB.mark()
        bd = AB.alloc([128, 8, 128])
        P.emit("V", lambda e: e.memset(bd[:, :, :], 0.0))
        ln = AB.alloc([128, KC, T])
        AB.mark()
        slot_ka = AB.alloc([128, 8 * D])
        slot_va = AB.alloc([128, 8 * D])
        t_ab, t_scr, t_mneg = ln_stats(lambda k: R[:, k, :])
        wk, wkt = load_weight(H["wka"], KC, D, into=slot_ka)
        wv, wvt = load_weight(H["wva"], KC, D, into=slot_va)
        kblk = AB.alloc([128, 2, D])
        vblk = AB.alloc([128, 2, D])
        tpair = None
        lt = None
        for tb in range(16):
            if tb % 4 == 0:
                qc0 = (tb // 4) * 512
                lt = ln_mat(lambda k, qc0=qc0: ln[:, k, qc0:qc0 + 512],
                            lambda k: R[:, k, :],
                            range(KC), qc0, 512, t_mneg, t_ab)
            sl = tb % 2
            for which, w_, wt_, blk in (("k", wk, wkt, kblk), ("v", wv, wvt, vblk)):
                ps = psA if which == "k" else psB
                tok = None
                for kc in range(KC):
                    for nt in range(2):
                        w0 = []
                        if kc == 0 and nt == 0:
                            w0 = [lt[-1], st.get(("blkf", which, sl))]
                        if nt == 0:
                            w0 = list(w0) + [wt_[kc]]
                        tok = P.emit("T", lambda e, kc=kc, nt=nt, tb=tb, ps=ps,
                                     w_=w_: e.matmul(
                            ps[:, nt, :], ln[:, kc, tb * 128:(tb + 1) * 128],
                            w_[:, kc, nt * 512:(nt + 1) * 512],
                            start=(kc == 0), stop=(kc == KC - 1)),
                            waits=w0,
                            inc="sT" if (kc == KC - 1 and nt == 1) else None)
                te = P.emit("S", lambda e, ps=ps, blk=blk, sl=sl: e.copy(
                    out=blk[:, sl, :], in_=ps[:, :, :].rearrange(
                        "p a b -> p (a b)")), waits=[tok], inc="sS", drain=True)
                st[("blkev", which, sl)] = te
            for pc in range(8):
                w0 = []
                if pc == 0:
                    w0 = [st[("blkev", "k", sl)], st[("blkev", "v", sl)]]
                tpair = P.emit("T", lambda e, pc=pc, sl=sl, tb=tb: e.matmul(
                    psP[:, pc, :], kblk[:, sl, pc * 128:(pc + 1) * 128],
                    vblk[:, sl, pc * 128:(pc + 1) * 128],
                    start=(tb == 0 and pc % 4 == 0),
                    stop=(tb == 15 and pc % 4 == 3)),
                    waits=w0, inc="sT" if pc == 7 else None)
            st[("blkf", "k", sl)] = tpair
            st[("blkf", "v", sl)] = tpair
        st["l1a_kv_done"] = tpair
        tbd = []
        for pc in range(8):
            P.emit("S", lambda e, pc=pc: e.mul(
                out=bd[0:64, pc, 0:64], in_=psP[0:64, pc, 0:64],
                mul=consts[0:64, 7:8]), waits=[tpair])
            tbd.append(P.emit("S", lambda e, pc=pc: e.mul(
                out=bd[64:128, pc, 64:128], in_=psP[64:128, pc, 64:128],
                mul=consts[64:128, 7:8]), inc="sS", drain=True))
        AB.release()
        for h in range(2):
            c0 = h * HT
            AB.mark()
            q = AB.alloc([128, KC, HT])
            wslot_a = AB.alloc([128, 8 * D])
            wq, wqt = load_weight(H["wqa"], KC, D, into=wslot_a,
                                  wait0=st.get("l1a_wq_free")
                                  or st["l1a_kv_done"])

            def evac_q(m, pg, waits, q=q):
                tq = P.emit("S", lambda e, m=m, pg=pg, q=q: e.copy(
                    out=q[:, m, :], in_=pg[:, :, :].rearrange(
                        "p a b -> p (a b)")), waits=waits, inc="sS", drain=True)
                st[("qv", m)] = tq
                return tq

            gemm(wq, KC, list(range(KC)), lambda k: ln[:, k, :], c0, HT,
                 evac_q, waits0=[lt[-1]], wk=wqt)
            st["l1a_wq_free"] = st["last_mm"]
            attn = AB.alloc([128, KC, HT])
            for ec in range(8):
                g = st["ppg"]; pg = pp[g % 2]
                w0 = [tbd[-1], st[("qv", ec)]]
                if st["ppfree"][g % 2] is not None:
                    w0.append(st["ppfree"][g % 2])
                tok = None
                for nt in range(2):
                    tok = P.emit("T", lambda e, ec=ec, nt=nt, pg=pg, q=q:
                                 e.matmul(
                        pg[:, nt, :], bd[:, ec, :],
                        q[:, ec, nt * 512:(nt + 1) * 512],
                        start=True, stop=True),
                        waits=w0 if nt == 0 else [],
                        inc="sT" if nt == 1 else None)
                ta = P.emit("S", lambda e, ec=ec, pg=pg, attn=attn: e.copy(
                    out=attn[:, ec, :], in_=pg[:, :, :].rearrange(
                        "p a b -> p (a b)")), waits=[tok], inc="sS", drain=True)
                st[("at", ec)] = ta
                st["ppfree"][g % 2] = ta
                st["ppg"] += 1
            wo, wot = load_weight(H["woa"], KC, D, into=wslot_a,
                                   wait0=st["last_mm"])

            def evac_oa(m, pg, waits, c0=c0):
                return P.emit("V", lambda e, m=m, pg=pg, c0=c0: e.tensor_add(
                    R[:, m, c0:c0 + HT], R[:, m, c0:c0 + HT],
                    pg[:, :, :].rearrange("p a b -> p (a b)")),
                    waits=waits, inc="sV", drain=True)

            gemm(wo, KC, list(range(KC)), lambda k, attn=attn: attn[:, k, :],
                 0, HT, evac_oa,
                 waits0=[st[("at", m)] for m in range(8)], wk=wot)
            st["l1a_wq_free"] = st["last_mm"]
            AB.release()
        AB.release()

    def sub_l1ffn():
        HT = T // 2
        AB.mark()
        st["l1f_wfree"] = None
        wslot = AB.alloc([128, 8 * D])
        t_ab, t_scr, t_mneg = ln_stats(lambda k: R[:, k, :])
        ln = AB.alloc([128, KC, HT])
        h1 = AB.alloc([128, KC, HT])
        h2 = AB.alloc([128, KC, HT])
        dtmp = AB.alloc([128, 2, HT])
        def emit_lt(c0, wtok):
            lts = []
            for k in range(KC):
                lts.append(P.emit("V", lambda e, k=k, c0=c0: e.tensor_add(
                    ln[:, k, :], R[:, k, c0:c0 + HT], mneg[:, c0:c0 + HT]),
                    waits=[t_mneg, wtok], inc="sV"))
            return lts

        lt = emit_lt(0, st.get("l1f_lnf"))
        for h in range(2):
            c0 = h * HT

            def evac_h(dst, scale_ab=False):
                def f(m, pg, waits, dst=dst, scale_ab=scale_ab):
                    if scale_ab:
                        t_ = P.emit("V", lambda e, m=m, pg=pg, dst=dst, c0=c0:
                                    e.tensor_mul(
                            dst[:, m, :], pg[:, :, :].rearrange(
                                "p a b -> p (a b)"), a_b[:, c0:c0 + HT]),
                            waits=list(waits) + [t_ab], inc="sV", drain=True)
                    else:
                        t_ = P.emit("S", lambda e, m=m, pg=pg, dst=dst: e.copy(
                            out=dst[:, m, :], in_=pg[:, :, :].rearrange(
                                "p a b -> p (a b)")), waits=waits,
                            inc="sS", drain=True)
                    st[("hv", m)] = t_
                    return t_
                return f

            w1, w1t = load_weight(H["wf1"], KC, D, into=wslot,
                                  wait0=st["l1f_wfree"])
            gemm(w1, KC, list(range(KC)), lambda k: ln[:, k, :], 0, HT,
                 evac_h(h1, True), wk=w1t, rk=lt)
            if h == 0:
                lt = emit_lt(HT, st["last_mm"])
            w2, w2t = load_weight(H["wf2"], KC, D, into=wslot,
                                  wait0=st["last_mm"])
            gemm(w2, KC, list(range(KC)), lambda k: h1[:, k, :], 0, HT,
                 evac_h(h2), waits0=[st[("hv", m)] for m in range(KC)], wk=w2t)
            w3, w3t = load_weight(H["wf3"], KC, D, into=wslot,
                                  wait0=st["last_mm"])

            def evac_t(m, pg, waits, c0=c0):
                sl = m % 2
                tt_ = P.emit("S", lambda e, m=m, pg=pg, sl=sl: e.activation(
                    out=dtmp[:, sl, :], in_=pg[:, :, :].rearrange(
                        "p a b -> p (a b)"), func=ACT.Tanh,
                    bias=vecs[:, VEC["f1b"] + m:VEC["f1b"] + m + 1]),
                    waits=list(waits) + [st.get(("dt2", sl))],
                    inc="sS", drain=True)
                st[("dt2", sl)] = P.emit("V", lambda e, m=m, sl=sl, c0=c0:
                                         e.tensor_add(
                    R[:, m, c0:c0 + HT], R[:, m, c0:c0 + HT], dtmp[:, sl, :]),
                    waits=[tt_], inc="sV", drain=True)
                return tt_

            gemm(w3, KC, list(range(KC)), lambda k: h2[:, k, :], 0, HT,
                 evac_t, waits0=[st[("hv", m)] for m in range(KC)], wk=w3t)
            st["l1f_wfree"] = st["last_mm"]
            st["l1f_lnf"] = st["last_mm"]
        AB.release()

    def sub_l1b():
        QT = 512
        AB.mark()
        t_ab, t_scr, t_mneg = ln_stats(lambda k: R[:, k, :])
        st["l1b_wfree"] = None
        for xi, nm in enumerate(("betaq", "betak", "betav", "betao")):
            P.emit("V", lambda e, xi=xi, nm=nm: e.tensor_scalar(
                out=smalls[:, 24 + 8 * xi:32 + 8 * xi],
                in0=vecs[:, VEC[nm]:VEC[nm] + 8],
                scalar1=consts[:, 1:2], scalar2=consts[:, 2:3],
                op0=ALU.mult, op1=ALU.add), waits=[st["tvec"], t_cb])
        P.emit("V", lambda e: e.tensor_scalar(
            out=smalls[:, 113:114], in0=vecs[:, VEC["scale_b"]:VEC["scale_b"] + 1],
            scalar1=2.0, scalar2=None, op0=ALU.mult))
        t_thr = P.emit("V", lambda e: e.reciprocal(
            out=smalls[:, 112:113], in_=smalls[:, 113:114]),
            inc="sV", drain=True)
        Q8 = AB.alloc([128, KC, T], FP8E5)
        Vtm = AB.alloc([128, 16, D], FP8)
        Ktm = AB.alloc([128, 16, D], FP8)
        AB.mark()
        wslot = AB.alloc([128, 8 * D])
        lnq = AB.alloc([128, KC, QT])
        pre = AB.alloc([128, KC, QT])
        spk = AB.alloc([128, KC, QT])
        for qt in range(4):
            c0 = qt * QT
            lt = ln_mat(lambda k, lnq=lnq: lnq[:, k, :], lambda k: R[:, k, :],
                        range(KC), c0, QT,
                        [t_mneg, st.get("l1b_wfree")], t_ab)
            st["l1b_lnmat_last"] = lt[-1]
            fin_prev = st.pop("l1b_fin", None)
            if fin_prev:
                fin_prev()
            for X, wh, bbase, gbase, ombase in (
                    ("k", "wkb", VEC["bk"], VEC["gk"], 32),
                    ("v", "wvb", VEC["bv"], VEC["gv"], 40),
                    ("q", "wqb", VEC["bq"], VEC["gq"], 24)):
                wx, wxt = load_weight(H[wh], KC, D, into=wslot,
                                      wait0=st["l1b_wfree"])

                def evac_p(m, pg, waits, bbase=bbase):
                    t_ = P.emit("S", lambda e, m=m, pg=pg, bbase=bbase:
                                e.activation(
                        out=pre[:, m, :], in_=pg[:, 0, :], func=ACT.Identity,
                        bias=vecs[:, bbase + m:bbase + m + 1]),
                        waits=list(waits) + [st.get("prejunk")],
                        inc="sS", drain=True)
                    st[("pv", m)] = t_
                    return t_

                gemm(wx, KC, list(range(KC)), lambda k, lnq=lnq: lnq[:, k, :],
                     0, QT, evac_p, wk=wxt, rk=lt)
                st["l1b_wfree"] = st["last_mm"]
                if X == "q":
                    pv = [st[("pv", m)] for m in range(KC)]

                    def fin(c0=c0, pv=pv, gbase=gbase, ombase=ombase):
                        spk_toks = ln_spike(pre, pv, gbase, ombase,
                                            lambda k: spk[:, k, :], spk)
                        st["prejunk"] = spk_toks[-1]
                        for k in range(KC):
                            P.emit("S", lambda e, k=k, c0=c0: e.copy(
                                out=Q8[:, k, c0:c0 + QT], in_=spk[:, k, :]),
                                waits=[spk_toks[k]], inc="sS", drain=True)
                    st["l1b_fin"] = fin
                else:
                    spk_toks = ln_spike(pre, [st[("pv", m)] for m in range(KC)],
                                        gbase, ombase, lambda k: spk[:, k, :],
                                        spk)
                    dst = Ktm if X == "k" else Vtm
                    for ec in range(KC):
                        for lb in range(4):
                            slot = ((ec * 4 + lb) % 2) * 4
                            tb = qt * 4 + lb
                            ttr = P.emit("T", lambda e, ec=ec, lb=lb, slot=slot:
                                         e.transpose(
                                psP[:, slot, 0:64].bitcast(BF16),
                                spk[:, ec, lb * 128:(lb + 1) * 128],
                                identb[:]),
                                waits=[spk_toks[ec], t_idb,
                                       st.get(("tps", slot))],
                                inc="sT")
                            st[("tps", slot)] = P.emit(
                                "S", lambda e, slot=slot, tb=tb, ec=ec, dst=dst:
                                e.copy(out=dst[:, tb, ec * 128:(ec + 1) * 128],
                                       in_=psP[:, slot, 0:64].bitcast(BF16)),
                                waits=[ttr], inc="sS", drain=True)
                if X == "k":
                    tks = col_sums(lambda k: spk[:, k, :], KC, 0,
                                   lambda e, i, p: e.activation(
                                       out=rowv(i), in_=p, func=ACT.Copy),
                                   spk_toks, cols=QT)
                    st["tksb"] = bcast_row(0, lambda nt: ksb[:, :], 1.0, tks,
                                           cols=QT)[-1]
                if X == "v":
                    for k in range(KC):
                        P.emit("V", lambda e, k=k, qt=qt: e.scalar_tensor_tensor(
                            out=pre[:, k, :], in0=spk[:, k, :], scalar=0.0,
                            in1=ksb[:], op0=ALU.bypass, op1=ALU.mult,
                            accum_out=smalls[:, 88 + qt * 8 + k:89 + qt * 8 + k]),
                            waits=[st["tksb"], spk_toks[k]])
                        st["prejunk"] = P.emit(
                            "V", lambda e, k=k, qt=qt: e.scalar_tensor_tensor(
                                out=pre[:, k, :], in0=spk[:, k, :], scalar=0.0,
                                in1=spk[:, k, :], op0=ALU.bypass, op1=ALU.max,
                                accum_out=smalls[:, 56 + qt * 8 + k:
                                                 57 + qt * 8 + k]),
                            inc="sV", drain=True)
        st.pop("l1b_fin")()
        AB.release()
        if L1BST < 2:
            AB.release()
            return
        for k in range(KC):
            P.emit("V", lambda e, k=k: e.reduce_sum(
                out=smalls[:, 120 + k:121 + k],
                in_=smalls[:, 56 + k:88 + k:8], axis=mybir.AxisListType.X))
            P.emit("V", lambda e, k=k: e.reduce_sum(
                out=smalls[:, 128 + k:129 + k],
                in_=smalls[:, 88 + k:120 + k:8], axis=mybir.AxisListType.X))
        t_sums = P.emit("V", lambda e: e.tensor_scalar(
            out=smalls[:, 136:144], in0=smalls[:, 128:136],
            scalar1=consts[:, 1:2], scalar2=None, op0=ALU.mult),
            inc="sV", drain=True)
        t_mnegfree = st["l1b_lnmat_last"]
        tpairs = []
        for pc in range(8):
            w0 = []
            if pc == 0:
                w0 = [st.get(("tps", 0)), st.get(("tps", 4))]
            tpair = None
            for tb in range(16):
                tpair = P.emit("T", lambda e, tb=tb, pc=pc: e.matmul(
                    psP[:, pc, :], Ktm[:, tb, pc * 128:(pc + 1) * 128],
                    Vtm[:, tb, pc * 128:(pc + 1) * 128],
                    start=(tb == 0 and pc % 4 == 0),
                    stop=(tb == 15 and pc % 4 == 3)),
                    waits=w0 if tb == 0 else [],
                    inc="sT" if tb == 15 else None)
            tpairs.append(tpair)
        tpair = tpairs[-1]
        AB.mark()
        pA = AB.alloc([128, 8, 128], FP8E5)
        pB = AB.alloc([128, 8, 128], FP8E5)
        pC = AB.alloc([128, 8, 128], FP8E5)
        pD = AB.alloc([128, 8, 128], FP8E5)
        P.emit("V", lambda e: e.memset(pA[:, :, :], 0.0))
        P.emit("V", lambda e: e.memset(pB[:, :, :], 0.0))
        P.emit("V", lambda e: e.memset(pC[:, :, :], 0.0))
        P.emit("V", lambda e: e.memset(pD[:, :, :], 0.0))
        tpc = []
        I32 = mybir.dt.int32
        for pc in range(8):
            fm2 = mneg2[:, 0:64]
            im2 = mneg2[:, 64:128].bitcast(I32)
            ir5 = mneg2[:, 128:192].bitcast(I32)
            ir6 = mneg2[:, 192:256].bitcast(I32)
            ir8 = mneg2[:, 256:320].bitcast(I32)
            fr5 = mneg2[:, 320:384]
            fr6 = mneg2[:, 384:448]
            fr8 = mneg2[:, 448:512]
            P.emit("V", lambda e, pc=pc: e.tensor_scalar(
                out=mneg2[0:64, 0:64], in0=psP[0:64, pc, 0:64],
                scalar1=2.0, scalar2=None, op0=ALU.mult),
                waits=[tpairs[(pc // 4) * 4 + 3]])
            P.emit("V", lambda e, pc=pc: e.tensor_scalar(
                out=mneg2[64:128, 0:64], in0=psP[64:128, pc, 64:128],
                scalar1=2.0, scalar2=None, op0=ALU.mult),
                waits=[tpairs[(pc // 4) * 4 + 3]])
            P.emit("V", lambda e, fm2=fm2, im2=im2: e.tensor_copy(
                out=im2, in_=fm2))
            P.emit("V", lambda e, im2=im2, ir5=ir5: e.tensor_scalar(
                out=ir5, in0=im2, scalar1=511, scalar2=None,
                op0=ALU.bitwise_and))
            P.emit("V", lambda e, im2=im2, ir6=ir6: e.tensor_scalar(
                out=ir6, in0=im2, scalar1=63, scalar2=None,
                op0=ALU.bitwise_and))
            P.emit("V", lambda e, im2=im2, ir8=ir8: e.tensor_scalar(
                out=ir8, in0=im2, scalar1=7, scalar2=None,
                op0=ALU.bitwise_and))
            P.emit("V", lambda e, ir5=ir5, fr5=fr5: e.tensor_copy(
                out=fr5, in_=ir5))
            P.emit("V", lambda e, ir6=ir6, fr6=fr6: e.tensor_copy(
                out=fr6, in_=ir6))
            P.emit("V", lambda e, ir8=ir8, fr8=fr8: e.tensor_copy(
                out=fr8, in_=ir8))
            tk = None
            for lo, hi in ((0, 64), (64, 128)):
                P.emit("V", lambda e, pc=pc, lo=lo, hi=hi:
                       e.tensor_sub(pA[lo:hi, pc, lo:hi],
                                    mneg2[lo:hi, 0:64],
                                    mneg2[lo:hi, 320:384]))
                P.emit("V", lambda e, pc=pc, lo=lo, hi=hi:
                       e.tensor_sub(pB[lo:hi, pc, lo:hi],
                                    mneg2[lo:hi, 320:384],
                                    mneg2[lo:hi, 384:448]))
                P.emit("V", lambda e, pc=pc, lo=lo, hi=hi:
                       e.tensor_sub(pC[lo:hi, pc, lo:hi],
                                    mneg2[lo:hi, 384:448],
                                    mneg2[lo:hi, 448:512]))
                tk = P.emit("V", lambda e, pc=pc, lo=lo, hi=hi:
                            e.tensor_scalar(
                    pD[lo:hi, pc, lo:hi], mneg2[lo:hi, 448:512], 0.0, None,
                    op0=ALU.add), inc="sV" if hi == 128 else None,
                    drain=(hi == 128))
            tpc.append(tk)
        if L1BST < 3:
            AB.release()
            AB.release()
            return
        bias8 = mneg[:, :].bitcast(FP8).rearrange("p (mb l) -> p mb l",
                                                  mb=16, l=QT)
        wo, wot = load_weight(H["wob"], KC, D, wait0=st["l1b_wfree"])
        spkout = AB.alloc([128, KC, QT])
        opre = AB.alloc([128, KC, QT])
        st["bias8f"] = None
        for lq in range(4):
            c0 = lq * QT
            tb8 = P.emit("G", lambda e, c0=c0: e.dma_start(
                out=bias8[:, :, :], in_=H["biasmx"].rearrange(
                    "(mb p) l -> p mb l", p=128)[:, :, c0:c0 + QT]),
                waits=[st["bias8f"], t_mnegfree,
                       st.get("mneg_dma")], inc="dG")
            st["mneg_dma"] = tb8
            tqs = col_sums(lambda k: Q8[:, k, :], KC, 0,
                           lambda e, i, p: e.activation(
                               out=rowv(i), in_=p, func=ACT.Identity,
                               scale=consts[0:1, 1:2], bias=consts[0:1, 6:7]),
                           None, c0=c0, cols=QT, ones=consts85[:, 0:1])
            ta2 = bcast_row(0, lambda nt: ksb[:, :], 1.0, tqs, cols=QT)
            for ec in range(8):
                g = st["ppg"]; pg = pp[g % 2]
                w0 = [tb8]
                if st["ppfree"][g % 2] is not None:
                    w0.append(st["ppfree"][g % 2])
                tok = None
                for mb in range(16):
                    tok = P.emit("T", lambda e, mb=mb, ec=ec, pg=pg: e.matmul(
                        pg[:, 0, :], Vtm[:, mb, ec * 128:(ec + 1) * 128],
                        bias8[:, mb, :], start=(mb == 0), stop=False),
                        waits=w0 if mb == 0 else [], inc=None)
                for pi, piece in enumerate((pA, pB, pC, pD)):
                    tok = P.emit("T", lambda e, ec=ec, pg=pg, piece=piece,
                                 c0=c0: e.matmul(
                        pg[:, 0, :], piece[:, ec, :], Q8[:, ec, c0:c0 + QT],
                        start=False, stop=(piece is pD)),
                        waits=[tpc[ec], t_sums] if pi == 0 else [],
                        inc="sT" if pi == 3 else None)
                sc3 = ab2 if ec % 2 == 0 else mneg2
                t3 = P.emit("V", lambda e, ec=ec, pg=pg, sc3=sc3:
                            e.scalar_tensor_tensor(
                    out=sc3[:, :], in0=ksb[:, :],
                    scalar=smalls[:, 120 + ec:121 + ec], in1=pg[:, 0, :],
                    op0=ALU.mult, op1=ALU.add),
                    waits=[tok, ta2[-1], st.get(("ab2f", ec % 2)),
                           tpc[-1]], inc="sV", drain=True)
                tso = P.emit("V", lambda e, ec=ec, sc3=sc3: e.tensor_scalar(
                    out=spkout[:, ec, :], in0=sc3[:, :],
                    scalar1=smalls[:, 136 + ec:137 + ec],
                    scalar2=smalls[:, 112:113],
                    op0=ALU.add, op1=ALU.is_ge),
                    waits=[t_thr], inc="sV", drain=True)
                st[("ab2f", ec % 2)] = tso
                st[("so", ec)] = tso
                st["ppfree"][g % 2] = t3
                st["ppg"] += 1
                st["bias8f"] = tok

            def evac_ob(m, pg, waits):
                t_ = P.emit("S", lambda e, m=m, pg=pg: e.activation(
                    out=opre[:, m, :], in_=pg[:, 0, :], func=ACT.Identity,
                    bias=vecs[:, VEC["bo"] + m:VEC["bo"] + m + 1]),
                    waits=waits, inc="sS", drain=True)
                st[("pv", m)] = t_
                return t_

            gemm(wo, KC, list(range(KC)), lambda k: spkout[:, k, :], 0, QT,
                 evac_ob, waits0=[st[("so", m)] for m in range(8)], wk=wot)
            dsp = ln_spike(opre, [st[("pv", m)] for m in range(KC)],
                           VEC["go"], 48, lambda k: spkout[:, k, :], spkout)
            for k in range(KC):
                P.emit("V", lambda e, k=k, c0=c0: e.tensor_add(
                    R[:, k, c0:c0 + QT], R[:, k, c0:c0 + QT], spkout[:, k, :]),
                    waits=[dsp[k]])
        P.emit("V", lambda e: e.nop(), inc="sV", drain=True)
        AB.release()
        AB.release()

    subs = [sub_mamba, sub_ffn0, sub_attnb0, sub_ffn0,
            sub_l1a, sub_l1ffn, sub_l1b, sub_l1ffn]
    for f in subs[:NSUB]:
        f()
        st["sl_last_mm"] = st.get("last_mm")
        P.barrier()

    t_out = P.emit("G", lambda e: e.dma_start(
        out=outT.rearrange("(c p) t -> p c t", p=128), in_=R[:]), inc="dG")
    P.emit("G", lambda e: e.nop(), waits=[t_out])



def _pack_vec(v):
    v = np.asarray(v, np.float32)
    return v.reshape(v.shape[0] // 128, 128).T


def make_vecs(inp):
    cols = np.zeros((128, VEC_COLS), np.float32)

    def put(name, arr):
        a = _pack_vec(arr)
        cols[:, VEC[name]:VEC[name] + a.shape[1]] = a

    cw = np.asarray(inp["L0_attn_a_conv_w"], np.float32).reshape(E_INNER, 4)
    for c in range(16):
        for j in range(4):
            cols[:, VEC["convw"] + c * 4 + j] = cw[c * 128:(c + 1) * 128, j]
    put("convb", inp["L0_attn_a_conv_b"])
    put("mu_r", inp["L0_ffn_mu_r"]); put("mu_k", inp["L0_ffn_mu_k"])
    put("b_in", inp["L0_attn_b_b_in"]); put("A0", inp["L0_attn_b_A"])
    put("C0", inp["L0_attn_b_C"]); put("b_out", inp["L0_attn_b_b_out"])
    for nm, key in [("bq", "L1_attn_b_bq"), ("bk", "L1_attn_b_bk"),
                    ("bv", "L1_attn_b_bv"), ("bo", "L1_attn_b_bo"),
                    ("gq", "L1_attn_b_gq"), ("gk", "L1_attn_b_gk"),
                    ("gv", "L1_attn_b_gv"), ("go", "L1_attn_b_go"),
                    ("betaq", "L1_attn_b_betaq"), ("betak", "L1_attn_b_betak"),
                    ("betav", "L1_attn_b_betav"), ("betao", "L1_attn_b_betao")]:
        put(nm, inp[key])
    cols[:, VEC["scale_b"]] = np.float32(np.asarray(inp["L1_attn_b_scale"]))
    put("f1b", inp["L1_ffn_bias"])
    return cols


def make_biasmx():
    i = np.arange(T, dtype=np.float64)
    dist = np.abs(i[:, None] - i[None, :])
    b = np.maximum(np.ceil(np.log2((T - 1.0) / (dist + 1.0))), 0.0)
    r32 = (np.float32(T - 1.0) / (dist.astype(np.float32) + np.float32(1.0)))
    b32 = np.maximum(np.ceil(np.log2(r32)), 0.0)
    assert np.array_equal(b, b32.astype(np.float64))
    return b.astype(ml_dtypes.float8_e4m3)


_NC_CACHE = {}


def kernel(**inp):
    x = np.asarray(inp["L0_attn_a_hidden_states"], np.float32)
    B = x.shape[0]
    if "p" not in _NC_CACHE:
        _NC_CACHE["p"] = build_program()
    nc = _NC_CACHE["p"]

    def ct(a):
        return np.ascontiguousarray(np.asarray(a, np.float32))

    shared = {
        "w_inproj": ct(np.asarray(inp["L0_attn_a_in_proj_w"])[:E_INNER, :].T),
        "w_outproj": ct(np.asarray(inp["L0_attn_a_out_proj_w"]).T),
        "w0r": ct(np.asarray(inp["L0_ffn_W_r"]).T),
        "w0k": ct(np.asarray(inp["L0_ffn_W_k"]).T),
        "w0v": ct(np.asarray(inp["L0_ffn_W_v"]).T),
        "w_in0": ct(inp["L0_attn_b_W_in"]),
        "w_out0": ct(inp["L0_attn_b_W_out"]),
        "wqa": ct(np.asarray(inp["L1_attn_a_W_q"]).T),
        "wka": ct(np.asarray(inp["L1_attn_a_W_k"]).T),
        "wva": ct(np.asarray(inp["L1_attn_a_W_v"]).T),
        "woa": ct(np.asarray(inp["L1_attn_a_W_o"]).T),
        "wqb": ct(np.asarray(inp["L1_attn_b_Wq"]).T),
        "wkb": ct(np.asarray(inp["L1_attn_b_Wk"]).T),
        "wvb": ct(np.asarray(inp["L1_attn_b_Wv"]).T),
        "wob": ct(np.asarray(inp["L1_attn_b_Wo"]).T),
        "wf1": ct(inp["L1_ffn_weight"]),
        "wf2": ct(inp["L1_ffn_aux_weight1"]),
        "wf3": ct(inp["L1_ffn_aux_weight2"]),
        "vecs": make_vecs(inp),
        "ident": np.eye(128, dtype=np.float32),
        "biasmx": make_biasmx(),
    }
    in_maps = []
    for b in range(B):
        m = dict(shared)
        m["xT"] = ct(x[b].T)
        in_maps.append(m)
    res = run_bass_kernel_spmd(nc, in_maps, core_ids=list(range(B)))
    out = np.stack([np.asarray(r["outT"], np.float32).T for r in res.results])
    return out
